# version 02
# baseline (speedup 1.0000x reference)
import numpy as np
import ml_dtypes
from contextlib import ExitStack

import concourse.bass as bass
import concourse.mybir as mybir
from concourse.bass_utils import run_bass_kernel_spmd
from concourse.aot_env import allow_ici_all_to_all

F32 = mybir.dt.float32
BF16 = mybir.dt.bfloat16
AF = mybir.ActivationFunctionType
ALU = mybir.AluOpType
AX = mybir.AxisListType

NCORE = 8
D = 4096
SEQ = 4096
HD = 128
CH = 512
NTOK = 1024
NT = 8
NJ = 32
NHKV = 8
Q_LO, Q_HI = 611, 4095
NQTOK = Q_HI - Q_LO
EPS = 1e-5
LN105 = float(np.log(1.05))
THETA = 10000.0

DBG = False
PHASES = 7
OLIM = 8
AFM_ONLY = False
FINMODE = 3


class Sem:
    __slots__ = ("h", "n", "name", "vmap")

    def __init__(self, h, name):
        self.h, self.n, self.name = h, 0, name
        self.vmap = {}


class Prog:

    SELF_ENGS = ("vector", "scalar")

    def __init__(self, nc, ctx):
        self.nc, self.ctx = nc, ctx
        self.cmds = {e: [] for e in ("tensor", "vector", "scalar", "gpsimd", "sync")}
        self.selfsem = {e: Sem(ctx.enter_context(nc.semaphore(f"self_{e}")),
                               f"self_{e}") for e in self.SELF_ENGS}
        self.owner = {}

    def sem(self, name):
        return Sem(self.ctx.enter_context(self.nc.semaphore(name)), name)

    def _resolve(self, sm, vv):
        eng = self.owner.get(sm.name)
        if eng is None:
            return (sm.h, vv)
        assert vv in sm.vmap, f"no vmap[{vv}] for {sm.name} (owner {eng})"
        return (self.selfsem[eng].h, sm.vmap[vv])

    def op(self, eng, meth, *args, waits=(), inc=None, inc_amt=1, future=False,
           **kwargs):
        wl = []
        for (sm, v) in waits:
            vv = sm.n if v is None else v
            if not future:
                assert vv <= sm.n, f"wait {sm.name}>={vv} but emitted {sm.n}"
            if vv > 0:
                wl.append((sm, vv))
        is_dma = meth in ("dma_start", "dma_start_transpose")
        ih = None
        if eng in self.SELF_ENGS:
            ss = self.selfsem[eng]
            if ss.n > 0:
                wl.insert(0, (ss, ss.n))
            if not is_dma:
                ss.n += 1
                ih = (ss.h, 1)
                if inc is not None:
                    assert inc_amt == 1
                    inc.n += 1
                    o = self.owner.setdefault(inc.name, eng)
                    assert o == eng, f"{inc.name} inc'd from {o} and {eng}"
                    inc.vmap[inc.n] = ss.n
            else:
                if inc is not None:
                    inc.n += (inc_amt if inc_amt is not None else 1)
                    ih = (inc.h, inc_amt)
        else:
            if inc is not None:
                assert inc.name not in self.owner, \
                    f"{inc.name} owned by {self.owner.get(inc.name)}"
                inc.n += (inc_amt if inc_amt is not None else 1)
                ih = (inc.h, inc_amt)

        def run(e, meth=meth, args=args, kwargs=kwargs, wl=wl, ih=ih):
            for (sm, v) in wl:
                h, v = self._resolve(sm, v)
                e.wait_ge(h, v)
            ins = getattr(e, meth)(*args, **kwargs)
            if ih is not None:
                if ih[1] is None or ih[1] == 1:
                    ins.then_inc(ih[0], 1)
                else:
                    ins.then_inc(ih[0], ih[1])
        self.cmds[eng].append(run)

    def coll(self, kind, op_, rg, ins_, outs_, waits=(), inc=None):
        wl = []
        for (sm, v) in waits:
            vv = sm.n if v is None else v
            if vv > 0:
                wl.append((sm, vv))
        inc.n += 1

        def run(e, wl=wl, ih=inc.h):
            for (sm, v) in wl:
                h, v = self._resolve(sm, v)
                e.wait_ge(h, v)
            with allow_ici_all_to_all():
                e.collective_compute(kind, op_, rg, ins_, outs_).then_inc(ih)
        self.cmds["gpsimd"].append(run)

    def bump(self, eng, sem, amt=1):
        raise RuntimeError("standalone bumps are forbidden (writeback race)")


def build_program(dbg=False):
    nc = bass.Bass("TRN2", target_bir_lowering=False)
    ctx = ExitStack()

    def din(name, shape, dt=F32):
        return nc.dram_tensor(name, shape, dt, kind="ExternalInput").ap()

    x_my = din("x_my", [NTOK, D])
    Wq = din("Wq", [D, D]); Wk = din("Wk", [D, 1024]); Wv = din("Wv", [D, 1024])
    Wo = din("Wo", [D, D]); WfT = din("WfT", [D, D])
    fb = din("fb", [1, D])
    A2T = din("A2T", [D, 128]); B2T = din("B2T", [128, D])
    rwt = din("rwt", [128, 256]); rb2 = din("rb2", [2, 8])
    cosq_i = din("cosq", [128, NTOK]); sinq_i = din("sinq", [128, NTOK])
    cosk_i = din("cosk", [128, NTOK]); sink_i = din("sink", [128, NTOK])
    masks_i = din("masks", [512, 512], BF16)
    identb_i = din("identb", [128, 128], BF16)
    identf_i = din("identf", [128, 128], F32)
    onescol_i = din("onescol", [128, 1], BF16)
    onesrow_i = din("onesrow", [1, 128], BF16)
    onesrow_f_i = din("onesrow_f", [1, 128], F32)
    rep_i = din("rep_f", [8, 128], F32)
    rtok_i = din("rtok", [1, NTOK], BF16)

    out_my = nc.dram_tensor("out_my", [NTOK, D], F32, kind="ExternalOutput").ap()

    kv_in = nc.dram_tensor("kv_in", [NCORE * NTOK, 256], BF16, kind="Internal").ap()
    kv_out = nc.dram_tensor("kv_out", [NCORE * NTOK, 256], BF16, kind="Internal").ap()
    q_in = nc.dram_tensor("q_in", [D, NTOK], BF16, kind="Internal").ap()
    q_out = nc.dram_tensor("q_out", [D, NTOK], BF16, kind="Internal").ap()
    ao_in = nc.dram_tensor("ao_in", [NCORE * NTOK, 512], BF16, kind="Internal").ap()
    ao_out = nc.dram_tensor("ao_out", [NCORE * NTOK, 512], BF16, kind="Internal").ap()
    r_in = nc.dram_tensor("r_in", [128, 64], F32, kind="Internal").ap()
    r_out = nc.dram_tensor("r_out", [128, 64], F32, kind="Internal").ap()

    dbg_o = {}
    if dbg:
        for nm_, shp, dt in (("d_xnfm", [128, NJ * NTOK], F32),
                             ("d_q", [D, NTOK], F32),
                             ("d_kv", [NCORE * NTOK, 256], F32),
                             ("d_ao", [NCORE * NTOK, 512], F32),
                             ("d_rout", [2, 8], F32),
                             ("d_rdbg", [2, 32], F32),
                             ("d_attn", [128, NJ * NTOK], F32)):
            dbg_o[nm_] = nc.dram_tensor(nm_, shp, dt, kind="ExternalOutput").ap()

    def sb(name, shape, dt):
        return nc.alloc_sbuf_tensor(name, shape, dt)

    xn_fm = sb("xn_fm", [128, NJ * NTOK], BF16)
    attn_fm = sb("attn_fm", [128, NJ * NTOK], BF16)
    tab = sb("tab", [128, 2048], F32)
    masks = sb("masks_s", [128, 2048], BF16)
    identb = sb("identb_s", [128, 128], BF16)
    identf = sb("identf_s", [128, 128], F32)
    onescol = sb("onescol_s", [128, 1], BF16)
    onesrow = sb("onesrow_s", [1, 128], BF16)
    onesrow_f = sb("onesrow_f_s", [1, 128], F32)
    rep_f = sb("rep_f_s", [8, 128], F32)
    rwt_sb = sb("rwt_sb", [128, 256], F32)
    rb2_sb = sb("rb2_sb", [2, 8], F32)
    tfm = sb("tfm", [128, NTOK], BF16)
    tsc = sb("tsc", [128, NTOK], BF16)
    kn_sb = sb("kn_sb", [1, 8], F32)
    kk = sb("kk", [128, 512], F32)
    rsum_sb = sb("rsum_sb", [128, 64], F32)
    rtok_sb = sb("rtok_sb", [1, NTOK], BF16)
    qn2_sb = sb("qn2_sb", [1, 4], F32)
    mtmp = sb("mtmp", [1, 4], F32)
    mtmp_bf = sb("mtmp_bf", [1, 2], BF16)
    K_res = sb("K_res", [128, 4096], BF16)
    lsum = sb("lsum", [128, 1], F32); lsq = sb("lsq", [128, 1], F32)
    lmean = sb("lmean", [128, 1], F32); lvar = sb("lvar", [128, 1], F32)
    lrstd = sb("lrstd", [128, 1], F32); lnegmr = sb("lnegmr", [128, 1], F32)
    ltmp = sb("ltmp", [128, 1], F32)
    logit_sb = sb("logit_sb", [2, 8], F32); rmax = sb("rmax", [2, 1], F32)
    rexp = sb("rexp", [2, 8], F32); rsumt = sb("rsumt", [2, 1], F32)
    rrec = sb("rrec", [2, 1], F32); rsm = sb("rsm", [2, 8], F32)
    rmax2 = sb("rmax2", [2, 1], F32)
    rdbg = sb("rdbg", [2, 32], F32)
    routT = sb("routT", [8, 2], F32)
    svec = sb("svec", [128, 2], F32)
    wo_t = [sb(f"wo_t{i}", [128, 512], BF16) for i in range(2)]
    wf_t = [sb(f"wf_t{i}", [128, 512], BF16) for i in range(2)]
    xres = [sb(f"xres{i}", [128, 512], F32) for i in range(2)]
    outsb = [sb(f"outsb{i}", [128, 512], F32) for i in range(2)]
    WBUF = sb("WBUF", [128, 4096], BF16)

    ARENA = sb("ARENA", [128, 6144], F32)
    ab = nc.lookup_mloc(ARENA).addr

    def at_(name, shape, dt, off):
        nb = shape[1] * (4 if dt == F32 else 2)
        assert off + nb <= 24576, name
        return nc.alloc_sbuf_tensor_at(name, shape, dt, offset=ab + off)

    XBUF = at_("XBUF", [128, 4096], F32, 0)
    xn_tm = at_("xn_tm", [128, 4096], BF16, 16384)
    rtok_bc = at_("rtok_bc", [128, 1024], F32, 0)
    K_sb = at_("K_sb", [128, 1024], BF16, 0)
    V_sb = at_("V_sb", [128, 1024], BF16, 2048)
    ktm = at_("ktm", [128, 1024], BF16, 4096)
    vtm = at_("vtm", [128, 1024], BF16, 6144)
    rtmp1 = at_("rtmp1", [128, 1024], F32, 8192)
    rtmp2 = at_("rtmp2", [128, 1024], F32, 12288)
    qstage = [at_(f"qstage{i}", [128, 1024], BF16, 2048 * i) for i in range(2)]
    At = at_("At", [128, 4096], BF16, 16384)
    qsb = at_("qsb", [128, 2048], BF16, 0)
    vt = [at_(f"vt{i}", [128, 512], BF16, 4096 + 1024 * i) for i in range(2)]
    pt = [at_(f"pt{i}", [128, 512], BF16, 6144 + 1024 * i) for i in range(3)]
    pm = [at_(f"pm{i}", [128, 512], BF16, 9216 + 1024 * i) for i in range(2)]
    linvb = at_("linvb", [128, 1024], F32, 11264)
    qq = at_("qq", [128, 512], BF16, 15360)
    qn_row = at_("qn_row", [1, 1024], F32, 16384)
    negm = at_("negm", [128, 4], F32, 20480)
    aosb = [at_(f"aosb{i}", [128, 512], BF16, 20512 + 1024 * i) for i in range(2)]
    fbb = at_("fbb", [1, 4096], BF16, 0)
    bcat = at_("bcat", [128, 4096], BF16, 8192)

    PT = [ctx.enter_context(nc.psum_tensor(f"PT{i}", [128, 1024], F32)) for i in range(4)]
    PA = [p.ap() for p in PT]
    PB = [p.ap().bitcast(BF16) for p in PT]

    P = Prog(nc, ctx)
    O, B_ = P.op, P.bump
    RG = [list(range(NCORE))]

    s = {n: P.sem(n) for n in (
        "wkv0", "wkv1", "wq0", "wq1", "wq2", "wq3", "wo0", "wo1", "wf0", "wf1",
        "xr0", "xr1", "vt0", "vt1", "qs0", "qs1", "aos0", "aos1", "ouF0", "ouF1",
        "collR", "collKV", "collQ", "collAO",
        "cst", "conv", "x", "xfree", "lnv", "lna", "xn", "xnfree",
        "trA", "trAF", "phA", "rtr", "rtrG", "rtrA2", "rtrT",
        "wkv", "wkvF", "kvt", "kve", "kvs",
        "wq", "wqF", "qp", "qe", "qs",
        "coll", "misc", "trK", "trKF", "tfm", "tfmT",
        "kres", "vt", "vtF", "qld", "qn", "nm", "nmA", "nmT",
        "sc", "scF", "exp", "ptF", "msk", "pmF", "lpv", "pvd", "aev", "aos",
        "afm", "wo", "woF", "wf", "wfF", "xr", "xrF", "ou", "ouV", "ouF",
        "dbg", "dbgV")}

    pt2_free = {0: None, 1: None}

    def pt2_wait(bank):
        f = pt2_free[bank]
        return [f] if f is not None else []

    for dst, src in ((identb[:], identb_i), (identf[:], identf_i),
                     (onescol[:], onescol_i), (onesrow[:], onesrow_i),
                     (onesrow_f[:], onesrow_f_i), (rep_f[:], rep_i),
                     (rwt_sb[:], rwt), (rb2_sb[:], rb2), (rtok_sb[:], rtok_i)):
        O("gpsimd", "dma_start", out=dst, in_=src, inc=s["cst"], inc_amt=16)
    O("gpsimd", "dma_start",
      out=masks[:].rearrange("p (m f) -> p m f", f=512),
      in_=masks_i.rearrange("(m p) f -> p m f", p=128), inc=s["cst"], inc_amt=16)
    O("gpsimd", "dma_start", out=tab[:, 0:1024], in_=cosk_i, inc=s["cst"], inc_amt=16)
    O("gpsimd", "dma_start", out=tab[:, 1024:2048], in_=sink_i, inc=s["cst"], inc_amt=16)
    N_CST = s["cst"].n

    TRSLOT = [(PB[0], 0), (PB[0], 1024), (PB[1], 0), (PB[1], 1024)]
    trb = 0
    for tt in range(NT):
        O("sync", "dma_start", out=XBUF[:], in_=x_my[tt * 128:(tt + 1) * 128, :],
          waits=[(s["xn"], None)], inc=s["x"], inc_amt=16)
        O("vector", "reduce_sum", lsum[:], XBUF[:], axis=AX.X,
          waits=[(s["x"], None)], inc=s["lnv"])
        O("scalar", "activation", xn_tm[:], XBUF[:], AF.Square, accum_out=lsq[:],
          waits=[(s["x"], None)] + ([(s["trAF"], 8 * tt)] if tt else []),
          inc=s["lna"])
        O("vector", "tensor_scalar_mul", lmean[:], lsum[:], 1.0 / D, inc=s["lnv"])
        O("vector", "tensor_scalar_mul", lvar[:], lsq[:], 1.0 / D,
          waits=[(s["lna"], None)], inc=s["lnv"])
        O("vector", "tensor_mul", ltmp[:], lmean[:], lmean[:], inc=s["lnv"])
        O("vector", "tensor_sub", lvar[:], lvar[:], ltmp[:], inc=s["lnv"])
        O("vector", "tensor_scalar_add", lvar[:], lvar[:], EPS, inc=s["lnv"])
        O("scalar", "activation", ltmp[:], lvar[:], AF.Sqrt,
          waits=[(s["lnv"], None)], inc=s["lna"])
        O("vector", "reciprocal", lrstd[:], ltmp[:], waits=[(s["lna"], None)],
          inc=s["lnv"])
        O("vector", "tensor_scalar", lnegmr[:], lmean[:], lrstd[:, 0:1], -1.0,
          ALU.mult, ALU.mult, inc=s["lnv"])
        O("scalar", "activation", xn_tm[:], XBUF[:], AF.Identity,
          scale=lrstd[:, 0:1], bias=lnegmr[:, 0:1],
          waits=[(s["lnv"], None)], inc=s["xn"])
        for b8 in range(8):
            ps, off = TRSLOT[trb % 4]
            for q in range(4):
                j = b8 * 4 + q
                wts = []
                if q == 0:
                    wts = [(s["xn"], None)]
                    if trb >= 4:
                        wts.append((s["trAF"], trb - 3))
                    if trb == 0:
                        wts.append((s["cst"], N_CST))
                O("tensor", "transpose", ps[:, off + q * 128: off + (q + 1) * 128],
                  xn_tm[:, j * 128:(j + 1) * 128], identb[:], waits=wts,
                  inc=(s["trA"] if q == 3 else None))
            dst = xn_fm[:].rearrange("p (j t) -> p j t", t=NTOK)[
                :, b8 * 4:(b8 + 1) * 4, tt * 128:(tt + 1) * 128]
            src = ps[:, off:off + 512].rearrange("p (j t) -> p j t", t=128)
            O("vector", "tensor_copy", dst, src, waits=[(s["trA"], None)],
              inc=s["trAF"])
            trb += 1

    def _finish():
        with nc.Block() as block:
            @block.tensor
            def _(eng):
                for fn in P.cmds["tensor"]:
                    fn(eng)

            @block.vector
            def _(eng):
                for fn in P.cmds["vector"]:
                    fn(eng)

            @block.scalar
            def _(eng):
                for fn in P.cmds["scalar"]:
                    fn(eng)

            @block.gpsimd
            def _(eng):
                for fn in P.cmds["gpsimd"]:
                    fn(eng)

            @block.sync
            def _(eng):
                for fn in P.cmds["sync"]:
                    fn(eng)
        nc._keepalive_ctx = ctx
        return nc

    if PHASES <= 1:
        return _finish()

    for half in range(2):
        O("tensor", "matmul", PA[2][:, half * 512:(half + 1) * 512],
          onesrow[:], rtok_sb[0:1, half * 512:(half + 1) * 512],
          start=True, stop=True, skip_group_check=True,
          waits=([(s["cst"], N_CST), (s["trAF"], NT * 8)] if half == 0 else []),
          inc=(s["rtrT"] if half == 1 else None))
    O("vector", "tensor_copy", rtok_bc[:], PA[2][:, 0:1024],
      waits=[(s["rtrT"], None)], inc=s["rtr"])
    pt2_free[0] = (s["rtr"], s["rtr"].n)
    pt2_free[1] = (s["rtr"], s["rtr"].n)
    for b in range(2):
        for j in range(NJ):
            src = xn_fm[:, j * NTOK + b * 512: j * NTOK + (b + 1) * 512]
            wts = [(s["rtr"], None)] if (b == 0 and j == 0) else []
            O("vector", "tensor_mul", kk[:], src,
              rtok_bc[:, b * 512:(b + 1) * 512], waits=wts)
            O("vector", "reduce_sum", rsum_sb[:, b * 32 + j: b * 32 + j + 1],
              kk[:], axis=AX.X,
              inc=(s["rtr"] if (b == 1 and j == NJ - 1) else None))
    O("gpsimd", "dma_start", out=r_in, in_=rsum_sb[:],
      waits=[(s["rtr"], None)], inc=s["rtrG"], inc_amt=16)
    P.coll("AllReduce", ALU.add, RG, [r_in.opt()], [r_out.opt()],
           waits=[(s["rtrG"], None)], inc=s["collR"])

    if PHASES <= 2:
        return _finish()

    trk = 0
    for h in range(NHKV):
        for g4 in range(4):
            slot = g4 % 2
            ksrc = Wk[:, h * 128:(h + 1) * 128].rearrange(
                "(j p) i -> p j i", p=128)[:, g4 * 8:(g4 + 1) * 8, :]
            vsrc = Wv[:, h * 128:(h + 1) * 128].rearrange(
                "(j p) i -> p j i", p=128)[:, g4 * 8:(g4 + 1) * 8, :]
            kdst = WBUF[:, slot * 1024:(slot + 1) * 1024].rearrange(
                "p (j i) -> p j i", i=128)
            vdst = WBUF[:, 2048 + slot * 1024: 2048 + (slot + 1) * 1024].rearrange(
                "p (j i) -> p j i", i=128)
            gidx = h * 4 + g4
            wts = [(s["wkvF"], gidx - 1)] if gidx >= 2 else []
            O("gpsimd", "dma_start", out=kdst, in_=ksrc, waits=wts,
              inc=s[f"wkv{slot}"], inc_amt=16, future=True)
            O("gpsimd", "dma_start", out=vdst, in_=vsrc,
              inc=s[f"wkv{slot}"], inc_amt=16)
        for g4 in range(4):
            slot = g4 % 2
            for jj in range(8):
                j = g4 * 8 + jj
                wts = []
                if jj == 0:
                    wts.append((s[f"wkv{slot}"], ((h * 4 + g4) // 2 + 1) * 32))
                    if j == 0:
                        wts.append((s["trAF"], NT * 8))
                        if h > 0:
                            wts.append((s["kve"], None))
                for half in range(2):
                    O("tensor", "matmul", PA[0][:, half * 512:(half + 1) * 512],
                      WBUF[:, slot * 1024 + jj * 128: slot * 1024 + (jj + 1) * 128],
                      xn_fm[:, j * NTOK + half * 512: j * NTOK + (half + 1) * 512],
                      start=(j == 0), stop=(j == NJ - 1), skip_group_check=True,
                      waits=(wts if half == 0 else []))
                    wts = []
                    O("tensor", "matmul", PA[1][:, half * 512:(half + 1) * 512],
                      WBUF[:, 2048 + slot * 1024 + jj * 128:
                           2048 + slot * 1024 + (jj + 1) * 128],
                      xn_fm[:, j * NTOK + half * 512: j * NTOK + (half + 1) * 512],
                      start=(j == 0), stop=(j == NJ - 1), skip_group_check=True,
                      inc=(s["wkvF"] if (jj == 7 and half == 1) else None))
        wts = [(s["wkvF"], 4 * (h + 1))]
        if h >= 1:
            wts.append((s["kvs"], None))
        O("vector", "tensor_mul", rtmp1[0:64, :], PA[0][64:128, :],
          tab[0:64, 1024:2048], waits=wts)
        O("vector", "tensor_mul", rtmp1[64:128, :], PA[0][0:64, :],
          tab[64:128, 1024:2048])
        O("vector", "tensor_mul", rtmp2[:], PA[0][:, :], tab[:, 0:1024])
        O("vector", "tensor_add", K_sb[:], rtmp2[:], rtmp1[:], inc=s["kve"])
        O("vector", "tensor_copy", V_sb[:], PA[1][:, :], inc=s["kve"])
        for src_sb, dst_sb in ((K_sb, ktm), (V_sb, vtm)):
            for b4 in range(2):
                ps, off = (PB[2], 0) if trk % 2 == 0 else (PB[2], 1024)
                for q in range(4):
                    tb = b4 * 4 + q
                    wts = []
                    if q == 0:
                        wts = [(s["kve"], None)]
                        if trk >= 2:
                            wts.append((s["trKF"], trk - 1))
                        else:
                            wts += pt2_wait(trk % 2)
                    O("tensor", "transpose",
                      ps[:, off + q * 128: off + (q + 1) * 128],
                      src_sb[:, tb * 128:(tb + 1) * 128], identb[:], waits=wts,
                      inc=(s["trK"] if q == 3 else None))
                dst = dst_sb[:].rearrange("p (t i) -> p t i", i=128)[
                    :, b4 * 4:(b4 + 1) * 4, :]
                src = ps[:, off:off + 512].rearrange("p (t i) -> p t i", i=128)
                O("vector", "tensor_copy", dst, src, waits=[(s["trK"], None)],
                  inc=s["trKF"])
                trk += 1
        pt2_free[0] = (s["trKF"], s["trKF"].n)
        pt2_free[1] = (s["trKF"], s["trKF"].n)
        kdst = kv_in[h * NTOK:(h + 1) * NTOK, 0:128].rearrange(
            "(t p) c -> p t c", p=128)
        vdst = kv_in[h * NTOK:(h + 1) * NTOK, 128:256].rearrange(
            "(t p) c -> p t c", p=128)
        O("gpsimd", "dma_start", out=kdst,
          in_=ktm[:].rearrange("p (t i) -> p t i", i=128),
          waits=[(s["trKF"], None)], inc=s["kvs"], inc_amt=16)
        O("gpsimd", "dma_start", out=vdst,
          in_=vtm[:].rearrange("p (t i) -> p t i", i=128),
          inc=s["kvs"], inc_amt=16)
    P.coll("AllToAll", ALU.bypass, RG, [kv_in.opt()], [kv_out.opt()],
           waits=[(s["kvs"], None)], inc=s["collKV"])

    O("gpsimd", "dma_start", out=tab[:, 0:1024], in_=cosq_i,
      waits=[(s["kve"], None)], inc=s["cst"], inc_amt=16)
    O("gpsimd", "dma_start", out=tab[:, 1024:2048], in_=sinq_i,
      inc=s["cst"], inc_amt=16)
    TAB_Q = s["cst"].n

    if PHASES <= 3:
        return _finish()

    for g in range(16):
        QPS = (PA[0], PA[1]) if g % 2 == 0 else (PA[2], PA[3])
        for g8 in range(8):
            slot = g8 % 4
            src = Wq[:, g * 256:(g + 1) * 256].rearrange(
                "(j p) i -> p j i", p=128)[:, g8 * 4:(g8 + 1) * 4, :]
            dst = WBUF[:, slot * 1024:(slot + 1) * 1024].rearrange(
                "p (j i) -> p j i", i=256)
            gidx = g * 8 + g8
            wts = []
            if gidx == 0:
                wts = [(s["wkvF"], 32)]
            elif gidx >= 4:
                wts = [(s["wqF"], gidx - 3)]
            O("gpsimd", "dma_start", out=dst, in_=src, waits=wts,
              inc=s[f"wq{slot}"], inc_amt=16, future=True)
        for g8 in range(8):
            slot = g8 % 4
            for jj in range(4):
                j = g8 * 4 + jj
                wts = []
                if jj == 0:
                    wts.append((s[f"wq{slot}"], ((g * 8 + g8) // 4 + 1) * 16))
                    if j == 0:
                        if g >= 2:
                            wts.append((s["qe"], 2 * (g - 1)))
                        if g == 0:
                            wts.append((s["kve"], None))
                            wts.append((s["cst"], TAB_Q))
                        if g == 1:
                            wts.append((s["trKF"], None))
                for hp in range(2):
                    for half in range(2):
                        O("tensor", "matmul",
                          QPS[hp][:, half * 512:(half + 1) * 512],
                          WBUF[:, slot * 1024 + jj * 256 + hp * 128:
                               slot * 1024 + jj * 256 + (hp + 1) * 128],
                          xn_fm[:, j * NTOK + half * 512:
                                j * NTOK + (half + 1) * 512],
                          start=(j == 0), stop=(j == NJ - 1),
                          skip_group_check=True, waits=wts,
                          inc=(s["wqF"] if (jj == 3 and hp == 1 and half == 1)
                               else None))
                        wts = []
        for hp in range(2):
            wts = [(s["wqF"], 8 * (g + 1))]
            if g > 0:
                wts.append((s[f"qs{hp}"], 16 * g))
            if g == 0 and hp == 0:
                wts.append((s["kvs"], None))
            O("vector", "tensor_mul", rtmp1[0:64, :], QPS[hp][64:128, :],
              tab[0:64, 1024:2048], waits=wts)
            O("vector", "tensor_mul", rtmp1[64:128, :], QPS[hp][0:64, :],
              tab[64:128, 1024:2048])
            O("vector", "tensor_mul", rtmp2[:], QPS[hp][:, :], tab[:, 0:1024])
            O("vector", "tensor_add", qstage[hp][:], rtmp2[:], rtmp1[:],
              inc=s["qe"])
            h = 2 * g + hp
            O("gpsimd", "dma_start", out=q_in[h * 128:(h + 1) * 128, :],
              in_=qstage[hp][:], waits=[(s["qe"], None)],
              inc=s[f"qs{hp}"], inc_amt=16)
    P.coll("AllToAll", ALU.bypass, RG, [q_in.opt()], [q_out.opt()],
           waits=[(s["qs0"], None), (s["qs1"], None)], inc=s["collQ"])

    if PHASES <= 4:
        return _finish()

    O("gpsimd", "dma_start",
      out=At[:].rearrange("p (j i) -> p j i", i=128),
      in_=A2T.rearrange("(j p) i -> p j i", p=128),
      waits=[(s["qs0"], None), (s["qs1"], None)], inc=s["misc"], inc_amt=16)
    for j in range(NJ):
        for half in range(2):
            wts = []
            if j == 0 and half == 0:
                wts = [(s["misc"], None), (s["qe"], None)]
            O("tensor", "matmul", PA[0][:, half * 512:(half + 1) * 512],
              At[:, j * 128:(j + 1) * 128],
              xn_fm[:, j * NTOK + half * 512: j * NTOK + (half + 1) * 512],
              start=(j == 0), stop=(j == NJ - 1), skip_group_check=True,
              waits=wts, inc=(s["tfmT"] if (j == NJ - 1 and half == 1) else None))
    O("vector", "tensor_copy", tfm[:, 0:512], PA[0][:, 0:512],
      waits=[(s["tfmT"], None)])
    O("vector", "tensor_copy", tfm[:, 512:1024], PA[0][:, 512:1024], inc=s["tfm"])

    O("gpsimd", "dma_start", out=rsum_sb[:], in_=r_out,
      waits=[(s["collR"], 1)], inc=s["rtrG"], inc_amt=16)
    O("vector", "tensor_scalar_mul", rsum_sb[:], rsum_sb[:], 1.0 / NQTOK,
      waits=[(s["rtrG"], None)], inc=s["rtr"])
    rq = rsum_sb[:].rearrange("p (b j) -> p j b", b=2)
    for j in range(NJ):
        wts = ([(s["rtr"], None), (s["qe"], None)] + pt2_wait(0)) if j == 0 else []
        O("tensor", "matmul", PA[2][0:2, 0:8], rq[:, j, :],
          rwt_sb[:, j * 8:(j + 1) * 8], start=(j == 0), stop=(j == NJ - 1),
          skip_group_check=True, waits=wts,
          inc=(s["rtrT"] if j == NJ - 1 else None))
    O("vector", "tensor_add", logit_sb[:], PA[2][0:2, 0:8], rb2_sb[:],
      waits=[(s["rtrT"], None)], inc=s["rtr"])
    O("vector", "reduce_max", rmax[:], logit_sb[:], axis=AX.X, inc=s["rtr"])
    O("vector", "tensor_scalar_mul", rmax2[:], rmax[:], -1.0, inc=s["rtr"])
    O("scalar", "activation", rexp[:], logit_sb[:], AF.Exp, bias=rmax2[:, 0:1],
      waits=[(s["rtr"], None)], inc=s["rtrA2"])
    O("vector", "reduce_sum", rsumt[:], rexp[:], axis=AX.X,
      waits=[(s["rtrA2"], None)], inc=s["rtr"])
    O("vector", "reciprocal", rrec[:], rsumt[:], inc=s["rtr"])
    O("vector", "tensor_scalar", rsm[:], rexp[:], rrec[:, 0:1], None, ALU.mult,
      inc=s["rtr"])
    O("tensor", "transpose", PA[3][0:8, 0:2], rsm[:], identf[0:2, 0:2],
      waits=[(s["rtr"], None), (s["qe"], None)], inc=s["rtrT"])
    O("vector", "tensor_copy", routT[:], PA[3][0:8, 0:2],
      waits=[(s["rtrT"], None)], inc=s["rtr"])
    for b in range(2):
        O("tensor", "matmul", PA[2][:, 16 + b:17 + b], rep_f[:],
          routT[0:8, b:b + 1], start=True, stop=True, skip_group_check=True,
          waits=[(s["rtr"], None)], inc=s["rtrT"])
    O("vector", "tensor_copy", svec[:, 0:1], PA[2][:, 16:17],
      waits=[(s["rtrT"], None)], inc=s["rtr"])
    O("vector", "tensor_copy", svec[:, 1:2], PA[2][:, 17:18], inc=s["rtr"])
    pt2_free[0] = (s["rtr"], s["rtr"].n)
    for b in range(2):
        O("vector", "tensor_scalar", tsc[:, b * 512:(b + 1) * 512],
          tfm[:, b * 512:(b + 1) * 512], svec[:, b:b + 1], None, ALU.mult,
          inc=s["tfm"])
    if dbg:
        O("vector", "tensor_copy", rdbg[:, 0:8], logit_sb[:],
          waits=[(s["rtr"], None)])
        O("vector", "tensor_copy", rdbg[:, 8:9], rmax[:])
        O("vector", "tensor_copy", rdbg[:, 9:17], rexp[:])
        O("vector", "tensor_copy", rdbg[:, 17:18], rsumt[:])
        O("vector", "tensor_copy", rdbg[:, 18:19], rrec[:])
        O("vector", "tensor_copy", rdbg[:, 19:27], rsm[:], inc=s["rtr"])
        O("gpsimd", "dma_start", out=dbg_o["d_rout"], in_=rdbg[:, 19:27],
          waits=[(s["rtr"], None)], inc=s["dbg"], inc_amt=16)
        O("gpsimd", "dma_start", out=dbg_o["d_rdbg"], in_=rdbg[:],
          waits=[(s["rtr"], None)], inc=s["dbg"], inc_amt=16)

    if PHASES <= 5:
        return _finish()

    sc_i = 0
    pt_i = 0
    pm_i = 0
    vt_n = 0
    n_pv = 0
    n_msk = 0
    pt_slot_free = {}
    pm_slot_free = {}
    vt_slot_free = {}
    first_vt = True
    for b in range(2):
        for p in range(8):
            wts = [(s["collKV"], 1)]
            if b == 1 and p == 0:
                wts.append((s["pvd"], None))
            O("sync", "dma_start", out=K_res[:, p * 512:(p + 1) * 512],
              in_=kv_out[1024 * p + 512 * b: 1024 * p + 512 * b + 512, 0:128],
              transpose=True, waits=wts, inc=s["kres"], inc_amt=16)
        for p in range(8):
            wts = [(s["kres"], (b * 8 + 8) * 16)]
            if b == 1 and p == 0:
                wts.append((s["aev"], None))
            if b == 0 and p == 0:
                wts.append((s["tfm"], None))
            O("vector", "tensor_mul", qq[:], K_res[:, p * 512:(p + 1) * 512],
              K_res[:, p * 512:(p + 1) * 512], waits=wts, inc=s["qn"])
            O("tensor", "matmul", PA[2][0:1, 0:512], onescol[:], qq[:],
              start=True, stop=True, skip_group_check=True,
              waits=[(s["qn"], None)] + pt2_wait(0), inc=s["nmT"])
            O("vector", "reduce_max", kn_sb[0:1, p:p + 1], PA[2][0:1, 0:512],
              axis=AX.X, waits=[(s["nmT"], None)], inc=s["scF"])
            pt2_free[0] = (s["scF"], s["scF"].n)
        for qp in range(8):
            for hh in range(4):
                wts = []
                if hh == 0:
                    wts = [(s["collQ"], 1)]
                    if b == 0 and qp == 0:
                        wts.append((s["tfm"], None))
                    else:
                        wts.append((s["pvd"], None))
                O("scalar", "dma_start", out=qsb[:, hh * 512:(hh + 1) * 512],
                  in_=q_out[512 * qp + 128 * hh: 512 * qp + 128 * (hh + 1),
                            512 * b: 512 * (b + 1)],
                  waits=wts, inc=s["qld"], inc_amt=16)
            for hh in range(4):
                O("vector", "tensor_mul", qq[:], qsb[:, hh * 512:(hh + 1) * 512],
                  qsb[:, hh * 512:(hh + 1) * 512],
                  waits=[(s["qld"], None)], inc=s["qn"])
                O("tensor", "matmul", PA[2][0:1, 0:512], onescol[:], qq[:],
                  start=True, stop=True, skip_group_check=True,
                  waits=[(s["qn"], None)] + pt2_wait(0), inc=s["nmT"])
                O("vector", "reduce_max", qn2_sb[0:1, hh:hh + 1],
                  PA[2][0:1, 0:512], axis=AX.X, waits=[(s["nmT"], None)],
                  inc=s["scF"])
                pt2_free[0] = (s["scF"], s["scF"].n)
                O("vector", "reduce_max", mtmp[0:1, 0:1], kn_sb[0:1, 0:qp + 1],
                  axis=AX.X, inc=s["nm"])
                O("vector", "tensor_mul", mtmp[0:1, 1:2], qn2_sb[0:1, hh:hh + 1],
                  mtmp[0:1, 0:1], inc=s["nm"])
                O("scalar", "activation", mtmp[0:1, 2:3], mtmp[0:1, 1:2], AF.Ln,
                  waits=[(s["nm"], None)], inc=s["nmA"])
                O("vector", "tensor_scalar", mtmp[0:1, 1:2], mtmp[0:1, 2:3],
                  0.5, LN105, ALU.mult, ALU.add,
                  waits=[(s["nmA"], None)], inc=s["nm"])
                O("scalar", "activation", mtmp[0:1, 3:4], mtmp[0:1, 1:2], AF.Exp,
                  waits=[(s["nm"], None)], inc=s["nmA"])
                O("vector", "tensor_scalar_mul", mtmp_bf[0:1, 0:1],
                  mtmp[0:1, 3:4], -1.0, waits=[(s["nmA"], None)], inc=s["nm"])
                wts = [(s["nm"], None), (s["aev"], None)]
                if b == 0 and qp == 0 and hh == 0:
                    wts.append((s["tfm"], None))
                O("tensor", "matmul", PA[0][:, 0:1], onesrow[:],
                  mtmp_bf[0:1, 0:1], start=True, stop=True,
                  skip_group_check=True, waits=wts, inc=s["nmT"])
                O("vector", "tensor_copy", negm[:, hh:hh + 1], PA[0][:, 0:1],
                  waits=[(s["nmT"], None)], inc=s["nm"])
            for hpair in range(2):
                pend = None

                def flush_pend():
                    nonlocal pend, n_pv
                    if pend is None:
                        return
                    O("tensor", "matmul",
                      PA[1][0:1, pend["h2"] * 512:(pend["h2"] + 1) * 512],
                      onescol[:], pend["puse"], start=pend["first"],
                      stop=pend["last"], skip_group_check=True,
                      waits=pend["lw"])
                    O("tensor", "matmul",
                      PA[3][:, pend["h2"] * 512:(pend["h2"] + 1) * 512],
                      pend["vtile"], pend["puse"], start=pend["first"],
                      stop=pend["last"], skip_group_check=True,
                      waits=[pend["vw"]], inc=s["pvd"])
                    n_pv += 1
                    if pend["tile_of_tk_done"] is not None:
                        vt_slot_free[pend["tile_of_tk_done"]] = (s["pvd"], n_pv)
                    if pend["ptidx"] is not None:
                        pt_slot_free[pend["ptidx"]] = (s["pvd"], n_pv)
                    if pend["pmidx"] is not None:
                        pm_slot_free[pend["pmidx"]] = (s["pvd"], n_pv)
                    pend = None

                for tk in range(qp + 1):
                    vslot = vt_n % 2
                    wts = [(s["collKV"], 1)]
                    if vt_n >= 2:
                        wts.append(vt_slot_free[vt_n - 2])
                    if first_vt:
                        wts.append((s["tfm"], None))
                        first_vt = False
                    O("scalar", "dma_start",
                      out=vt[vslot][:].rearrange("p (t c) -> p t c", c=128),
                      in_=kv_out[1024 * tk + 512 * b: 1024 * tk + 512 * b + 512,
                                 128:256].rearrange("(t p) c -> p t c", p=128),
                      waits=wts, inc=s[f"vt{vslot}"], inc_amt=16)
                    for tb in range(4):
                        for h2 in range(2):
                            hh = hpair * 2 + h2
                            first = (tk == 0 and tb == 0)
                            last = (tk == qp and tb == 3)
                            scs = sc_i % 2
                            wts = list(pt2_wait(scs))
                            if first and tb == 0 and h2 == 0:
                                wts.append((s["nm"], None))
                                wts.append((s["kres"], (b * 8 + 8) * 16))
                            O("tensor", "matmul",
                              PA[2][:, scs * 512:(scs + 1) * 512],
                              K_res[:, tk * 512 + tb * 128:
                                    tk * 512 + (tb + 1) * 128],
                              qsb[:, hh * 512:(hh + 1) * 512],
                              start=True, stop=True, skip_group_check=True,
                              waits=wts, inc=s["sc"])
                            ptidx = pt_i
                            pts = ptidx % 3
                            ew = [(s["sc"], None)]
                            if ptidx >= 3:
                                ew.append(pt_slot_free[ptidx - 3])
                            O("scalar", "activation", pt[pts][:],
                              PA[2][:, scs * 512:(scs + 1) * 512], AF.Exp,
                              bias=negm[:, hh:hh + 1],
                              waits=ew, inc=s["exp"])
                            pt2_free[scs] = (s["exp"], s["exp"].n)
                            sc_i += 1
                            pt_i += 1
                            pmidx = None
                            if tk == qp:
                                pms = pm_i % 2
                                mw = [(s["exp"], None)]
                                if pm_i >= 2:
                                    mw.append(pm_slot_free[pm_i - 2])
                                O("vector", "tensor_mul", pm[pms][:], pt[pts][:],
                                  masks[:, tb * 512:(tb + 1) * 512],
                                  waits=mw, inc=s["msk"])
                                n_msk += 1
                                pt_slot_free[ptidx] = (s["msk"], n_msk)
                                pmidx = pm_i
                                pm_i += 1
                                puse = pm[pms][:]
                                pwait = [(s["msk"], None)]
                                ptrec = None
                            else:
                                puse = pt[pts][:]
                                pwait = [(s["exp"], None)]
                                ptrec = ptidx
                            lw = list(pwait)
                            if first:
                                lw.append((s["aev"], None))
                                if b == 0 and qp == 0 and hpair == 0:
                                    lw.append((s["rtr"], None))
                            cur = dict(h2=h2, puse=puse, first=first, last=last,
                                       lw=lw,
                                       vtile=vt[vslot][:, tb * 128:(tb + 1) * 128],
                                       vw=(s[f"vt{vslot}"], (vt_n // 2 + 1) * 16),
                                       ptidx=ptrec, pmidx=pmidx,
                                       tile_of_tk_done=(
                                           vt_n if (tb == 3 and h2 == 1) else None))
                            flush_pend()
                            pend = cur
                    vt_n += 1
                flush_pend()
                for h2 in range(2):
                    hh = hpair * 2 + h2
                    O("vector", "reciprocal",
                      qn_row[0:1, h2 * 512:(h2 + 1) * 512],
                      PA[1][0:1, h2 * 512:(h2 + 1) * 512],
                      waits=[(s["pvd"], None)], inc=s["aev"])
                    O("vector", "tensor_copy", mtmp_bf[0:1, 1:2],
                      mtmp[0:1, 0:1]) if False else None
                    O("tensor", "matmul", PA[0][:, h2 * 512:(h2 + 1) * 512],
                      onesrow_f[:], qn_row[0:1, h2 * 512:(h2 + 1) * 512],
                      start=True, stop=True, skip_group_check=True,
                      waits=[(s["aev"], None)], inc=s["nmT"])
                    O("vector", "tensor_copy", linvb[:, h2 * 512:(h2 + 1) * 512],
                      PA[0][:, h2 * 512:(h2 + 1) * 512],
                      waits=[(s["nmT"], None)], inc=s["aev"])
                    O("vector", "tensor_mul", aosb[h2][:],
                      PA[3][:, h2 * 512:(h2 + 1) * 512],
                      linvb[:, h2 * 512:(h2 + 1) * 512],
                      waits=[(s[f"aos{h2}"], 16 * ((b * 8 + qp) * 2 + hpair))],
                      inc=s["aev"])
                    O("gpsimd", "dma_start",
                      out=ao_in[1024 * qp + 512 * b + 128 * hh:
                                1024 * qp + 512 * b + 128 * (hh + 1), :],
                      in_=aosb[h2][:], waits=[(s["aev"], None)],
                      inc=s[f"aos{h2}"], inc_amt=16)
    P.coll("AllToAll", ALU.bypass, RG, [ao_in.opt()], [ao_out.opt()],
           waits=[(s["aos0"], None), (s["aos1"], None)], inc=s["collAO"])

    if PHASES <= 6:
        return _finish()

    for j in range(NCORE):
        for bb in range(2):
            for hp in range(4):
                h = 4 * j + hp
                O("sync", "dma_start",
                  out=attn_fm[:, h * NTOK + 512 * bb: h * NTOK + 512 * bb + 512],
                  in_=ao_out[1024 * j + 512 * bb + 128 * hp:
                             1024 * j + 512 * bb + 128 * (hp + 1), :],
                  waits=([(s["collAO"], 1)] if (j == 0 and bb == 0 and hp == 0)
                         else []),
                  inc=s["afm"], inc_amt=16)
    N_AFM = s["afm"].n
    O("gpsimd", "dma_start", out=fbb[:], in_=fb,
      waits=[(s["pvd"], None), (s["exp"], None), (s["msk"], None),
             (s["aev"], None), (s["qld"], None), (s["aos0"], None),
             (s["aos1"], None)],
      inc=s["misc"], inc_amt=16)
    O("gpsimd", "dma_start", out=bcat[:], in_=B2T,
      waits=[(s["pvd"], None), (s["aev"], None), (s["msk"], None),
             (s["qn"], None), (s["afm"], N_AFM)],
      inc=s["misc"], inc_amt=16)
    N_MISC = s["misc"].n

    if AFM_ONLY:
        return _finish()
    OUTB = [(PA[i // 2], (i % 2) * 512) for i in range(8)]
    for o in range(OLIM):
        for j in range(NJ):
            gidx = o * NJ + j
            O("gpsimd", "dma_start", out=wo_t[j % 2][:],
              in_=Wo[j * 128:(j + 1) * 128, o * 512:(o + 1) * 512],
              waits=([(s["woF"], gidx - 1)] if gidx >= 2 else []),
              inc=s[f"wo{j % 2}"], inc_amt=16, future=True)
        if FINMODE >= 2:
            for j in range(NJ):
                gidx = o * NJ + j
                O("gpsimd", "dma_start", out=wf_t[j % 2][:],
                  in_=WfT[j * 128:(j + 1) * 128, o * 512:(o + 1) * 512],
                  waits=([(s["wfF"], gidx - 1)] if gidx >= 2 else []),
                  inc=s[f"wf{j % 2}"], inc_amt=16, future=True)
        if FINMODE >= 3:
            for tt in range(NT):
                nxr = o * NT + tt
                O("scalar", "dma_start", out=xres[tt % 2][:],
                  in_=x_my[tt * 128:(tt + 1) * 128, o * 512:(o + 1) * 512],
                  waits=([(s["ouV"], nxr - 1)] if nxr >= 2 else []),
                  inc=s[f"xr{tt % 2}"], inc_amt=16, future=True)
        for j in range(NJ):
            for tt in range(NT):
                pa, po = OUTB[tt]
                wts = []
                if tt == 0:
                    wts.append((s[f"wo{j % 2}"], ((o * NJ + j) // 2 + 1) * 16))
                    if j == 0:
                        wts.append((s["afm"], N_AFM))
                        if o == 0:
                            wts.append((s["nmT"], None))
                            wts.append((s["aev"], None))
                            wts.append((s["exp"], None))
                        elif FINMODE >= 3:
                            wts.append((s["ouV"], o * 8))
                O("tensor", "matmul", pa[:, po:po + 512],
                  attn_fm[:, j * NTOK + tt * 128: j * NTOK + (tt + 1) * 128],
                  wo_t[j % 2][:], start=(j == 0), stop=False,
                  skip_group_check=True, waits=wts,
                  inc=(s["woF"] if tt == NT - 1 else None))
        if FINMODE >= 2:
            for j in range(NJ):
                for tt in range(NT):
                    pa, po = OUTB[tt]
                    wts = []
                    if tt == 0:
                        wts.append((s[f"wf{j % 2}"], ((o * NJ + j) // 2 + 1) * 16))
                    O("tensor", "matmul", pa[:, po:po + 512],
                      xn_fm[:, j * NTOK + tt * 128: j * NTOK + (tt + 1) * 128],
                      wf_t[j % 2][:], start=False, stop=False,
                      skip_group_check=True, waits=wts,
                      inc=(s["wfF"] if tt == NT - 1 else None))
        if FINMODE >= 3:
            for tt in range(NT):
                pa, po = OUTB[tt]
                wts = [(s["misc"], N_MISC), (s["tfm"], None)] if tt == 0 else []
                O("tensor", "matmul", pa[:, po:po + 512],
                  tsc[:, tt * 128:(tt + 1) * 128], bcat[:, o * 512:(o + 1) * 512],
                  start=False, stop=False, skip_group_check=True, waits=wts)
                O("tensor", "matmul", pa[:, po:po + 512],
                  onesrow[:], fbb[0:1, o * 512:(o + 1) * 512],
                  start=False, stop=True, skip_group_check=True, inc=s["ou"])
            for tt in range(NT):
                pa, po = OUTB[tt]
                nxr2 = o * NT + tt
                O("vector", "tensor_add", outsb[tt % 2][:], pa[:, po:po + 512],
                  xres[tt % 2][:],
                  waits=[(s["ou"], o * NT + tt + 1),
                         (s[f"xr{tt % 2}"], (nxr2 // 2 + 1) * 16),
                         (s[f"ouF{tt % 2}"], (nxr2 // 2) * 16)],
                  inc=s["ouV"])
                O("sync", "dma_start",
                  out=out_my[tt * 128:(tt + 1) * 128, o * 512:(o + 1) * 512],
                  in_=outsb[tt % 2][:], waits=[(s["ouV"], None)],
                  inc=s[f"ouF{tt % 2}"], inc_amt=16)
    P.cmds["sync"].append(lambda e, h=s["ouF0"].h, v=s["ouF0"].n: e.wait_ge(h, v))
    P.cmds["sync"].append(lambda e, h=s["ouF1"].h, v=s["ouF1"].n: e.wait_ge(h, v))

    if dbg:
        for j in range(NJ):
            O("vector", "tensor_copy", linvb[:, 0:1024].bitcast(BF16)[:, 0:1024]
              if False else linvb[:, 0:512],
              xn_fm[:, j * NTOK: j * NTOK + 512], waits=[(s["ouF0"], None), (s["ouF1"], None)])
            O("vector", "tensor_copy", linvb[:, 512:1024],
              xn_fm[:, j * NTOK + 512: (j + 1) * NTOK], inc=s["dbgV"])
            O("gpsimd", "dma_start",
              out=dbg_o["d_xnfm"][:, j * NTOK:(j + 1) * NTOK],
              in_=linvb[:], waits=[(s["dbgV"], None)], inc=s["dbg"], inc_amt=16)
            O("vector", "tensor_copy", linvb[:, 0:512],
              attn_fm[:, j * NTOK: j * NTOK + 512], waits=[(s["dbg"], None)])
            O("vector", "tensor_copy", linvb[:, 512:1024],
              attn_fm[:, j * NTOK + 512: (j + 1) * NTOK], inc=s["dbgV"])
            O("gpsimd", "dma_start",
              out=dbg_o["d_attn"][:, j * NTOK:(j + 1) * NTOK],
              in_=linvb[:], waits=[(s["dbgV"], None)], inc=s["dbg"], inc_amt=16)
        O("gpsimd", "dma_start", out=dbg_o["d_q"], in_=q_out,
          waits=[(s["collQ"], 1)], inc=s["dbg"], inc_amt=16)
        O("gpsimd", "dma_start", out=dbg_o["d_kv"], in_=kv_out,
          waits=[(s["collKV"], 1)], inc=s["dbg"], inc_amt=16)
        O("gpsimd", "dma_start", out=dbg_o["d_ao"], in_=ao_out,
          waits=[(s["collAO"], 1)], inc=s["dbg"], inc_amt=16)
        P.cmds["gpsimd"].append(
            lambda e, h=s["dbg"].h, v=s["dbg"].n: e.wait_ge(h, v))

    return _finish()



def _consts():
    c = {}
    c["identb"] = np.eye(128, dtype=ml_dtypes.bfloat16)
    c["identf"] = np.eye(128, dtype=np.float32)
    c["onescol"] = np.ones((128, 1), dtype=ml_dtypes.bfloat16)
    c["onesrow"] = np.ones((1, 128), dtype=ml_dtypes.bfloat16)
    c["onesrow_f"] = np.ones((1, 128), dtype=np.float32)
    rp = np.zeros((8, 128), dtype=np.float32)
    for e in range(8):
        rp[e, e * 16:(e + 1) * 16] = 2.0
    c["rep_f"] = rp
    m = np.zeros((512, 512), dtype=np.float32)
    i = np.arange(128)[:, None]
    jj = np.arange(512)[None, :]
    for blk in range(4):
        m[blk * 128:(blk + 1) * 128] = (blk * 128 + i <= jj)
    c["masks"] = m.astype(ml_dtypes.bfloat16)
    return c


def _rope_tables(core):
    pos = np.arange(512 * core, 512 * core + 512, dtype=np.float64)
    inv = 1.0 / (THETA ** (np.arange(0, HD, 2, dtype=np.float64) / HD))
    ang = pos[None, :] * inv[:, None]
    cos = np.cos(ang); sin = np.sin(ang)
    cosf = np.concatenate([cos, cos], 0)
    sinf = np.concatenate([-sin, sin], 0)
    cosf = np.concatenate([cosf, cosf], 1)
    sinf = np.concatenate([sinf, sinf], 1)
    sc = 1.0 / np.sqrt(HD)
    return (np.asarray(cosf * sc, np.float32), np.asarray(sinf * sc, np.float32),
            np.asarray(cosf, np.float32), np.asarray(sinf, np.float32))


def kernel(L0_attn_x, L0_attn_Wq, L0_attn_Wk, L0_attn_Wv, L0_attn_Wo,
           L0_ffn_W, L0_ffn_b, L0_ffn_A, L0_ffn_B, L0_ffn_router_W,
           L0_ffn_router_b):
    x = np.ascontiguousarray(np.asarray(L0_attn_x, np.float32))
    Wq_ = np.ascontiguousarray(np.asarray(L0_attn_Wq, np.float32))
    Wk_ = np.ascontiguousarray(np.asarray(L0_attn_Wk, np.float32))
    Wv_ = np.ascontiguousarray(np.asarray(L0_attn_Wv, np.float32))
    Wo_ = np.ascontiguousarray(np.asarray(L0_attn_Wo, np.float32))
    WfT_ = np.ascontiguousarray(np.asarray(L0_ffn_W, np.float32).T)
    fb_ = np.ascontiguousarray(np.asarray(L0_ffn_b, np.float32)).reshape(1, D)
    A2T_ = np.ascontiguousarray(np.asarray(L0_ffn_A, np.float32).reshape(128, D).T)
    B2T_ = np.ascontiguousarray(
        np.asarray(L0_ffn_B, np.float32).transpose(0, 2, 1).reshape(128, 4096))
    rW_ = np.ascontiguousarray(np.asarray(L0_ffn_router_W, np.float32))
    rb_ = np.ascontiguousarray(np.asarray(L0_ffn_router_b, np.float32))

    rwt_ = np.ascontiguousarray(
        rW_.T.reshape(32, 128, 8).transpose(1, 0, 2).reshape(128, 256))
    rb2_ = np.ascontiguousarray(np.tile(rb_.reshape(1, 8), (2, 1)))

    consts = _consts()
    shared = dict(Wq=Wq_, Wk=Wk_, Wv=Wv_, Wo=Wo_, WfT=WfT_, fb=fb_, A2T=A2T_,
                  B2T=B2T_, rwt=rwt_, rb2=rb2_, **consts)

    in_maps = []
    for c in range(NCORE):
        cq, sq, ck, sk = _rope_tables(c)
        x_myc = np.ascontiguousarray(
            np.concatenate([x[0, 512 * c:512 * (c + 1), :],
                            x[1, 512 * c:512 * (c + 1), :]], axis=0))
        rt = np.zeros((1, NTOK), np.float32)
        lo = max(Q_LO, 512 * c) - 512 * c
        hi = min(Q_HI, 512 * c + 512) - 512 * c
        if hi > lo:
            rt[0, lo:hi] = 1.0
            rt[0, 512 + lo:512 + hi] = 1.0
        m = dict(shared)
        m.update(x_my=x_myc, cosq=cq, sinq=sq, cosk=ck, sink=sk,
                 rtok=rt.astype(ml_dtypes.bfloat16))
        in_maps.append(m)

    nc = build_program(dbg=DBG)
    res = run_bass_kernel_spmd(nc, in_maps, core_ids=list(range(NCORE)))

    out = np.empty((2, SEQ, D), dtype=np.float32)
    for c in range(NCORE):
        o = res.results[c]["out_my"]
        out[0, 512 * c:512 * (c + 1), :] = o[0:512, :]
        out[1, 512 * c:512 * (c + 1), :] = o[512:1024, :]
    if DBG:
        kernel._dbg = res
    return out


# version 04
# speedup vs baseline: 1.1468x; 1.1468x over previous
import numpy as np
import ml_dtypes
from contextlib import ExitStack

import concourse.bass as bass
import concourse.mybir as mybir
from concourse.bass_utils import run_bass_kernel_spmd
from concourse.aot_env import allow_ici_all_to_all

F32 = mybir.dt.float32
BF16 = mybir.dt.bfloat16
AF = mybir.ActivationFunctionType
ALU = mybir.AluOpType
AX = mybir.AxisListType

NCORE = 8
D = 4096
SEQ = 4096
HD = 128
CH = 512
NTOK = 1024
NT = 8
NJ = 32
NHKV = 8
Q_LO, Q_HI = 611, 4095
NQTOK = Q_HI - Q_LO
EPS = 1e-5
LN105 = float(np.log(1.05))
THETA = 10000.0

DBG = False
PHASES = 7
OLIM = 8
AFM_ONLY = False
FINMODE = 3


class Sem:
    __slots__ = ("h", "n", "name", "vmap")

    def __init__(self, h, name):
        self.h, self.n, self.name = h, 0, name
        self.vmap = {}


class Prog:

    SELF_ENGS = ("vector", "scalar")

    def __init__(self, nc, ctx):
        self.nc, self.ctx = nc, ctx
        self.cmds = {e: [] for e in ("tensor", "vector", "scalar", "gpsimd", "sync")}
        self.selfsem = {e: Sem(ctx.enter_context(nc.semaphore(f"self_{e}")),
                               f"self_{e}") for e in self.SELF_ENGS}
        self.deps = {e: {"w": {}, "r": {}} for e in self.SELF_ENGS}
        self.fence_val = {e: 0 for e in self.SELF_ENGS}
        self.owner = {}

    def sem(self, name):
        return Sem(self.ctx.enter_context(self.nc.semaphore(name)), name)

    @staticmethod
    def _ap_name(a):
        t = getattr(a, "tensor", None)
        if t is not None and hasattr(t, "name"):
            return t.name
        if hasattr(a, "name") and hasattr(a, "ap"):
            return a.name
        return None

    def _rw_sets(self, args, kwargs):
        rset, wset = set(), set()
        for i, a in enumerate(args):
            nm = self._ap_name(a)
            if nm is None:
                continue
            (wset if i == 0 else rset).add(nm)
        for k, a in kwargs.items():
            nm = self._ap_name(a)
            if nm is None:
                continue
            if k in ("out", "accum_out"):
                wset.add(nm)
            else:
                rset.add(nm)
        return rset, wset

    def _resolve(self, sm, vv):
        eng = self.owner.get(sm.name)
        if eng is None:
            return (sm.h, vv)
        assert vv in sm.vmap, f"no vmap[{vv}] for {sm.name} (owner {eng})"
        return (self.selfsem[eng].h, sm.vmap[vv])

    def op(self, eng, meth, *args, waits=(), inc=None, inc_amt=1, future=False,
           **kwargs):
        wl = []
        for (sm, v) in waits:
            vv = sm.n if v is None else v
            if not future:
                assert vv <= sm.n, f"wait {sm.name}>={vv} but emitted {sm.n}"
            if vv > 0:
                wl.append((sm, vv))
        is_dma = meth in ("dma_start", "dma_start_transpose")
        ih = None
        if eng in self.SELF_ENGS:
            ss = self.selfsem[eng]
            if not is_dma:
                rset, wset = self._rw_sets(args, kwargs)
                dep = self.deps[eng]
                need = self.fence_val[eng]
                for t in rset | wset:
                    need = max(need, dep["w"].get(t, 0))
                for t in wset:
                    need = max(need, dep["r"].get(t, 0))
                if need > 0:
                    wl.insert(0, (ss, need))
                ss.n += 1
                for t in wset:
                    dep["w"][t] = ss.n
                for t in rset:
                    dep["r"][t] = max(dep["r"].get(t, 0), ss.n)
                ih = (ss.h, 1)
                if inc is not None:
                    assert inc_amt == 1
                    inc.n += 1
                    o = self.owner.setdefault(inc.name, eng)
                    assert o == eng, f"{inc.name} inc'd from {o} and {eng}"
                    inc.vmap[inc.n] = ss.n
            else:
                if inc is not None:
                    inc.n += (inc_amt if inc_amt is not None else 1)
                    ih = (inc.h, inc_amt)
        else:
            if inc is not None:
                assert inc.name not in self.owner, \
                    f"{inc.name} owned by {self.owner.get(inc.name)}"
                inc.n += (inc_amt if inc_amt is not None else 1)
                ih = (inc.h, inc_amt)

        def run(e, meth=meth, args=args, kwargs=kwargs, wl=wl, ih=ih):
            for (sm, v) in wl:
                h, v = self._resolve(sm, v)
                e.wait_ge(h, v)
            ins = getattr(e, meth)(*args, **kwargs)
            if ih is not None:
                if ih[1] is None or ih[1] == 1:
                    ins.then_inc(ih[0], 1)
                else:
                    ins.then_inc(ih[0], ih[1])
        self.cmds[eng].append(run)

    def coll(self, kind, op_, rg, ins_, outs_, waits=(), inc=None):
        wl = []
        for (sm, v) in waits:
            vv = sm.n if v is None else v
            if vv > 0:
                wl.append((sm, vv))
        inc.n += 1

        def run(e, wl=wl, ih=inc.h):
            for (sm, v) in wl:
                h, v = self._resolve(sm, v)
                e.wait_ge(h, v)
            with allow_ici_all_to_all():
                e.collective_compute(kind, op_, rg, ins_, outs_).then_inc(ih)
        self.cmds["gpsimd"].append(run)

    def fence(self):
        for e in self.SELF_ENGS:
            self.fence_val[e] = self.selfsem[e].n

    def bump(self, eng, sem, amt=1):
        raise RuntimeError("standalone bumps are forbidden (writeback race)")


def build_program(dbg=False):
    nc = bass.Bass("TRN2", target_bir_lowering=False)
    ctx = ExitStack()

    def din(name, shape, dt=F32):
        return nc.dram_tensor(name, shape, dt, kind="ExternalInput").ap()

    x_my = din("x_my", [NTOK, D])
    Wq = din("Wq", [D, D]); Wk = din("Wk", [D, 1024]); Wv = din("Wv", [D, 1024])
    Wo = din("Wo", [D, D]); WfT = din("WfT", [D, D])
    fb = din("fb", [1, D])
    A2T = din("A2T", [D, 128]); B2T = din("B2T", [128, D])
    rwt = din("rwt", [128, 256]); rb2 = din("rb2", [2, 8])
    cosq_i = din("cosq", [128, NTOK]); sinq_i = din("sinq", [128, NTOK])
    cosk_i = din("cosk", [128, NTOK]); sink_i = din("sink", [128, NTOK])
    masks_i = din("masks", [512, 512], BF16)
    identb_i = din("identb", [128, 128], BF16)
    identf_i = din("identf", [128, 128], F32)
    onescol_i = din("onescol", [128, 1], BF16)
    onesrow_i = din("onesrow", [1, 128], BF16)
    onesrow_f_i = din("onesrow_f", [1, 128], F32)
    rep_i = din("rep_f", [8, 128], F32)
    rtok_i = din("rtok", [1, NTOK], BF16)

    out_my = nc.dram_tensor("out_my", [NTOK, D], F32, kind="ExternalOutput").ap()

    kv_in = nc.dram_tensor("kv_in", [NCORE * NTOK, 256], BF16, kind="Internal").ap()
    kv_out = nc.dram_tensor("kv_out", [NCORE * NTOK, 256], BF16, kind="Internal").ap()
    q_in = nc.dram_tensor("q_in", [D, NTOK], BF16, kind="Internal").ap()
    q_out = nc.dram_tensor("q_out", [D, NTOK], BF16, kind="Internal").ap()
    ao_in = nc.dram_tensor("ao_in", [NCORE * NTOK, 512], BF16, kind="Internal").ap()
    ao_out = nc.dram_tensor("ao_out", [NCORE * NTOK, 512], BF16, kind="Internal").ap()
    r_in = nc.dram_tensor("r_in", [128, 64], F32, kind="Internal").ap()
    r_out = nc.dram_tensor("r_out", [128, 64], F32, kind="Internal").ap()

    dbg_o = {}
    if dbg:
        for nm_, shp, dt in (("d_xnfm", [128, NJ * NTOK], F32),
                             ("d_q", [D, NTOK], F32),
                             ("d_kv", [NCORE * NTOK, 256], F32),
                             ("d_ao", [NCORE * NTOK, 512], F32),
                             ("d_rout", [2, 8], F32),
                             ("d_rdbg", [2, 32], F32),
                             ("d_attn", [128, NJ * NTOK], F32)):
            dbg_o[nm_] = nc.dram_tensor(nm_, shp, dt, kind="ExternalOutput").ap()

    def sb(name, shape, dt):
        return nc.alloc_sbuf_tensor(name, shape, dt)

    xn_fm = sb("xn_fm", [128, NJ * NTOK], BF16)
    attn_fm = sb("attn_fm", [128, NJ * NTOK], BF16)
    tab = sb("tab", [128, 2048], F32)
    masks = sb("masks_s", [128, 2048], BF16)
    identb = sb("identb_s", [128, 128], BF16)
    identf = sb("identf_s", [128, 128], F32)
    onescol = sb("onescol_s", [128, 1], BF16)
    onesrow = sb("onesrow_s", [1, 128], BF16)
    onesrow_f = sb("onesrow_f_s", [1, 128], F32)
    rep_f = sb("rep_f_s", [8, 128], F32)
    rwt_sb = sb("rwt_sb", [128, 256], F32)
    rb2_sb = sb("rb2_sb", [2, 8], F32)
    tfm = sb("tfm", [128, NTOK], BF16)
    tsc = sb("tsc", [128, NTOK], BF16)
    kn_sb = sb("kn_sb", [1, 8], F32)
    kk = sb("kk", [128, 512], F32)
    rsum_sb = sb("rsum_sb", [128, 64], F32)
    rtok_sb = sb("rtok_sb", [1, NTOK], BF16)
    qn2_sb = sb("qn2_sb", [1, 4], F32)
    mtmp = sb("mtmp", [1, 4], F32)
    mtmp_bf = sb("mtmp_bf", [1, 2], BF16)
    K_res = sb("K_res", [128, 4096], BF16)
    lsum = sb("lsum", [128, 1], F32); lsq = sb("lsq", [128, 1], F32)
    lmean = sb("lmean", [128, 1], F32); lvar = sb("lvar", [128, 1], F32)
    lrstd = sb("lrstd", [128, 1], F32); lnegmr = sb("lnegmr", [128, 1], F32)
    ltmp = sb("ltmp", [128, 1], F32)
    logit_sb = sb("logit_sb", [2, 8], F32); rmax = sb("rmax", [2, 1], F32)
    rexp = sb("rexp", [2, 8], F32); rsumt = sb("rsumt", [2, 1], F32)
    rrec = sb("rrec", [2, 1], F32); rsm = sb("rsm", [2, 8], F32)
    rmax2 = sb("rmax2", [2, 1], F32)
    rdbg = sb("rdbg", [2, 32], F32)
    routT = sb("routT", [8, 2], F32)
    svec = sb("svec", [128, 2], F32)
    wo_t = [sb(f"wo_t{i}", [128, 512], BF16) for i in range(2)]
    wf_t = [sb(f"wf_t{i}", [128, 512], BF16) for i in range(2)]
    xres = [sb(f"xres{i}", [128, 512], F32) for i in range(2)]
    outsb = [sb(f"outsb{i}", [128, 512], F32) for i in range(2)]
    WBUF = sb("WBUF", [128, 4096], BF16)

    ARENA = sb("ARENA", [128, 6144], F32)
    ab = nc.lookup_mloc(ARENA).addr

    def at_(name, shape, dt, off):
        nb = shape[1] * (4 if dt == F32 else 2)
        assert off + nb <= 24576, name
        return nc.alloc_sbuf_tensor_at(name, shape, dt, offset=ab + off)

    XBUF = at_("XBUF", [128, 4096], F32, 0)
    xn_tm = at_("xn_tm", [128, 4096], BF16, 16384)
    rtok_bc = at_("rtok_bc", [128, 1024], F32, 0)
    K_sb = at_("K_sb", [128, 1024], BF16, 0)
    V_sb = at_("V_sb", [128, 1024], BF16, 2048)
    ktm = at_("ktm", [128, 1024], BF16, 4096)
    vtm = at_("vtm", [128, 1024], BF16, 6144)
    rtmp1 = at_("rtmp1", [128, 1024], F32, 8192)
    rtmp2 = at_("rtmp2", [128, 1024], F32, 12288)
    qstage = [at_(f"qstage{i}", [128, 1024], BF16, 2048 * i) for i in range(2)]
    At = at_("At", [128, 4096], BF16, 16384)
    qsb = at_("qsb", [128, 2048], BF16, 0)
    vt = [at_(f"vt{i}", [128, 512], BF16, 4096 + 1024 * i) for i in range(2)]
    pt = [at_(f"pt{i}", [128, 512], BF16, off)
          for i, off in enumerate((6144, 7168, 8192, 15360, 22560))]
    pm = [at_(f"pm{i}", [128, 512], BF16, 9216 + 1024 * i) for i in range(2)]
    linvb = at_("linvb", [128, 1024], F32, 11264)
    qq = nc.alloc_sbuf_tensor_at("qq", [128, 512], BF16,
                                 offset=nc.lookup_mloc(kk).addr)
    qn_row = at_("qn_row", [1, 1024], F32, 16384)
    negm = at_("negm", [128, 4], F32, 20480)
    aosb = [at_(f"aosb{i}", [128, 512], BF16, 20512 + 1024 * i) for i in range(2)]
    fbb = at_("fbb", [1, 4096], BF16, 0)
    bcat = at_("bcat", [128, 4096], BF16, 8192)

    PT = [ctx.enter_context(nc.psum_tensor(f"PT{i}", [128, 1024], F32)) for i in range(4)]
    PA = [p.ap() for p in PT]
    PB = [p.ap().bitcast(BF16) for p in PT]

    P = Prog(nc, ctx)
    O, B_ = P.op, P.bump
    RG = [list(range(NCORE))]

    s = {n: P.sem(n) for n in (
        "wkv0", "wkv1", "wq0", "wq1", "wq2", "wq3", "wo0", "wo1", "wf0", "wf1",
        "xr0", "xr1", "vt0", "vt1", "qs0", "qs1", "aos0", "aos1", "ouF0", "ouF1",
        "collR", "collKV", "collQ", "collAO",
        "cst", "conv", "x", "xfree", "lnv", "lna", "xn", "xnfree",
        "trA", "trAF", "phA", "rtr", "rtrG", "rtrA2", "rtrT",
        "wkv", "wkvF", "kvt", "kve", "kvs",
        "wq", "wqF", "qp", "qe", "qs",
        "coll", "misc", "trK", "trKF", "tfm", "tfmT",
        "kres", "vt", "vtF", "qld", "qn", "nm", "nmA", "nmT",
        "sc", "scF", "exp", "ptF", "msk", "pmF", "lpv", "pvd", "aev", "aos",
        "afm", "wo", "woF", "wf", "wfF", "xr", "xrF", "ou", "ouV", "ouF",
        "dbg", "dbgV")}

    pt2_free = {0: None, 1: None, 2: None, 3: None}
    SCSLOT = None

    def pt2_wait(bank):
        f = pt2_free[bank]
        return [f] if f is not None else []

    for dst, src in ((identb[:], identb_i), (identf[:], identf_i),
                     (onescol[:], onescol_i), (onesrow[:], onesrow_i),
                     (onesrow_f[:], onesrow_f_i), (rep_f[:], rep_i),
                     (rwt_sb[:], rwt), (rb2_sb[:], rb2), (rtok_sb[:], rtok_i)):
        O("gpsimd", "dma_start", out=dst, in_=src, inc=s["cst"], inc_amt=16)
    O("gpsimd", "dma_start",
      out=masks[:].rearrange("p (m f) -> p m f", f=512),
      in_=masks_i.rearrange("(m p) f -> p m f", p=128), inc=s["cst"], inc_amt=16)
    O("gpsimd", "dma_start", out=tab[:, 0:1024], in_=cosk_i, inc=s["cst"], inc_amt=16)
    O("gpsimd", "dma_start", out=tab[:, 1024:2048], in_=sink_i, inc=s["cst"], inc_amt=16)
    N_CST = s["cst"].n

    TRSLOT = [(PB[0], 0), (PB[0], 1024), (PB[1], 0), (PB[1], 1024)]
    trb = 0
    for tt in range(NT):
        O("sync", "dma_start", out=XBUF[:], in_=x_my[tt * 128:(tt + 1) * 128, :],
          waits=[(s["xn"], None)], inc=s["x"], inc_amt=16)
        O("vector", "reduce_sum", lsum[:], XBUF[:], axis=AX.X,
          waits=[(s["x"], None)], inc=s["lnv"])
        O("scalar", "activation", xn_tm[:], XBUF[:], AF.Square, accum_out=lsq[:],
          waits=[(s["x"], None)] + ([(s["trAF"], 8 * tt)] if tt else []),
          inc=s["lna"])
        O("vector", "tensor_scalar_mul", lmean[:], lsum[:], 1.0 / D, inc=s["lnv"])
        O("vector", "tensor_scalar_mul", lvar[:], lsq[:], 1.0 / D,
          waits=[(s["lna"], None)], inc=s["lnv"])
        O("vector", "tensor_mul", ltmp[:], lmean[:], lmean[:], inc=s["lnv"])
        O("vector", "tensor_sub", lvar[:], lvar[:], ltmp[:], inc=s["lnv"])
        O("vector", "tensor_scalar_add", lvar[:], lvar[:], EPS, inc=s["lnv"])
        O("scalar", "activation", ltmp[:], lvar[:], AF.Sqrt,
          waits=[(s["lnv"], None)], inc=s["lna"])
        O("vector", "reciprocal", lrstd[:], ltmp[:], waits=[(s["lna"], None)],
          inc=s["lnv"])
        O("vector", "tensor_scalar", lnegmr[:], lmean[:], lrstd[:, 0:1], -1.0,
          ALU.mult, ALU.mult, inc=s["lnv"])
        O("scalar", "activation", xn_tm[:], XBUF[:], AF.Identity,
          scale=lrstd[:, 0:1], bias=lnegmr[:, 0:1],
          waits=[(s["lnv"], None)], inc=s["xn"])
        for b8 in range(8):
            ps, off = TRSLOT[trb % 4]
            for q in range(4):
                j = b8 * 4 + q
                wts = []
                if q == 0:
                    wts = [(s["xn"], None)]
                    if trb >= 4:
                        wts.append((s["trAF"], trb - 3))
                    if trb == 0:
                        wts.append((s["cst"], N_CST))
                O("tensor", "transpose", ps[:, off + q * 128: off + (q + 1) * 128],
                  xn_tm[:, j * 128:(j + 1) * 128], identb[:], waits=wts,
                  inc=(s["trA"] if q == 3 else None))
            dst = xn_fm[:].rearrange("p (j t) -> p j t", t=NTOK)[
                :, b8 * 4:(b8 + 1) * 4, tt * 128:(tt + 1) * 128]
            src = ps[:, off:off + 512].rearrange("p (j t) -> p j t", t=128)
            O("vector", "tensor_copy", dst, src, waits=[(s["trA"], None)],
              inc=s["trAF"])
            trb += 1

    def _finish():
        with nc.Block() as block:
            @block.tensor
            def _(eng):
                for fn in P.cmds["tensor"]:
                    fn(eng)

            @block.vector
            def _(eng):
                for fn in P.cmds["vector"]:
                    fn(eng)

            @block.scalar
            def _(eng):
                for fn in P.cmds["scalar"]:
                    fn(eng)

            @block.gpsimd
            def _(eng):
                for fn in P.cmds["gpsimd"]:
                    fn(eng)

            @block.sync
            def _(eng):
                for fn in P.cmds["sync"]:
                    fn(eng)
        nc._keepalive_ctx = ctx
        return nc

    if PHASES <= 1:
        return _finish()

    P.fence()
    for half in range(2):
        O("tensor", "matmul", PA[2][:, half * 512:(half + 1) * 512],
          onesrow[:], rtok_sb[0:1, half * 512:(half + 1) * 512],
          start=True, stop=True, skip_group_check=True,
          waits=([(s["cst"], N_CST), (s["trAF"], NT * 8)] if half == 0 else []),
          inc=(s["rtrT"] if half == 1 else None))
    O("vector", "tensor_copy", rtok_bc[:], PA[2][:, 0:1024],
      waits=[(s["rtrT"], None)], inc=s["rtr"])
    pt2_free[0] = (s["rtr"], s["rtr"].n)
    pt2_free[1] = (s["rtr"], s["rtr"].n)
    for b in range(2):
        for j in range(NJ):
            src = xn_fm[:, j * NTOK + b * 512: j * NTOK + (b + 1) * 512]
            wts = [(s["rtr"], None)] if (b == 0 and j == 0) else []
            O("vector", "tensor_mul", kk[:], src,
              rtok_bc[:, b * 512:(b + 1) * 512], waits=wts)
            O("vector", "reduce_sum", rsum_sb[:, b * 32 + j: b * 32 + j + 1],
              kk[:], axis=AX.X,
              inc=(s["rtr"] if (b == 1 and j == NJ - 1) else None))
    O("gpsimd", "dma_start", out=r_in, in_=rsum_sb[:],
      waits=[(s["rtr"], None)], inc=s["rtrG"], inc_amt=16)
    P.coll("AllReduce", ALU.add, RG, [r_in.opt()], [r_out.opt()],
           waits=[(s["rtrG"], None)], inc=s["collR"])

    if PHASES <= 2:
        return _finish()

    P.fence()
    trk = 0
    for h in range(NHKV):
        for g4 in range(4):
            slot = g4 % 2
            ksrc = Wk[:, h * 128:(h + 1) * 128].rearrange(
                "(j p) i -> p j i", p=128)[:, g4 * 8:(g4 + 1) * 8, :]
            vsrc = Wv[:, h * 128:(h + 1) * 128].rearrange(
                "(j p) i -> p j i", p=128)[:, g4 * 8:(g4 + 1) * 8, :]
            kdst = WBUF[:, slot * 1024:(slot + 1) * 1024].rearrange(
                "p (j i) -> p j i", i=128)
            vdst = WBUF[:, 2048 + slot * 1024: 2048 + (slot + 1) * 1024].rearrange(
                "p (j i) -> p j i", i=128)
            gidx = h * 4 + g4
            wts = [(s["wkvF"], gidx - 1)] if gidx >= 2 else []
            O("gpsimd", "dma_start", out=kdst, in_=ksrc, waits=wts,
              inc=s[f"wkv{slot}"], inc_amt=16, future=True)
            O("gpsimd", "dma_start", out=vdst, in_=vsrc,
              inc=s[f"wkv{slot}"], inc_amt=16)
        for g4 in range(4):
            slot = g4 % 2
            for jj in range(8):
                j = g4 * 8 + jj
                wts = []
                if jj == 0:
                    wts.append((s[f"wkv{slot}"], ((h * 4 + g4) // 2 + 1) * 32))
                    if j == 0:
                        wts.append((s["trAF"], NT * 8))
                        if h > 0:
                            wts.append((s["kve"], None))
                for half in range(2):
                    O("tensor", "matmul", PA[0][:, half * 512:(half + 1) * 512],
                      WBUF[:, slot * 1024 + jj * 128: slot * 1024 + (jj + 1) * 128],
                      xn_fm[:, j * NTOK + half * 512: j * NTOK + (half + 1) * 512],
                      start=(j == 0), stop=(j == NJ - 1), skip_group_check=True,
                      waits=(wts if half == 0 else []))
                    wts = []
                    O("tensor", "matmul", PA[1][:, half * 512:(half + 1) * 512],
                      WBUF[:, 2048 + slot * 1024 + jj * 128:
                           2048 + slot * 1024 + (jj + 1) * 128],
                      xn_fm[:, j * NTOK + half * 512: j * NTOK + (half + 1) * 512],
                      start=(j == 0), stop=(j == NJ - 1), skip_group_check=True,
                      inc=(s["wkvF"] if (jj == 7 and half == 1) else None))
        wts = [(s["wkvF"], 4 * (h + 1))]
        if h >= 1:
            wts.append((s["kvs"], None))
        O("vector", "tensor_mul", rtmp1[0:64, :], PA[0][64:128, :],
          tab[0:64, 1024:2048], waits=wts)
        O("vector", "tensor_mul", rtmp1[64:128, :], PA[0][0:64, :],
          tab[64:128, 1024:2048])
        O("vector", "tensor_mul", rtmp2[:], PA[0][:, :], tab[:, 0:1024])
        O("vector", "tensor_add", K_sb[:], rtmp2[:], rtmp1[:], inc=s["kve"])
        O("vector", "tensor_copy", V_sb[:], PA[1][:, :], inc=s["kve"])
        for src_sb, dst_sb in ((K_sb, ktm), (V_sb, vtm)):
            for b4 in range(2):
                ps, off = (PB[2], 0) if trk % 2 == 0 else (PB[2], 1024)
                for q in range(4):
                    tb = b4 * 4 + q
                    wts = []
                    if q == 0:
                        wts = [(s["kve"], None)]
                        if trk >= 2:
                            wts.append((s["trKF"], trk - 1))
                        else:
                            wts += pt2_wait(trk % 2)
                    O("tensor", "transpose",
                      ps[:, off + q * 128: off + (q + 1) * 128],
                      src_sb[:, tb * 128:(tb + 1) * 128], identb[:], waits=wts,
                      inc=(s["trK"] if q == 3 else None))
                dst = dst_sb[:].rearrange("p (t i) -> p t i", i=128)[
                    :, b4 * 4:(b4 + 1) * 4, :]
                src = ps[:, off:off + 512].rearrange("p (t i) -> p t i", i=128)
                O("vector", "tensor_copy", dst, src, waits=[(s["trK"], None)],
                  inc=s["trKF"])
                trk += 1
        pt2_free[0] = (s["trKF"], s["trKF"].n)
        pt2_free[1] = (s["trKF"], s["trKF"].n)
        kdst = kv_in[h * NTOK:(h + 1) * NTOK, 0:128].rearrange(
            "(t p) c -> p t c", p=128)
        vdst = kv_in[h * NTOK:(h + 1) * NTOK, 128:256].rearrange(
            "(t p) c -> p t c", p=128)
        O("gpsimd", "dma_start", out=kdst,
          in_=ktm[:].rearrange("p (t i) -> p t i", i=128),
          waits=[(s["trKF"], None)], inc=s["kvs"], inc_amt=16)
        O("gpsimd", "dma_start", out=vdst,
          in_=vtm[:].rearrange("p (t i) -> p t i", i=128),
          inc=s["kvs"], inc_amt=16)
    P.coll("AllToAll", ALU.bypass, RG, [kv_in.opt()], [kv_out.opt()],
           waits=[(s["kvs"], None)], inc=s["collKV"])

    O("gpsimd", "dma_start", out=tab[:, 0:1024], in_=cosq_i,
      waits=[(s["kve"], None)], inc=s["cst"], inc_amt=16)
    O("gpsimd", "dma_start", out=tab[:, 1024:2048], in_=sinq_i,
      inc=s["cst"], inc_amt=16)
    TAB_Q = s["cst"].n

    if PHASES <= 3:
        return _finish()

    P.fence()
    for g in range(16):
        QPS = (PA[0], PA[1]) if g % 2 == 0 else (PA[2], PA[3])
        for g8 in range(8):
            slot = g8 % 4
            src = Wq[:, g * 256:(g + 1) * 256].rearrange(
                "(j p) i -> p j i", p=128)[:, g8 * 4:(g8 + 1) * 4, :]
            dst = WBUF[:, slot * 1024:(slot + 1) * 1024].rearrange(
                "p (j i) -> p j i", i=256)
            gidx = g * 8 + g8
            wts = []
            if gidx == 0:
                wts = [(s["wkvF"], 32)]
            elif gidx >= 4:
                wts = [(s["wqF"], gidx - 3)]
            O("gpsimd", "dma_start", out=dst, in_=src, waits=wts,
              inc=s[f"wq{slot}"], inc_amt=16, future=True)
        for g8 in range(8):
            slot = g8 % 4
            for jj in range(4):
                j = g8 * 4 + jj
                wts = []
                if jj == 0:
                    wts.append((s[f"wq{slot}"], ((g * 8 + g8) // 4 + 1) * 16))
                    if j == 0:
                        if g >= 2:
                            wts.append((s["qe"], 2 * (g - 1)))
                        if g == 0:
                            wts.append((s["kve"], None))
                            wts.append((s["cst"], TAB_Q))
                        if g == 1:
                            wts.append((s["trKF"], None))
                for hp in range(2):
                    for half in range(2):
                        O("tensor", "matmul",
                          QPS[hp][:, half * 512:(half + 1) * 512],
                          WBUF[:, slot * 1024 + jj * 256 + hp * 128:
                               slot * 1024 + jj * 256 + (hp + 1) * 128],
                          xn_fm[:, j * NTOK + half * 512:
                                j * NTOK + (half + 1) * 512],
                          start=(j == 0), stop=(j == NJ - 1),
                          skip_group_check=True, waits=wts,
                          inc=(s["wqF"] if (jj == 3 and hp == 1 and half == 1)
                               else None))
                        wts = []
        for hp in range(2):
            wts = [(s["wqF"], 8 * (g + 1))]
            if g > 0:
                wts.append((s[f"qs{hp}"], 16 * g))
            if g == 0 and hp == 0:
                wts.append((s["kvs"], None))
            O("vector", "tensor_mul", rtmp1[0:64, :], QPS[hp][64:128, :],
              tab[0:64, 1024:2048], waits=wts)
            O("vector", "tensor_mul", rtmp1[64:128, :], QPS[hp][0:64, :],
              tab[64:128, 1024:2048])
            O("vector", "tensor_mul", rtmp2[:], QPS[hp][:, :], tab[:, 0:1024])
            O("vector", "tensor_add", qstage[hp][:], rtmp2[:], rtmp1[:],
              inc=s["qe"])
            h = 2 * g + hp
            O("gpsimd", "dma_start", out=q_in[h * 128:(h + 1) * 128, :],
              in_=qstage[hp][:], waits=[(s["qe"], None)],
              inc=s[f"qs{hp}"], inc_amt=16)
    P.coll("AllToAll", ALU.bypass, RG, [q_in.opt()], [q_out.opt()],
           waits=[(s["qs0"], None), (s["qs1"], None)], inc=s["collQ"])

    if PHASES <= 4:
        return _finish()

    P.fence()
    O("gpsimd", "dma_start",
      out=At[:].rearrange("p (j i) -> p j i", i=128),
      in_=A2T.rearrange("(j p) i -> p j i", p=128),
      waits=[(s["qs0"], None), (s["qs1"], None)], inc=s["misc"], inc_amt=16)
    for j in range(NJ):
        for half in range(2):
            wts = []
            if j == 0 and half == 0:
                wts = [(s["misc"], None), (s["qe"], None)]
            O("tensor", "matmul", PA[0][:, half * 512:(half + 1) * 512],
              At[:, j * 128:(j + 1) * 128],
              xn_fm[:, j * NTOK + half * 512: j * NTOK + (half + 1) * 512],
              start=(j == 0), stop=(j == NJ - 1), skip_group_check=True,
              waits=wts, inc=(s["tfmT"] if (j == NJ - 1 and half == 1) else None))
    O("vector", "tensor_copy", tfm[:, 0:512], PA[0][:, 0:512],
      waits=[(s["tfmT"], None)])
    O("vector", "tensor_copy", tfm[:, 512:1024], PA[0][:, 512:1024], inc=s["tfm"])

    O("gpsimd", "dma_start", out=rsum_sb[:], in_=r_out,
      waits=[(s["collR"], 1)], inc=s["rtrG"], inc_amt=16)
    O("vector", "tensor_scalar_mul", rsum_sb[:], rsum_sb[:], 1.0 / NQTOK,
      waits=[(s["rtrG"], None)], inc=s["rtr"])
    rq = rsum_sb[:].rearrange("p (b j) -> p j b", b=2)
    for j in range(NJ):
        wts = ([(s["rtr"], None), (s["qe"], None)] + pt2_wait(0)) if j == 0 else []
        O("tensor", "matmul", PA[2][0:2, 0:8], rq[:, j, :],
          rwt_sb[:, j * 8:(j + 1) * 8], start=(j == 0), stop=(j == NJ - 1),
          skip_group_check=True, waits=wts,
          inc=(s["rtrT"] if j == NJ - 1 else None))
    O("vector", "tensor_add", logit_sb[:], PA[2][0:2, 0:8], rb2_sb[:],
      waits=[(s["rtrT"], None)], inc=s["rtr"])
    O("vector", "reduce_max", rmax[:], logit_sb[:], axis=AX.X, inc=s["rtr"])
    O("vector", "tensor_scalar_mul", rmax2[:], rmax[:], -1.0, inc=s["rtr"])
    O("scalar", "activation", rexp[:], logit_sb[:], AF.Exp, bias=rmax2[:, 0:1],
      waits=[(s["rtr"], None)], inc=s["rtrA2"])
    O("vector", "reduce_sum", rsumt[:], rexp[:], axis=AX.X,
      waits=[(s["rtrA2"], None)], inc=s["rtr"])
    O("vector", "reciprocal", rrec[:], rsumt[:], inc=s["rtr"])
    O("vector", "tensor_scalar", rsm[:], rexp[:], rrec[:, 0:1], None, ALU.mult,
      inc=s["rtr"])
    O("tensor", "transpose", PA[3][0:8, 0:2], rsm[:], identf[0:2, 0:2],
      waits=[(s["rtr"], None), (s["qe"], None)], inc=s["rtrT"])
    O("vector", "tensor_copy", routT[:], PA[3][0:8, 0:2],
      waits=[(s["rtrT"], None)], inc=s["rtr"])
    for b in range(2):
        O("tensor", "matmul", PA[2][:, 16 + b:17 + b], rep_f[:],
          routT[0:8, b:b + 1], start=True, stop=True, skip_group_check=True,
          waits=[(s["rtr"], None)], inc=s["rtrT"])
    O("vector", "tensor_copy", svec[:, 0:1], PA[2][:, 16:17],
      waits=[(s["rtrT"], None)], inc=s["rtr"])
    O("vector", "tensor_copy", svec[:, 1:2], PA[2][:, 17:18], inc=s["rtr"])
    pt2_free[0] = (s["rtr"], s["rtr"].n)
    pt2_free[1] = (s["qe"], s["qe"].n)
    pt2_free[2] = (s["tfm"], s["tfm"].n)
    pt2_free[3] = (s["tfm"], s["tfm"].n)
    for b in range(2):
        O("vector", "tensor_scalar", tsc[:, b * 512:(b + 1) * 512],
          tfm[:, b * 512:(b + 1) * 512], svec[:, b:b + 1], None, ALU.mult,
          inc=s["tfm"])
    if dbg:
        O("vector", "tensor_copy", rdbg[:, 0:8], logit_sb[:],
          waits=[(s["rtr"], None)])
        O("vector", "tensor_copy", rdbg[:, 8:9], rmax[:])
        O("vector", "tensor_copy", rdbg[:, 9:17], rexp[:])
        O("vector", "tensor_copy", rdbg[:, 17:18], rsumt[:])
        O("vector", "tensor_copy", rdbg[:, 18:19], rrec[:])
        O("vector", "tensor_copy", rdbg[:, 19:27], rsm[:], inc=s["rtr"])
        O("gpsimd", "dma_start", out=dbg_o["d_rout"], in_=rdbg[:, 19:27],
          waits=[(s["rtr"], None)], inc=s["dbg"], inc_amt=16)
        O("gpsimd", "dma_start", out=dbg_o["d_rdbg"], in_=rdbg[:],
          waits=[(s["rtr"], None)], inc=s["dbg"], inc_amt=16)

    if PHASES <= 5:
        return _finish()

    P.fence()
    SCSLOT = [(PA[2], 0), (PA[2], 512), (PA[0], 0), (PA[0], 512)]
    sc_i = 0
    pt_i = 0
    pm_i = 0
    vt_n = 0
    n_pv = 0
    n_msk = 0
    pt_slot_free = {}
    pm_slot_free = {}
    vt_slot_free = {}
    first_vt = True
    for b in range(2):
        for p in range(8):
            wts = [(s["collKV"], 1)]
            if b == 1 and p == 0:
                wts.append((s["pvd"], None))
            O("sync", "dma_start", out=K_res[:, p * 512:(p + 1) * 512],
              in_=kv_out[1024 * p + 512 * b: 1024 * p + 512 * b + 512, 0:128],
              transpose=True, waits=wts, inc=s["kres"], inc_amt=16)
        for p in range(8):
            wts = [(s["kres"], (b * 8 + 8) * 16)]
            if b == 1 and p == 0:
                wts.append((s["aev"], None))
            if b == 0 and p == 0:
                wts.append((s["tfm"], None))
            O("vector", "tensor_mul", qq[:], K_res[:, p * 512:(p + 1) * 512],
              K_res[:, p * 512:(p + 1) * 512], waits=wts, inc=s["qn"])
            O("tensor", "matmul", PA[2][0:1, 0:512], onescol[:], qq[:],
              start=True, stop=True, skip_group_check=True,
              waits=[(s["qn"], None)] + pt2_wait(0), inc=s["nmT"])
            O("vector", "reduce_max", kn_sb[0:1, p:p + 1], PA[2][0:1, 0:512],
              axis=AX.X, waits=[(s["nmT"], None)], inc=s["scF"])
            pt2_free[0] = (s["scF"], s["scF"].n)
        for qp in range(8):
            for hh in range(4):
                wts = []
                if hh == 0:
                    wts = [(s["collQ"], 1)]
                    if b == 0 and qp == 0:
                        wts.append((s["tfm"], None))
                    else:
                        wts.append((s["pvd"], None))
                O("scalar", "dma_start", out=qsb[:, hh * 512:(hh + 1) * 512],
                  in_=q_out[512 * qp + 128 * hh: 512 * qp + 128 * (hh + 1),
                            512 * b: 512 * (b + 1)],
                  waits=wts, inc=s["qld"], inc_amt=16)
            for hh in range(4):
                O("vector", "tensor_mul", qq[:], qsb[:, hh * 512:(hh + 1) * 512],
                  qsb[:, hh * 512:(hh + 1) * 512],
                  waits=[(s["qld"], None)], inc=s["qn"])
                O("tensor", "matmul", PA[2][0:1, 0:512], onescol[:], qq[:],
                  start=True, stop=True, skip_group_check=True,
                  waits=[(s["qn"], None)] + pt2_wait(0), inc=s["nmT"])
                O("vector", "reduce_max", qn2_sb[0:1, hh:hh + 1],
                  PA[2][0:1, 0:512], axis=AX.X, waits=[(s["nmT"], None)],
                  inc=s["scF"])
                pt2_free[0] = (s["scF"], s["scF"].n)
                O("vector", "reduce_max", mtmp[0:1, 0:1], kn_sb[0:1, 0:qp + 1],
                  axis=AX.X, inc=s["nm"])
                O("vector", "tensor_mul", mtmp[0:1, 1:2], qn2_sb[0:1, hh:hh + 1],
                  mtmp[0:1, 0:1], inc=s["nm"])
                O("scalar", "activation", mtmp[0:1, 2:3], mtmp[0:1, 1:2], AF.Ln,
                  waits=[(s["nm"], None)], inc=s["nmA"])
                O("vector", "tensor_scalar", mtmp[0:1, 1:2], mtmp[0:1, 2:3],
                  0.5, LN105, ALU.mult, ALU.add,
                  waits=[(s["nmA"], None)], inc=s["nm"])
                O("scalar", "activation", mtmp[0:1, 3:4], mtmp[0:1, 1:2], AF.Exp,
                  waits=[(s["nm"], None)], inc=s["nmA"])
                O("vector", "tensor_scalar_mul", mtmp_bf[0:1, 0:1],
                  mtmp[0:1, 3:4], -1.0, waits=[(s["nmA"], None)], inc=s["nm"])
                wts = [(s["nm"], None), (s["aev"], None)] + pt2_wait(2)
                if b == 0 and qp == 0 and hh == 0:
                    wts.append((s["tfm"], None))
                O("tensor", "matmul", PA[0][:, 0:1], onesrow[:],
                  mtmp_bf[0:1, 0:1], start=True, stop=True,
                  skip_group_check=True, waits=wts, inc=s["nmT"])
                O("vector", "tensor_copy", negm[:, hh:hh + 1], PA[0][:, 0:1],
                  waits=[(s["nmT"], None)], inc=s["nm"])
                pt2_free[2] = (s["nm"], s["nm"].n)
            for hpair in range(2):
                pend = []

                def flush_one():
                    nonlocal pend, n_pv
                    if not pend:
                        return
                    pd = pend.pop(0)
                    O("tensor", "matmul",
                      PA[1][0:1, pd["h2"] * 512:(pd["h2"] + 1) * 512],
                      onescol[:], pd["puse"], start=pd["first"],
                      stop=pd["last"], skip_group_check=True,
                      waits=pd["lw"])
                    O("tensor", "matmul",
                      PA[3][:, pd["h2"] * 512:(pd["h2"] + 1) * 512],
                      pd["vtile"], pd["puse"], start=pd["first"],
                      stop=pd["last"], skip_group_check=True,
                      waits=[pd["vw"]], inc=s["pvd"])
                    n_pv += 1
                    if pd["tile_of_tk_done"] is not None:
                        vt_slot_free[pd["tile_of_tk_done"]] = (s["pvd"], n_pv)
                    if pd["ptidx"] is not None:
                        pt_slot_free[pd["ptidx"]] = (s["pvd"], n_pv)
                    if pd["pmidx"] is not None:
                        pm_slot_free[pd["pmidx"]] = (s["pvd"], n_pv)

                for tk in range(qp + 1):
                    vslot = vt_n % 2
                    wts = [(s["collKV"], 1)]
                    if vt_n >= 2:
                        wts.append(vt_slot_free[vt_n - 2])
                    if first_vt:
                        wts.append((s["tfm"], None))
                        first_vt = False
                    O("scalar", "dma_start",
                      out=vt[vslot][:].rearrange("p (t c) -> p t c", c=128),
                      in_=kv_out[1024 * tk + 512 * b: 1024 * tk + 512 * b + 512,
                                 128:256].rearrange("(t p) c -> p t c", p=128),
                      waits=wts, inc=s[f"vt{vslot}"], inc_amt=16)
                    for tb in range(4):
                        for h2 in range(2):
                            if len(pend) >= 2:
                                flush_one()
                            hh = hpair * 2 + h2
                            first = (tk == 0 and tb == 0)
                            last = (tk == qp and tb == 3)
                            scs = sc_i % 4
                            sap, soff = SCSLOT[scs]
                            wts = list(pt2_wait(scs))
                            if first and tb == 0 and h2 == 0:
                                wts.append((s["nm"], None))
                                wts.append((s["kres"], (b * 8 + 8) * 16))
                            O("tensor", "matmul",
                              sap[:, soff:soff + 512],
                              K_res[:, tk * 512 + tb * 128:
                                    tk * 512 + (tb + 1) * 128],
                              qsb[:, hh * 512:(hh + 1) * 512],
                              start=True, stop=True, skip_group_check=True,
                              waits=wts, inc=s["sc"])
                            ptidx = pt_i
                            pts = ptidx % 5
                            ew = [(s["sc"], None)]
                            if ptidx >= 5:
                                ew.append(pt_slot_free[ptidx - 5])
                            O("scalar", "activation", pt[pts][:],
                              sap[:, soff:soff + 512], AF.Exp,
                              bias=negm[:, hh:hh + 1],
                              waits=ew, inc=s["exp"])
                            pt2_free[scs] = (s["exp"], s["exp"].n)
                            sc_i += 1
                            pt_i += 1
                            pmidx = None
                            if tk == qp:
                                pms = pm_i % 2
                                mw = [(s["exp"], None)]
                                if pm_i >= 2:
                                    mw.append(pm_slot_free[pm_i - 2])
                                O("vector", "tensor_mul", pm[pms][:], pt[pts][:],
                                  masks[:, tb * 512:(tb + 1) * 512],
                                  waits=mw, inc=s["msk"])
                                n_msk += 1
                                pt_slot_free[ptidx] = (s["msk"], n_msk)
                                pmidx = pm_i
                                pm_i += 1
                                puse = pm[pms][:]
                                pwait = [(s["msk"], None)]
                                ptrec = None
                            else:
                                puse = pt[pts][:]
                                pwait = [(s["exp"], None)]
                                ptrec = ptidx
                            lw = list(pwait)
                            if first:
                                lw.append((s["aev"], None))
                                if b == 0 and qp == 0 and hpair == 0:
                                    lw.append((s["rtr"], None))
                            cur = dict(h2=h2, puse=puse, first=first, last=last,
                                       lw=lw,
                                       vtile=vt[vslot][:, tb * 128:(tb + 1) * 128],
                                       vw=(s[f"vt{vslot}"], (vt_n // 2 + 1) * 16),
                                       ptidx=ptrec, pmidx=pmidx,
                                       tile_of_tk_done=(
                                           vt_n if (tb == 3 and h2 == 1) else None))
                            pend.append(cur)
                    vt_n += 1
                while pend:
                    flush_one()
                for h2 in range(2):
                    hh = hpair * 2 + h2
                    O("vector", "reciprocal",
                      qn_row[0:1, h2 * 512:(h2 + 1) * 512],
                      PA[1][0:1, h2 * 512:(h2 + 1) * 512],
                      waits=[(s["pvd"], None)], inc=s["aev"])
                    O("vector", "tensor_copy", mtmp_bf[0:1, 1:2],
                      mtmp[0:1, 0:1]) if False else None
                    O("tensor", "matmul", PA[0][:, h2 * 512:(h2 + 1) * 512],
                      onesrow_f[:], qn_row[0:1, h2 * 512:(h2 + 1) * 512],
                      start=True, stop=True, skip_group_check=True,
                      waits=[(s["aev"], None)] + pt2_wait(2 + h2), inc=s["nmT"])
                    O("vector", "tensor_copy", linvb[:, h2 * 512:(h2 + 1) * 512],
                      PA[0][:, h2 * 512:(h2 + 1) * 512],
                      waits=[(s["nmT"], None)], inc=s["aev"])
                    pt2_free[2 + h2] = (s["aev"], s["aev"].n)
                    O("vector", "tensor_mul", aosb[h2][:],
                      PA[3][:, h2 * 512:(h2 + 1) * 512],
                      linvb[:, h2 * 512:(h2 + 1) * 512],
                      waits=[(s[f"aos{h2}"], 16 * ((b * 8 + qp) * 2 + hpair))],
                      inc=s["aev"])
                    O("gpsimd", "dma_start",
                      out=ao_in[1024 * qp + 512 * b + 128 * hh:
                                1024 * qp + 512 * b + 128 * (hh + 1), :],
                      in_=aosb[h2][:], waits=[(s["aev"], None)],
                      inc=s[f"aos{h2}"], inc_amt=16)
    P.coll("AllToAll", ALU.bypass, RG, [ao_in.opt()], [ao_out.opt()],
           waits=[(s["aos0"], None), (s["aos1"], None)], inc=s["collAO"])

    if PHASES <= 6:
        return _finish()

    for j in range(NCORE):
        for bb in range(2):
            for hp in range(4):
                h = 4 * j + hp
                O("sync", "dma_start",
                  out=attn_fm[:, h * NTOK + 512 * bb: h * NTOK + 512 * bb + 512],
                  in_=ao_out[1024 * j + 512 * bb + 128 * hp:
                             1024 * j + 512 * bb + 128 * (hp + 1), :],
                  waits=([(s["collAO"], 1)] if (j == 0 and bb == 0 and hp == 0)
                         else []),
                  inc=s["afm"], inc_amt=16)
    N_AFM = s["afm"].n
    P.fence()
    O("gpsimd", "dma_start", out=fbb[:], in_=fb,
      waits=[(s["pvd"], None), (s["exp"], None), (s["msk"], None),
             (s["aev"], None), (s["qld"], None), (s["aos0"], None),
             (s["aos1"], None)],
      inc=s["misc"], inc_amt=16)
    O("gpsimd", "dma_start", out=bcat[:], in_=B2T,
      waits=[(s["pvd"], None), (s["aev"], None), (s["msk"], None),
             (s["qn"], None), (s["afm"], N_AFM)],
      inc=s["misc"], inc_amt=16)
    N_MISC = s["misc"].n

    if AFM_ONLY:
        return _finish()
    OUTB = [(PA[i // 2], (i % 2) * 512) for i in range(8)]
    for o in range(OLIM):
        for j in range(NJ):
            gidx = o * NJ + j
            O("gpsimd", "dma_start", out=wo_t[j % 2][:],
              in_=Wo[j * 128:(j + 1) * 128, o * 512:(o + 1) * 512],
              waits=([(s["woF"], gidx - 1)] if gidx >= 2 else []),
              inc=s[f"wo{j % 2}"], inc_amt=16, future=True)
        if FINMODE >= 2:
            for j in range(NJ):
                gidx = o * NJ + j
                O("gpsimd", "dma_start", out=wf_t[j % 2][:],
                  in_=WfT[j * 128:(j + 1) * 128, o * 512:(o + 1) * 512],
                  waits=([(s["wfF"], gidx - 1)] if gidx >= 2 else []),
                  inc=s[f"wf{j % 2}"], inc_amt=16, future=True)
        if FINMODE >= 3:
            for tt in range(NT):
                nxr = o * NT + tt
                O("scalar", "dma_start", out=xres[tt % 2][:],
                  in_=x_my[tt * 128:(tt + 1) * 128, o * 512:(o + 1) * 512],
                  waits=([(s["ouV"], nxr - 1)] if nxr >= 2 else []),
                  inc=s[f"xr{tt % 2}"], inc_amt=16, future=True)
        for j in range(NJ):
            for tt in range(NT):
                pa, po = OUTB[tt]
                wts = []
                if tt == 0:
                    wts.append((s[f"wo{j % 2}"], ((o * NJ + j) // 2 + 1) * 16))
                    if j == 0:
                        wts.append((s["afm"], N_AFM))
                        if o == 0:
                            wts.append((s["nmT"], None))
                            wts.append((s["aev"], None))
                            wts.append((s["exp"], None))
                        elif FINMODE >= 3:
                            wts.append((s["ouV"], o * 8))
                O("tensor", "matmul", pa[:, po:po + 512],
                  attn_fm[:, j * NTOK + tt * 128: j * NTOK + (tt + 1) * 128],
                  wo_t[j % 2][:], start=(j == 0), stop=False,
                  skip_group_check=True, waits=wts,
                  inc=(s["woF"] if tt == NT - 1 else None))
        if FINMODE >= 2:
            for j in range(NJ):
                for tt in range(NT):
                    pa, po = OUTB[tt]
                    wts = []
                    if tt == 0:
                        wts.append((s[f"wf{j % 2}"], ((o * NJ + j) // 2 + 1) * 16))
                    O("tensor", "matmul", pa[:, po:po + 512],
                      xn_fm[:, j * NTOK + tt * 128: j * NTOK + (tt + 1) * 128],
                      wf_t[j % 2][:], start=False, stop=False,
                      skip_group_check=True, waits=wts,
                      inc=(s["wfF"] if tt == NT - 1 else None))
        if FINMODE >= 3:
            for tt in range(NT):
                pa, po = OUTB[tt]
                wts = [(s["misc"], N_MISC), (s["tfm"], None)] if tt == 0 else []
                O("tensor", "matmul", pa[:, po:po + 512],
                  tsc[:, tt * 128:(tt + 1) * 128], bcat[:, o * 512:(o + 1) * 512],
                  start=False, stop=False, skip_group_check=True, waits=wts)
                O("tensor", "matmul", pa[:, po:po + 512],
                  onesrow[:], fbb[0:1, o * 512:(o + 1) * 512],
                  start=False, stop=True, skip_group_check=True, inc=s["ou"])
            for tt in range(NT):
                pa, po = OUTB[tt]
                nxr2 = o * NT + tt
                O("vector", "tensor_add", outsb[tt % 2][:], pa[:, po:po + 512],
                  xres[tt % 2][:],
                  waits=[(s["ou"], o * NT + tt + 1),
                         (s[f"xr{tt % 2}"], (nxr2 // 2 + 1) * 16),
                         (s[f"ouF{tt % 2}"], (nxr2 // 2) * 16)],
                  inc=s["ouV"])
                O("sync", "dma_start",
                  out=out_my[tt * 128:(tt + 1) * 128, o * 512:(o + 1) * 512],
                  in_=outsb[tt % 2][:], waits=[(s["ouV"], None)],
                  inc=s[f"ouF{tt % 2}"], inc_amt=16)
    P.cmds["sync"].append(lambda e, h=s["ouF0"].h, v=s["ouF0"].n: e.wait_ge(h, v))
    P.cmds["sync"].append(lambda e, h=s["ouF1"].h, v=s["ouF1"].n: e.wait_ge(h, v))

    if dbg:
        for j in range(NJ):
            O("vector", "tensor_copy", linvb[:, 0:1024].bitcast(BF16)[:, 0:1024]
              if False else linvb[:, 0:512],
              xn_fm[:, j * NTOK: j * NTOK + 512], waits=[(s["ouF0"], None), (s["ouF1"], None)])
            O("vector", "tensor_copy", linvb[:, 512:1024],
              xn_fm[:, j * NTOK + 512: (j + 1) * NTOK], inc=s["dbgV"])
            O("gpsimd", "dma_start",
              out=dbg_o["d_xnfm"][:, j * NTOK:(j + 1) * NTOK],
              in_=linvb[:], waits=[(s["dbgV"], None)], inc=s["dbg"], inc_amt=16)
            O("vector", "tensor_copy", linvb[:, 0:512],
              attn_fm[:, j * NTOK: j * NTOK + 512], waits=[(s["dbg"], None)])
            O("vector", "tensor_copy", linvb[:, 512:1024],
              attn_fm[:, j * NTOK + 512: (j + 1) * NTOK], inc=s["dbgV"])
            O("gpsimd", "dma_start",
              out=dbg_o["d_attn"][:, j * NTOK:(j + 1) * NTOK],
              in_=linvb[:], waits=[(s["dbgV"], None)], inc=s["dbg"], inc_amt=16)
        O("gpsimd", "dma_start", out=dbg_o["d_q"], in_=q_out,
          waits=[(s["collQ"], 1)], inc=s["dbg"], inc_amt=16)
        O("gpsimd", "dma_start", out=dbg_o["d_kv"], in_=kv_out,
          waits=[(s["collKV"], 1)], inc=s["dbg"], inc_amt=16)
        O("gpsimd", "dma_start", out=dbg_o["d_ao"], in_=ao_out,
          waits=[(s["collAO"], 1)], inc=s["dbg"], inc_amt=16)
        P.cmds["gpsimd"].append(
            lambda e, h=s["dbg"].h, v=s["dbg"].n: e.wait_ge(h, v))

    return _finish()



def _consts():
    c = {}
    c["identb"] = np.eye(128, dtype=ml_dtypes.bfloat16)
    c["identf"] = np.eye(128, dtype=np.float32)
    c["onescol"] = np.ones((128, 1), dtype=ml_dtypes.bfloat16)
    c["onesrow"] = np.ones((1, 128), dtype=ml_dtypes.bfloat16)
    c["onesrow_f"] = np.ones((1, 128), dtype=np.float32)
    rp = np.zeros((8, 128), dtype=np.float32)
    for e in range(8):
        rp[e, e * 16:(e + 1) * 16] = 2.0
    c["rep_f"] = rp
    m = np.zeros((512, 512), dtype=np.float32)
    i = np.arange(128)[:, None]
    jj = np.arange(512)[None, :]
    for blk in range(4):
        m[blk * 128:(blk + 1) * 128] = (blk * 128 + i <= jj)
    c["masks"] = m.astype(ml_dtypes.bfloat16)
    return c


def _rope_tables(core):
    pos = np.arange(512 * core, 512 * core + 512, dtype=np.float64)
    inv = 1.0 / (THETA ** (np.arange(0, HD, 2, dtype=np.float64) / HD))
    ang = pos[None, :] * inv[:, None]
    cos = np.cos(ang); sin = np.sin(ang)
    cosf = np.concatenate([cos, cos], 0)
    sinf = np.concatenate([-sin, sin], 0)
    cosf = np.concatenate([cosf, cosf], 1)
    sinf = np.concatenate([sinf, sinf], 1)
    sc = 1.0 / np.sqrt(HD)
    return (np.asarray(cosf * sc, np.float32), np.asarray(sinf * sc, np.float32),
            np.asarray(cosf, np.float32), np.asarray(sinf, np.float32))


def kernel(L0_attn_x, L0_attn_Wq, L0_attn_Wk, L0_attn_Wv, L0_attn_Wo,
           L0_ffn_W, L0_ffn_b, L0_ffn_A, L0_ffn_B, L0_ffn_router_W,
           L0_ffn_router_b):
    x = np.ascontiguousarray(np.asarray(L0_attn_x, np.float32))
    Wq_ = np.ascontiguousarray(np.asarray(L0_attn_Wq, np.float32))
    Wk_ = np.ascontiguousarray(np.asarray(L0_attn_Wk, np.float32))
    Wv_ = np.ascontiguousarray(np.asarray(L0_attn_Wv, np.float32))
    Wo_ = np.ascontiguousarray(np.asarray(L0_attn_Wo, np.float32))
    WfT_ = np.ascontiguousarray(np.asarray(L0_ffn_W, np.float32).T)
    fb_ = np.ascontiguousarray(np.asarray(L0_ffn_b, np.float32)).reshape(1, D)
    A2T_ = np.ascontiguousarray(np.asarray(L0_ffn_A, np.float32).reshape(128, D).T)
    B2T_ = np.ascontiguousarray(
        np.asarray(L0_ffn_B, np.float32).transpose(0, 2, 1).reshape(128, 4096))
    rW_ = np.ascontiguousarray(np.asarray(L0_ffn_router_W, np.float32))
    rb_ = np.ascontiguousarray(np.asarray(L0_ffn_router_b, np.float32))

    rwt_ = np.ascontiguousarray(
        rW_.T.reshape(32, 128, 8).transpose(1, 0, 2).reshape(128, 256))
    rb2_ = np.ascontiguousarray(np.tile(rb_.reshape(1, 8), (2, 1)))

    consts = _consts()
    shared = dict(Wq=Wq_, Wk=Wk_, Wv=Wv_, Wo=Wo_, WfT=WfT_, fb=fb_, A2T=A2T_,
                  B2T=B2T_, rwt=rwt_, rb2=rb2_, **consts)

    in_maps = []
    for c in range(NCORE):
        cq, sq, ck, sk = _rope_tables(c)
        x_myc = np.ascontiguousarray(
            np.concatenate([x[0, 512 * c:512 * (c + 1), :],
                            x[1, 512 * c:512 * (c + 1), :]], axis=0))
        rt = np.zeros((1, NTOK), np.float32)
        lo = max(Q_LO, 512 * c) - 512 * c
        hi = min(Q_HI, 512 * c + 512) - 512 * c
        if hi > lo:
            rt[0, lo:hi] = 1.0
            rt[0, 512 + lo:512 + hi] = 1.0
        m = dict(shared)
        m.update(x_my=x_myc, cosq=cq, sinq=sq, cosk=ck, sink=sk,
                 rtok=rt.astype(ml_dtypes.bfloat16))
        in_maps.append(m)

    nc = build_program(dbg=DBG)
    res = run_bass_kernel_spmd(nc, in_maps, core_ids=list(range(NCORE)))

    out = np.empty((2, SEQ, D), dtype=np.float32)
    for c in range(NCORE):
        o = res.results[c]["out_my"]
        out[0, 512 * c:512 * (c + 1), :] = o[0:512, :]
        out[1, 512 * c:512 * (c + 1), :] = o[512:1024, :]
    if DBG:
        kernel._dbg = res
    return out


# version 05
# speedup vs baseline: 1.3379x; 1.1666x over previous
import numpy as np
import ml_dtypes
from contextlib import ExitStack

import concourse.bass as bass
import concourse.mybir as mybir
from concourse.bass_utils import run_bass_kernel_spmd
from concourse.aot_env import allow_ici_all_to_all

F32 = mybir.dt.float32
BF16 = mybir.dt.bfloat16
AF = mybir.ActivationFunctionType
ALU = mybir.AluOpType
AX = mybir.AxisListType

NCORE = 8
D = 4096
SEQ = 4096
HD = 128
CH = 512
NTOK = 1024
NT = 8
NJ = 32
NHKV = 8
Q_LO, Q_HI = 611, 4095
NQTOK = Q_HI - Q_LO
EPS = 1e-5
LN105 = float(np.log(1.05))
THETA = 10000.0

DBG = False
PHASES = 7
OLIM = 8
AFM_ONLY = False
FINMODE = 3


class Sem:
    __slots__ = ("h", "n", "name", "vmap")

    def __init__(self, h, name):
        self.h, self.n, self.name = h, 0, name
        self.vmap = {}


class Prog:

    SELF_ENGS = ("vector", "scalar")

    def __init__(self, nc, ctx):
        self.nc, self.ctx = nc, ctx
        self.cmds = {e: [] for e in ("tensor", "vector", "scalar", "gpsimd", "sync")}
        self.selfsem = {e: Sem(ctx.enter_context(nc.semaphore(f"self_{e}")),
                               f"self_{e}") for e in self.SELF_ENGS}
        self.deps = {e: {"w": {}, "r": {}} for e in self.SELF_ENGS}
        self.fence_val = {e: 0 for e in self.SELF_ENGS}
        self.owner = {}

    def sem(self, name):
        return Sem(self.ctx.enter_context(self.nc.semaphore(name)), name)

    @staticmethod
    def _ap_name(a):
        t = getattr(a, "tensor", None)
        if t is not None and hasattr(t, "name"):
            return t.name
        if hasattr(a, "name") and hasattr(a, "ap"):
            return a.name
        return None

    def _rw_sets(self, args, kwargs):
        rset, wset = set(), set()
        for i, a in enumerate(args):
            nm = self._ap_name(a)
            if nm is None:
                continue
            (wset if i == 0 else rset).add(nm)
        for k, a in kwargs.items():
            nm = self._ap_name(a)
            if nm is None:
                continue
            if k in ("out", "accum_out"):
                wset.add(nm)
            else:
                rset.add(nm)
        return rset, wset

    def _resolve(self, sm, vv):
        eng = self.owner.get(sm.name)
        if eng is None:
            return (sm.h, vv)
        assert vv in sm.vmap, f"no vmap[{vv}] for {sm.name} (owner {eng})"
        return (self.selfsem[eng].h, sm.vmap[vv])

    def op(self, eng, meth, *args, waits=(), inc=None, inc_amt=1, future=False,
           **kwargs):
        wl = []
        for (sm, v) in waits:
            vv = sm.n if v is None else v
            if not future:
                assert vv <= sm.n, f"wait {sm.name}>={vv} but emitted {sm.n}"
            if vv > 0:
                wl.append((sm, vv))
        is_dma = meth in ("dma_start", "dma_start_transpose")
        ih = None
        if eng in self.SELF_ENGS:
            ss = self.selfsem[eng]
            if not is_dma:
                rset, wset = self._rw_sets(args, kwargs)
                dep = self.deps[eng]
                need = self.fence_val[eng]
                for t in rset | wset:
                    need = max(need, dep["w"].get(t, 0))
                for t in wset:
                    need = max(need, dep["r"].get(t, 0))
                if need > 0:
                    wl.insert(0, (ss, need))
                ss.n += 1
                for t in wset:
                    dep["w"][t] = ss.n
                for t in rset:
                    dep["r"][t] = max(dep["r"].get(t, 0), ss.n)
                ih = (ss.h, 1)
                if inc is not None:
                    assert inc_amt == 1
                    inc.n += 1
                    o = self.owner.setdefault(inc.name, eng)
                    assert o == eng, f"{inc.name} inc'd from {o} and {eng}"
                    inc.vmap[inc.n] = ss.n
            else:
                if inc is not None:
                    inc.n += (inc_amt if inc_amt is not None else 1)
                    ih = (inc.h, inc_amt)
        else:
            if inc is not None:
                assert inc.name not in self.owner, \
                    f"{inc.name} owned by {self.owner.get(inc.name)}"
                inc.n += (inc_amt if inc_amt is not None else 1)
                ih = (inc.h, inc_amt)

        def run(e, meth=meth, args=args, kwargs=kwargs, wl=wl, ih=ih):
            for (sm, v) in wl:
                h, v = self._resolve(sm, v)
                e.wait_ge(h, v)
            ins = getattr(e, meth)(*args, **kwargs)
            if ih is not None:
                if ih[1] is None or ih[1] == 1:
                    ins.then_inc(ih[0], 1)
                else:
                    ins.then_inc(ih[0], ih[1])
        self.cmds[eng].append(run)

    def coll(self, kind, op_, rg, ins_, outs_, waits=(), inc=None):
        wl = []
        for (sm, v) in waits:
            vv = sm.n if v is None else v
            if vv > 0:
                wl.append((sm, vv))
        inc.n += 1

        def run(e, wl=wl, ih=inc.h):
            for (sm, v) in wl:
                h, v = self._resolve(sm, v)
                e.wait_ge(h, v)
            with allow_ici_all_to_all():
                e.collective_compute(kind, op_, rg, ins_, outs_).then_inc(ih)
        self.cmds["gpsimd"].append(run)

    def fence(self):
        for e in self.SELF_ENGS:
            self.fence_val[e] = self.selfsem[e].n

    def bump(self, eng, sem, amt=1):
        raise RuntimeError("standalone bumps are forbidden (writeback race)")


def build_program(dbg=False):
    nc = bass.Bass("TRN2", target_bir_lowering=False)
    ctx = ExitStack()

    def din(name, shape, dt=F32):
        return nc.dram_tensor(name, shape, dt, kind="ExternalInput").ap()

    x_my = din("x_my", [NTOK, D])
    Wq = din("Wq", [D, D]); Wk = din("Wk", [D, 1024]); Wv = din("Wv", [D, 1024])
    Wo = din("Wo", [D, D]); WfT = din("WfT", [D, D])
    fb = din("fb", [1, D])
    A2T = din("A2T", [D, 128]); B2T = din("B2T", [128, D])
    rwt = din("rwt", [128, 256]); rb2 = din("rb2", [2, 8])
    cosq_i = din("cosq", [128, NTOK]); sinq_i = din("sinq", [128, NTOK])
    cosk_i = din("cosk", [128, NTOK]); sink_i = din("sink", [128, NTOK])
    masks_i = din("masks", [512, 512], BF16)
    identb_i = din("identb", [128, 128], BF16)
    identf_i = din("identf", [128, 128], F32)
    onescol_i = din("onescol", [128, 1], BF16)
    onesrow_i = din("onesrow", [1, 128], BF16)
    onesrow_f_i = din("onesrow_f", [1, 128], F32)
    rep_i = din("rep_f", [8, 128], F32)
    rtok_i = din("rtok", [1, NTOK], BF16)

    out_my = nc.dram_tensor("out_my", [NTOK, D], F32, kind="ExternalOutput").ap()

    kv_in = nc.dram_tensor("kv_in", [NCORE * NTOK, 256], BF16, kind="Internal").ap()
    kv_out = nc.dram_tensor("kv_out", [NCORE * NTOK, 256], BF16, kind="Internal").ap()
    q_in = nc.dram_tensor("q_in", [D, NTOK], BF16, kind="Internal").ap()
    q_out = nc.dram_tensor("q_out", [D, NTOK], BF16, kind="Internal").ap()
    ao_in = nc.dram_tensor("ao_in", [NCORE * NTOK, 512], BF16, kind="Internal").ap()
    ao_out = nc.dram_tensor("ao_out", [NCORE * NTOK, 512], BF16, kind="Internal").ap()
    r_in = nc.dram_tensor("r_in", [128, 64], F32, kind="Internal").ap()
    r_out = nc.dram_tensor("r_out", [128, 64], F32, kind="Internal").ap()

    dbg_o = {}
    if dbg:
        for nm_, shp, dt in (("d_xnfm", [128, NJ * NTOK], F32),
                             ("d_q", [D, NTOK], F32),
                             ("d_kv", [NCORE * NTOK, 256], F32),
                             ("d_ao", [NCORE * NTOK, 512], F32),
                             ("d_rout", [2, 8], F32),
                             ("d_rdbg", [2, 32], F32),
                             ("d_attn", [128, NJ * NTOK], F32)):
            dbg_o[nm_] = nc.dram_tensor(nm_, shp, dt, kind="ExternalOutput").ap()

    def sb(name, shape, dt):
        return nc.alloc_sbuf_tensor(name, shape, dt)

    xn_fm = sb("xn_fm", [128, NJ * NTOK], BF16)
    attn_fm = sb("attn_fm", [128, NJ * NTOK], BF16)
    tab = sb("tab", [128, 2048], F32)
    masks = sb("masks_s", [128, 2048], BF16)
    identb = sb("identb_s", [128, 128], BF16)
    identf = sb("identf_s", [128, 128], F32)
    onescol = sb("onescol_s", [128, 1], BF16)
    onesrow = sb("onesrow_s", [1, 128], BF16)
    onesrow_f = sb("onesrow_f_s", [1, 128], F32)
    rep_f = sb("rep_f_s", [8, 128], F32)
    rwt_sb = sb("rwt_sb", [128, 256], F32)
    rb2_sb = sb("rb2_sb", [2, 8], F32)
    tfm = sb("tfm", [128, NTOK], BF16)
    tsc = sb("tsc", [128, NTOK], BF16)
    kn_sb = sb("kn_sb", [1, 8], F32)
    kk = sb("kk", [128, 512], F32)
    rsum_sb = sb("rsum_sb", [128, 64], F32)
    rtok_sb = sb("rtok_sb", [1, NTOK], BF16)
    qn2_sb = sb("qn2_sb", [1, 4], F32)
    mtmp = sb("mtmp", [1, 4], F32)
    mtmp_bf = sb("mtmp_bf", [1, 2], BF16)
    K_res = sb("K_res", [128, 4096], BF16)
    lsum = sb("lsum", [128, 1], F32); lsq = sb("lsq", [128, 1], F32)
    lmean = sb("lmean", [128, 1], F32); lvar = sb("lvar", [128, 1], F32)
    lrstd = sb("lrstd", [128, 1], F32); lnegmr = sb("lnegmr", [128, 1], F32)
    ltmp = sb("ltmp", [128, 1], F32)
    logit_sb = sb("logit_sb", [2, 8], F32); rmax = sb("rmax", [2, 1], F32)
    rexp = sb("rexp", [2, 8], F32); rsumt = sb("rsumt", [2, 1], F32)
    rrec = sb("rrec", [2, 1], F32); rsm = sb("rsm", [2, 8], F32)
    rmax2 = sb("rmax2", [2, 1], F32)
    rdbg = sb("rdbg", [2, 32], F32)
    routT = sb("routT", [8, 2], F32)
    svec = sb("svec", [128, 2], F32)
    wo_t = [sb(f"wo_t{i}", [128, 512], BF16) for i in range(2)]
    wf_t = [sb(f"wf_t{i}", [128, 512], BF16) for i in range(2)]
    xres = [sb(f"xres{i}", [128, 512], F32) for i in range(2)]
    outsb = [sb(f"outsb{i}", [128, 512], F32) for i in range(2)]
    WBUF = sb("WBUF", [128, 4096], BF16)

    ARENA = sb("ARENA", [128, 6144], F32)
    ab = nc.lookup_mloc(ARENA).addr

    def at_(name, shape, dt, off):
        nb = shape[1] * (4 if dt == F32 else 2)
        assert off + nb <= 24576, name
        return nc.alloc_sbuf_tensor_at(name, shape, dt, offset=ab + off)

    XBUF = at_("XBUF", [128, 4096], F32, 0)
    xn_tm = at_("xn_tm", [128, 4096], BF16, 16384)
    rtok_bc = at_("rtok_bc", [128, 1024], F32, 0)
    K_sb = at_("K_sb", [128, 1024], BF16, 0)
    V_sb = at_("V_sb", [128, 1024], BF16, 2048)
    ktm = at_("ktm", [128, 1024], BF16, 4096)
    vtm = at_("vtm", [128, 1024], BF16, 6144)
    rtmp1 = at_("rtmp1", [128, 1024], F32, 8192)
    rtmp2 = at_("rtmp2", [128, 1024], F32, 12288)
    qstage = [at_(f"qstage{i}", [128, 1024], BF16, 2048 * i) for i in range(2)]
    At = at_("At", [128, 4096], BF16, 16384)
    qsb = at_("qsb", [128, 2048], BF16, 0)
    vt = [at_(f"vt{i}", [128, 512], BF16, 4096 + 1024 * i) for i in range(2)]
    pt = [at_(f"pt{i}", [128, 512], BF16, off)
          for i, off in enumerate((6144, 7168, 8192, 15360, 22560))]
    pm = [at_(f"pm{i}", [128, 512], BF16, 9216 + 1024 * i) for i in range(2)]
    linvb = at_("linvb", [128, 1024], F32, 11264)
    qq = nc.alloc_sbuf_tensor_at("qq", [128, 512], BF16,
                                 offset=nc.lookup_mloc(kk).addr)
    qn_row = at_("qn_row", [1, 1024], F32, 16384)
    negm = at_("negm", [128, 4], F32, 20480)
    aosb = [at_(f"aosb{i}", [128, 512], BF16, 20512 + 1024 * i) for i in range(2)]
    fbb = at_("fbb", [1, 4096], BF16, 0)
    bcat = at_("bcat", [128, 4096], BF16, 8192)

    PT = [ctx.enter_context(nc.psum_tensor(f"PT{i}", [128, 1024], F32)) for i in range(4)]
    PA = [p.ap() for p in PT]
    PB = [p.ap().bitcast(BF16) for p in PT]

    P = Prog(nc, ctx)
    O, B_ = P.op, P.bump
    RG = [list(range(NCORE))]

    s = {n: P.sem(n) for n in (
        "wkv0", "wkv1", "wq0", "wq1", "wq2", "wq3", "wo0", "wo1", "wf0", "wf1",
        "xr0", "xr1", "vt0", "vt1", "qs0", "qs1", "aos0", "aos1", "ouF0", "ouF1",
        "collR", "collKV", "collQ", "collAO",
        "cst", "conv", "x", "xfree", "lnv", "lna", "xn", "xnfree",
        "trA", "trAF", "phA", "rtr", "rtrG", "rtrA2", "rtrT",
        "wkv", "wkvF", "kvt", "kve", "kvs",
        "wq", "wqF", "qp", "qe", "qs",
        "coll", "misc", "trK", "trKF", "tfm", "tfmT",
        "kres", "vt", "vtF", "qld", "qn", "nm", "nmA", "nmT",
        "sc", "scF", "exp", "ptF", "msk", "pmF", "lpv", "pvd", "aev", "aos",
        "afm", "wo", "woF", "wf", "wfF", "xr", "xrF", "ou", "ouV", "ouF",
        "dbg", "dbgV")}

    pt2_free = {0: None, 1: None, 2: None, 3: None}
    SCSLOT = None

    def pt2_wait(bank):
        f = pt2_free[bank]
        return [f] if f is not None else []

    for dst, src in ((identb[:], identb_i), (identf[:], identf_i),
                     (onescol[:], onescol_i), (onesrow[:], onesrow_i),
                     (onesrow_f[:], onesrow_f_i), (rep_f[:], rep_i),
                     (rwt_sb[:], rwt), (rb2_sb[:], rb2), (rtok_sb[:], rtok_i)):
        O("gpsimd", "dma_start", out=dst, in_=src, inc=s["cst"], inc_amt=16)
    O("gpsimd", "dma_start",
      out=masks[:].rearrange("p (m f) -> p m f", f=512),
      in_=masks_i.rearrange("(m p) f -> p m f", p=128), inc=s["cst"], inc_amt=16)
    O("gpsimd", "dma_start", out=tab[:, 0:1024], in_=cosk_i, inc=s["cst"], inc_amt=16)
    O("gpsimd", "dma_start", out=tab[:, 1024:2048], in_=sink_i, inc=s["cst"], inc_amt=16)
    N_CST = s["cst"].n

    TRSLOT = [(PB[0], 0), (PB[0], 1024), (PB[1], 0), (PB[1], 1024)]
    trb = 0
    for tt in range(NT):
        O("sync", "dma_start", out=XBUF[:], in_=x_my[tt * 128:(tt + 1) * 128, :],
          waits=[(s["xn"], None)], inc=s["x"], inc_amt=16)
        O("vector", "reduce_sum", lsum[:], XBUF[:], axis=AX.X,
          waits=[(s["x"], None)], inc=s["lnv"])
        O("scalar", "activation", xn_tm[:], XBUF[:], AF.Square, accum_out=lsq[:],
          waits=[(s["x"], None)] + ([(s["trAF"], 8 * tt)] if tt else []),
          inc=s["lna"])
        O("vector", "tensor_scalar_mul", lmean[:], lsum[:], 1.0 / D, inc=s["lnv"])
        O("vector", "tensor_scalar_mul", lvar[:], lsq[:], 1.0 / D,
          waits=[(s["lna"], None)], inc=s["lnv"])
        O("vector", "tensor_mul", ltmp[:], lmean[:], lmean[:], inc=s["lnv"])
        O("vector", "tensor_sub", lvar[:], lvar[:], ltmp[:], inc=s["lnv"])
        O("vector", "tensor_scalar_add", lvar[:], lvar[:], EPS, inc=s["lnv"])
        O("scalar", "activation", ltmp[:], lvar[:], AF.Sqrt,
          waits=[(s["lnv"], None)], inc=s["lna"])
        O("vector", "reciprocal", lrstd[:], ltmp[:], waits=[(s["lna"], None)],
          inc=s["lnv"])
        O("vector", "tensor_scalar", lnegmr[:], lmean[:], lrstd[:, 0:1], -1.0,
          ALU.mult, ALU.mult, inc=s["lnv"])
        O("scalar", "activation", xn_tm[:], XBUF[:], AF.Identity,
          scale=lrstd[:, 0:1], bias=lnegmr[:, 0:1],
          waits=[(s["lnv"], None)], inc=s["xn"])
        for b8 in range(8):
            ps, off = TRSLOT[trb % 4]
            for q in range(4):
                j = b8 * 4 + q
                wts = []
                if q == 0:
                    wts = [(s["xn"], None)]
                    if trb >= 4:
                        wts.append((s["trAF"], trb - 3))
                    if trb == 0:
                        wts.append((s["cst"], N_CST))
                O("tensor", "transpose", ps[:, off + q * 128: off + (q + 1) * 128],
                  xn_tm[:, j * 128:(j + 1) * 128], identb[:], waits=wts,
                  inc=(s["trA"] if q == 3 else None))
            dst = xn_fm[:].rearrange("p (j t) -> p j t", t=NTOK)[
                :, b8 * 4:(b8 + 1) * 4, tt * 128:(tt + 1) * 128]
            src = ps[:, off:off + 512].rearrange("p (j t) -> p j t", t=128)
            O("vector", "tensor_copy", dst, src, waits=[(s["trA"], None)],
              inc=s["trAF"])
            trb += 1

    def _finish():
        with nc.Block() as block:
            @block.tensor
            def _(eng):
                for fn in P.cmds["tensor"]:
                    fn(eng)

            @block.vector
            def _(eng):
                for fn in P.cmds["vector"]:
                    fn(eng)

            @block.scalar
            def _(eng):
                for fn in P.cmds["scalar"]:
                    fn(eng)

            @block.gpsimd
            def _(eng):
                for fn in P.cmds["gpsimd"]:
                    fn(eng)

            @block.sync
            def _(eng):
                for fn in P.cmds["sync"]:
                    fn(eng)
        nc._keepalive_ctx = ctx
        return nc

    if PHASES <= 1:
        return _finish()

    P.fence()
    for half in range(2):
        O("tensor", "matmul", PA[2][:, half * 512:(half + 1) * 512],
          onesrow[:], rtok_sb[0:1, half * 512:(half + 1) * 512],
          start=True, stop=True, skip_group_check=True,
          waits=([(s["cst"], N_CST), (s["trAF"], NT * 8)] if half == 0 else []),
          inc=(s["rtrT"] if half == 1 else None))
    O("vector", "tensor_copy", rtok_bc[:], PA[2][:, 0:1024],
      waits=[(s["rtrT"], None)], inc=s["rtr"])
    pt2_free[0] = (s["rtr"], s["rtr"].n)
    pt2_free[1] = (s["rtr"], s["rtr"].n)
    for b in range(2):
        for j in range(NJ):
            src = xn_fm[:, j * NTOK + b * 512: j * NTOK + (b + 1) * 512]
            wts = [(s["rtr"], None)] if (b == 0 and j == 0) else []
            O("vector", "tensor_mul", kk[:], src,
              rtok_bc[:, b * 512:(b + 1) * 512], waits=wts)
            O("vector", "reduce_sum", rsum_sb[:, b * 32 + j: b * 32 + j + 1],
              kk[:], axis=AX.X,
              inc=(s["rtr"] if (b == 1 and j == NJ - 1) else None))
    O("gpsimd", "dma_start", out=r_in, in_=rsum_sb[:],
      waits=[(s["rtr"], None)], inc=s["rtrG"], inc_amt=16)
    P.coll("AllReduce", ALU.add, RG, [r_in.opt()], [r_out.opt()],
           waits=[(s["rtrG"], None)], inc=s["collR"])

    if PHASES <= 2:
        return _finish()

    P.fence()
    trk = 0
    for h in range(NHKV):
        for g4 in range(4):
            slot = g4 % 2
            ksrc = Wk[:, h * 128:(h + 1) * 128].rearrange(
                "(j p) i -> p j i", p=128)[:, g4 * 8:(g4 + 1) * 8, :]
            vsrc = Wv[:, h * 128:(h + 1) * 128].rearrange(
                "(j p) i -> p j i", p=128)[:, g4 * 8:(g4 + 1) * 8, :]
            kdst = WBUF[:, slot * 1024:(slot + 1) * 1024].rearrange(
                "p (j i) -> p j i", i=128)
            vdst = WBUF[:, 2048 + slot * 1024: 2048 + (slot + 1) * 1024].rearrange(
                "p (j i) -> p j i", i=128)
            gidx = h * 4 + g4
            wts = [(s["wkvF"], gidx - 1)] if gidx >= 2 else []
            O("gpsimd", "dma_start", out=kdst, in_=ksrc, waits=wts,
              inc=s[f"wkv{slot}"], inc_amt=16, future=True)
            O("gpsimd", "dma_start", out=vdst, in_=vsrc,
              inc=s[f"wkv{slot}"], inc_amt=16)
        for g4 in range(4):
            slot = g4 % 2
            for jj in range(8):
                j = g4 * 8 + jj
                wts = []
                if jj == 0:
                    wts.append((s[f"wkv{slot}"], ((h * 4 + g4) // 2 + 1) * 32))
                    if j == 0:
                        wts.append((s["trAF"], NT * 8))
                        if h > 0:
                            wts.append((s["kve"], None))
                for half in range(2):
                    O("tensor", "matmul", PA[0][:, half * 512:(half + 1) * 512],
                      WBUF[:, slot * 1024 + jj * 128: slot * 1024 + (jj + 1) * 128],
                      xn_fm[:, j * NTOK + half * 512: j * NTOK + (half + 1) * 512],
                      start=(j == 0), stop=(j == NJ - 1), skip_group_check=True,
                      waits=(wts if half == 0 else []))
                    wts = []
                    O("tensor", "matmul", PA[1][:, half * 512:(half + 1) * 512],
                      WBUF[:, 2048 + slot * 1024 + jj * 128:
                           2048 + slot * 1024 + (jj + 1) * 128],
                      xn_fm[:, j * NTOK + half * 512: j * NTOK + (half + 1) * 512],
                      start=(j == 0), stop=(j == NJ - 1), skip_group_check=True,
                      inc=(s["wkvF"] if (jj == 7 and half == 1) else None))
        wts = [(s["wkvF"], 4 * (h + 1))]
        if h >= 1:
            wts.append((s["kvs"], None))
        O("vector", "tensor_mul", rtmp1[0:64, :], PA[0][64:128, :],
          tab[0:64, 1024:2048], waits=wts)
        O("vector", "tensor_mul", rtmp1[64:128, :], PA[0][0:64, :],
          tab[64:128, 1024:2048])
        O("vector", "tensor_mul", rtmp2[:], PA[0][:, :], tab[:, 0:1024])
        O("vector", "tensor_add", K_sb[:], rtmp2[:], rtmp1[:], inc=s["kve"])
        O("vector", "tensor_copy", V_sb[:], PA[1][:, :], inc=s["kve"])
        for src_sb, dst_sb in ((K_sb, ktm), (V_sb, vtm)):
            for b4 in range(2):
                ps, off = (PB[2], 0) if trk % 2 == 0 else (PB[2], 1024)
                for q in range(4):
                    tb = b4 * 4 + q
                    wts = []
                    if q == 0:
                        wts = [(s["kve"], None)]
                        if trk >= 2:
                            wts.append((s["trKF"], trk - 1))
                        else:
                            wts += pt2_wait(trk % 2)
                    O("tensor", "transpose",
                      ps[:, off + q * 128: off + (q + 1) * 128],
                      src_sb[:, tb * 128:(tb + 1) * 128], identb[:], waits=wts,
                      inc=(s["trK"] if q == 3 else None))
                dst = dst_sb[:].rearrange("p (t i) -> p t i", i=128)[
                    :, b4 * 4:(b4 + 1) * 4, :]
                src = ps[:, off:off + 512].rearrange("p (t i) -> p t i", i=128)
                O("vector", "tensor_copy", dst, src, waits=[(s["trK"], None)],
                  inc=s["trKF"])
                trk += 1
        pt2_free[0] = (s["trKF"], s["trKF"].n)
        pt2_free[1] = (s["trKF"], s["trKF"].n)
        kdst = kv_in[h * NTOK:(h + 1) * NTOK, 0:128].rearrange(
            "(t p) c -> p t c", p=128)
        vdst = kv_in[h * NTOK:(h + 1) * NTOK, 128:256].rearrange(
            "(t p) c -> p t c", p=128)
        O("gpsimd", "dma_start", out=kdst,
          in_=ktm[:].rearrange("p (t i) -> p t i", i=128),
          waits=[(s["trKF"], None)], inc=s["kvs"], inc_amt=16)
        O("gpsimd", "dma_start", out=vdst,
          in_=vtm[:].rearrange("p (t i) -> p t i", i=128),
          inc=s["kvs"], inc_amt=16)
    P.coll("AllToAll", ALU.bypass, RG, [kv_in.opt()], [kv_out.opt()],
           waits=[(s["kvs"], None)], inc=s["collKV"])

    O("gpsimd", "dma_start", out=tab[:, 0:1024], in_=cosq_i,
      waits=[(s["kve"], None)], inc=s["cst"], inc_amt=16)
    O("gpsimd", "dma_start", out=tab[:, 1024:2048], in_=sinq_i,
      inc=s["cst"], inc_amt=16)
    TAB_Q = s["cst"].n

    if PHASES <= 3:
        return _finish()

    P.fence()
    for g in range(16):
        QPS = (PA[0], PA[1]) if g % 2 == 0 else (PA[2], PA[3])
        for g8 in range(8):
            slot = g8 % 4
            src = Wq[:, g * 256:(g + 1) * 256].rearrange(
                "(j p) i -> p j i", p=128)[:, g8 * 4:(g8 + 1) * 4, :]
            dst = WBUF[:, slot * 1024:(slot + 1) * 1024].rearrange(
                "p (j i) -> p j i", i=256)
            gidx = g * 8 + g8
            wts = []
            if gidx == 0:
                wts = [(s["wkvF"], 32)]
            elif gidx >= 4:
                wts = [(s["wqF"], gidx - 3)]
            O("gpsimd", "dma_start", out=dst, in_=src, waits=wts,
              inc=s[f"wq{slot}"], inc_amt=16, future=True)
        for g8 in range(8):
            slot = g8 % 4
            for jj in range(4):
                j = g8 * 4 + jj
                wts = []
                if jj == 0:
                    wts.append((s[f"wq{slot}"], ((g * 8 + g8) // 4 + 1) * 16))
                    if j == 0:
                        if g >= 2:
                            wts.append((s["qe"], 2 * (g - 1)))
                        if g == 0:
                            wts.append((s["kve"], None))
                            wts.append((s["cst"], TAB_Q))
                        if g == 1:
                            wts.append((s["trKF"], None))
                for hp in range(2):
                    for half in range(2):
                        O("tensor", "matmul",
                          QPS[hp][:, half * 512:(half + 1) * 512],
                          WBUF[:, slot * 1024 + jj * 256 + hp * 128:
                               slot * 1024 + jj * 256 + (hp + 1) * 128],
                          xn_fm[:, j * NTOK + half * 512:
                                j * NTOK + (half + 1) * 512],
                          start=(j == 0), stop=(j == NJ - 1),
                          skip_group_check=True, waits=wts,
                          inc=(s["wqF"] if (jj == 3 and hp == 1 and half == 1)
                               else None))
                        wts = []
        for hp in range(2):
            wts = [(s["wqF"], 8 * (g + 1))]
            if g > 0:
                wts.append((s[f"qs{hp}"], 16 * g))
            if g == 0 and hp == 0:
                wts.append((s["kvs"], None))
            O("vector", "tensor_mul", rtmp1[0:64, :], QPS[hp][64:128, :],
              tab[0:64, 1024:2048], waits=wts)
            O("vector", "tensor_mul", rtmp1[64:128, :], QPS[hp][0:64, :],
              tab[64:128, 1024:2048])
            O("vector", "tensor_mul", rtmp2[:], QPS[hp][:, :], tab[:, 0:1024])
            O("vector", "tensor_add", qstage[hp][:], rtmp2[:], rtmp1[:],
              inc=s["qe"])
            h = 2 * g + hp
            O("gpsimd", "dma_start", out=q_in[h * 128:(h + 1) * 128, :],
              in_=qstage[hp][:], waits=[(s["qe"], None)],
              inc=s[f"qs{hp}"], inc_amt=16)
    P.coll("AllToAll", ALU.bypass, RG, [q_in.opt()], [q_out.opt()],
           waits=[(s["qs0"], None), (s["qs1"], None)], inc=s["collQ"])

    if PHASES <= 4:
        return _finish()

    P.fence()
    O("gpsimd", "dma_start",
      out=At[:].rearrange("p (j i) -> p j i", i=128),
      in_=A2T.rearrange("(j p) i -> p j i", p=128),
      waits=[(s["qs0"], None), (s["qs1"], None)], inc=s["misc"], inc_amt=16)
    for j in range(NJ):
        for half in range(2):
            wts = []
            if j == 0 and half == 0:
                wts = [(s["misc"], None), (s["qe"], None)]
            O("tensor", "matmul", PA[0][:, half * 512:(half + 1) * 512],
              At[:, j * 128:(j + 1) * 128],
              xn_fm[:, j * NTOK + half * 512: j * NTOK + (half + 1) * 512],
              start=(j == 0), stop=(j == NJ - 1), skip_group_check=True,
              waits=wts, inc=(s["tfmT"] if (j == NJ - 1 and half == 1) else None))
    O("vector", "tensor_copy", tfm[:, 0:512], PA[0][:, 0:512],
      waits=[(s["tfmT"], None)])
    O("vector", "tensor_copy", tfm[:, 512:1024], PA[0][:, 512:1024], inc=s["tfm"])

    O("gpsimd", "dma_start", out=rsum_sb[:], in_=r_out,
      waits=[(s["collR"], 1)], inc=s["rtrG"], inc_amt=16)
    O("vector", "tensor_scalar_mul", rsum_sb[:], rsum_sb[:], 1.0 / NQTOK,
      waits=[(s["rtrG"], None)], inc=s["rtr"])
    rq = rsum_sb[:].rearrange("p (b j) -> p j b", b=2)
    for j in range(NJ):
        wts = ([(s["rtr"], None), (s["qe"], None)] + pt2_wait(0)) if j == 0 else []
        O("tensor", "matmul", PA[2][0:2, 0:8], rq[:, j, :],
          rwt_sb[:, j * 8:(j + 1) * 8], start=(j == 0), stop=(j == NJ - 1),
          skip_group_check=True, waits=wts,
          inc=(s["rtrT"] if j == NJ - 1 else None))
    O("vector", "tensor_add", logit_sb[:], PA[2][0:2, 0:8], rb2_sb[:],
      waits=[(s["rtrT"], None)], inc=s["rtr"])
    O("vector", "reduce_max", rmax[:], logit_sb[:], axis=AX.X, inc=s["rtr"])
    O("vector", "tensor_scalar_mul", rmax2[:], rmax[:], -1.0, inc=s["rtr"])
    O("scalar", "activation", rexp[:], logit_sb[:], AF.Exp, bias=rmax2[:, 0:1],
      waits=[(s["rtr"], None)], inc=s["rtrA2"])
    O("vector", "reduce_sum", rsumt[:], rexp[:], axis=AX.X,
      waits=[(s["rtrA2"], None)], inc=s["rtr"])
    O("vector", "reciprocal", rrec[:], rsumt[:], inc=s["rtr"])
    O("vector", "tensor_scalar", rsm[:], rexp[:], rrec[:, 0:1], None, ALU.mult,
      inc=s["rtr"])
    O("tensor", "transpose", PA[3][0:8, 0:2], rsm[:], identf[0:2, 0:2],
      waits=[(s["rtr"], None), (s["qe"], None)], inc=s["rtrT"])
    O("vector", "tensor_copy", routT[:], PA[3][0:8, 0:2],
      waits=[(s["rtrT"], None)], inc=s["rtr"])
    for b in range(2):
        O("tensor", "matmul", PA[2][:, 16 + b:17 + b], rep_f[:],
          routT[0:8, b:b + 1], start=True, stop=True, skip_group_check=True,
          waits=[(s["rtr"], None)], inc=s["rtrT"])
    O("vector", "tensor_copy", svec[:, 0:1], PA[2][:, 16:17],
      waits=[(s["rtrT"], None)], inc=s["rtr"])
    O("vector", "tensor_copy", svec[:, 1:2], PA[2][:, 17:18], inc=s["rtr"])
    pt2_free[0] = (s["rtr"], s["rtr"].n)
    pt2_free[1] = (s["qe"], s["qe"].n)
    pt2_free[2] = (s["tfm"], s["tfm"].n)
    pt2_free[3] = (s["tfm"], s["tfm"].n)
    for b in range(2):
        O("vector", "tensor_scalar", tsc[:, b * 512:(b + 1) * 512],
          tfm[:, b * 512:(b + 1) * 512], svec[:, b:b + 1], None, ALU.mult,
          inc=s["tfm"])
    if dbg:
        O("vector", "tensor_copy", rdbg[:, 0:8], logit_sb[:],
          waits=[(s["rtr"], None)])
        O("vector", "tensor_copy", rdbg[:, 8:9], rmax[:])
        O("vector", "tensor_copy", rdbg[:, 9:17], rexp[:])
        O("vector", "tensor_copy", rdbg[:, 17:18], rsumt[:])
        O("vector", "tensor_copy", rdbg[:, 18:19], rrec[:])
        O("vector", "tensor_copy", rdbg[:, 19:27], rsm[:], inc=s["rtr"])
        O("gpsimd", "dma_start", out=dbg_o["d_rout"], in_=rdbg[:, 19:27],
          waits=[(s["rtr"], None)], inc=s["dbg"], inc_amt=16)
        O("gpsimd", "dma_start", out=dbg_o["d_rdbg"], in_=rdbg[:],
          waits=[(s["rtr"], None)], inc=s["dbg"], inc_amt=16)

    if PHASES <= 5:
        return _finish()

    P.fence()
    SCSLOT = [(PA[2], 0), (PA[2], 512), (PA[0], 0), (PA[0], 512)]
    sc_i = 0
    pt_i = 0
    pm_i = 0
    vt_n = 0
    n_pv = 0
    n_msk = 0
    pt_slot_free = {}
    pm_slot_free = {}
    vt_slot_free = {}
    first_vt = True
    for b in range(2):
        for p in range(8):
            wts = [(s["collKV"], 1)]
            if b == 1 and p == 0:
                wts.append((s["pvd"], None))
            O("sync", "dma_start", out=K_res[:, p * 512:(p + 1) * 512],
              in_=kv_out[1024 * p + 512 * b: 1024 * p + 512 * b + 512, 0:128],
              transpose=True, waits=wts, inc=s["kres"], inc_amt=16)
        for p in range(8):
            wts = [(s["kres"], (b * 8 + 8) * 16)]
            if b == 1 and p == 0:
                wts.append((s["aev"], None))
            if b == 0 and p == 0:
                wts.append((s["tfm"], None))
            O("vector", "tensor_mul", qq[:], K_res[:, p * 512:(p + 1) * 512],
              K_res[:, p * 512:(p + 1) * 512], waits=wts, inc=s["qn"])
            O("tensor", "matmul", PA[2][0:1, 0:512], onescol[:], qq[:],
              start=True, stop=True, skip_group_check=True,
              waits=[(s["qn"], None)] + pt2_wait(0), inc=s["nmT"])
            O("vector", "reduce_max", kn_sb[0:1, p:p + 1], PA[2][0:1, 0:512],
              axis=AX.X, waits=[(s["nmT"], None)], inc=s["scF"])
            pt2_free[0] = (s["scF"], s["scF"].n)
        for qp in range(8):
            for hh in range(4):
                wts = []
                if hh == 0:
                    wts = [(s["collQ"], 1)]
                    if b == 0 and qp == 0:
                        wts.append((s["tfm"], None))
                    else:
                        wts.append((s["pvd"], None))
                O("scalar", "dma_start", out=qsb[:, hh * 512:(hh + 1) * 512],
                  in_=q_out[512 * qp + 128 * hh: 512 * qp + 128 * (hh + 1),
                            512 * b: 512 * (b + 1)],
                  waits=wts, inc=s["qld"], inc_amt=16)
            for hh in range(4):
                O("vector", "tensor_mul", qq[:], qsb[:, hh * 512:(hh + 1) * 512],
                  qsb[:, hh * 512:(hh + 1) * 512],
                  waits=[(s["qld"], None)], inc=s["qn"])
                O("tensor", "matmul", PA[2][0:1, 0:512], onescol[:], qq[:],
                  start=True, stop=True, skip_group_check=True,
                  waits=[(s["qn"], None)] + pt2_wait(0), inc=s["nmT"])
                O("vector", "reduce_max", qn2_sb[0:1, hh:hh + 1],
                  PA[2][0:1, 0:512], axis=AX.X, waits=[(s["nmT"], None)],
                  inc=s["scF"])
                pt2_free[0] = (s["scF"], s["scF"].n)
                O("vector", "reduce_max", mtmp[0:1, 0:1], kn_sb[0:1, 0:qp + 1],
                  axis=AX.X, inc=s["nm"])
                O("vector", "tensor_mul", mtmp[0:1, 1:2], qn2_sb[0:1, hh:hh + 1],
                  mtmp[0:1, 0:1], inc=s["nm"])
                O("scalar", "activation", mtmp[0:1, 2:3], mtmp[0:1, 1:2], AF.Ln,
                  waits=[(s["nm"], None)], inc=s["nmA"])
                O("vector", "tensor_scalar", mtmp[0:1, 1:2], mtmp[0:1, 2:3],
                  0.5, LN105, ALU.mult, ALU.add,
                  waits=[(s["nmA"], None)], inc=s["nm"])
                O("scalar", "activation", mtmp[0:1, 3:4], mtmp[0:1, 1:2], AF.Exp,
                  waits=[(s["nm"], None)], inc=s["nmA"])
                O("vector", "tensor_scalar_mul", mtmp_bf[0:1, 0:1],
                  mtmp[0:1, 3:4], -1.0, waits=[(s["nmA"], None)], inc=s["nm"])
                wts = [(s["nm"], None), (s["aev"], None)] + pt2_wait(2)
                if b == 0 and qp == 0 and hh == 0:
                    wts.append((s["tfm"], None))
                O("tensor", "matmul", PA[0][:, 0:1], onesrow[:],
                  mtmp_bf[0:1, 0:1], start=True, stop=True,
                  skip_group_check=True, waits=wts, inc=s["nmT"])
                O("vector", "tensor_copy", negm[:, hh:hh + 1], PA[0][:, 0:1],
                  waits=[(s["nmT"], None)], inc=s["nm"])
                pt2_free[2] = (s["nm"], s["nm"].n)
            for hpair in range(2):
                pend = []

                def flush_one():
                    nonlocal pend, n_pv
                    if not pend:
                        return
                    pd = pend.pop(0)
                    O("tensor", "matmul",
                      PA[1][0:1, pd["h2"] * 512:(pd["h2"] + 1) * 512],
                      onescol[:], pd["puse"], start=pd["first"],
                      stop=pd["last"], skip_group_check=True,
                      waits=pd["lw"])
                    O("tensor", "matmul",
                      PA[3][:, pd["h2"] * 512:(pd["h2"] + 1) * 512],
                      pd["vtile"], pd["puse"], start=pd["first"],
                      stop=pd["last"], skip_group_check=True,
                      waits=[pd["vw"]], inc=s["pvd"])
                    n_pv += 1
                    if pd["tile_of_tk_done"] is not None:
                        vt_slot_free[pd["tile_of_tk_done"]] = (s["pvd"], n_pv)
                    if pd["ptidx"] is not None:
                        pt_slot_free[pd["ptidx"]] = (s["pvd"], n_pv)
                    if pd["pmidx"] is not None:
                        pm_slot_free[pd["pmidx"]] = (s["pvd"], n_pv)

                for tk in range(qp + 1):
                    vslot = vt_n % 2
                    wts = [(s["collKV"], 1)]
                    if vt_n >= 2:
                        wts.append(vt_slot_free[vt_n - 2])
                    if first_vt:
                        wts.append((s["tfm"], None))
                        first_vt = False
                    O("scalar", "dma_start",
                      out=vt[vslot][:].rearrange("p (t c) -> p t c", c=128),
                      in_=kv_out[1024 * tk + 512 * b: 1024 * tk + 512 * b + 512,
                                 128:256].rearrange("(t p) c -> p t c", p=128),
                      waits=wts, inc=s[f"vt{vslot}"], inc_amt=16)
                    for tb in range(4):
                        for h2 in range(2):
                            if len(pend) >= 2:
                                flush_one()
                            hh = hpair * 2 + h2
                            first = (tk == 0 and tb == 0)
                            last = (tk == qp and tb == 3)
                            scs = sc_i % 4
                            sap, soff = SCSLOT[scs]
                            wts = list(pt2_wait(scs))
                            if first and tb == 0 and h2 == 0:
                                wts.append((s["nm"], None))
                                wts.append((s["kres"], (b * 8 + 8) * 16))
                            O("tensor", "matmul",
                              sap[:, soff:soff + 512],
                              K_res[:, tk * 512 + tb * 128:
                                    tk * 512 + (tb + 1) * 128],
                              qsb[:, hh * 512:(hh + 1) * 512],
                              start=True, stop=True, skip_group_check=True,
                              waits=wts, inc=s["sc"])
                            ptidx = pt_i
                            pts = ptidx % 5
                            ew = [(s["sc"], None)]
                            if ptidx >= 5:
                                ew.append(pt_slot_free[ptidx - 5])
                            O("scalar", "activation", pt[pts][:],
                              sap[:, soff:soff + 512], AF.Exp,
                              bias=negm[:, hh:hh + 1],
                              waits=ew, inc=s["exp"])
                            pt2_free[scs] = (s["exp"], s["exp"].n)
                            sc_i += 1
                            pt_i += 1
                            pmidx = None
                            if tk == qp:
                                pms = pm_i % 2
                                mw = [(s["exp"], None)]
                                if pm_i >= 2:
                                    mw.append(pm_slot_free[pm_i - 2])
                                O("vector", "tensor_mul", pm[pms][:], pt[pts][:],
                                  masks[:, tb * 512:(tb + 1) * 512],
                                  waits=mw, inc=s["msk"])
                                n_msk += 1
                                pt_slot_free[ptidx] = (s["msk"], n_msk)
                                pmidx = pm_i
                                pm_i += 1
                                puse = pm[pms][:]
                                pwait = [(s["msk"], s["msk"].n)]
                                ptrec = None
                            else:
                                puse = pt[pts][:]
                                pwait = [(s["exp"], s["exp"].n)]
                                ptrec = ptidx
                            lw = list(pwait)
                            if first:
                                lw.append((s["aev"], s["aev"].n))
                                if b == 0 and qp == 0 and hpair == 0:
                                    lw.append((s["rtr"], s["rtr"].n))
                            cur = dict(h2=h2, puse=puse, first=first, last=last,
                                       lw=lw,
                                       vtile=vt[vslot][:, tb * 128:(tb + 1) * 128],
                                       vw=(s[f"vt{vslot}"], (vt_n // 2 + 1) * 16),
                                       ptidx=ptrec, pmidx=pmidx,
                                       tile_of_tk_done=(
                                           vt_n if (tb == 3 and h2 == 1) else None))
                            pend.append(cur)
                    vt_n += 1
                while pend:
                    flush_one()
                for h2 in range(2):
                    hh = hpair * 2 + h2
                    O("vector", "reciprocal",
                      qn_row[0:1, h2 * 512:(h2 + 1) * 512],
                      PA[1][0:1, h2 * 512:(h2 + 1) * 512],
                      waits=[(s["pvd"], None)], inc=s["aev"])
                    O("vector", "tensor_copy", mtmp_bf[0:1, 1:2],
                      mtmp[0:1, 0:1]) if False else None
                    O("tensor", "matmul", PA[0][:, h2 * 512:(h2 + 1) * 512],
                      onesrow_f[:], qn_row[0:1, h2 * 512:(h2 + 1) * 512],
                      start=True, stop=True, skip_group_check=True,
                      waits=[(s["aev"], None)] + pt2_wait(2 + h2), inc=s["nmT"])
                    O("vector", "tensor_copy", linvb[:, h2 * 512:(h2 + 1) * 512],
                      PA[0][:, h2 * 512:(h2 + 1) * 512],
                      waits=[(s["nmT"], None)], inc=s["aev"])
                    pt2_free[2 + h2] = (s["aev"], s["aev"].n)
                    O("vector", "tensor_mul", aosb[h2][:],
                      PA[3][:, h2 * 512:(h2 + 1) * 512],
                      linvb[:, h2 * 512:(h2 + 1) * 512],
                      waits=[(s[f"aos{h2}"], 16 * ((b * 8 + qp) * 2 + hpair))],
                      inc=s["aev"])
                    O("gpsimd", "dma_start",
                      out=ao_in[1024 * qp + 512 * b + 128 * hh:
                                1024 * qp + 512 * b + 128 * (hh + 1), :],
                      in_=aosb[h2][:], waits=[(s["aev"], None)],
                      inc=s[f"aos{h2}"], inc_amt=16)
    P.coll("AllToAll", ALU.bypass, RG, [ao_in.opt()], [ao_out.opt()],
           waits=[(s["aos0"], None), (s["aos1"], None)], inc=s["collAO"])

    if PHASES <= 6:
        return _finish()

    for j in range(NCORE):
        for bb in range(2):
            for hp in range(4):
                h = 4 * j + hp
                O("sync", "dma_start",
                  out=attn_fm[:, h * NTOK + 512 * bb: h * NTOK + 512 * bb + 512],
                  in_=ao_out[1024 * j + 512 * bb + 128 * hp:
                             1024 * j + 512 * bb + 128 * (hp + 1), :],
                  waits=([(s["collAO"], 1)] if (j == 0 and bb == 0 and hp == 0)
                         else []),
                  inc=s["afm"], inc_amt=16)
    N_AFM = s["afm"].n
    P.fence()
    O("gpsimd", "dma_start", out=fbb[:], in_=fb,
      waits=[(s["pvd"], None), (s["exp"], None), (s["msk"], None),
             (s["aev"], None), (s["qld"], None), (s["aos0"], None),
             (s["aos1"], None)],
      inc=s["misc"], inc_amt=16)
    O("gpsimd", "dma_start", out=bcat[:], in_=B2T,
      waits=[(s["pvd"], None), (s["aev"], None), (s["msk"], None),
             (s["qn"], None), (s["afm"], N_AFM)],
      inc=s["misc"], inc_amt=16)
    N_MISC = s["misc"].n

    if AFM_ONLY:
        return _finish()
    OUTB = [(PA[i // 2], (i % 2) * 512) for i in range(8)]
    for o in range(OLIM):
        for j in range(NJ):
            gidx = o * NJ + j
            O("gpsimd", "dma_start", out=wo_t[j % 2][:],
              in_=Wo[j * 128:(j + 1) * 128, o * 512:(o + 1) * 512],
              waits=([(s["woF"], gidx - 1)] if gidx >= 2 else []),
              inc=s[f"wo{j % 2}"], inc_amt=16, future=True)
        if FINMODE >= 2:
            for j in range(NJ):
                gidx = o * NJ + j
                O("gpsimd", "dma_start", out=wf_t[j % 2][:],
                  in_=WfT[j * 128:(j + 1) * 128, o * 512:(o + 1) * 512],
                  waits=([(s["wfF"], gidx - 1)] if gidx >= 2 else []),
                  inc=s[f"wf{j % 2}"], inc_amt=16, future=True)
        if FINMODE >= 3:
            for tt in range(NT):
                nxr = o * NT + tt
                O("scalar", "dma_start", out=xres[tt % 2][:],
                  in_=x_my[tt * 128:(tt + 1) * 128, o * 512:(o + 1) * 512],
                  waits=([(s["ouV"], nxr - 1)] if nxr >= 2 else []),
                  inc=s[f"xr{tt % 2}"], inc_amt=16, future=True)
        for j in range(NJ):
            for tt in range(NT):
                pa, po = OUTB[tt]
                wts = []
                if tt == 0:
                    wts.append((s[f"wo{j % 2}"], ((o * NJ + j) // 2 + 1) * 16))
                    if j == 0:
                        wts.append((s["afm"], N_AFM))
                        if o == 0:
                            wts.append((s["nmT"], None))
                            wts.append((s["aev"], None))
                            wts.append((s["exp"], None))
                        elif FINMODE >= 3:
                            wts.append((s["ouV"], o * 8))
                O("tensor", "matmul", pa[:, po:po + 512],
                  attn_fm[:, j * NTOK + tt * 128: j * NTOK + (tt + 1) * 128],
                  wo_t[j % 2][:], start=(j == 0), stop=False,
                  skip_group_check=True, waits=wts,
                  inc=(s["woF"] if tt == NT - 1 else None))
        if FINMODE >= 2:
            for j in range(NJ):
                for tt in range(NT):
                    pa, po = OUTB[tt]
                    wts = []
                    if tt == 0:
                        wts.append((s[f"wf{j % 2}"], ((o * NJ + j) // 2 + 1) * 16))
                    O("tensor", "matmul", pa[:, po:po + 512],
                      xn_fm[:, j * NTOK + tt * 128: j * NTOK + (tt + 1) * 128],
                      wf_t[j % 2][:], start=False, stop=False,
                      skip_group_check=True, waits=wts,
                      inc=(s["wfF"] if tt == NT - 1 else None))
        if FINMODE >= 3:
            for tt in range(NT):
                pa, po = OUTB[tt]
                wts = [(s["misc"], N_MISC), (s["tfm"], None)] if tt == 0 else []
                O("tensor", "matmul", pa[:, po:po + 512],
                  tsc[:, tt * 128:(tt + 1) * 128], bcat[:, o * 512:(o + 1) * 512],
                  start=False, stop=False, skip_group_check=True, waits=wts)
                O("tensor", "matmul", pa[:, po:po + 512],
                  onesrow[:], fbb[0:1, o * 512:(o + 1) * 512],
                  start=False, stop=True, skip_group_check=True, inc=s["ou"])
            for tt in range(NT):
                pa, po = OUTB[tt]
                nxr2 = o * NT + tt
                O("vector", "tensor_add", outsb[tt % 2][:], pa[:, po:po + 512],
                  xres[tt % 2][:],
                  waits=[(s["ou"], o * NT + tt + 1),
                         (s[f"xr{tt % 2}"], (nxr2 // 2 + 1) * 16),
                         (s[f"ouF{tt % 2}"], (nxr2 // 2) * 16)],
                  inc=s["ouV"])
                O("sync", "dma_start",
                  out=out_my[tt * 128:(tt + 1) * 128, o * 512:(o + 1) * 512],
                  in_=outsb[tt % 2][:], waits=[(s["ouV"], None)],
                  inc=s[f"ouF{tt % 2}"], inc_amt=16)
    P.cmds["sync"].append(lambda e, h=s["ouF0"].h, v=s["ouF0"].n: e.wait_ge(h, v))
    P.cmds["sync"].append(lambda e, h=s["ouF1"].h, v=s["ouF1"].n: e.wait_ge(h, v))

    if dbg:
        for j in range(NJ):
            O("vector", "tensor_copy", linvb[:, 0:1024].bitcast(BF16)[:, 0:1024]
              if False else linvb[:, 0:512],
              xn_fm[:, j * NTOK: j * NTOK + 512], waits=[(s["ouF0"], None), (s["ouF1"], None)])
            O("vector", "tensor_copy", linvb[:, 512:1024],
              xn_fm[:, j * NTOK + 512: (j + 1) * NTOK], inc=s["dbgV"])
            O("gpsimd", "dma_start",
              out=dbg_o["d_xnfm"][:, j * NTOK:(j + 1) * NTOK],
              in_=linvb[:], waits=[(s["dbgV"], None)], inc=s["dbg"], inc_amt=16)
            O("vector", "tensor_copy", linvb[:, 0:512],
              attn_fm[:, j * NTOK: j * NTOK + 512], waits=[(s["dbg"], None)])
            O("vector", "tensor_copy", linvb[:, 512:1024],
              attn_fm[:, j * NTOK + 512: (j + 1) * NTOK], inc=s["dbgV"])
            O("gpsimd", "dma_start",
              out=dbg_o["d_attn"][:, j * NTOK:(j + 1) * NTOK],
              in_=linvb[:], waits=[(s["dbgV"], None)], inc=s["dbg"], inc_amt=16)
        O("gpsimd", "dma_start", out=dbg_o["d_q"], in_=q_out,
          waits=[(s["collQ"], 1)], inc=s["dbg"], inc_amt=16)
        O("gpsimd", "dma_start", out=dbg_o["d_kv"], in_=kv_out,
          waits=[(s["collKV"], 1)], inc=s["dbg"], inc_amt=16)
        O("gpsimd", "dma_start", out=dbg_o["d_ao"], in_=ao_out,
          waits=[(s["collAO"], 1)], inc=s["dbg"], inc_amt=16)
        P.cmds["gpsimd"].append(
            lambda e, h=s["dbg"].h, v=s["dbg"].n: e.wait_ge(h, v))

    return _finish()



def _consts():
    c = {}
    c["identb"] = np.eye(128, dtype=ml_dtypes.bfloat16)
    c["identf"] = np.eye(128, dtype=np.float32)
    c["onescol"] = np.ones((128, 1), dtype=ml_dtypes.bfloat16)
    c["onesrow"] = np.ones((1, 128), dtype=ml_dtypes.bfloat16)
    c["onesrow_f"] = np.ones((1, 128), dtype=np.float32)
    rp = np.zeros((8, 128), dtype=np.float32)
    for e in range(8):
        rp[e, e * 16:(e + 1) * 16] = 2.0
    c["rep_f"] = rp
    m = np.zeros((512, 512), dtype=np.float32)
    i = np.arange(128)[:, None]
    jj = np.arange(512)[None, :]
    for blk in range(4):
        m[blk * 128:(blk + 1) * 128] = (blk * 128 + i <= jj)
    c["masks"] = m.astype(ml_dtypes.bfloat16)
    return c


def _rope_tables(core):
    pos = np.arange(512 * core, 512 * core + 512, dtype=np.float64)
    inv = 1.0 / (THETA ** (np.arange(0, HD, 2, dtype=np.float64) / HD))
    ang = pos[None, :] * inv[:, None]
    cos = np.cos(ang); sin = np.sin(ang)
    cosf = np.concatenate([cos, cos], 0)
    sinf = np.concatenate([-sin, sin], 0)
    cosf = np.concatenate([cosf, cosf], 1)
    sinf = np.concatenate([sinf, sinf], 1)
    sc = 1.0 / np.sqrt(HD)
    return (np.asarray(cosf * sc, np.float32), np.asarray(sinf * sc, np.float32),
            np.asarray(cosf, np.float32), np.asarray(sinf, np.float32))


def kernel(L0_attn_x, L0_attn_Wq, L0_attn_Wk, L0_attn_Wv, L0_attn_Wo,
           L0_ffn_W, L0_ffn_b, L0_ffn_A, L0_ffn_B, L0_ffn_router_W,
           L0_ffn_router_b):
    x = np.ascontiguousarray(np.asarray(L0_attn_x, np.float32))
    Wq_ = np.ascontiguousarray(np.asarray(L0_attn_Wq, np.float32))
    Wk_ = np.ascontiguousarray(np.asarray(L0_attn_Wk, np.float32))
    Wv_ = np.ascontiguousarray(np.asarray(L0_attn_Wv, np.float32))
    Wo_ = np.ascontiguousarray(np.asarray(L0_attn_Wo, np.float32))
    WfT_ = np.ascontiguousarray(np.asarray(L0_ffn_W, np.float32).T)
    fb_ = np.ascontiguousarray(np.asarray(L0_ffn_b, np.float32)).reshape(1, D)
    A2T_ = np.ascontiguousarray(np.asarray(L0_ffn_A, np.float32).reshape(128, D).T)
    B2T_ = np.ascontiguousarray(
        np.asarray(L0_ffn_B, np.float32).transpose(0, 2, 1).reshape(128, 4096))
    rW_ = np.ascontiguousarray(np.asarray(L0_ffn_router_W, np.float32))
    rb_ = np.ascontiguousarray(np.asarray(L0_ffn_router_b, np.float32))

    rwt_ = np.ascontiguousarray(
        rW_.T.reshape(32, 128, 8).transpose(1, 0, 2).reshape(128, 256))
    rb2_ = np.ascontiguousarray(np.tile(rb_.reshape(1, 8), (2, 1)))

    consts = _consts()
    shared = dict(Wq=Wq_, Wk=Wk_, Wv=Wv_, Wo=Wo_, WfT=WfT_, fb=fb_, A2T=A2T_,
                  B2T=B2T_, rwt=rwt_, rb2=rb2_, **consts)

    in_maps = []
    for c in range(NCORE):
        cq, sq, ck, sk = _rope_tables(c)
        x_myc = np.ascontiguousarray(
            np.concatenate([x[0, 512 * c:512 * (c + 1), :],
                            x[1, 512 * c:512 * (c + 1), :]], axis=0))
        rt = np.zeros((1, NTOK), np.float32)
        lo = max(Q_LO, 512 * c) - 512 * c
        hi = min(Q_HI, 512 * c + 512) - 512 * c
        if hi > lo:
            rt[0, lo:hi] = 1.0
            rt[0, 512 + lo:512 + hi] = 1.0
        m = dict(shared)
        m.update(x_my=x_myc, cosq=cq, sinq=sq, cosk=ck, sink=sk,
                 rtok=rt.astype(ml_dtypes.bfloat16))
        in_maps.append(m)

    nc = build_program(dbg=DBG)
    res = run_bass_kernel_spmd(nc, in_maps, core_ids=list(range(NCORE)))

    out = np.empty((2, SEQ, D), dtype=np.float32)
    for c in range(NCORE):
        o = res.results[c]["out_my"]
        out[0, 512 * c:512 * (c + 1), :] = o[0:512, :]
        out[1, 512 * c:512 * (c + 1), :] = o[512:1024, :]
    if DBG:
        kernel._dbg = res
    return out


# version 07
# speedup vs baseline: 1.5498x; 1.1584x over previous
import numpy as np
import ml_dtypes
from contextlib import ExitStack

import concourse.bass as bass
import concourse.mybir as mybir
from concourse.bass_utils import run_bass_kernel_spmd
from concourse.aot_env import allow_ici_all_to_all

F32 = mybir.dt.float32
BF16 = mybir.dt.bfloat16
AF = mybir.ActivationFunctionType
ALU = mybir.AluOpType
AX = mybir.AxisListType

NCORE = 8
D = 4096
SEQ = 4096
HD = 128
CH = 512
NTOK = 1024
NT = 8
NJ = 32
NHKV = 8
Q_LO, Q_HI = 611, 4095
NQTOK = Q_HI - Q_LO
EPS = 1e-5
LN105 = float(np.log(1.05))
THETA = 10000.0

DBG = False
PHASES = 7
OLIM = 8
AFM_ONLY = False
FINMODE = 3


class Sem:
    __slots__ = ("h", "n", "name", "vmap")

    def __init__(self, h, name):
        self.h, self.n, self.name = h, 0, name
        self.vmap = {}


class Prog:

    SELF_ENGS = ("vector", "scalar")

    def __init__(self, nc, ctx):
        self.nc, self.ctx = nc, ctx
        self.cmds = {e: [] for e in ("tensor", "vector", "scalar", "gpsimd", "sync")}
        self.selfsem = {e: Sem(ctx.enter_context(nc.semaphore(f"self_{e}")),
                               f"self_{e}") for e in self.SELF_ENGS}
        self.deps = {e: {"w": {}, "r": {}} for e in self.SELF_ENGS}
        self.fence_val = {e: 0 for e in self.SELF_ENGS}
        self.owner = {}

    def sem(self, name):
        return Sem(self.ctx.enter_context(self.nc.semaphore(name)), name)

    @staticmethod
    def _ap_name(a):
        t = getattr(a, "tensor", None)
        if t is not None and hasattr(t, "name"):
            return t.name
        if hasattr(a, "name") and hasattr(a, "ap"):
            return a.name
        return None

    def _rw_sets(self, args, kwargs):
        rset, wset = set(), set()
        for i, a in enumerate(args):
            nm = self._ap_name(a)
            if nm is None:
                continue
            (wset if i == 0 else rset).add(nm)
        for k, a in kwargs.items():
            nm = self._ap_name(a)
            if nm is None:
                continue
            if k in ("out", "accum_out"):
                wset.add(nm)
            else:
                rset.add(nm)
        return rset, wset

    def _resolve(self, sm, vv):
        eng = self.owner.get(sm.name)
        if eng is None:
            return (sm.h, vv)
        assert vv in sm.vmap, f"no vmap[{vv}] for {sm.name} (owner {eng})"
        return (self.selfsem[eng].h, sm.vmap[vv])

    def op(self, eng, meth, *args, waits=(), inc=None, inc_amt=1, future=False,
           **kwargs):
        wl = []
        for (sm, v) in waits:
            vv = sm.n if v is None else v
            if not future:
                assert vv <= sm.n, f"wait {sm.name}>={vv} but emitted {sm.n}"
            if vv > 0:
                wl.append((sm, vv))
        is_dma = meth in ("dma_start", "dma_start_transpose")
        ih = None
        if eng in self.SELF_ENGS:
            ss = self.selfsem[eng]
            if not is_dma:
                rset, wset = self._rw_sets(args, kwargs)
                dep = self.deps[eng]
                need = self.fence_val[eng]
                for t in rset | wset:
                    need = max(need, dep["w"].get(t, 0))
                for t in wset:
                    need = max(need, dep["r"].get(t, 0))
                if need > 0:
                    wl.insert(0, (ss, need))
                ss.n += 1
                for t in wset:
                    dep["w"][t] = ss.n
                for t in rset:
                    dep["r"][t] = max(dep["r"].get(t, 0), ss.n)
                ih = (ss.h, 1)
                if inc is not None:
                    assert inc_amt == 1
                    inc.n += 1
                    o = self.owner.setdefault(inc.name, eng)
                    assert o == eng, f"{inc.name} inc'd from {o} and {eng}"
                    inc.vmap[inc.n] = ss.n
            else:
                if inc is not None:
                    inc.n += (inc_amt if inc_amt is not None else 1)
                    ih = (inc.h, inc_amt)
        else:
            if inc is not None:
                assert inc.name not in self.owner, \
                    f"{inc.name} owned by {self.owner.get(inc.name)}"
                inc.n += (inc_amt if inc_amt is not None else 1)
                ih = (inc.h, inc_amt)

        def run(e, meth=meth, args=args, kwargs=kwargs, wl=wl, ih=ih):
            for (sm, v) in wl:
                h, v = self._resolve(sm, v)
                e.wait_ge(h, v)
            ins = getattr(e, meth)(*args, **kwargs)
            if ih is not None:
                if ih[1] is None or ih[1] == 1:
                    ins.then_inc(ih[0], 1)
                else:
                    ins.then_inc(ih[0], ih[1])
        self.cmds[eng].append(run)

    def coll(self, kind, op_, rg, ins_, outs_, waits=(), inc=None):
        wl = []
        for (sm, v) in waits:
            vv = sm.n if v is None else v
            if vv > 0:
                wl.append((sm, vv))
        inc.n += 1

        def run(e, wl=wl, ih=inc.h):
            for (sm, v) in wl:
                h, v = self._resolve(sm, v)
                e.wait_ge(h, v)
            with allow_ici_all_to_all():
                e.collective_compute(kind, op_, rg, ins_, outs_).then_inc(ih)
        self.cmds["gpsimd"].append(run)

    def fence(self):
        for e in self.SELF_ENGS:
            self.fence_val[e] = self.selfsem[e].n

    def bump(self, eng, sem, amt=1):
        raise RuntimeError("standalone bumps are forbidden (writeback race)")


def build_program(dbg=False):
    nc = bass.Bass("TRN2", target_bir_lowering=False)
    ctx = ExitStack()

    def din(name, shape, dt=F32):
        return nc.dram_tensor(name, shape, dt, kind="ExternalInput").ap()

    x_my = din("x_my", [NTOK, D])
    Wq = din("Wq", [D, D]); Wk = din("Wk", [D, 1024]); Wv = din("Wv", [D, 1024])
    Wo = din("Wo", [D, D]); WfT = din("WfT", [D, D])
    fb = din("fb", [1, D])
    A2T = din("A2T", [D, 128]); B2T = din("B2T", [128, D])
    rwt = din("rwt", [128, 256]); rb2 = din("rb2", [2, 8])
    cosq_i = din("cosq", [128, NTOK]); sinq_i = din("sinq", [128, NTOK])
    cosk_i = din("cosk", [128, NTOK]); sink_i = din("sink", [128, NTOK])
    masks_i = din("masks", [512, 512], BF16)
    identb_i = din("identb", [128, 128], BF16)
    identf_i = din("identf", [128, 128], F32)
    onescol_i = din("onescol", [128, 1], BF16)
    onesrow_i = din("onesrow", [1, 128], BF16)
    onesrow_f_i = din("onesrow_f", [1, 128], F32)
    rep_i = din("rep_f", [8, 128], F32)
    rtok_i = din("rtok", [1, NTOK], BF16)

    out_my = nc.dram_tensor("out_my", [NTOK, D], F32, kind="ExternalOutput").ap()

    kv_in = nc.dram_tensor("kv_in", [NCORE * NTOK, 256], BF16, kind="Internal").ap()
    kv_out = nc.dram_tensor("kv_out", [NCORE * NTOK, 256], BF16, kind="Internal").ap()
    q_in = nc.dram_tensor("q_in", [D, NTOK], BF16, kind="Internal").ap()
    q_out = nc.dram_tensor("q_out", [D, NTOK], BF16, kind="Internal").ap()
    ao_in = nc.dram_tensor("ao_in", [NCORE * NTOK, 512], BF16, kind="Internal").ap()
    ao_out = nc.dram_tensor("ao_out", [NCORE * NTOK, 512], BF16, kind="Internal").ap()
    r_in = nc.dram_tensor("r_in", [128, 64], F32, kind="Internal").ap()
    r_out = nc.dram_tensor("r_out", [128, 64], F32, kind="Internal").ap()

    dbg_o = {}
    if dbg:
        for nm_, shp, dt in (("d_xnfm", [128, NJ * NTOK], F32),
                             ("d_q", [D, NTOK], F32),
                             ("d_kv", [NCORE * NTOK, 256], F32),
                             ("d_ao", [NCORE * NTOK, 512], F32),
                             ("d_rout", [2, 8], F32),
                             ("d_rdbg", [2, 32], F32),
                             ("d_attn", [128, NJ * NTOK], F32)):
            dbg_o[nm_] = nc.dram_tensor(nm_, shp, dt, kind="ExternalOutput").ap()

    def sb(name, shape, dt):
        return nc.alloc_sbuf_tensor(name, shape, dt)

    xn_fm = sb("xn_fm", [128, NJ * NTOK], BF16)
    attn_fm = sb("attn_fm", [128, NJ * NTOK], BF16)
    tab = sb("tab", [128, 2048], F32)
    masks = sb("masks_s", [128, 2048], BF16)
    identb = sb("identb_s", [128, 128], BF16)
    identf = sb("identf_s", [128, 128], F32)
    onescol = sb("onescol_s", [128, 1], BF16)
    onesrow = sb("onesrow_s", [1, 128], BF16)
    onesrow_f = sb("onesrow_f_s", [1, 128], F32)
    rep_f = sb("rep_f_s", [8, 128], F32)
    rwt_sb = sb("rwt_sb", [128, 256], F32)
    rb2_sb = sb("rb2_sb", [2, 8], F32)
    tfm = sb("tfm", [128, NTOK], BF16)
    tsc = sb("tsc", [128, NTOK], BF16)
    kn_sb = sb("kn_sb", [1, 8], F32)
    kk = sb("kk", [128, 512], F32)
    rsum_sb = sb("rsum_sb", [128, 64], F32)
    rtok_sb = sb("rtok_sb", [1, NTOK], BF16)
    qn2_sb = sb("qn2_sb", [1, 4], F32)
    mtmp = sb("mtmp", [1, 4], F32)
    mtmp_bf = sb("mtmp_bf", [1, 2], BF16)
    mtmp2_row = sb("mtmp2_row", [1, 32], F32)
    K_res = sb("K_res", [128, 4096], BF16)
    lsum = sb("lsum", [128, 1], F32); lsq = sb("lsq", [128, 1], F32)
    lmean = sb("lmean", [128, 1], F32); lvar = sb("lvar", [128, 1], F32)
    lrstd = sb("lrstd", [128, 1], F32); lnegmr = sb("lnegmr", [128, 1], F32)
    ltmp = sb("ltmp", [128, 1], F32)
    logit_sb = sb("logit_sb", [2, 8], F32); rmax = sb("rmax", [2, 1], F32)
    rexp = sb("rexp", [2, 8], F32); rsumt = sb("rsumt", [2, 1], F32)
    rrec = sb("rrec", [2, 1], F32); rsm = sb("rsm", [2, 8], F32)
    rmax2 = sb("rmax2", [2, 1], F32)
    rdbg = sb("rdbg", [2, 32], F32)
    routT = sb("routT", [8, 2], F32)
    svec = sb("svec", [128, 2], F32)
    wo_t = [sb(f"wo_t{i}", [128, 512], BF16) for i in range(2)]
    wf_t = [sb(f"wf_t{i}", [128, 512], BF16) for i in range(2)]
    xres = [sb(f"xres{i}", [128, 512], F32) for i in range(2)]
    outsb = [sb(f"outsb{i}", [128, 512], F32) for i in range(2)]
    WBUF = sb("WBUF", [128, 4096], BF16)

    ARENA = sb("ARENA", [128, 6144], F32)
    ab = nc.lookup_mloc(ARENA).addr

    def at_(name, shape, dt, off):
        nb = shape[1] * (4 if dt == F32 else 2)
        assert off + nb <= 24576, name
        return nc.alloc_sbuf_tensor_at(name, shape, dt, offset=ab + off)

    XBUF = at_("XBUF", [128, 4096], F32, 0)
    xn_tm = at_("xn_tm", [128, 4096], BF16, 16384)
    rtok_bc = at_("rtok_bc", [128, 1024], F32, 0)
    K_sb = at_("K_sb", [128, 1024], BF16, 0)
    V_sb = at_("V_sb", [128, 1024], BF16, 2048)
    ktm = at_("ktm", [128, 1024], BF16, 4096)
    vtm = at_("vtm", [128, 1024], BF16, 6144)
    rtmp1 = at_("rtmp1", [128, 1024], F32, 8192)
    rtmp2 = at_("rtmp2", [128, 1024], F32, 12288)
    qstage = [at_(f"qstage{i}", [128, 1024], BF16, 2048 * i) for i in range(2)]
    At = at_("At", [128, 4096], BF16, 16384)
    qsb = at_("qsb", [128, 2048], BF16, 0)
    vt = [at_(f"vt{i}", [128, 512], BF16, 4096 + 1024 * i) for i in range(2)]
    pt = [at_(f"pt{i}", [128, 512], BF16, off)
          for i, off in enumerate((6144, 7168, 8192, 15360, 22560))]
    pm = [at_(f"pm{i}", [128, 512], BF16, 9216 + 1024 * i) for i in range(2)]
    linvb = at_("linvb", [128, 1024], F32, 11264)
    qq = nc.alloc_sbuf_tensor_at("qq", [128, 512], BF16,
                                 offset=nc.lookup_mloc(kk).addr)
    qn_row = at_("qn_row", [1, 1024], F32, 16384)
    negm = at_("negm", [128, 32], F32, 23584)
    qn2a = sb("qn2a", [1, 32], F32)
    kmaxr = sb("kmaxr", [1, 8], F32)
    negm_bf = sb("negm_bf", [1, 32], BF16)
    aosb = [at_(f"aosb{i}", [128, 512], BF16, 20512 + 1024 * i) for i in range(2)]
    fbb = at_("fbb", [1, 4096], BF16, 0)
    bcat = at_("bcat", [128, 4096], BF16, 8192)

    PT = [ctx.enter_context(nc.psum_tensor(f"PT{i}", [128, 1024], F32)) for i in range(4)]
    PA = [p.ap() for p in PT]
    PB = [p.ap().bitcast(BF16) for p in PT]

    P = Prog(nc, ctx)
    O, B_ = P.op, P.bump
    RG = [list(range(NCORE))]

    s = {n: P.sem(n) for n in (
        "wkv0", "wkv1", "wq0", "wq1", "wq2", "wq3", "wo0", "wo1", "wf0", "wf1",
        "qnl0", "qnl1", "qnl2", "qnl3",
        "xr0", "xr1", "vt0", "vt1", "qs0", "qs1", "aos0", "aos1", "ouF0", "ouF1",
        "collR", "collKV", "collQ", "collAO",
        "cst", "conv", "x", "xfree", "lnv", "lna", "xn", "xnfree",
        "trA", "trAF", "phA", "rtr", "rtrG", "rtrA2", "rtrT",
        "wkv", "wkvF", "kvt", "kve", "kvs",
        "wq", "wqF", "qp", "qe", "qs",
        "coll", "misc", "trK", "trKF", "tfm", "tfmT",
        "kres", "vt", "vtF", "qld", "qn", "nm", "nmA", "nmT",
        "sc", "scF", "exp", "ptF", "msk", "pmF", "lpv", "pvd", "aev", "aos",
        "afm", "wo", "woF", "wf", "wfF", "xr", "xrF", "ou", "ouV", "ouF",
        "dbg", "dbgV")}

    pt2_free = {0: None, 1: None, 2: None, 3: None}
    SCSLOT = None

    def pt2_wait(bank):
        f = pt2_free[bank]
        return [f] if f is not None else []

    for dst, src in ((identb[:], identb_i), (identf[:], identf_i),
                     (onescol[:], onescol_i), (onesrow[:], onesrow_i),
                     (onesrow_f[:], onesrow_f_i), (rep_f[:], rep_i),
                     (rwt_sb[:], rwt), (rb2_sb[:], rb2), (rtok_sb[:], rtok_i)):
        O("gpsimd", "dma_start", out=dst, in_=src, inc=s["cst"], inc_amt=16)
    O("gpsimd", "dma_start",
      out=masks[:].rearrange("p (m f) -> p m f", f=512),
      in_=masks_i.rearrange("(m p) f -> p m f", p=128), inc=s["cst"], inc_amt=16)
    O("gpsimd", "dma_start", out=tab[:, 0:1024], in_=cosk_i, inc=s["cst"], inc_amt=16)
    O("gpsimd", "dma_start", out=tab[:, 1024:2048], in_=sink_i, inc=s["cst"], inc_amt=16)
    N_CST = s["cst"].n

    TRSLOT = [(PB[0], 0), (PB[0], 1024), (PB[1], 0), (PB[1], 1024)]
    trb = 0
    for tt in range(NT):
        O("sync", "dma_start", out=XBUF[:], in_=x_my[tt * 128:(tt + 1) * 128, :],
          waits=[(s["xn"], None)], inc=s["x"], inc_amt=16)
        O("vector", "reduce_sum", lsum[:], XBUF[:], axis=AX.X,
          waits=[(s["x"], None)], inc=s["lnv"])
        O("scalar", "activation", xn_tm[:], XBUF[:], AF.Square, accum_out=lsq[:],
          waits=[(s["x"], None)] + ([(s["trAF"], 8 * tt)] if tt else []),
          inc=s["lna"])
        O("vector", "tensor_scalar_mul", lmean[:], lsum[:], 1.0 / D, inc=s["lnv"])
        O("vector", "tensor_scalar_mul", lvar[:], lsq[:], 1.0 / D,
          waits=[(s["lna"], None)], inc=s["lnv"])
        O("vector", "tensor_mul", ltmp[:], lmean[:], lmean[:], inc=s["lnv"])
        O("vector", "tensor_sub", lvar[:], lvar[:], ltmp[:], inc=s["lnv"])
        O("vector", "tensor_scalar_add", lvar[:], lvar[:], EPS, inc=s["lnv"])
        O("scalar", "activation", ltmp[:], lvar[:], AF.Sqrt,
          waits=[(s["lnv"], None)], inc=s["lna"])
        O("vector", "reciprocal", lrstd[:], ltmp[:], waits=[(s["lna"], None)],
          inc=s["lnv"])
        O("vector", "tensor_scalar", lnegmr[:], lmean[:], lrstd[:, 0:1], -1.0,
          ALU.mult, ALU.mult, inc=s["lnv"])
        O("scalar", "activation", xn_tm[:], XBUF[:], AF.Identity,
          scale=lrstd[:, 0:1], bias=lnegmr[:, 0:1],
          waits=[(s["lnv"], None)], inc=s["xn"])
        for b8 in range(8):
            ps, off = TRSLOT[trb % 4]
            for q in range(4):
                j = b8 * 4 + q
                wts = []
                if q == 0:
                    wts = [(s["xn"], None)]
                    if trb >= 4:
                        wts.append((s["trAF"], trb - 3))
                    if trb == 0:
                        wts.append((s["cst"], N_CST))
                O("tensor", "transpose", ps[:, off + q * 128: off + (q + 1) * 128],
                  xn_tm[:, j * 128:(j + 1) * 128], identb[:], waits=wts,
                  inc=(s["trA"] if q == 3 else None))
            dst = xn_fm[:].rearrange("p (j t) -> p j t", t=NTOK)[
                :, b8 * 4:(b8 + 1) * 4, tt * 128:(tt + 1) * 128]
            src = ps[:, off:off + 512].rearrange("p (j t) -> p j t", t=128)
            O("vector", "tensor_copy", dst, src, waits=[(s["trA"], None)],
              inc=s["trAF"])
            trb += 1

    def _finish():
        with nc.Block() as block:
            @block.tensor
            def _(eng):
                for fn in P.cmds["tensor"]:
                    fn(eng)

            @block.vector
            def _(eng):
                for fn in P.cmds["vector"]:
                    fn(eng)

            @block.scalar
            def _(eng):
                for fn in P.cmds["scalar"]:
                    fn(eng)

            @block.gpsimd
            def _(eng):
                for fn in P.cmds["gpsimd"]:
                    fn(eng)

            @block.sync
            def _(eng):
                for fn in P.cmds["sync"]:
                    fn(eng)
        nc._keepalive_ctx = ctx
        return nc

    if PHASES <= 1:
        return _finish()

    P.fence()
    for half in range(2):
        O("tensor", "matmul", PA[2][:, half * 512:(half + 1) * 512],
          onesrow[:], rtok_sb[0:1, half * 512:(half + 1) * 512],
          start=True, stop=True, skip_group_check=True,
          waits=([(s["cst"], N_CST), (s["trAF"], NT * 8)] if half == 0 else []),
          inc=(s["rtrT"] if half == 1 else None))
    O("vector", "tensor_copy", rtok_bc[:], PA[2][:, 0:1024],
      waits=[(s["rtrT"], None)], inc=s["rtr"])
    pt2_free[0] = (s["rtr"], s["rtr"].n)
    pt2_free[1] = (s["rtr"], s["rtr"].n)
    for b in range(2):
        for j in range(NJ):
            src = xn_fm[:, j * NTOK + b * 512: j * NTOK + (b + 1) * 512]
            wts = [(s["rtr"], None)] if (b == 0 and j == 0) else []
            O("vector", "tensor_mul", kk[:], src,
              rtok_bc[:, b * 512:(b + 1) * 512], waits=wts)
            O("vector", "reduce_sum", rsum_sb[:, b * 32 + j: b * 32 + j + 1],
              kk[:], axis=AX.X,
              inc=(s["rtr"] if (b == 1 and j == NJ - 1) else None))
    O("gpsimd", "dma_start", out=r_in, in_=rsum_sb[:],
      waits=[(s["rtr"], None)], inc=s["rtrG"], inc_amt=16)
    P.coll("AllReduce", ALU.add, RG, [r_in.opt()], [r_out.opt()],
           waits=[(s["rtrG"], None)], inc=s["collR"])

    if PHASES <= 2:
        return _finish()

    P.fence()
    trk = 0
    for h in range(NHKV):
        for g4 in range(4):
            slot = g4 % 2
            ksrc = Wk[:, h * 128:(h + 1) * 128].rearrange(
                "(j p) i -> p j i", p=128)[:, g4 * 8:(g4 + 1) * 8, :]
            vsrc = Wv[:, h * 128:(h + 1) * 128].rearrange(
                "(j p) i -> p j i", p=128)[:, g4 * 8:(g4 + 1) * 8, :]
            kdst = WBUF[:, slot * 1024:(slot + 1) * 1024].rearrange(
                "p (j i) -> p j i", i=128)
            vdst = WBUF[:, 2048 + slot * 1024: 2048 + (slot + 1) * 1024].rearrange(
                "p (j i) -> p j i", i=128)
            gidx = h * 4 + g4
            wts = [(s["wkvF"], gidx - 1)] if gidx >= 2 else []
            O("gpsimd", "dma_start", out=kdst, in_=ksrc, waits=wts,
              inc=s[f"wkv{slot}"], inc_amt=16, future=True)
            O("gpsimd", "dma_start", out=vdst, in_=vsrc,
              inc=s[f"wkv{slot}"], inc_amt=16)
        for g4 in range(4):
            slot = g4 % 2
            for jj in range(8):
                j = g4 * 8 + jj
                wts = []
                if jj == 0:
                    wts.append((s[f"wkv{slot}"], ((h * 4 + g4) // 2 + 1) * 32))
                    if j == 0:
                        wts.append((s["trAF"], NT * 8))
                        if h > 0:
                            wts.append((s["kve"], None))
                for half in range(2):
                    O("tensor", "matmul", PA[0][:, half * 512:(half + 1) * 512],
                      WBUF[:, slot * 1024 + jj * 128: slot * 1024 + (jj + 1) * 128],
                      xn_fm[:, j * NTOK + half * 512: j * NTOK + (half + 1) * 512],
                      start=(j == 0), stop=(j == NJ - 1), skip_group_check=True,
                      waits=(wts if half == 0 else []))
                    wts = []
                    O("tensor", "matmul", PA[1][:, half * 512:(half + 1) * 512],
                      WBUF[:, 2048 + slot * 1024 + jj * 128:
                           2048 + slot * 1024 + (jj + 1) * 128],
                      xn_fm[:, j * NTOK + half * 512: j * NTOK + (half + 1) * 512],
                      start=(j == 0), stop=(j == NJ - 1), skip_group_check=True,
                      inc=(s["wkvF"] if (jj == 7 and half == 1) else None))
        wts = [(s["wkvF"], 4 * (h + 1))]
        if h >= 1:
            wts.append((s["kvs"], None))
        O("vector", "tensor_mul", rtmp1[0:64, :], PA[0][64:128, :],
          tab[0:64, 1024:2048], waits=wts)
        O("vector", "tensor_mul", rtmp1[64:128, :], PA[0][0:64, :],
          tab[64:128, 1024:2048])
        O("vector", "tensor_mul", rtmp2[:], PA[0][:, :], tab[:, 0:1024])
        O("vector", "tensor_add", K_sb[:], rtmp2[:], rtmp1[:], inc=s["kve"])
        O("vector", "tensor_copy", V_sb[:], PA[1][:, :], inc=s["kve"])
        for src_sb, dst_sb in ((K_sb, ktm), (V_sb, vtm)):
            for b4 in range(2):
                ps, off = (PB[2], 0) if trk % 2 == 0 else (PB[2], 1024)
                for q in range(4):
                    tb = b4 * 4 + q
                    wts = []
                    if q == 0:
                        wts = [(s["kve"], None)]
                        if trk >= 2:
                            wts.append((s["trKF"], trk - 1))
                        else:
                            wts += pt2_wait(trk % 2)
                    O("tensor", "transpose",
                      ps[:, off + q * 128: off + (q + 1) * 128],
                      src_sb[:, tb * 128:(tb + 1) * 128], identb[:], waits=wts,
                      inc=(s["trK"] if q == 3 else None))
                dst = dst_sb[:].rearrange("p (t i) -> p t i", i=128)[
                    :, b4 * 4:(b4 + 1) * 4, :]
                src = ps[:, off:off + 512].rearrange("p (t i) -> p t i", i=128)
                O("vector", "tensor_copy", dst, src, waits=[(s["trK"], None)],
                  inc=s["trKF"])
                trk += 1
        pt2_free[0] = (s["trKF"], s["trKF"].n)
        pt2_free[1] = (s["trKF"], s["trKF"].n)
        kdst = kv_in[h * NTOK:(h + 1) * NTOK, 0:128].rearrange(
            "(t p) c -> p t c", p=128)
        vdst = kv_in[h * NTOK:(h + 1) * NTOK, 128:256].rearrange(
            "(t p) c -> p t c", p=128)
        O("gpsimd", "dma_start", out=kdst,
          in_=ktm[:].rearrange("p (t i) -> p t i", i=128),
          waits=[(s["trKF"], None)], inc=s["kvs"], inc_amt=16)
        O("gpsimd", "dma_start", out=vdst,
          in_=vtm[:].rearrange("p (t i) -> p t i", i=128),
          inc=s["kvs"], inc_amt=16)
    P.coll("AllToAll", ALU.bypass, RG, [kv_in.opt()], [kv_out.opt()],
           waits=[(s["kvs"], None)], inc=s["collKV"])

    O("gpsimd", "dma_start", out=tab[:, 0:1024], in_=cosq_i,
      waits=[(s["kve"], None)], inc=s["cst"], inc_amt=16)
    O("gpsimd", "dma_start", out=tab[:, 1024:2048], in_=sinq_i,
      inc=s["cst"], inc_amt=16)
    TAB_Q = s["cst"].n

    if PHASES <= 3:
        return _finish()

    P.fence()
    for g in range(16):
        QPS = (PA[0], PA[1]) if g % 2 == 0 else (PA[2], PA[3])
        for g8 in range(8):
            slot = g8 % 4
            src = Wq[:, g * 256:(g + 1) * 256].rearrange(
                "(j p) i -> p j i", p=128)[:, g8 * 4:(g8 + 1) * 4, :]
            dst = WBUF[:, slot * 1024:(slot + 1) * 1024].rearrange(
                "p (j i) -> p j i", i=256)
            gidx = g * 8 + g8
            wts = []
            if gidx == 0:
                wts = [(s["wkvF"], 32)]
            elif gidx >= 4:
                wts = [(s["wqF"], gidx - 3)]
            O("gpsimd", "dma_start", out=dst, in_=src, waits=wts,
              inc=s[f"wq{slot}"], inc_amt=16, future=True)
        for g8 in range(8):
            slot = g8 % 4
            for jj in range(4):
                j = g8 * 4 + jj
                wts = []
                if jj == 0:
                    wts.append((s[f"wq{slot}"], ((g * 8 + g8) // 4 + 1) * 16))
                    if j == 0:
                        if g >= 2:
                            wts.append((s["qe"], 2 * (g - 1)))
                        if g == 0:
                            wts.append((s["kve"], None))
                            wts.append((s["cst"], TAB_Q))
                        if g == 1:
                            wts.append((s["trKF"], None))
                for hp in range(2):
                    for half in range(2):
                        O("tensor", "matmul",
                          QPS[hp][:, half * 512:(half + 1) * 512],
                          WBUF[:, slot * 1024 + jj * 256 + hp * 128:
                               slot * 1024 + jj * 256 + (hp + 1) * 128],
                          xn_fm[:, j * NTOK + half * 512:
                                j * NTOK + (half + 1) * 512],
                          start=(j == 0), stop=(j == NJ - 1),
                          skip_group_check=True, waits=wts,
                          inc=(s["wqF"] if (jj == 3 and hp == 1 and half == 1)
                               else None))
                        wts = []
        for hp in range(2):
            wts = [(s["wqF"], 8 * (g + 1))]
            if g > 0:
                wts.append((s[f"qs{hp}"], 16 * g))
            if g == 0 and hp == 0:
                wts.append((s["kvs"], None))
            O("vector", "tensor_mul", rtmp1[0:64, :], QPS[hp][64:128, :],
              tab[0:64, 1024:2048], waits=wts)
            O("vector", "tensor_mul", rtmp1[64:128, :], QPS[hp][0:64, :],
              tab[64:128, 1024:2048])
            O("vector", "tensor_mul", rtmp2[:], QPS[hp][:, :], tab[:, 0:1024])
            O("vector", "tensor_add", qstage[hp][:], rtmp2[:], rtmp1[:],
              inc=s["qe"])
            h = 2 * g + hp
            O("gpsimd", "dma_start", out=q_in[h * 128:(h + 1) * 128, :],
              in_=qstage[hp][:], waits=[(s["qe"], None)],
              inc=s[f"qs{hp}"], inc_amt=16)
    P.coll("AllToAll", ALU.bypass, RG, [q_in.opt()], [q_out.opt()],
           waits=[(s["qs0"], None), (s["qs1"], None)], inc=s["collQ"])

    if PHASES <= 4:
        return _finish()

    P.fence()
    O("gpsimd", "dma_start",
      out=At[:].rearrange("p (j i) -> p j i", i=128),
      in_=A2T.rearrange("(j p) i -> p j i", p=128),
      waits=[(s["qs0"], None), (s["qs1"], None)], inc=s["misc"], inc_amt=16)
    for j in range(NJ):
        for half in range(2):
            wts = []
            if j == 0 and half == 0:
                wts = [(s["misc"], None), (s["qe"], None)]
            O("tensor", "matmul", PA[0][:, half * 512:(half + 1) * 512],
              At[:, j * 128:(j + 1) * 128],
              xn_fm[:, j * NTOK + half * 512: j * NTOK + (half + 1) * 512],
              start=(j == 0), stop=(j == NJ - 1), skip_group_check=True,
              waits=wts, inc=(s["tfmT"] if (j == NJ - 1 and half == 1) else None))
    O("vector", "tensor_copy", tfm[:, 0:512], PA[0][:, 0:512],
      waits=[(s["tfmT"], None)])
    O("vector", "tensor_copy", tfm[:, 512:1024], PA[0][:, 512:1024], inc=s["tfm"])

    O("gpsimd", "dma_start", out=rsum_sb[:], in_=r_out,
      waits=[(s["collR"], 1)], inc=s["rtrG"], inc_amt=16)
    O("vector", "tensor_scalar_mul", rsum_sb[:], rsum_sb[:], 1.0 / NQTOK,
      waits=[(s["rtrG"], None)], inc=s["rtr"])
    rq = rsum_sb[:].rearrange("p (b j) -> p j b", b=2)
    for j in range(NJ):
        wts = ([(s["rtr"], None), (s["qe"], None)] + pt2_wait(0)) if j == 0 else []
        O("tensor", "matmul", PA[2][0:2, 0:8], rq[:, j, :],
          rwt_sb[:, j * 8:(j + 1) * 8], start=(j == 0), stop=(j == NJ - 1),
          skip_group_check=True, waits=wts,
          inc=(s["rtrT"] if j == NJ - 1 else None))
    O("vector", "tensor_add", logit_sb[:], PA[2][0:2, 0:8], rb2_sb[:],
      waits=[(s["rtrT"], None)], inc=s["rtr"])
    O("vector", "reduce_max", rmax[:], logit_sb[:], axis=AX.X, inc=s["rtr"])
    O("vector", "tensor_scalar_mul", rmax2[:], rmax[:], -1.0, inc=s["rtr"])
    O("scalar", "activation", rexp[:], logit_sb[:], AF.Exp, bias=rmax2[:, 0:1],
      waits=[(s["rtr"], None)], inc=s["rtrA2"])
    O("vector", "reduce_sum", rsumt[:], rexp[:], axis=AX.X,
      waits=[(s["rtrA2"], None)], inc=s["rtr"])
    O("vector", "reciprocal", rrec[:], rsumt[:], inc=s["rtr"])
    O("vector", "tensor_scalar", rsm[:], rexp[:], rrec[:, 0:1], None, ALU.mult,
      inc=s["rtr"])
    O("tensor", "transpose", PA[3][0:8, 0:2], rsm[:], identf[0:2, 0:2],
      waits=[(s["rtr"], None), (s["qe"], None)], inc=s["rtrT"])
    O("vector", "tensor_copy", routT[:], PA[3][0:8, 0:2],
      waits=[(s["rtrT"], None)], inc=s["rtr"])
    for b in range(2):
        O("tensor", "matmul", PA[2][:, 16 + b:17 + b], rep_f[:],
          routT[0:8, b:b + 1], start=True, stop=True, skip_group_check=True,
          waits=[(s["rtr"], None)], inc=s["rtrT"])
    O("vector", "tensor_copy", svec[:, 0:1], PA[2][:, 16:17],
      waits=[(s["rtrT"], None)], inc=s["rtr"])
    O("vector", "tensor_copy", svec[:, 1:2], PA[2][:, 17:18], inc=s["rtr"])
    pt2_free[0] = (s["rtr"], s["rtr"].n)
    pt2_free[1] = (s["qe"], s["qe"].n)
    pt2_free[2] = (s["tfm"], s["tfm"].n)
    pt2_free[3] = (s["tfm"], s["tfm"].n)
    for b in range(2):
        O("vector", "tensor_scalar", tsc[:, b * 512:(b + 1) * 512],
          tfm[:, b * 512:(b + 1) * 512], svec[:, b:b + 1], None, ALU.mult,
          inc=s["tfm"])
    if dbg:
        O("vector", "tensor_copy", rdbg[:, 0:8], logit_sb[:],
          waits=[(s["rtr"], None)])
        O("vector", "tensor_copy", rdbg[:, 8:9], rmax[:])
        O("vector", "tensor_copy", rdbg[:, 9:17], rexp[:])
        O("vector", "tensor_copy", rdbg[:, 17:18], rsumt[:])
        O("vector", "tensor_copy", rdbg[:, 18:19], rrec[:])
        O("vector", "tensor_copy", rdbg[:, 19:27], rsm[:], inc=s["rtr"])
        O("gpsimd", "dma_start", out=dbg_o["d_rout"], in_=rdbg[:, 19:27],
          waits=[(s["rtr"], None)], inc=s["dbg"], inc_amt=16)
        O("gpsimd", "dma_start", out=dbg_o["d_rdbg"], in_=rdbg[:],
          waits=[(s["rtr"], None)], inc=s["dbg"], inc_amt=16)

    if PHASES <= 5:
        return _finish()

    P.fence()
    SCSLOT = [(PA[2], 0), (PA[2], 512), (PA[0], 0), (PA[0], 512)]
    deferq = []
    sc_i = 0
    pt_i = 0
    pm_i = 0
    vt_n = 0
    n_pv = 0
    n_msk = 0
    pt_slot_free = {}
    pm_slot_free = {}
    vt_slot_free = {}
    first_vt = True
    for b in range(2):
        for p in range(8):
            wts = [(s["collKV"], 1)]
            if b == 1 and p == 0:
                wts.append((s["pvd"], None))
            O("sync", "dma_start", out=K_res[:, p * 512:(p + 1) * 512],
              in_=kv_out[1024 * p + 512 * b: 1024 * p + 512 * b + 512, 0:128],
              transpose=True, waits=wts, inc=s["kres"], inc_amt=16)
        for p in range(8):
            wts = [(s["kres"], (b * 8 + 8) * 16)]
            if b == 1 and p == 0:
                wts.append((s["aev"], None))
            if b == 0 and p == 0:
                wts.append((s["tfm"], None))
            O("vector", "tensor_mul", qq[:], K_res[:, p * 512:(p + 1) * 512],
              K_res[:, p * 512:(p + 1) * 512], waits=wts, inc=s["qn"])
            O("tensor", "matmul", PA[2][0:1, 0:512], onescol[:], qq[:],
              start=True, stop=True, skip_group_check=True,
              waits=[(s["qn"], None)] + pt2_wait(0), inc=s["nmT"])
            O("vector", "reduce_max", kn_sb[0:1, p:p + 1], PA[2][0:1, 0:512],
              axis=AX.X, waits=[(s["nmT"], None)], inc=s["scF"])
            pt2_free[0] = (s["scF"], s["scF"].n)
        while deferq:
            deferq.pop(0)()
        for qp in range(8):
            O("vector", "reduce_max", kmaxr[0:1, qp:qp + 1],
              kn_sb[0:1, 0:qp + 1], axis=AX.X, inc=s["qn"])
        qn_slot = 0
        qn_mul_at = {}
        for qp in range(8):
            for hh in range(4):
                sl = qn_slot % 4
                wts = [(s["collQ"], 1)]
                if qn_slot >= 4:
                    wts.append((s["qn"], qn_mul_at[qn_slot - 4]))
                O("scalar", "dma_start", out=qsb[:, sl * 512:(sl + 1) * 512],
                  in_=q_out[512 * qp + 128 * hh: 512 * qp + 128 * (hh + 1),
                            512 * b: 512 * (b + 1)],
                  waits=wts, inc=s[f"qnl{sl}"], inc_amt=16)
                O("vector", "tensor_mul", qq[:],
                  qsb[:, sl * 512:(sl + 1) * 512],
                  qsb[:, sl * 512:(sl + 1) * 512],
                  waits=[(s[f"qnl{sl}"], s[f"qnl{sl}"].n)], inc=s["qn"])
                qn_mul_at[qn_slot] = s["qn"].n
                scs0 = qn_slot % 2
                O("tensor", "matmul", PA[2][0:1, scs0 * 512:scs0 * 512 + 512],
                  onescol[:], qq[:], start=True, stop=True,
                  skip_group_check=True,
                  waits=[(s["qn"], None)] + pt2_wait(scs0), inc=s["nmT"])
                O("vector", "reduce_max", qn2a[0:1, qp * 4 + hh: qp * 4 + hh + 1],
                  PA[2][0:1, scs0 * 512:scs0 * 512 + 512], axis=AX.X,
                  waits=[(s["nmT"], None)], inc=s["scF"])
                pt2_free[scs0] = (s["scF"], s["scF"].n)
                qn_slot += 1
        for qp in range(8):
            O("vector", "tensor_scalar", qn2a[0:1, qp * 4: qp * 4 + 4],
              qn2a[0:1, qp * 4: qp * 4 + 4], kmaxr[0:1, qp:qp + 1], None,
              ALU.mult, inc=s["qn"])
        O("scalar", "activation", mtmp2_row[:], qn2a[:], AF.Ln,
          waits=[(s["qn"], None)], inc=s["nmA"])
        O("vector", "tensor_scalar", mtmp2_row[:], mtmp2_row[:], 0.5, LN105,
          ALU.mult, ALU.add, waits=[(s["nmA"], None)], inc=s["qn"])
        O("scalar", "activation", mtmp2_row[:], mtmp2_row[:], AF.Exp,
          waits=[(s["qn"], None)], inc=s["nmA"])
        O("vector", "tensor_scalar_mul", negm_bf[:], mtmp2_row[:], -1.0,
          waits=[(s["nmA"], None)], inc=s["qn"])
        wts = [(s["qn"], None), (s["aev"], None)] + pt2_wait(0)
        if b == 0:
            wts.append((s["tfm"], None))
        O("tensor", "matmul", PA[2][:, 0:32], onesrow[:], negm_bf[:],
          start=True, stop=True, skip_group_check=True, waits=wts,
          inc=s["nmT"])
        O("vector", "tensor_copy", negm[:], PA[2][:, 0:32],
          waits=[(s["nmT"], None)], inc=s["nm"])
        pt2_free[0] = (s["nm"], s["nm"].n)
        for qp in range(8):
            for hh in range(4):
                wts = []
                if hh == 0:
                    wts = [(s["collQ"], 1)]
                    if qp > 0:
                        wts.append((s["pvd"], None))
                    else:
                        wts.append((s["qn"], None))
                O("scalar", "dma_start", out=qsb[:, hh * 512:(hh + 1) * 512],
                  in_=q_out[512 * qp + 128 * hh: 512 * qp + 128 * (hh + 1),
                            512 * b: 512 * (b + 1)],
                  waits=wts, inc=s["qld"], inc_amt=16)
            for hpair in range(2):
                pend = []

                def flush_one():
                    nonlocal pend, n_pv
                    if not pend:
                        return
                    pd = pend.pop(0)
                    O("tensor", "matmul",
                      PA[1][0:1, pd["h2"] * 512:(pd["h2"] + 1) * 512],
                      onescol[:], pd["puse"], start=pd["first"],
                      stop=pd["last"], skip_group_check=True,
                      waits=pd["lw"])
                    O("tensor", "matmul",
                      PA[3][:, pd["h2"] * 512:(pd["h2"] + 1) * 512],
                      pd["vtile"], pd["puse"], start=pd["first"],
                      stop=pd["last"], skip_group_check=True,
                      waits=[pd["vw"]], inc=s["pvd"])
                    n_pv += 1
                    if pd["tile_of_tk_done"] is not None:
                        vt_slot_free[pd["tile_of_tk_done"]] = (s["pvd"], n_pv)
                    if pd["ptidx"] is not None:
                        pt_slot_free[pd["ptidx"]] = (s["pvd"], n_pv)
                    if pd["pmidx"] is not None:
                        pm_slot_free[pd["pmidx"]] = (s["pvd"], n_pv)

                for tk in range(qp + 1):
                    vslot = vt_n % 2
                    wts = [(s["collKV"], 1)]
                    if vt_n >= 2:
                        wts.append(vt_slot_free[vt_n - 2])
                    if first_vt:
                        wts.append((s["tfm"], None))
                        first_vt = False
                    O("scalar", "dma_start",
                      out=vt[vslot][:].rearrange("p (t c) -> p t c", c=128),
                      in_=kv_out[1024 * tk + 512 * b: 1024 * tk + 512 * b + 512,
                                 128:256].rearrange("(t p) c -> p t c", p=128),
                      waits=wts, inc=s[f"vt{vslot}"], inc_amt=16)
                    for tb in range(4):
                        for h2 in range(2):
                            if len(pend) >= 2:
                                flush_one()
                            if deferq:
                                deferq.pop(0)()
                            hh = hpair * 2 + h2
                            first = (tk == 0 and tb == 0)
                            last = (tk == qp and tb == 3)
                            scs = sc_i % 4
                            sap, soff = SCSLOT[scs]
                            wts = list(pt2_wait(scs))
                            if first and tb == 0 and h2 == 0:
                                wts.append((s["qld"], s["qld"].n))
                                if hpair == 0:
                                    wts.append((s["nm"], None))
                                    wts.append((s["kres"], (b * 8 + 8) * 16))
                            O("tensor", "matmul",
                              sap[:, soff:soff + 512],
                              K_res[:, tk * 512 + tb * 128:
                                    tk * 512 + (tb + 1) * 128],
                              qsb[:, hh * 512:(hh + 1) * 512],
                              start=True, stop=True, skip_group_check=True,
                              waits=wts, inc=s["sc"])
                            ptidx = pt_i
                            pts = ptidx % 5
                            ew = [(s["sc"], None)]
                            if ptidx >= 5:
                                ew.append(pt_slot_free[ptidx - 5])
                            O("scalar", "activation", pt[pts][:],
                              sap[:, soff:soff + 512], AF.Exp,
                              bias=negm[:, qp * 4 + hh: qp * 4 + hh + 1],
                              waits=ew, inc=s["exp"])
                            pt2_free[scs] = (s["exp"], s["exp"].n)
                            sc_i += 1
                            pt_i += 1
                            pmidx = None
                            if tk == qp:
                                pms = pm_i % 2
                                mw = [(s["exp"], None)]
                                if pm_i >= 2:
                                    mw.append(pm_slot_free[pm_i - 2])
                                O("vector", "tensor_mul", pm[pms][:], pt[pts][:],
                                  masks[:, tb * 512:(tb + 1) * 512],
                                  waits=mw, inc=s["msk"])
                                n_msk += 1
                                pt_slot_free[ptidx] = (s["msk"], n_msk)
                                pmidx = pm_i
                                pm_i += 1
                                puse = pm[pms][:]
                                pwait = [(s["msk"], s["msk"].n)]
                                ptrec = None
                            else:
                                puse = pt[pts][:]
                                pwait = [(s["exp"], s["exp"].n)]
                                ptrec = ptidx
                            lw = list(pwait)
                            if first:
                                lw.append((s["aev"], s["aev"].n))
                                if b == 0 and qp == 0 and hpair == 0:
                                    lw.append((s["rtr"], s["rtr"].n))
                            cur = dict(h2=h2, puse=puse, first=first, last=last,
                                       lw=lw,
                                       vtile=vt[vslot][:, tb * 128:(tb + 1) * 128],
                                       vw=(s[f"vt{vslot}"], (vt_n // 2 + 1) * 16),
                                       ptidx=ptrec, pmidx=pmidx,
                                       tile_of_tk_done=(
                                           vt_n if (tb == 3 and h2 == 1) else None))
                            pend.append(cur)
                    vt_n += 1
                while pend:
                    flush_one()
                for h2 in range(2):
                    hh = hpair * 2 + h2
                    O("vector", "reciprocal",
                      qn_row[0:1, h2 * 512:(h2 + 1) * 512],
                      PA[1][0:1, h2 * 512:(h2 + 1) * 512],
                      waits=[(s["pvd"], None)], inc=s["aev"])
                    O("vector", "tensor_copy", linvb[:, h2 * 512:(h2 + 1) * 512],
                      PA[3][:, h2 * 512:(h2 + 1) * 512], inc=s["aev"])
                    aouse = (b * 8 + qp) * 2 + hpair

                    def d_bcast_mul(h2=h2, aev_n=s["aev"].n, aouse=aouse):
                        O("tensor", "matmul",
                          PA[0][:, h2 * 512:(h2 + 1) * 512],
                          onesrow_f[:], qn_row[0:1, h2 * 512:(h2 + 1) * 512],
                          start=True, stop=True, skip_group_check=True,
                          waits=[(s["aev"], aev_n)] + pt2_wait(2 + h2),
                          inc=s["nmT"])
                        O("vector", "tensor_mul", aosb[h2][:],
                          PA[0][:, h2 * 512:(h2 + 1) * 512],
                          linvb[:, h2 * 512:(h2 + 1) * 512],
                          waits=[(s["nmT"], None),
                                 (s[f"aos{h2}"], 16 * aouse)],
                          inc=s["aev"])
                        pt2_free[2 + h2] = (s["aev"], s["aev"].n)

                    def d_store(h2=h2, hh=hh, qp=qp, b=b):
                        O("gpsimd", "dma_start",
                          out=ao_in[1024 * qp + 512 * b + 128 * hh:
                                    1024 * qp + 512 * b + 128 * (hh + 1), :],
                          in_=aosb[h2][:], waits=[(s["aev"], None)],
                          inc=s[f"aos{h2}"], inc_amt=16)
                    deferq.extend([d_bcast_mul, d_store])
    while deferq:
        deferq.pop(0)()
    P.coll("AllToAll", ALU.bypass, RG, [ao_in.opt()], [ao_out.opt()],
           waits=[(s["aos0"], None), (s["aos1"], None)], inc=s["collAO"])

    if PHASES <= 6:
        return _finish()

    for j in range(NCORE):
        for bb in range(2):
            for hp in range(4):
                h = 4 * j + hp
                O("sync", "dma_start",
                  out=attn_fm[:, h * NTOK + 512 * bb: h * NTOK + 512 * bb + 512],
                  in_=ao_out[1024 * j + 512 * bb + 128 * hp:
                             1024 * j + 512 * bb + 128 * (hp + 1), :],
                  waits=([(s["collAO"], 1)] if (j == 0 and bb == 0 and hp == 0)
                         else []),
                  inc=s["afm"], inc_amt=16)
    N_AFM = s["afm"].n
    P.fence()
    O("gpsimd", "dma_start", out=fbb[:], in_=fb,
      waits=[(s["pvd"], None), (s["exp"], None), (s["msk"], None),
             (s["aev"], None), (s["qld"], None), (s["aos0"], None),
             (s["aos1"], None)],
      inc=s["misc"], inc_amt=16)
    O("gpsimd", "dma_start", out=bcat[:], in_=B2T,
      waits=[(s["pvd"], None), (s["aev"], None), (s["msk"], None),
             (s["qn"], None), (s["afm"], N_AFM)],
      inc=s["misc"], inc_amt=16)
    N_MISC = s["misc"].n

    if AFM_ONLY:
        return _finish()
    OUTB = [(PA[i // 2], (i % 2) * 512) for i in range(8)]
    for o in range(OLIM):
        for j in range(NJ):
            gidx = o * NJ + j
            O("gpsimd", "dma_start", out=wo_t[j % 2][:],
              in_=Wo[j * 128:(j + 1) * 128, o * 512:(o + 1) * 512],
              waits=([(s["woF"], gidx - 1)] if gidx >= 2 else []),
              inc=s[f"wo{j % 2}"], inc_amt=16, future=True)
        if FINMODE >= 2:
            for j in range(NJ):
                gidx = o * NJ + j
                O("gpsimd", "dma_start", out=wf_t[j % 2][:],
                  in_=WfT[j * 128:(j + 1) * 128, o * 512:(o + 1) * 512],
                  waits=([(s["wfF"], gidx - 1)] if gidx >= 2 else []),
                  inc=s[f"wf{j % 2}"], inc_amt=16, future=True)
        if FINMODE >= 3:
            for tt in range(NT):
                nxr = o * NT + tt
                O("scalar", "dma_start", out=xres[tt % 2][:],
                  in_=x_my[tt * 128:(tt + 1) * 128, o * 512:(o + 1) * 512],
                  waits=([(s["ouV"], nxr - 1)] if nxr >= 2 else []),
                  inc=s[f"xr{tt % 2}"], inc_amt=16, future=True)
        for j in range(NJ):
            for tt in range(NT):
                pa, po = OUTB[tt]
                wts = []
                if tt == 0:
                    wts.append((s[f"wo{j % 2}"], ((o * NJ + j) // 2 + 1) * 16))
                    if j == 0:
                        wts.append((s["afm"], N_AFM))
                        if o == 0:
                            wts.append((s["nmT"], None))
                            wts.append((s["aev"], None))
                            wts.append((s["exp"], None))
                        elif FINMODE >= 3:
                            wts.append((s["ouV"], o * 8))
                O("tensor", "matmul", pa[:, po:po + 512],
                  attn_fm[:, j * NTOK + tt * 128: j * NTOK + (tt + 1) * 128],
                  wo_t[j % 2][:], start=(j == 0), stop=False,
                  skip_group_check=True, waits=wts,
                  inc=(s["woF"] if tt == NT - 1 else None))
        if FINMODE >= 2:
            for j in range(NJ):
                for tt in range(NT):
                    pa, po = OUTB[tt]
                    wts = []
                    if tt == 0:
                        wts.append((s[f"wf{j % 2}"], ((o * NJ + j) // 2 + 1) * 16))
                    O("tensor", "matmul", pa[:, po:po + 512],
                      xn_fm[:, j * NTOK + tt * 128: j * NTOK + (tt + 1) * 128],
                      wf_t[j % 2][:], start=False, stop=False,
                      skip_group_check=True, waits=wts,
                      inc=(s["wfF"] if tt == NT - 1 else None))
        if FINMODE >= 3:
            for tt in range(NT):
                pa, po = OUTB[tt]
                wts = [(s["misc"], N_MISC), (s["tfm"], None)] if tt == 0 else []
                O("tensor", "matmul", pa[:, po:po + 512],
                  tsc[:, tt * 128:(tt + 1) * 128], bcat[:, o * 512:(o + 1) * 512],
                  start=False, stop=False, skip_group_check=True, waits=wts)
                O("tensor", "matmul", pa[:, po:po + 512],
                  onesrow[:], fbb[0:1, o * 512:(o + 1) * 512],
                  start=False, stop=True, skip_group_check=True, inc=s["ou"])
            for tt in range(NT):
                pa, po = OUTB[tt]
                nxr2 = o * NT + tt
                O("vector", "tensor_add", outsb[tt % 2][:], pa[:, po:po + 512],
                  xres[tt % 2][:],
                  waits=[(s["ou"], o * NT + tt + 1),
                         (s[f"xr{tt % 2}"], (nxr2 // 2 + 1) * 16),
                         (s[f"ouF{tt % 2}"], (nxr2 // 2) * 16)],
                  inc=s["ouV"])
                O("sync", "dma_start",
                  out=out_my[tt * 128:(tt + 1) * 128, o * 512:(o + 1) * 512],
                  in_=outsb[tt % 2][:], waits=[(s["ouV"], None)],
                  inc=s[f"ouF{tt % 2}"], inc_amt=16)
    P.cmds["sync"].append(lambda e, h=s["ouF0"].h, v=s["ouF0"].n: e.wait_ge(h, v))
    P.cmds["sync"].append(lambda e, h=s["ouF1"].h, v=s["ouF1"].n: e.wait_ge(h, v))

    if dbg:
        for j in range(NJ):
            O("vector", "tensor_copy", linvb[:, 0:1024].bitcast(BF16)[:, 0:1024]
              if False else linvb[:, 0:512],
              xn_fm[:, j * NTOK: j * NTOK + 512], waits=[(s["ouF0"], None), (s["ouF1"], None)])
            O("vector", "tensor_copy", linvb[:, 512:1024],
              xn_fm[:, j * NTOK + 512: (j + 1) * NTOK], inc=s["dbgV"])
            O("gpsimd", "dma_start",
              out=dbg_o["d_xnfm"][:, j * NTOK:(j + 1) * NTOK],
              in_=linvb[:], waits=[(s["dbgV"], None)], inc=s["dbg"], inc_amt=16)
            O("vector", "tensor_copy", linvb[:, 0:512],
              attn_fm[:, j * NTOK: j * NTOK + 512], waits=[(s["dbg"], None)])
            O("vector", "tensor_copy", linvb[:, 512:1024],
              attn_fm[:, j * NTOK + 512: (j + 1) * NTOK], inc=s["dbgV"])
            O("gpsimd", "dma_start",
              out=dbg_o["d_attn"][:, j * NTOK:(j + 1) * NTOK],
              in_=linvb[:], waits=[(s["dbgV"], None)], inc=s["dbg"], inc_amt=16)
        O("gpsimd", "dma_start", out=dbg_o["d_q"], in_=q_out,
          waits=[(s["collQ"], 1)], inc=s["dbg"], inc_amt=16)
        O("gpsimd", "dma_start", out=dbg_o["d_kv"], in_=kv_out,
          waits=[(s["collKV"], 1)], inc=s["dbg"], inc_amt=16)
        O("gpsimd", "dma_start", out=dbg_o["d_ao"], in_=ao_out,
          waits=[(s["collAO"], 1)], inc=s["dbg"], inc_amt=16)
        P.cmds["gpsimd"].append(
            lambda e, h=s["dbg"].h, v=s["dbg"].n: e.wait_ge(h, v))

    return _finish()



def _consts():
    c = {}
    c["identb"] = np.eye(128, dtype=ml_dtypes.bfloat16)
    c["identf"] = np.eye(128, dtype=np.float32)
    c["onescol"] = np.ones((128, 1), dtype=ml_dtypes.bfloat16)
    c["onesrow"] = np.ones((1, 128), dtype=ml_dtypes.bfloat16)
    c["onesrow_f"] = np.ones((1, 128), dtype=np.float32)
    rp = np.zeros((8, 128), dtype=np.float32)
    for e in range(8):
        rp[e, e * 16:(e + 1) * 16] = 2.0
    c["rep_f"] = rp
    m = np.zeros((512, 512), dtype=np.float32)
    i = np.arange(128)[:, None]
    jj = np.arange(512)[None, :]
    for blk in range(4):
        m[blk * 128:(blk + 1) * 128] = (blk * 128 + i <= jj)
    c["masks"] = m.astype(ml_dtypes.bfloat16)
    return c


def _rope_tables(core):
    pos = np.arange(512 * core, 512 * core + 512, dtype=np.float64)
    inv = 1.0 / (THETA ** (np.arange(0, HD, 2, dtype=np.float64) / HD))
    ang = pos[None, :] * inv[:, None]
    cos = np.cos(ang); sin = np.sin(ang)
    cosf = np.concatenate([cos, cos], 0)
    sinf = np.concatenate([-sin, sin], 0)
    cosf = np.concatenate([cosf, cosf], 1)
    sinf = np.concatenate([sinf, sinf], 1)
    sc = 1.0 / np.sqrt(HD)
    return (np.asarray(cosf * sc, np.float32), np.asarray(sinf * sc, np.float32),
            np.asarray(cosf, np.float32), np.asarray(sinf, np.float32))


def kernel(L0_attn_x, L0_attn_Wq, L0_attn_Wk, L0_attn_Wv, L0_attn_Wo,
           L0_ffn_W, L0_ffn_b, L0_ffn_A, L0_ffn_B, L0_ffn_router_W,
           L0_ffn_router_b):
    x = np.ascontiguousarray(np.asarray(L0_attn_x, np.float32))
    Wq_ = np.ascontiguousarray(np.asarray(L0_attn_Wq, np.float32))
    Wk_ = np.ascontiguousarray(np.asarray(L0_attn_Wk, np.float32))
    Wv_ = np.ascontiguousarray(np.asarray(L0_attn_Wv, np.float32))
    Wo_ = np.ascontiguousarray(np.asarray(L0_attn_Wo, np.float32))
    WfT_ = np.ascontiguousarray(np.asarray(L0_ffn_W, np.float32).T)
    fb_ = np.ascontiguousarray(np.asarray(L0_ffn_b, np.float32)).reshape(1, D)
    A2T_ = np.ascontiguousarray(np.asarray(L0_ffn_A, np.float32).reshape(128, D).T)
    B2T_ = np.ascontiguousarray(
        np.asarray(L0_ffn_B, np.float32).transpose(0, 2, 1).reshape(128, 4096))
    rW_ = np.ascontiguousarray(np.asarray(L0_ffn_router_W, np.float32))
    rb_ = np.ascontiguousarray(np.asarray(L0_ffn_router_b, np.float32))

    rwt_ = np.ascontiguousarray(
        rW_.T.reshape(32, 128, 8).transpose(1, 0, 2).reshape(128, 256))
    rb2_ = np.ascontiguousarray(np.tile(rb_.reshape(1, 8), (2, 1)))

    consts = _consts()
    shared = dict(Wq=Wq_, Wk=Wk_, Wv=Wv_, Wo=Wo_, WfT=WfT_, fb=fb_, A2T=A2T_,
                  B2T=B2T_, rwt=rwt_, rb2=rb2_, **consts)

    in_maps = []
    for c in range(NCORE):
        cq, sq, ck, sk = _rope_tables(c)
        x_myc = np.ascontiguousarray(
            np.concatenate([x[0, 512 * c:512 * (c + 1), :],
                            x[1, 512 * c:512 * (c + 1), :]], axis=0))
        rt = np.zeros((1, NTOK), np.float32)
        lo = max(Q_LO, 512 * c) - 512 * c
        hi = min(Q_HI, 512 * c + 512) - 512 * c
        if hi > lo:
            rt[0, lo:hi] = 1.0
            rt[0, 512 + lo:512 + hi] = 1.0
        m = dict(shared)
        m.update(x_my=x_myc, cosq=cq, sinq=sq, cosk=ck, sink=sk,
                 rtok=rt.astype(ml_dtypes.bfloat16))
        in_maps.append(m)

    nc = build_program(dbg=DBG)
    res = run_bass_kernel_spmd(nc, in_maps, core_ids=list(range(NCORE)))

    out = np.empty((2, SEQ, D), dtype=np.float32)
    for c in range(NCORE):
        o = res.results[c]["out_my"]
        out[0, 512 * c:512 * (c + 1), :] = o[0:512, :]
        out[1, 512 * c:512 * (c + 1), :] = o[512:1024, :]
    if DBG:
        kernel._dbg = res
    return out


# version 08
# speedup vs baseline: 1.5662x; 1.0106x over previous
import numpy as np
import ml_dtypes
from contextlib import ExitStack

import concourse.bass as bass
import concourse.mybir as mybir
from concourse.bass_utils import run_bass_kernel_spmd
from concourse.aot_env import allow_ici_all_to_all

F32 = mybir.dt.float32
BF16 = mybir.dt.bfloat16
AF = mybir.ActivationFunctionType
ALU = mybir.AluOpType
AX = mybir.AxisListType

NCORE = 8
D = 4096
SEQ = 4096
HD = 128
CH = 512
NTOK = 1024
NT = 8
NJ = 32
NHKV = 8
Q_LO, Q_HI = 611, 4095
NQTOK = Q_HI - Q_LO
EPS = 1e-5
LN105 = float(np.log(1.05))
THETA = 10000.0

DBG = False
PHASES = 7
OLIM = 8
AFM_ONLY = False
FINMODE = 3


class Sem:
    __slots__ = ("h", "n", "name", "vmap")

    def __init__(self, h, name):
        self.h, self.n, self.name = h, 0, name
        self.vmap = {}


class Prog:

    SELF_ENGS = ("vector", "scalar")

    def __init__(self, nc, ctx):
        self.nc, self.ctx = nc, ctx
        self.cmds = {e: [] for e in ("tensor", "vector", "scalar", "gpsimd", "sync")}
        self.selfsem = {e: Sem(ctx.enter_context(nc.semaphore(f"self_{e}")),
                               f"self_{e}") for e in self.SELF_ENGS}
        self.deps = {e: {"w": {}, "r": {}} for e in self.SELF_ENGS}
        self.fence_val = {e: 0 for e in self.SELF_ENGS}
        self.owner = {}

    def sem(self, name):
        return Sem(self.ctx.enter_context(self.nc.semaphore(name)), name)

    @staticmethod
    def _ap_name(a):
        t = getattr(a, "tensor", None)
        if t is not None and hasattr(t, "name"):
            return t.name
        if hasattr(a, "name") and hasattr(a, "ap"):
            return a.name
        return None

    def _rw_sets(self, args, kwargs):
        rset, wset = set(), set()
        for i, a in enumerate(args):
            nm = self._ap_name(a)
            if nm is None:
                continue
            (wset if i == 0 else rset).add(nm)
        for k, a in kwargs.items():
            nm = self._ap_name(a)
            if nm is None:
                continue
            if k in ("out", "accum_out"):
                wset.add(nm)
            else:
                rset.add(nm)
        return rset, wset

    def _resolve(self, sm, vv):
        eng = self.owner.get(sm.name)
        if eng is None:
            return (sm.h, vv)
        assert vv in sm.vmap, f"no vmap[{vv}] for {sm.name} (owner {eng})"
        return (self.selfsem[eng].h, sm.vmap[vv])

    def op(self, eng, meth, *args, waits=(), inc=None, inc_amt=1, future=False,
           **kwargs):
        wl = []
        for (sm, v) in waits:
            vv = sm.n if v is None else v
            if not future:
                assert vv <= sm.n, f"wait {sm.name}>={vv} but emitted {sm.n}"
            if vv > 0:
                wl.append((sm, vv))
        is_dma = meth in ("dma_start", "dma_start_transpose")
        ih = None
        if eng in self.SELF_ENGS:
            ss = self.selfsem[eng]
            if not is_dma:
                rset, wset = self._rw_sets(args, kwargs)
                dep = self.deps[eng]
                need = self.fence_val[eng]
                for t in rset | wset:
                    need = max(need, dep["w"].get(t, 0))
                for t in wset:
                    need = max(need, dep["r"].get(t, 0))
                if need > 0:
                    wl.insert(0, (ss, need))
                ss.n += 1
                for t in wset:
                    dep["w"][t] = ss.n
                for t in rset:
                    dep["r"][t] = max(dep["r"].get(t, 0), ss.n)
                ih = (ss.h, 1)
                if inc is not None:
                    assert inc_amt == 1
                    inc.n += 1
                    o = self.owner.setdefault(inc.name, eng)
                    assert o == eng, f"{inc.name} inc'd from {o} and {eng}"
                    inc.vmap[inc.n] = ss.n
            else:
                if inc is not None:
                    inc.n += (inc_amt if inc_amt is not None else 1)
                    ih = (inc.h, inc_amt)
        else:
            if inc is not None:
                assert inc.name not in self.owner, \
                    f"{inc.name} owned by {self.owner.get(inc.name)}"
                inc.n += (inc_amt if inc_amt is not None else 1)
                ih = (inc.h, inc_amt)

        def run(e, meth=meth, args=args, kwargs=kwargs, wl=wl, ih=ih):
            for (sm, v) in wl:
                h, v = self._resolve(sm, v)
                e.wait_ge(h, v)
            ins = getattr(e, meth)(*args, **kwargs)
            if ih is not None:
                if ih[1] is None or ih[1] == 1:
                    ins.then_inc(ih[0], 1)
                else:
                    ins.then_inc(ih[0], ih[1])
        self.cmds[eng].append(run)

    def coll(self, kind, op_, rg, ins_, outs_, waits=(), inc=None):
        wl = []
        for (sm, v) in waits:
            vv = sm.n if v is None else v
            if vv > 0:
                wl.append((sm, vv))
        inc.n += 1

        def run(e, wl=wl, ih=inc.h):
            for (sm, v) in wl:
                h, v = self._resolve(sm, v)
                e.wait_ge(h, v)
            with allow_ici_all_to_all():
                e.collective_compute(kind, op_, rg, ins_, outs_).then_inc(ih)
        self.cmds["gpsimd"].append(run)

    def fence(self):
        for e in self.SELF_ENGS:
            self.fence_val[e] = self.selfsem[e].n

    def bump(self, eng, sem, amt=1):
        raise RuntimeError("standalone bumps are forbidden (writeback race)")


def build_program(dbg=False):
    nc = bass.Bass("TRN2", target_bir_lowering=False)
    ctx = ExitStack()

    def din(name, shape, dt=F32):
        return nc.dram_tensor(name, shape, dt, kind="ExternalInput").ap()

    x_my = din("x_my", [NTOK, D])
    Wq = din("Wq", [D, D]); Wk = din("Wk", [D, 1024]); Wv = din("Wv", [D, 1024])
    Wo = din("Wo", [D, D]); WfT = din("WfT", [D, D])
    fb = din("fb", [1, D])
    A2T = din("A2T", [D, 128]); B2T = din("B2T", [128, D])
    rwt = din("rwt", [128, 256]); rb2 = din("rb2", [2, 8])
    cosq_i = din("cosq", [128, NTOK]); sinq_i = din("sinq", [128, NTOK])
    cosk_i = din("cosk", [128, NTOK]); sink_i = din("sink", [128, NTOK])
    masks_i = din("masks", [512, 512], BF16)
    identb_i = din("identb", [128, 128], BF16)
    identf_i = din("identf", [128, 128], F32)
    onescol_i = din("onescol", [128, 1], BF16)
    onesrow_i = din("onesrow", [1, 128], BF16)
    onesrow_f_i = din("onesrow_f", [1, 128], F32)
    rep_i = din("rep_f", [8, 128], F32)
    rtok_i = din("rtok", [1, NTOK], BF16)

    out_my = nc.dram_tensor("out_my", [NTOK, D], F32, kind="ExternalOutput").ap()

    kv_in = nc.dram_tensor("kv_in", [NCORE * NTOK, 256], BF16, kind="Internal").ap()
    kv_out = nc.dram_tensor("kv_out", [NCORE * NTOK, 256], BF16, kind="Internal").ap()
    q_in = nc.dram_tensor("q_in", [D, NTOK], BF16, kind="Internal").ap()
    q_out = nc.dram_tensor("q_out", [D, NTOK], BF16, kind="Internal").ap()
    ao_in = [nc.dram_tensor(f"ao_in{i}", [NCORE * 512, 512], BF16,
                            kind="Internal").ap() for i in range(2)]
    ao_out = [nc.dram_tensor(f"ao_out{i}", [NCORE * 512, 512], BF16,
                             kind="Internal").ap() for i in range(2)]
    r_in = nc.dram_tensor("r_in", [128, 64], F32, kind="Internal").ap()
    r_out = nc.dram_tensor("r_out", [128, 64], F32, kind="Internal").ap()

    dbg_o = {}
    if dbg:
        for nm_, shp, dt in (("d_xnfm", [128, NJ * NTOK], F32),
                             ("d_q", [D, NTOK], F32),
                             ("d_kv", [NCORE * NTOK, 256], F32),
                             ("d_ao", [NCORE * NTOK, 512], F32),
                             ("d_rout", [2, 8], F32),
                             ("d_rdbg", [2, 32], F32),
                             ("d_attn", [128, NJ * NTOK], F32)):
            dbg_o[nm_] = nc.dram_tensor(nm_, shp, dt, kind="ExternalOutput").ap()

    def sb(name, shape, dt):
        return nc.alloc_sbuf_tensor(name, shape, dt)

    xn_fm = sb("xn_fm", [128, NJ * NTOK], BF16)
    attn_fm = sb("attn_fm", [128, NJ * NTOK], BF16)
    tab = sb("tab", [128, 2048], F32)
    masks = sb("masks_s", [128, 2048], BF16)
    identb = sb("identb_s", [128, 128], BF16)
    identf = sb("identf_s", [128, 128], F32)
    onescol = sb("onescol_s", [128, 1], BF16)
    onesrow = sb("onesrow_s", [1, 128], BF16)
    onesrow_f = sb("onesrow_f_s", [1, 128], F32)
    rep_f = sb("rep_f_s", [8, 128], F32)
    rwt_sb = sb("rwt_sb", [128, 256], F32)
    rb2_sb = sb("rb2_sb", [2, 8], F32)
    tfm = sb("tfm", [128, NTOK], BF16)
    tsc = sb("tsc", [128, NTOK], BF16)
    kn_sb = sb("kn_sb", [1, 8], F32)
    kk = sb("kk", [128, 512], F32)
    rsum_sb = sb("rsum_sb", [128, 64], F32)
    rtok_sb = sb("rtok_sb", [1, NTOK], BF16)
    qn2_sb = sb("qn2_sb", [1, 4], F32)
    mtmp = sb("mtmp", [1, 4], F32)
    mtmp_bf = sb("mtmp_bf", [1, 2], BF16)
    mtmp2_row = sb("mtmp2_row", [1, 32], F32)
    K_res = sb("K_res", [128, 4096], BF16)
    lsum = sb("lsum", [128, 1], F32); lsq = sb("lsq", [128, 1], F32)
    lmean = sb("lmean", [128, 1], F32); lvar = sb("lvar", [128, 1], F32)
    lrstd = sb("lrstd", [128, 1], F32); lnegmr = sb("lnegmr", [128, 1], F32)
    ltmp = sb("ltmp", [128, 1], F32)
    logit_sb = sb("logit_sb", [2, 8], F32); rmax = sb("rmax", [2, 1], F32)
    rexp = sb("rexp", [2, 8], F32); rsumt = sb("rsumt", [2, 1], F32)
    rrec = sb("rrec", [2, 1], F32); rsm = sb("rsm", [2, 8], F32)
    rmax2 = sb("rmax2", [2, 1], F32)
    rdbg = sb("rdbg", [2, 32], F32)
    routT = sb("routT", [8, 2], F32)
    svec = sb("svec", [128, 2], F32)
    wo_t = [sb(f"wo_t{i}", [128, 512], BF16) for i in range(2)]
    wf_t = [sb(f"wf_t{i}", [128, 512], BF16) for i in range(2)]
    xres = [sb(f"xres{i}", [128, 512], F32) for i in range(2)]
    outsb = [sb(f"outsb{i}", [128, 512], F32) for i in range(2)]
    WBUF = sb("WBUF", [128, 4096], BF16)

    ARENA = sb("ARENA", [128, 6144], F32)
    ab = nc.lookup_mloc(ARENA).addr

    def at_(name, shape, dt, off):
        nb = shape[1] * (4 if dt == F32 else 2)
        assert off + nb <= 24576, name
        return nc.alloc_sbuf_tensor_at(name, shape, dt, offset=ab + off)

    XBUF = at_("XBUF", [128, 4096], F32, 0)
    xn_tm = at_("xn_tm", [128, 4096], BF16, 16384)
    rtok_bc = at_("rtok_bc", [128, 1024], F32, 0)
    K_sb = at_("K_sb", [128, 1024], BF16, 0)
    V_sb = at_("V_sb", [128, 1024], BF16, 2048)
    ktm = at_("ktm", [128, 1024], BF16, 4096)
    vtm = at_("vtm", [128, 1024], BF16, 6144)
    rtmp1 = at_("rtmp1", [128, 1024], F32, 8192)
    rtmp2 = at_("rtmp2", [128, 1024], F32, 12288)
    qstage = [at_(f"qstage{i}", [128, 1024], BF16, 2048 * i) for i in range(2)]
    At = at_("At", [128, 4096], BF16, 16384)
    qsb = at_("qsb", [128, 2048], BF16, 0)
    vt = [at_(f"vt{i}", [128, 512], BF16, 4096 + 1024 * i) for i in range(2)]
    pt = [at_(f"pt{i}", [128, 512], BF16, off)
          for i, off in enumerate((6144, 7168, 8192, 15360, 22560))]
    pm = [at_(f"pm{i}", [128, 512], BF16, 9216 + 1024 * i) for i in range(2)]
    linvb = at_("linvb", [128, 1024], F32, 11264)
    qq = nc.alloc_sbuf_tensor_at("qq", [128, 512], BF16,
                                 offset=nc.lookup_mloc(kk).addr)
    qn_row = at_("qn_row", [1, 1024], F32, 16384)
    negm = at_("negm", [128, 32], F32, 23584)
    qn2a = sb("qn2a", [1, 32], F32)
    kmaxr = sb("kmaxr", [1, 8], F32)
    negm_bf = sb("negm_bf", [1, 32], BF16)
    aosb = [at_(f"aosb{i}", [128, 512], BF16, 20512 + 1024 * i) for i in range(2)]
    fbb = at_("fbb", [1, 4096], BF16, 0)
    bcat = at_("bcat", [128, 4096], BF16, 8192)

    PT = [ctx.enter_context(nc.psum_tensor(f"PT{i}", [128, 1024], F32)) for i in range(4)]
    PA = [p.ap() for p in PT]
    PB = [p.ap().bitcast(BF16) for p in PT]

    P = Prog(nc, ctx)
    O, B_ = P.op, P.bump
    RG = [list(range(NCORE))]

    s = {n: P.sem(n) for n in (
        "wkv0", "wkv1", "wq0", "wq1", "wq2", "wq3", "wo0", "wo1", "wf0", "wf1",
        "qnl0", "qnl1", "qnl2", "qnl3",
        "xr0", "xr1", "vt0", "vt1", "qs0", "qs1", "aos0", "aos1", "ouF0", "ouF1",
        "collR", "collKV", "collQ", "collAO", "collAO1", "afm1",
        "cst", "conv", "x", "xfree", "lnv", "lna", "xn", "xnfree",
        "trA", "trAF", "phA", "rtr", "rtrG", "rtrA2", "rtrT",
        "wkv", "wkvF", "kvt", "kve", "kvs",
        "wq", "wqF", "qp", "qe", "qs",
        "coll", "misc", "trK", "trKF", "tfm", "tfmT",
        "kres", "vt", "vtF", "qld", "qn", "nm", "nmA", "nmT",
        "sc", "scF", "exp", "ptF", "msk", "pmF", "lpv", "pvd", "aev", "aos",
        "afm", "wo", "woF", "wf", "wfF", "xr", "xrF", "ou", "ouV", "ouF",
        "dbg", "dbgV")}

    pt2_free = {0: None, 1: None, 2: None, 3: None}
    SCSLOT = None

    def pt2_wait(bank):
        f = pt2_free[bank]
        return [f] if f is not None else []

    for dst, src in ((identb[:], identb_i), (identf[:], identf_i),
                     (onescol[:], onescol_i), (onesrow[:], onesrow_i),
                     (onesrow_f[:], onesrow_f_i), (rep_f[:], rep_i),
                     (rwt_sb[:], rwt), (rb2_sb[:], rb2), (rtok_sb[:], rtok_i)):
        O("gpsimd", "dma_start", out=dst, in_=src, inc=s["cst"], inc_amt=16)
    O("gpsimd", "dma_start",
      out=masks[:].rearrange("p (m f) -> p m f", f=512),
      in_=masks_i.rearrange("(m p) f -> p m f", p=128), inc=s["cst"], inc_amt=16)
    O("gpsimd", "dma_start", out=tab[:, 0:1024], in_=cosk_i, inc=s["cst"], inc_amt=16)
    O("gpsimd", "dma_start", out=tab[:, 1024:2048], in_=sink_i, inc=s["cst"], inc_amt=16)
    N_CST = s["cst"].n

    TRSLOT = [(PB[0], 0), (PB[0], 1024), (PB[1], 0), (PB[1], 1024)]
    trb = 0
    for tt in range(NT):
        O("sync", "dma_start", out=XBUF[:], in_=x_my[tt * 128:(tt + 1) * 128, :],
          waits=[(s["xn"], None)], inc=s["x"], inc_amt=16)
        O("vector", "reduce_sum", lsum[:], XBUF[:], axis=AX.X,
          waits=[(s["x"], None)], inc=s["lnv"])
        O("scalar", "activation", xn_tm[:], XBUF[:], AF.Square, accum_out=lsq[:],
          waits=[(s["x"], None)] + ([(s["trAF"], 8 * tt)] if tt else []),
          inc=s["lna"])
        O("vector", "tensor_scalar_mul", lmean[:], lsum[:], 1.0 / D, inc=s["lnv"])
        O("vector", "tensor_scalar_mul", lvar[:], lsq[:], 1.0 / D,
          waits=[(s["lna"], None)], inc=s["lnv"])
        O("vector", "tensor_mul", ltmp[:], lmean[:], lmean[:], inc=s["lnv"])
        O("vector", "tensor_sub", lvar[:], lvar[:], ltmp[:], inc=s["lnv"])
        O("vector", "tensor_scalar_add", lvar[:], lvar[:], EPS, inc=s["lnv"])
        O("scalar", "activation", ltmp[:], lvar[:], AF.Sqrt,
          waits=[(s["lnv"], None)], inc=s["lna"])
        O("vector", "reciprocal", lrstd[:], ltmp[:], waits=[(s["lna"], None)],
          inc=s["lnv"])
        O("vector", "tensor_scalar", lnegmr[:], lmean[:], lrstd[:, 0:1], -1.0,
          ALU.mult, ALU.mult, inc=s["lnv"])
        O("scalar", "activation", xn_tm[:], XBUF[:], AF.Identity,
          scale=lrstd[:, 0:1], bias=lnegmr[:, 0:1],
          waits=[(s["lnv"], None)], inc=s["xn"])
        for b8 in range(8):
            ps, off = TRSLOT[trb % 4]
            for q in range(4):
                j = b8 * 4 + q
                wts = []
                if q == 0:
                    wts = [(s["xn"], None)]
                    if trb >= 4:
                        wts.append((s["trAF"], trb - 3))
                    if trb == 0:
                        wts.append((s["cst"], N_CST))
                O("tensor", "transpose", ps[:, off + q * 128: off + (q + 1) * 128],
                  xn_tm[:, j * 128:(j + 1) * 128], identb[:], waits=wts,
                  inc=(s["trA"] if q == 3 else None))
            dst = xn_fm[:].rearrange("p (j t) -> p j t", t=NTOK)[
                :, b8 * 4:(b8 + 1) * 4, tt * 128:(tt + 1) * 128]
            src = ps[:, off:off + 512].rearrange("p (j t) -> p j t", t=128)
            O("vector", "tensor_copy", dst, src, waits=[(s["trA"], None)],
              inc=s["trAF"])
            trb += 1

    def _finish():
        with nc.Block() as block:
            @block.tensor
            def _(eng):
                for fn in P.cmds["tensor"]:
                    fn(eng)

            @block.vector
            def _(eng):
                for fn in P.cmds["vector"]:
                    fn(eng)

            @block.scalar
            def _(eng):
                for fn in P.cmds["scalar"]:
                    fn(eng)

            @block.gpsimd
            def _(eng):
                for fn in P.cmds["gpsimd"]:
                    fn(eng)

            @block.sync
            def _(eng):
                for fn in P.cmds["sync"]:
                    fn(eng)
        nc._keepalive_ctx = ctx
        return nc

    if PHASES <= 1:
        return _finish()

    P.fence()
    for half in range(2):
        O("tensor", "matmul", PA[2][:, half * 512:(half + 1) * 512],
          onesrow[:], rtok_sb[0:1, half * 512:(half + 1) * 512],
          start=True, stop=True, skip_group_check=True,
          waits=([(s["cst"], N_CST), (s["trAF"], NT * 8)] if half == 0 else []),
          inc=(s["rtrT"] if half == 1 else None))
    O("vector", "tensor_copy", rtok_bc[:], PA[2][:, 0:1024],
      waits=[(s["rtrT"], None)], inc=s["rtr"])
    pt2_free[0] = (s["rtr"], s["rtr"].n)
    pt2_free[1] = (s["rtr"], s["rtr"].n)
    for b in range(2):
        for j in range(NJ):
            src = xn_fm[:, j * NTOK + b * 512: j * NTOK + (b + 1) * 512]
            wts = [(s["rtr"], None)] if (b == 0 and j == 0) else []
            O("vector", "tensor_mul", kk[:], src,
              rtok_bc[:, b * 512:(b + 1) * 512], waits=wts)
            O("vector", "reduce_sum", rsum_sb[:, b * 32 + j: b * 32 + j + 1],
              kk[:], axis=AX.X,
              inc=(s["rtr"] if (b == 1 and j == NJ - 1) else None))
    O("gpsimd", "dma_start", out=r_in, in_=rsum_sb[:],
      waits=[(s["rtr"], None)], inc=s["rtrG"], inc_amt=16)
    P.coll("AllReduce", ALU.add, RG, [r_in.opt()], [r_out.opt()],
           waits=[(s["rtrG"], None)], inc=s["collR"])

    if PHASES <= 2:
        return _finish()

    P.fence()
    trk = 0
    for h in range(NHKV):
        for g4 in range(4):
            slot = g4 % 2
            ksrc = Wk[:, h * 128:(h + 1) * 128].rearrange(
                "(j p) i -> p j i", p=128)[:, g4 * 8:(g4 + 1) * 8, :]
            vsrc = Wv[:, h * 128:(h + 1) * 128].rearrange(
                "(j p) i -> p j i", p=128)[:, g4 * 8:(g4 + 1) * 8, :]
            kdst = WBUF[:, slot * 1024:(slot + 1) * 1024].rearrange(
                "p (j i) -> p j i", i=128)
            vdst = WBUF[:, 2048 + slot * 1024: 2048 + (slot + 1) * 1024].rearrange(
                "p (j i) -> p j i", i=128)
            gidx = h * 4 + g4
            wts = [(s["wkvF"], gidx - 1)] if gidx >= 2 else []
            O("gpsimd", "dma_start", out=kdst, in_=ksrc, waits=wts,
              inc=s[f"wkv{slot}"], inc_amt=16, future=True)
            O("gpsimd", "dma_start", out=vdst, in_=vsrc,
              inc=s[f"wkv{slot}"], inc_amt=16)
        for g4 in range(4):
            slot = g4 % 2
            for jj in range(8):
                j = g4 * 8 + jj
                wts = []
                if jj == 0:
                    wts.append((s[f"wkv{slot}"], ((h * 4 + g4) // 2 + 1) * 32))
                    if j == 0:
                        wts.append((s["trAF"], NT * 8))
                        if h > 0:
                            wts.append((s["kve"], None))
                for half in range(2):
                    O("tensor", "matmul", PA[0][:, half * 512:(half + 1) * 512],
                      WBUF[:, slot * 1024 + jj * 128: slot * 1024 + (jj + 1) * 128],
                      xn_fm[:, j * NTOK + half * 512: j * NTOK + (half + 1) * 512],
                      start=(j == 0), stop=(j == NJ - 1), skip_group_check=True,
                      waits=(wts if half == 0 else []))
                    wts = []
                    O("tensor", "matmul", PA[1][:, half * 512:(half + 1) * 512],
                      WBUF[:, 2048 + slot * 1024 + jj * 128:
                           2048 + slot * 1024 + (jj + 1) * 128],
                      xn_fm[:, j * NTOK + half * 512: j * NTOK + (half + 1) * 512],
                      start=(j == 0), stop=(j == NJ - 1), skip_group_check=True,
                      inc=(s["wkvF"] if (jj == 7 and half == 1) else None))
        wts = [(s["wkvF"], 4 * (h + 1))]
        if h >= 1:
            wts.append((s["kvs"], None))
        O("vector", "tensor_mul", rtmp1[0:64, :], PA[0][64:128, :],
          tab[0:64, 1024:2048], waits=wts)
        O("vector", "tensor_mul", rtmp1[64:128, :], PA[0][0:64, :],
          tab[64:128, 1024:2048])
        O("vector", "tensor_mul", rtmp2[:], PA[0][:, :], tab[:, 0:1024])
        O("vector", "tensor_add", K_sb[:], rtmp2[:], rtmp1[:], inc=s["kve"])
        O("vector", "tensor_copy", V_sb[:], PA[1][:, :], inc=s["kve"])
        for src_sb, dst_sb in ((K_sb, ktm), (V_sb, vtm)):
            for b4 in range(2):
                ps, off = (PB[2], 0) if trk % 2 == 0 else (PB[2], 1024)
                for q in range(4):
                    tb = b4 * 4 + q
                    wts = []
                    if q == 0:
                        wts = [(s["kve"], None)]
                        if trk >= 2:
                            wts.append((s["trKF"], trk - 1))
                        else:
                            wts += pt2_wait(trk % 2)
                    O("tensor", "transpose",
                      ps[:, off + q * 128: off + (q + 1) * 128],
                      src_sb[:, tb * 128:(tb + 1) * 128], identb[:], waits=wts,
                      inc=(s["trK"] if q == 3 else None))
                dst = dst_sb[:].rearrange("p (t i) -> p t i", i=128)[
                    :, b4 * 4:(b4 + 1) * 4, :]
                src = ps[:, off:off + 512].rearrange("p (t i) -> p t i", i=128)
                O("vector", "tensor_copy", dst, src, waits=[(s["trK"], None)],
                  inc=s["trKF"])
                trk += 1
        pt2_free[0] = (s["trKF"], s["trKF"].n)
        pt2_free[1] = (s["trKF"], s["trKF"].n)
        kdst = kv_in[h * NTOK:(h + 1) * NTOK, 0:128].rearrange(
            "(t p) c -> p t c", p=128)
        vdst = kv_in[h * NTOK:(h + 1) * NTOK, 128:256].rearrange(
            "(t p) c -> p t c", p=128)
        O("gpsimd", "dma_start", out=kdst,
          in_=ktm[:].rearrange("p (t i) -> p t i", i=128),
          waits=[(s["trKF"], None)], inc=s["kvs"], inc_amt=16)
        O("gpsimd", "dma_start", out=vdst,
          in_=vtm[:].rearrange("p (t i) -> p t i", i=128),
          inc=s["kvs"], inc_amt=16)
    P.coll("AllToAll", ALU.bypass, RG, [kv_in.opt()], [kv_out.opt()],
           waits=[(s["kvs"], None)], inc=s["collKV"])

    O("gpsimd", "dma_start", out=tab[:, 0:1024], in_=cosq_i,
      waits=[(s["kve"], None)], inc=s["cst"], inc_amt=16)
    O("gpsimd", "dma_start", out=tab[:, 1024:2048], in_=sinq_i,
      inc=s["cst"], inc_amt=16)
    TAB_Q = s["cst"].n

    if PHASES <= 3:
        return _finish()

    P.fence()
    for g in range(16):
        QPS = (PA[0], PA[1]) if g % 2 == 0 else (PA[2], PA[3])
        for g8 in range(8):
            slot = g8 % 4
            src = Wq[:, g * 256:(g + 1) * 256].rearrange(
                "(j p) i -> p j i", p=128)[:, g8 * 4:(g8 + 1) * 4, :]
            dst = WBUF[:, slot * 1024:(slot + 1) * 1024].rearrange(
                "p (j i) -> p j i", i=256)
            gidx = g * 8 + g8
            wts = []
            if gidx == 0:
                wts = [(s["wkvF"], 32)]
            elif gidx >= 4:
                wts = [(s["wqF"], gidx - 3)]
            O("gpsimd", "dma_start", out=dst, in_=src, waits=wts,
              inc=s[f"wq{slot}"], inc_amt=16, future=True)
        for g8 in range(8):
            slot = g8 % 4
            for jj in range(4):
                j = g8 * 4 + jj
                wts = []
                if jj == 0:
                    wts.append((s[f"wq{slot}"], ((g * 8 + g8) // 4 + 1) * 16))
                    if j == 0:
                        if g >= 2:
                            wts.append((s["qe"], 2 * (g - 1)))
                        if g == 0:
                            wts.append((s["kve"], None))
                            wts.append((s["cst"], TAB_Q))
                        if g == 1:
                            wts.append((s["trKF"], None))
                for hp in range(2):
                    for half in range(2):
                        O("tensor", "matmul",
                          QPS[hp][:, half * 512:(half + 1) * 512],
                          WBUF[:, slot * 1024 + jj * 256 + hp * 128:
                               slot * 1024 + jj * 256 + (hp + 1) * 128],
                          xn_fm[:, j * NTOK + half * 512:
                                j * NTOK + (half + 1) * 512],
                          start=(j == 0), stop=(j == NJ - 1),
                          skip_group_check=True, waits=wts,
                          inc=(s["wqF"] if (jj == 3 and hp == 1 and half == 1)
                               else None))
                        wts = []
        for hp in range(2):
            wts = [(s["wqF"], 8 * (g + 1))]
            if g > 0:
                wts.append((s[f"qs{hp}"], 16 * g))
            if g == 0 and hp == 0:
                wts.append((s["kvs"], None))
            O("vector", "tensor_mul", rtmp1[0:64, :], QPS[hp][64:128, :],
              tab[0:64, 1024:2048], waits=wts)
            O("vector", "tensor_mul", rtmp1[64:128, :], QPS[hp][0:64, :],
              tab[64:128, 1024:2048])
            O("vector", "tensor_mul", rtmp2[:], QPS[hp][:, :], tab[:, 0:1024])
            O("vector", "tensor_add", qstage[hp][:], rtmp2[:], rtmp1[:],
              inc=s["qe"])
            h = 2 * g + hp
            O("gpsimd", "dma_start", out=q_in[h * 128:(h + 1) * 128, :],
              in_=qstage[hp][:], waits=[(s["qe"], None)],
              inc=s[f"qs{hp}"], inc_amt=16)
    P.coll("AllToAll", ALU.bypass, RG, [q_in.opt()], [q_out.opt()],
           waits=[(s["qs0"], None), (s["qs1"], None)], inc=s["collQ"])

    if PHASES <= 4:
        return _finish()

    P.fence()
    O("gpsimd", "dma_start",
      out=At[:].rearrange("p (j i) -> p j i", i=128),
      in_=A2T.rearrange("(j p) i -> p j i", p=128),
      waits=[(s["qs0"], None), (s["qs1"], None)], inc=s["misc"], inc_amt=16)
    for j in range(NJ):
        for half in range(2):
            wts = []
            if j == 0 and half == 0:
                wts = [(s["misc"], None), (s["qe"], None)]
            O("tensor", "matmul", PA[0][:, half * 512:(half + 1) * 512],
              At[:, j * 128:(j + 1) * 128],
              xn_fm[:, j * NTOK + half * 512: j * NTOK + (half + 1) * 512],
              start=(j == 0), stop=(j == NJ - 1), skip_group_check=True,
              waits=wts, inc=(s["tfmT"] if (j == NJ - 1 and half == 1) else None))
    O("vector", "tensor_copy", tfm[:, 0:512], PA[0][:, 0:512],
      waits=[(s["tfmT"], None)])
    O("vector", "tensor_copy", tfm[:, 512:1024], PA[0][:, 512:1024], inc=s["tfm"])

    O("gpsimd", "dma_start", out=rsum_sb[:], in_=r_out,
      waits=[(s["collR"], 1)], inc=s["rtrG"], inc_amt=16)
    O("vector", "tensor_scalar_mul", rsum_sb[:], rsum_sb[:], 1.0 / NQTOK,
      waits=[(s["rtrG"], None)], inc=s["rtr"])
    rq = rsum_sb[:].rearrange("p (b j) -> p j b", b=2)
    for j in range(NJ):
        wts = ([(s["rtr"], None), (s["qe"], None)] + pt2_wait(0)) if j == 0 else []
        O("tensor", "matmul", PA[2][0:2, 0:8], rq[:, j, :],
          rwt_sb[:, j * 8:(j + 1) * 8], start=(j == 0), stop=(j == NJ - 1),
          skip_group_check=True, waits=wts,
          inc=(s["rtrT"] if j == NJ - 1 else None))
    O("vector", "tensor_add", logit_sb[:], PA[2][0:2, 0:8], rb2_sb[:],
      waits=[(s["rtrT"], None)], inc=s["rtr"])
    O("vector", "reduce_max", rmax[:], logit_sb[:], axis=AX.X, inc=s["rtr"])
    O("vector", "tensor_scalar_mul", rmax2[:], rmax[:], -1.0, inc=s["rtr"])
    O("scalar", "activation", rexp[:], logit_sb[:], AF.Exp, bias=rmax2[:, 0:1],
      waits=[(s["rtr"], None)], inc=s["rtrA2"])
    O("vector", "reduce_sum", rsumt[:], rexp[:], axis=AX.X,
      waits=[(s["rtrA2"], None)], inc=s["rtr"])
    O("vector", "reciprocal", rrec[:], rsumt[:], inc=s["rtr"])
    O("vector", "tensor_scalar", rsm[:], rexp[:], rrec[:, 0:1], None, ALU.mult,
      inc=s["rtr"])
    O("tensor", "transpose", PA[3][0:8, 0:2], rsm[:], identf[0:2, 0:2],
      waits=[(s["rtr"], None), (s["qe"], None)], inc=s["rtrT"])
    O("vector", "tensor_copy", routT[:], PA[3][0:8, 0:2],
      waits=[(s["rtrT"], None)], inc=s["rtr"])
    for b in range(2):
        O("tensor", "matmul", PA[2][:, 16 + b:17 + b], rep_f[:],
          routT[0:8, b:b + 1], start=True, stop=True, skip_group_check=True,
          waits=[(s["rtr"], None)], inc=s["rtrT"])
    O("vector", "tensor_copy", svec[:, 0:1], PA[2][:, 16:17],
      waits=[(s["rtrT"], None)], inc=s["rtr"])
    O("vector", "tensor_copy", svec[:, 1:2], PA[2][:, 17:18], inc=s["rtr"])
    pt2_free[0] = (s["rtr"], s["rtr"].n)
    pt2_free[1] = (s["qe"], s["qe"].n)
    pt2_free[2] = (s["tfm"], s["tfm"].n)
    pt2_free[3] = (s["tfm"], s["tfm"].n)
    for b in range(2):
        O("vector", "tensor_scalar", tsc[:, b * 512:(b + 1) * 512],
          tfm[:, b * 512:(b + 1) * 512], svec[:, b:b + 1], None, ALU.mult,
          inc=s["tfm"])
    if dbg:
        O("vector", "tensor_copy", rdbg[:, 0:8], logit_sb[:],
          waits=[(s["rtr"], None)])
        O("vector", "tensor_copy", rdbg[:, 8:9], rmax[:])
        O("vector", "tensor_copy", rdbg[:, 9:17], rexp[:])
        O("vector", "tensor_copy", rdbg[:, 17:18], rsumt[:])
        O("vector", "tensor_copy", rdbg[:, 18:19], rrec[:])
        O("vector", "tensor_copy", rdbg[:, 19:27], rsm[:], inc=s["rtr"])
        O("gpsimd", "dma_start", out=dbg_o["d_rout"], in_=rdbg[:, 19:27],
          waits=[(s["rtr"], None)], inc=s["dbg"], inc_amt=16)
        O("gpsimd", "dma_start", out=dbg_o["d_rdbg"], in_=rdbg[:],
          waits=[(s["rtr"], None)], inc=s["dbg"], inc_amt=16)

    if PHASES <= 5:
        return _finish()

    P.fence()
    SCSLOT = [(PA[2], 0), (PA[2], 512), (PA[0], 0), (PA[0], 512)]
    deferq = []
    sc_i = 0
    pt_i = 0
    pm_i = 0
    vt_n = 0
    n_pv = 0
    n_msk = 0
    pt_slot_free = {}
    pm_slot_free = {}
    vt_slot_free = {}
    first_vt = True
    for b in range(2):
        for p in range(8):
            wts = [(s["collKV"], 1)]
            if b == 1 and p == 0:
                wts.append((s["pvd"], None))
            O("sync", "dma_start", out=K_res[:, p * 512:(p + 1) * 512],
              in_=kv_out[1024 * p + 512 * b: 1024 * p + 512 * b + 512, 0:128],
              transpose=True, waits=wts, inc=s["kres"], inc_amt=16)
        for p in range(8):
            wts = [(s["kres"], (b * 8 + 8) * 16)]
            if b == 1 and p == 0:
                wts.append((s["aev"], None))
            if b == 0 and p == 0:
                wts.append((s["tfm"], None))
            O("vector", "tensor_mul", qq[:], K_res[:, p * 512:(p + 1) * 512],
              K_res[:, p * 512:(p + 1) * 512], waits=wts, inc=s["qn"])
            O("tensor", "matmul", PA[2][0:1, 0:512], onescol[:], qq[:],
              start=True, stop=True, skip_group_check=True,
              waits=[(s["qn"], None)] + pt2_wait(0), inc=s["nmT"])
            O("vector", "reduce_max", kn_sb[0:1, p:p + 1], PA[2][0:1, 0:512],
              axis=AX.X, waits=[(s["nmT"], None)], inc=s["scF"])
            pt2_free[0] = (s["scF"], s["scF"].n)
        while deferq:
            deferq.pop(0)()
        if b == 1:
            P.coll("AllToAll", ALU.bypass, RG, [ao_in[0].opt()],
                   [ao_out[0].opt()],
                   waits=[(s["aos0"], None), (s["aos1"], None)],
                   inc=s["collAO"])
            for j in range(NCORE):
                for hp in range(4):
                    h = 4 * j + hp
                    O("gpsimd", "dma_start",
                      out=attn_fm[:, h * NTOK: h * NTOK + 512],
                      in_=ao_out[0][512 * j + 128 * hp:
                                    512 * j + 128 * (hp + 1), :],
                      waits=([(s["collAO"], 1)] if (j == 0 and hp == 0) else []),
                      inc=s["afm"], inc_amt=16)
        for qp in range(8):
            O("vector", "reduce_max", kmaxr[0:1, qp:qp + 1],
              kn_sb[0:1, 0:qp + 1], axis=AX.X, inc=s["qn"])
        qn_slot = 0
        qn_mul_at = {}
        for qp in range(8):
            for hh in range(4):
                sl = qn_slot % 4
                wts = [(s["collQ"], 1)]
                if qn_slot >= 4:
                    wts.append((s["qn"], qn_mul_at[qn_slot - 4]))
                O("scalar", "dma_start", out=qsb[:, sl * 512:(sl + 1) * 512],
                  in_=q_out[512 * qp + 128 * hh: 512 * qp + 128 * (hh + 1),
                            512 * b: 512 * (b + 1)],
                  waits=wts, inc=s[f"qnl{sl}"], inc_amt=16)
                O("vector", "tensor_mul", qq[:],
                  qsb[:, sl * 512:(sl + 1) * 512],
                  qsb[:, sl * 512:(sl + 1) * 512],
                  waits=[(s[f"qnl{sl}"], s[f"qnl{sl}"].n)], inc=s["qn"])
                qn_mul_at[qn_slot] = s["qn"].n
                scs0 = qn_slot % 2
                O("tensor", "matmul", PA[2][0:1, scs0 * 512:scs0 * 512 + 512],
                  onescol[:], qq[:], start=True, stop=True,
                  skip_group_check=True,
                  waits=[(s["qn"], None)] + pt2_wait(scs0), inc=s["nmT"])
                O("vector", "reduce_max", qn2a[0:1, qp * 4 + hh: qp * 4 + hh + 1],
                  PA[2][0:1, scs0 * 512:scs0 * 512 + 512], axis=AX.X,
                  waits=[(s["nmT"], None)], inc=s["scF"])
                pt2_free[scs0] = (s["scF"], s["scF"].n)
                qn_slot += 1
        for qp in range(8):
            O("vector", "tensor_scalar", qn2a[0:1, qp * 4: qp * 4 + 4],
              qn2a[0:1, qp * 4: qp * 4 + 4], kmaxr[0:1, qp:qp + 1], None,
              ALU.mult, inc=s["qn"])
        O("scalar", "activation", mtmp2_row[:], qn2a[:], AF.Ln,
          waits=[(s["qn"], None)], inc=s["nmA"])
        O("vector", "tensor_scalar", mtmp2_row[:], mtmp2_row[:], 0.5, LN105,
          ALU.mult, ALU.add, waits=[(s["nmA"], None)], inc=s["qn"])
        O("scalar", "activation", mtmp2_row[:], mtmp2_row[:], AF.Exp,
          waits=[(s["qn"], None)], inc=s["nmA"])
        O("vector", "tensor_scalar_mul", negm_bf[:], mtmp2_row[:], -1.0,
          waits=[(s["nmA"], None)], inc=s["qn"])
        wts = [(s["qn"], None), (s["aev"], None)] + pt2_wait(0)
        if b == 0:
            wts.append((s["tfm"], None))
        O("tensor", "matmul", PA[2][:, 0:32], onesrow[:], negm_bf[:],
          start=True, stop=True, skip_group_check=True, waits=wts,
          inc=s["nmT"])
        O("vector", "tensor_copy", negm[:], PA[2][:, 0:32],
          waits=[(s["nmT"], None)], inc=s["nm"])
        pt2_free[0] = (s["nm"], s["nm"].n)
        for qp in range(8):
            for hh in range(4):
                wts = []
                if hh == 0:
                    wts = [(s["collQ"], 1)]
                    if qp > 0:
                        wts.append((s["pvd"], None))
                    else:
                        wts.append((s["qn"], None))
                O("scalar", "dma_start", out=qsb[:, hh * 512:(hh + 1) * 512],
                  in_=q_out[512 * qp + 128 * hh: 512 * qp + 128 * (hh + 1),
                            512 * b: 512 * (b + 1)],
                  waits=wts, inc=s["qld"], inc_amt=16)
            for hpair in range(2):
                pend = []

                def flush_one():
                    nonlocal pend, n_pv
                    if not pend:
                        return
                    pd = pend.pop(0)
                    O("tensor", "matmul",
                      PA[1][0:1, pd["h2"] * 512:(pd["h2"] + 1) * 512],
                      onescol[:], pd["puse"], start=pd["first"],
                      stop=pd["last"], skip_group_check=True,
                      waits=pd["lw"])
                    O("tensor", "matmul",
                      PA[3][:, pd["h2"] * 512:(pd["h2"] + 1) * 512],
                      pd["vtile"], pd["puse"], start=pd["first"],
                      stop=pd["last"], skip_group_check=True,
                      waits=[pd["vw"]], inc=s["pvd"])
                    n_pv += 1
                    if pd["tile_of_tk_done"] is not None:
                        vt_slot_free[pd["tile_of_tk_done"]] = (s["pvd"], n_pv)
                    if pd["ptidx"] is not None:
                        pt_slot_free[pd["ptidx"]] = (s["pvd"], n_pv)
                    if pd["pmidx"] is not None:
                        pm_slot_free[pd["pmidx"]] = (s["pvd"], n_pv)

                for tk in range(qp + 1):
                    vslot = vt_n % 2
                    wts = [(s["collKV"], 1)]
                    if vt_n >= 2:
                        wts.append(vt_slot_free[vt_n - 2])
                    if first_vt:
                        wts.append((s["tfm"], None))
                        first_vt = False
                    O("scalar", "dma_start",
                      out=vt[vslot][:].rearrange("p (t c) -> p t c", c=128),
                      in_=kv_out[1024 * tk + 512 * b: 1024 * tk + 512 * b + 512,
                                 128:256].rearrange("(t p) c -> p t c", p=128),
                      waits=wts, inc=s[f"vt{vslot}"], inc_amt=16)
                    for tb in range(4):
                        for h2 in range(2):
                            if len(pend) >= 2:
                                flush_one()
                            if deferq:
                                deferq.pop(0)()
                            hh = hpair * 2 + h2
                            first = (tk == 0 and tb == 0)
                            last = (tk == qp and tb == 3)
                            scs = sc_i % 4
                            sap, soff = SCSLOT[scs]
                            wts = list(pt2_wait(scs))
                            if first and tb == 0 and h2 == 0:
                                wts.append((s["qld"], s["qld"].n))
                                if hpair == 0:
                                    wts.append((s["nm"], None))
                                    wts.append((s["kres"], (b * 8 + 8) * 16))
                            O("tensor", "matmul",
                              sap[:, soff:soff + 512],
                              K_res[:, tk * 512 + tb * 128:
                                    tk * 512 + (tb + 1) * 128],
                              qsb[:, hh * 512:(hh + 1) * 512],
                              start=True, stop=True, skip_group_check=True,
                              waits=wts, inc=s["sc"])
                            ptidx = pt_i
                            pts = ptidx % 5
                            ew = [(s["sc"], None)]
                            if ptidx >= 5:
                                ew.append(pt_slot_free[ptidx - 5])
                            O("scalar", "activation", pt[pts][:],
                              sap[:, soff:soff + 512], AF.Exp,
                              bias=negm[:, qp * 4 + hh: qp * 4 + hh + 1],
                              waits=ew, inc=s["exp"])
                            pt2_free[scs] = (s["exp"], s["exp"].n)
                            sc_i += 1
                            pt_i += 1
                            pmidx = None
                            if tk == qp:
                                pms = pm_i % 2
                                mw = [(s["exp"], None)]
                                if pm_i >= 2:
                                    mw.append(pm_slot_free[pm_i - 2])
                                O("vector", "tensor_mul", pm[pms][:], pt[pts][:],
                                  masks[:, tb * 512:(tb + 1) * 512],
                                  waits=mw, inc=s["msk"])
                                n_msk += 1
                                pt_slot_free[ptidx] = (s["msk"], n_msk)
                                pmidx = pm_i
                                pm_i += 1
                                puse = pm[pms][:]
                                pwait = [(s["msk"], s["msk"].n)]
                                ptrec = None
                            else:
                                puse = pt[pts][:]
                                pwait = [(s["exp"], s["exp"].n)]
                                ptrec = ptidx
                            lw = list(pwait)
                            if first:
                                lw.append((s["aev"], s["aev"].n))
                                if b == 0 and qp == 0 and hpair == 0:
                                    lw.append((s["rtr"], s["rtr"].n))
                            cur = dict(h2=h2, puse=puse, first=first, last=last,
                                       lw=lw,
                                       vtile=vt[vslot][:, tb * 128:(tb + 1) * 128],
                                       vw=(s[f"vt{vslot}"], (vt_n // 2 + 1) * 16),
                                       ptidx=ptrec, pmidx=pmidx,
                                       tile_of_tk_done=(
                                           vt_n if (tb == 3 and h2 == 1) else None))
                            pend.append(cur)
                    vt_n += 1
                while pend:
                    flush_one()
                for h2 in range(2):
                    hh = hpair * 2 + h2
                    O("vector", "reciprocal",
                      qn_row[0:1, h2 * 512:(h2 + 1) * 512],
                      PA[1][0:1, h2 * 512:(h2 + 1) * 512],
                      waits=[(s["pvd"], None)], inc=s["aev"])
                    O("vector", "tensor_copy", linvb[:, h2 * 512:(h2 + 1) * 512],
                      PA[3][:, h2 * 512:(h2 + 1) * 512], inc=s["aev"])
                    aouse = (b * 8 + qp) * 2 + hpair

                    def d_bcast_mul(h2=h2, aev_n=s["aev"].n, aouse=aouse):
                        O("tensor", "matmul",
                          PA[0][:, h2 * 512:(h2 + 1) * 512],
                          onesrow_f[:], qn_row[0:1, h2 * 512:(h2 + 1) * 512],
                          start=True, stop=True, skip_group_check=True,
                          waits=[(s["aev"], aev_n)] + pt2_wait(2 + h2),
                          inc=s["nmT"])
                        O("vector", "tensor_mul", aosb[h2][:],
                          PA[0][:, h2 * 512:(h2 + 1) * 512],
                          linvb[:, h2 * 512:(h2 + 1) * 512],
                          waits=[(s["nmT"], None),
                                 (s[f"aos{h2}"], 16 * aouse)],
                          inc=s["aev"])
                        pt2_free[2 + h2] = (s["aev"], s["aev"].n)

                    def d_store(h2=h2, hh=hh, qp=qp, b=b):
                        O("gpsimd", "dma_start",
                          out=ao_in[b][512 * qp + 128 * hh:
                                       512 * qp + 128 * (hh + 1), :],
                          in_=aosb[h2][:], waits=[(s["aev"], None)],
                          inc=s[f"aos{h2}"], inc_amt=16)
                    deferq.extend([d_bcast_mul, d_store])
    while deferq:
        deferq.pop(0)()
    P.coll("AllToAll", ALU.bypass, RG, [ao_in[1].opt()], [ao_out[1].opt()],
           waits=[(s["aos0"], None), (s["aos1"], None)], inc=s["collAO1"])

    if PHASES <= 6:
        return _finish()

    for j in range(NCORE):
        for hp in range(4):
            h = 4 * j + hp
            O("sync", "dma_start",
              out=attn_fm[:, h * NTOK + 512: h * NTOK + 1024],
              in_=ao_out[1][512 * j + 128 * hp: 512 * j + 128 * (hp + 1), :],
              waits=([(s["collAO1"], 1)] if (j == 0 and hp == 0) else []),
              inc=s["afm1"], inc_amt=16)
    N_AFM = s["afm"].n
    N_AFM1 = s["afm1"].n
    P.fence()
    O("gpsimd", "dma_start", out=fbb[:], in_=fb,
      waits=[(s["pvd"], None), (s["exp"], None), (s["msk"], None),
             (s["aev"], None), (s["qld"], None), (s["aos0"], None),
             (s["aos1"], None)],
      inc=s["misc"], inc_amt=16)
    O("gpsimd", "dma_start", out=bcat[:], in_=B2T,
      waits=[(s["pvd"], None), (s["aev"], None), (s["msk"], None),
             (s["qn"], None), (s["afm"], N_AFM), (s["afm1"], N_AFM1)],
      inc=s["misc"], inc_amt=16)
    N_MISC = s["misc"].n

    if AFM_ONLY:
        return _finish()
    OUTB = [(PA[i // 2], (i % 2) * 512) for i in range(8)]
    for o in range(OLIM):
        for j in range(NJ):
            gidx = o * NJ + j
            O("gpsimd", "dma_start", out=wo_t[j % 2][:],
              in_=Wo[j * 128:(j + 1) * 128, o * 512:(o + 1) * 512],
              waits=([(s["woF"], gidx - 1)] if gidx >= 2 else []),
              inc=s[f"wo{j % 2}"], inc_amt=16, future=True)
        if FINMODE >= 2:
            for j in range(NJ):
                gidx = o * NJ + j
                O("gpsimd", "dma_start", out=wf_t[j % 2][:],
                  in_=WfT[j * 128:(j + 1) * 128, o * 512:(o + 1) * 512],
                  waits=([(s["wfF"], gidx - 1)] if gidx >= 2 else []),
                  inc=s[f"wf{j % 2}"], inc_amt=16, future=True)
        if FINMODE >= 3:
            for tt in range(NT):
                nxr = o * NT + tt
                O("scalar", "dma_start", out=xres[tt % 2][:],
                  in_=x_my[tt * 128:(tt + 1) * 128, o * 512:(o + 1) * 512],
                  waits=([(s["ouV"], nxr - 1)] if nxr >= 2 else []),
                  inc=s[f"xr{tt % 2}"], inc_amt=16, future=True)
        for j in range(NJ):
            for tt in range(NT):
                pa, po = OUTB[tt]
                wts = []
                if tt == 0:
                    wts.append((s[f"wo{j % 2}"], ((o * NJ + j) // 2 + 1) * 16))
                    if j == 0:
                        wts.append((s["afm"], N_AFM))
                        wts.append((s["afm1"], N_AFM1))
                        if o == 0:
                            wts.append((s["nmT"], None))
                            wts.append((s["aev"], None))
                            wts.append((s["exp"], None))
                        elif FINMODE >= 3:
                            wts.append((s["ouV"], o * 8))
                O("tensor", "matmul", pa[:, po:po + 512],
                  attn_fm[:, j * NTOK + tt * 128: j * NTOK + (tt + 1) * 128],
                  wo_t[j % 2][:], start=(j == 0), stop=False,
                  skip_group_check=True, waits=wts,
                  inc=(s["woF"] if tt == NT - 1 else None))
        if FINMODE >= 2:
            for j in range(NJ):
                for tt in range(NT):
                    pa, po = OUTB[tt]
                    wts = []
                    if tt == 0:
                        wts.append((s[f"wf{j % 2}"], ((o * NJ + j) // 2 + 1) * 16))
                    O("tensor", "matmul", pa[:, po:po + 512],
                      xn_fm[:, j * NTOK + tt * 128: j * NTOK + (tt + 1) * 128],
                      wf_t[j % 2][:], start=False, stop=False,
                      skip_group_check=True, waits=wts,
                      inc=(s["wfF"] if tt == NT - 1 else None))
        if FINMODE >= 3:
            for tt in range(NT):
                pa, po = OUTB[tt]
                wts = [(s["misc"], N_MISC), (s["tfm"], None)] if tt == 0 else []
                O("tensor", "matmul", pa[:, po:po + 512],
                  tsc[:, tt * 128:(tt + 1) * 128], bcat[:, o * 512:(o + 1) * 512],
                  start=False, stop=False, skip_group_check=True, waits=wts)
                O("tensor", "matmul", pa[:, po:po + 512],
                  onesrow[:], fbb[0:1, o * 512:(o + 1) * 512],
                  start=False, stop=True, skip_group_check=True, inc=s["ou"])
            for tt in range(NT):
                pa, po = OUTB[tt]
                nxr2 = o * NT + tt
                O("vector", "tensor_add", outsb[tt % 2][:], pa[:, po:po + 512],
                  xres[tt % 2][:],
                  waits=[(s["ou"], o * NT + tt + 1),
                         (s[f"xr{tt % 2}"], (nxr2 // 2 + 1) * 16),
                         (s[f"ouF{tt % 2}"], (nxr2 // 2) * 16)],
                  inc=s["ouV"])
                O("sync", "dma_start",
                  out=out_my[tt * 128:(tt + 1) * 128, o * 512:(o + 1) * 512],
                  in_=outsb[tt % 2][:], waits=[(s["ouV"], None)],
                  inc=s[f"ouF{tt % 2}"], inc_amt=16)
    P.cmds["sync"].append(lambda e, h=s["ouF0"].h, v=s["ouF0"].n: e.wait_ge(h, v))
    P.cmds["sync"].append(lambda e, h=s["ouF1"].h, v=s["ouF1"].n: e.wait_ge(h, v))

    if dbg:
        for j in range(NJ):
            O("vector", "tensor_copy", linvb[:, 0:1024].bitcast(BF16)[:, 0:1024]
              if False else linvb[:, 0:512],
              xn_fm[:, j * NTOK: j * NTOK + 512], waits=[(s["ouF0"], None), (s["ouF1"], None)])
            O("vector", "tensor_copy", linvb[:, 512:1024],
              xn_fm[:, j * NTOK + 512: (j + 1) * NTOK], inc=s["dbgV"])
            O("gpsimd", "dma_start",
              out=dbg_o["d_xnfm"][:, j * NTOK:(j + 1) * NTOK],
              in_=linvb[:], waits=[(s["dbgV"], None)], inc=s["dbg"], inc_amt=16)
            O("vector", "tensor_copy", linvb[:, 0:512],
              attn_fm[:, j * NTOK: j * NTOK + 512], waits=[(s["dbg"], None)])
            O("vector", "tensor_copy", linvb[:, 512:1024],
              attn_fm[:, j * NTOK + 512: (j + 1) * NTOK], inc=s["dbgV"])
            O("gpsimd", "dma_start",
              out=dbg_o["d_attn"][:, j * NTOK:(j + 1) * NTOK],
              in_=linvb[:], waits=[(s["dbgV"], None)], inc=s["dbg"], inc_amt=16)
        O("gpsimd", "dma_start", out=dbg_o["d_q"], in_=q_out,
          waits=[(s["collQ"], 1)], inc=s["dbg"], inc_amt=16)
        O("gpsimd", "dma_start", out=dbg_o["d_kv"], in_=kv_out,
          waits=[(s["collKV"], 1)], inc=s["dbg"], inc_amt=16)
        for bb in range(2):
            O("gpsimd", "dma_start",
              out=dbg_o["d_ao"][bb * 4096:(bb + 1) * 4096, :],
              in_=ao_out[bb],
              waits=[(s["collAO" if bb == 0 else "collAO1"], 1)],
              inc=s["dbg"], inc_amt=16)
        P.cmds["gpsimd"].append(
            lambda e, h=s["dbg"].h, v=s["dbg"].n: e.wait_ge(h, v))

    return _finish()



def _consts():
    c = {}
    c["identb"] = np.eye(128, dtype=ml_dtypes.bfloat16)
    c["identf"] = np.eye(128, dtype=np.float32)
    c["onescol"] = np.ones((128, 1), dtype=ml_dtypes.bfloat16)
    c["onesrow"] = np.ones((1, 128), dtype=ml_dtypes.bfloat16)
    c["onesrow_f"] = np.ones((1, 128), dtype=np.float32)
    rp = np.zeros((8, 128), dtype=np.float32)
    for e in range(8):
        rp[e, e * 16:(e + 1) * 16] = 2.0
    c["rep_f"] = rp
    m = np.zeros((512, 512), dtype=np.float32)
    i = np.arange(128)[:, None]
    jj = np.arange(512)[None, :]
    for blk in range(4):
        m[blk * 128:(blk + 1) * 128] = (blk * 128 + i <= jj)
    c["masks"] = m.astype(ml_dtypes.bfloat16)
    return c


def _rope_tables(core):
    pos = np.arange(512 * core, 512 * core + 512, dtype=np.float64)
    inv = 1.0 / (THETA ** (np.arange(0, HD, 2, dtype=np.float64) / HD))
    ang = pos[None, :] * inv[:, None]
    cos = np.cos(ang); sin = np.sin(ang)
    cosf = np.concatenate([cos, cos], 0)
    sinf = np.concatenate([-sin, sin], 0)
    cosf = np.concatenate([cosf, cosf], 1)
    sinf = np.concatenate([sinf, sinf], 1)
    sc = 1.0 / np.sqrt(HD)
    return (np.asarray(cosf * sc, np.float32), np.asarray(sinf * sc, np.float32),
            np.asarray(cosf, np.float32), np.asarray(sinf, np.float32))


def kernel(L0_attn_x, L0_attn_Wq, L0_attn_Wk, L0_attn_Wv, L0_attn_Wo,
           L0_ffn_W, L0_ffn_b, L0_ffn_A, L0_ffn_B, L0_ffn_router_W,
           L0_ffn_router_b):
    x = np.ascontiguousarray(np.asarray(L0_attn_x, np.float32))
    Wq_ = np.ascontiguousarray(np.asarray(L0_attn_Wq, np.float32))
    Wk_ = np.ascontiguousarray(np.asarray(L0_attn_Wk, np.float32))
    Wv_ = np.ascontiguousarray(np.asarray(L0_attn_Wv, np.float32))
    Wo_ = np.ascontiguousarray(np.asarray(L0_attn_Wo, np.float32))
    WfT_ = np.ascontiguousarray(np.asarray(L0_ffn_W, np.float32).T)
    fb_ = np.ascontiguousarray(np.asarray(L0_ffn_b, np.float32)).reshape(1, D)
    A2T_ = np.ascontiguousarray(np.asarray(L0_ffn_A, np.float32).reshape(128, D).T)
    B2T_ = np.ascontiguousarray(
        np.asarray(L0_ffn_B, np.float32).transpose(0, 2, 1).reshape(128, 4096))
    rW_ = np.ascontiguousarray(np.asarray(L0_ffn_router_W, np.float32))
    rb_ = np.ascontiguousarray(np.asarray(L0_ffn_router_b, np.float32))

    rwt_ = np.ascontiguousarray(
        rW_.T.reshape(32, 128, 8).transpose(1, 0, 2).reshape(128, 256))
    rb2_ = np.ascontiguousarray(np.tile(rb_.reshape(1, 8), (2, 1)))

    consts = _consts()
    shared = dict(Wq=Wq_, Wk=Wk_, Wv=Wv_, Wo=Wo_, WfT=WfT_, fb=fb_, A2T=A2T_,
                  B2T=B2T_, rwt=rwt_, rb2=rb2_, **consts)

    in_maps = []
    for c in range(NCORE):
        cq, sq, ck, sk = _rope_tables(c)
        x_myc = np.ascontiguousarray(
            np.concatenate([x[0, 512 * c:512 * (c + 1), :],
                            x[1, 512 * c:512 * (c + 1), :]], axis=0))
        rt = np.zeros((1, NTOK), np.float32)
        lo = max(Q_LO, 512 * c) - 512 * c
        hi = min(Q_HI, 512 * c + 512) - 512 * c
        if hi > lo:
            rt[0, lo:hi] = 1.0
            rt[0, 512 + lo:512 + hi] = 1.0
        m = dict(shared)
        m.update(x_my=x_myc, cosq=cq, sinq=sq, cosk=ck, sink=sk,
                 rtok=rt.astype(ml_dtypes.bfloat16))
        in_maps.append(m)

    nc = build_program(dbg=DBG)
    res = run_bass_kernel_spmd(nc, in_maps, core_ids=list(range(NCORE)))

    out = np.empty((2, SEQ, D), dtype=np.float32)
    for c in range(NCORE):
        o = res.results[c]["out_my"]
        out[0, 512 * c:512 * (c + 1), :] = o[0:512, :]
        out[1, 512 * c:512 * (c + 1), :] = o[512:1024, :]
    if DBG:
        kernel._dbg = res
    return out


# version 09
# speedup vs baseline: 1.7473x; 1.1156x over previous
import numpy as np
import ml_dtypes
from contextlib import ExitStack

import concourse.bass as bass
import concourse.mybir as mybir
from concourse.bass_utils import run_bass_kernel_spmd
from concourse.aot_env import allow_ici_all_to_all

F32 = mybir.dt.float32
BF16 = mybir.dt.bfloat16
AF = mybir.ActivationFunctionType
ALU = mybir.AluOpType
AX = mybir.AxisListType

NCORE = 8
D = 4096
SEQ = 4096
HD = 128
CH = 512
NTOK = 1024
NT = 8
NJ = 32
NHKV = 8
Q_LO, Q_HI = 611, 4095
NQTOK = Q_HI - Q_LO
EPS = 1e-5
LN105 = float(np.log(1.05))
THETA = 10000.0

DBG = False
PHASES = 7
OLIM = 8
AFM_ONLY = False
FINMODE = 3


class Sem:
    __slots__ = ("h", "n", "name", "vmap")

    def __init__(self, h, name):
        self.h, self.n, self.name = h, 0, name
        self.vmap = {}


class Prog:

    SELF_ENGS = ("vector", "scalar")

    def __init__(self, nc, ctx):
        self.nc, self.ctx = nc, ctx
        self.cmds = {e: [] for e in ("tensor", "vector", "scalar", "gpsimd", "sync")}
        self.selfsem = {e: Sem(ctx.enter_context(nc.semaphore(f"self_{e}")),
                               f"self_{e}") for e in self.SELF_ENGS}
        self.deps = {e: {"w": {}, "r": {}} for e in self.SELF_ENGS}
        self.fence_val = {e: 0 for e in self.SELF_ENGS}
        self.owner = {}

    def sem(self, name):
        return Sem(self.ctx.enter_context(self.nc.semaphore(name)), name)

    @staticmethod
    def _ap_name(a):
        t = getattr(a, "tensor", None)
        if t is not None and hasattr(t, "name"):
            return t.name
        if hasattr(a, "name") and hasattr(a, "ap"):
            return a.name
        return None

    def _rw_sets(self, args, kwargs):
        rset, wset = set(), set()
        for i, a in enumerate(args):
            nm = self._ap_name(a)
            if nm is None:
                continue
            (wset if i == 0 else rset).add(nm)
        for k, a in kwargs.items():
            nm = self._ap_name(a)
            if nm is None:
                continue
            if k in ("out", "accum_out"):
                wset.add(nm)
            else:
                rset.add(nm)
        return rset, wset

    def _resolve(self, sm, vv):
        eng = self.owner.get(sm.name)
        if eng is None:
            return (sm.h, vv)
        assert vv in sm.vmap, f"no vmap[{vv}] for {sm.name} (owner {eng})"
        return (self.selfsem[eng].h, sm.vmap[vv])

    def op(self, eng, meth, *args, waits=(), inc=None, inc_amt=1, future=False,
           **kwargs):
        wl = []
        for (sm, v) in waits:
            vv = sm.n if v is None else v
            if not future:
                assert vv <= sm.n, f"wait {sm.name}>={vv} but emitted {sm.n}"
            if vv > 0:
                wl.append((sm, vv))
        is_dma = meth in ("dma_start", "dma_start_transpose")
        ih = None
        if eng in self.SELF_ENGS:
            ss = self.selfsem[eng]
            if not is_dma:
                rset, wset = self._rw_sets(args, kwargs)
                dep = self.deps[eng]
                need = self.fence_val[eng]
                for t in rset | wset:
                    need = max(need, dep["w"].get(t, 0))
                for t in wset:
                    need = max(need, dep["r"].get(t, 0))
                if need > 0:
                    wl.insert(0, (ss, need))
                ss.n += 1
                for t in wset:
                    dep["w"][t] = ss.n
                for t in rset:
                    dep["r"][t] = max(dep["r"].get(t, 0), ss.n)
                ih = (ss.h, 1)
                if inc is not None:
                    assert inc_amt == 1
                    inc.n += 1
                    o = self.owner.setdefault(inc.name, eng)
                    assert o == eng, f"{inc.name} inc'd from {o} and {eng}"
                    inc.vmap[inc.n] = ss.n
            else:
                if inc is not None:
                    inc.n += (inc_amt if inc_amt is not None else 1)
                    ih = (inc.h, inc_amt)
        else:
            if inc is not None:
                assert inc.name not in self.owner, \
                    f"{inc.name} owned by {self.owner.get(inc.name)}"
                inc.n += (inc_amt if inc_amt is not None else 1)
                ih = (inc.h, inc_amt)

        def run(e, meth=meth, args=args, kwargs=kwargs, wl=wl, ih=ih):
            for (sm, v) in wl:
                h, v = self._resolve(sm, v)
                e.wait_ge(h, v)
            ins = getattr(e, meth)(*args, **kwargs)
            if ih is not None:
                if ih[1] is None or ih[1] == 1:
                    ins.then_inc(ih[0], 1)
                else:
                    ins.then_inc(ih[0], ih[1])
        self.cmds[eng].append(run)

    def coll(self, kind, op_, rg, ins_, outs_, waits=(), inc=None):
        wl = []
        for (sm, v) in waits:
            vv = sm.n if v is None else v
            if vv > 0:
                wl.append((sm, vv))
        inc.n += 1

        def run(e, wl=wl, ih=inc.h):
            for (sm, v) in wl:
                h, v = self._resolve(sm, v)
                e.wait_ge(h, v)
            with allow_ici_all_to_all():
                e.collective_compute(kind, op_, rg, ins_, outs_).then_inc(ih)
        self.cmds["gpsimd"].append(run)

    def fence(self):
        for e in self.SELF_ENGS:
            self.fence_val[e] = self.selfsem[e].n

    def bump(self, eng, sem, amt=1):
        raise RuntimeError("standalone bumps are forbidden (writeback race)")


def build_program(dbg=False):
    nc = bass.Bass("TRN2", target_bir_lowering=False)
    ctx = ExitStack()

    def din(name, shape, dt=F32):
        return nc.dram_tensor(name, shape, dt, kind="ExternalInput").ap()

    x_my = din("x_my", [NTOK, D])
    Wq = din("Wq", [D, D]); Wk = din("Wk", [D, 1024]); Wv = din("Wv", [D, 1024])
    Wo = din("Wo", [D, D]); WfT = din("WfT", [D, D])
    fb = din("fb", [1, D])
    A2T = din("A2T", [D, 128]); B2T = din("B2T", [128, D])
    rwt = din("rwt", [128, 256]); rb2 = din("rb2", [2, 8])
    cosq_i = din("cosq", [128, NTOK]); sinq_i = din("sinq", [128, NTOK])
    cosk_i = din("cosk", [128, NTOK]); sink_i = din("sink", [128, NTOK])
    masks_i = din("masks", [512, 512], BF16)
    identb_i = din("identb", [128, 128], BF16)
    identf_i = din("identf", [128, 128], F32)
    onescol_i = din("onescol", [128, 1], BF16)
    onesrow_i = din("onesrow", [1, 128], BF16)
    onesrow_f_i = din("onesrow_f", [1, 128], F32)
    rep_i = din("rep_f", [8, 128], F32)
    rtok_i = din("rtok", [1, NTOK], BF16)

    out_my = nc.dram_tensor("out_my", [NTOK, D], F32, kind="ExternalOutput").ap()

    kv_in = nc.dram_tensor("kv_in", [NCORE * NTOK, 256], BF16, kind="Internal").ap()
    kv_out = nc.dram_tensor("kv_out", [NCORE * NTOK, 256], BF16, kind="Internal").ap()
    q_in = nc.dram_tensor("q_in", [D, NTOK], BF16, kind="Internal").ap()
    q_out = nc.dram_tensor("q_out", [D, NTOK], BF16, kind="Internal").ap()
    ao_in = [nc.dram_tensor(f"ao_in{i}", [NCORE * 512, 512], BF16,
                            kind="Internal").ap() for i in range(2)]
    ao_out = [nc.dram_tensor(f"ao_out{i}", [NCORE * 512, 512], BF16,
                             kind="Internal").ap() for i in range(2)]
    r_in = nc.dram_tensor("r_in", [128, 64], F32, kind="Internal").ap()
    r_out = nc.dram_tensor("r_out", [128, 64], F32, kind="Internal").ap()

    dbg_o = {}
    if dbg:
        for nm_, shp, dt in (("d_xnfm", [128, NJ * NTOK], F32),
                             ("d_q", [D, NTOK], F32),
                             ("d_kv", [NCORE * NTOK, 256], F32),
                             ("d_ao", [NCORE * NTOK, 512], F32),
                             ("d_rout", [2, 8], F32),
                             ("d_rdbg", [2, 32], F32),
                             ("d_attn", [128, NJ * NTOK], F32)):
            dbg_o[nm_] = nc.dram_tensor(nm_, shp, dt, kind="ExternalOutput").ap()

    def sb(name, shape, dt):
        return nc.alloc_sbuf_tensor(name, shape, dt)

    xn_fm = sb("xn_fm", [128, NJ * NTOK], BF16)
    attn_fm = sb("attn_fm", [128, NJ * NTOK], BF16)
    tab = sb("tab", [128, 2048], F32)
    masks = sb("masks_s", [128, 2048], BF16)
    identb = sb("identb_s", [128, 128], BF16)
    identf = sb("identf_s", [128, 128], F32)
    onescol = sb("onescol_s", [128, 1], BF16)
    onesrow = sb("onesrow_s", [1, 128], BF16)
    onesrow_f = sb("onesrow_f_s", [1, 128], F32)
    rep_f = sb("rep_f_s", [8, 128], F32)
    rwt_sb = sb("rwt_sb", [128, 256], F32)
    rb2_sb = sb("rb2_sb", [2, 8], F32)
    tfm = sb("tfm", [128, NTOK], BF16)
    tsc = sb("tsc", [128, NTOK], BF16)
    kn_sb = sb("kn_sb", [1, 8], F32)
    qq = sb("qq", [128, 512], BF16)
    rsum_sb = sb("rsum_sb", [128, 64], F32)
    rtok_sb = sb("rtok_sb", [1, NTOK], BF16)
    qn2_sb = sb("qn2_sb", [1, 4], F32)
    mtmp = sb("mtmp", [1, 4], F32)
    mtmp_bf = sb("mtmp_bf", [1, 2], BF16)
    mtmp2_row = sb("mtmp2_row", [1, 32], F32)
    K_res = sb("K_res", [128, 4096], BF16)
    lsum = sb("lsum", [128, 1], F32); lsq = sb("lsq", [128, 1], F32)
    lmean = sb("lmean", [128, 1], F32); lvar = sb("lvar", [128, 1], F32)
    lrstd = sb("lrstd", [128, 1], F32); lnegmr = sb("lnegmr", [128, 1], F32)
    ltmp = sb("ltmp", [128, 1], F32)
    logit_sb = sb("logit_sb", [2, 8], F32); rmax = sb("rmax", [2, 1], F32)
    rexp = sb("rexp", [2, 8], F32); rsumt = sb("rsumt", [2, 1], F32)
    rrec = sb("rrec", [2, 1], F32); rsm = sb("rsm", [2, 8], F32)
    rmax2 = sb("rmax2", [2, 1], F32)
    rdbg = sb("rdbg", [2, 32], F32)
    routT = sb("routT", [8, 2], F32)
    svec = sb("svec", [128, 2], F32)
    wo_t = [sb(f"wo_t{i}", [128, 512], BF16) for i in range(4)]
    wf_t = [sb(f"wf_t{i}", [128, 512], BF16) for i in range(4)]
    xres = [sb(f"xres{i}", [128, 512], F32) for i in range(2)]
    outsb = [sb(f"outsb{i}", [128, 512], F32) for i in range(2)]
    WBUF = sb("WBUF", [128, 4096], BF16)

    ARENA = sb("ARENA", [128, 6144], F32)
    ab = nc.lookup_mloc(ARENA).addr

    def at_(name, shape, dt, off):
        nb = shape[1] * (4 if dt == F32 else 2)
        assert off + nb <= 24576, name
        return nc.alloc_sbuf_tensor_at(name, shape, dt, offset=ab + off)

    XBUF = at_("XBUF", [128, 4096], F32, 0)
    xn_tm = at_("xn_tm", [128, 4096], BF16, 16384)
    rtok_bc = at_("rtok_bc", [128, 1024], F32, 0)
    kk = at_("kk", [128, 512], F32, 4096)
    K_sb = at_("K_sb", [128, 1024], BF16, 0)
    V_sb = at_("V_sb", [128, 1024], BF16, 2048)
    ktm = at_("ktm", [128, 1024], BF16, 4096)
    vtm = at_("vtm", [128, 1024], BF16, 6144)
    rtmp1 = at_("rtmp1", [128, 1024], F32, 8192)
    rtmp2 = at_("rtmp2", [128, 1024], F32, 12288)
    qstage = [at_(f"qstage{i}", [128, 1024], BF16, 2048 * i) for i in range(2)]
    At = at_("At", [128, 4096], BF16, 16384)
    qsb = at_("qsb", [128, 2048], BF16, 0)
    vt = [at_(f"vt{i}", [128, 512], BF16, 4096 + 1024 * i) for i in range(2)]
    pt = [at_(f"pt{i}", [128, 512], BF16, off)
          for i, off in enumerate((6144, 7168, 8192, 15360, 22560))]
    pm = [at_(f"pm{i}", [128, 512], BF16, 9216 + 1024 * i) for i in range(2)]
    linvb = at_("linvb", [128, 1024], F32, 11264)
    qn_row = at_("qn_row", [1, 1024], F32, 16384)
    negm = at_("negm", [128, 32], F32, 23584)
    qn2a = sb("qn2a", [1, 32], F32)
    kmaxr = sb("kmaxr", [1, 8], F32)
    negm_bf = sb("negm_bf", [1, 32], BF16)
    aosb = [at_(f"aosb{i}", [128, 512], BF16, 20512 + 1024 * i) for i in range(2)]
    fbb = at_("fbb", [1, 4096], BF16, 0)
    bcat = at_("bcat", [128, 4096], BF16, 8192)

    PT = [ctx.enter_context(nc.psum_tensor(f"PT{i}", [128, 1024], F32)) for i in range(4)]
    PA = [p.ap() for p in PT]
    PB = [p.ap().bitcast(BF16) for p in PT]

    P = Prog(nc, ctx)
    O, B_ = P.op, P.bump
    RG = [list(range(NCORE))]

    s = {n: P.sem(n) for n in (
        "wkv0", "wkv1", "wq0", "wq1", "wq2", "wq3", "wo0", "wo1", "wf0", "wf1",
        "qnl0", "qnl1", "qnl2", "qnl3", "wo2", "wo3", "wf2", "wf3",
        "xr0", "xr1", "vt0", "vt1", "qs0", "qs1", "aos0", "aos1", "ouF0", "ouF1",
        "collR", "collKV", "collQ", "collAO", "collAO1", "afm1",
        "cst", "conv", "x", "xfree", "lnv", "lna", "xn", "xnfree",
        "trA", "trAF", "phA", "rtr", "rtrG", "rtrA2", "rtrT",
        "wkv", "wkvF", "kvt", "kve", "kvs",
        "wq", "wqF", "qp", "qe", "qs",
        "coll", "misc", "trK", "trKF", "tfm", "tfmT",
        "kres", "vt", "vtF", "qld", "qn", "nm", "nmA", "nmT",
        "sc", "scF", "exp", "ptF", "msk", "pmF", "lpv", "pvd", "aev", "aos",
        "afm", "wo", "woF", "wf", "wfF", "xr", "xrF", "ou", "ouV", "ouF",
        "dbg", "dbgV")}

    pt2_free = {0: None, 1: None, 2: None, 3: None}
    SCSLOT = None

    def pt2_wait(bank):
        f = pt2_free[bank]
        return [f] if f is not None else []

    for dst, src in ((identb[:], identb_i), (identf[:], identf_i),
                     (onescol[:], onescol_i), (onesrow[:], onesrow_i),
                     (onesrow_f[:], onesrow_f_i), (rep_f[:], rep_i),
                     (rwt_sb[:], rwt), (rb2_sb[:], rb2), (rtok_sb[:], rtok_i)):
        O("gpsimd", "dma_start", out=dst, in_=src, inc=s["cst"], inc_amt=16)
    O("gpsimd", "dma_start",
      out=masks[:].rearrange("p (m f) -> p m f", f=512),
      in_=masks_i.rearrange("(m p) f -> p m f", p=128), inc=s["cst"], inc_amt=16)
    O("gpsimd", "dma_start", out=tab[:, 0:1024], in_=cosk_i, inc=s["cst"], inc_amt=16)
    O("gpsimd", "dma_start", out=tab[:, 1024:2048], in_=sink_i, inc=s["cst"], inc_amt=16)
    N_CST = s["cst"].n

    TRSLOT = [(PB[0], 0), (PB[0], 1024), (PB[1], 0), (PB[1], 1024)]
    trb = 0
    for tt in range(NT):
        O("sync", "dma_start", out=XBUF[:], in_=x_my[tt * 128:(tt + 1) * 128, :],
          waits=[(s["xn"], None)], inc=s["x"], inc_amt=16)
        O("vector", "reduce_sum", lsum[:], XBUF[:], axis=AX.X,
          waits=[(s["x"], None)], inc=s["lnv"])
        O("scalar", "activation", xn_tm[:], XBUF[:], AF.Square, accum_out=lsq[:],
          waits=[(s["x"], None)] + ([(s["trAF"], 8 * tt)] if tt else []),
          inc=s["lna"])
        O("vector", "tensor_scalar_mul", lmean[:], lsum[:], 1.0 / D, inc=s["lnv"])
        O("vector", "tensor_scalar_mul", lvar[:], lsq[:], 1.0 / D,
          waits=[(s["lna"], None)], inc=s["lnv"])
        O("vector", "tensor_mul", ltmp[:], lmean[:], lmean[:], inc=s["lnv"])
        O("vector", "tensor_sub", lvar[:], lvar[:], ltmp[:], inc=s["lnv"])
        O("vector", "tensor_scalar_add", lvar[:], lvar[:], EPS, inc=s["lnv"])
        O("scalar", "activation", ltmp[:], lvar[:], AF.Sqrt,
          waits=[(s["lnv"], None)], inc=s["lna"])
        O("vector", "reciprocal", lrstd[:], ltmp[:], waits=[(s["lna"], None)],
          inc=s["lnv"])
        O("vector", "tensor_scalar", lnegmr[:], lmean[:], lrstd[:, 0:1], -1.0,
          ALU.mult, ALU.mult, inc=s["lnv"])
        O("scalar", "activation", xn_tm[:], XBUF[:], AF.Identity,
          scale=lrstd[:, 0:1], bias=lnegmr[:, 0:1],
          waits=[(s["lnv"], None)], inc=s["xn"])
        for b8 in range(8):
            ps, off = TRSLOT[trb % 4]
            for q in range(4):
                j = b8 * 4 + q
                wts = []
                if q == 0:
                    wts = [(s["xn"], None)]
                    if trb >= 4:
                        wts.append((s["trAF"], trb - 3))
                    if trb == 0:
                        wts.append((s["cst"], N_CST))
                O("tensor", "transpose", ps[:, off + q * 128: off + (q + 1) * 128],
                  xn_tm[:, j * 128:(j + 1) * 128], identb[:], waits=wts,
                  inc=(s["trA"] if q == 3 else None))
            dst = xn_fm[:].rearrange("p (j t) -> p j t", t=NTOK)[
                :, b8 * 4:(b8 + 1) * 4, tt * 128:(tt + 1) * 128]
            src = ps[:, off:off + 512].rearrange("p (j t) -> p j t", t=128)
            O("vector", "tensor_copy", dst, src, waits=[(s["trA"], None)],
              inc=s["trAF"])
            trb += 1

    def _finish():
        with nc.Block() as block:
            @block.tensor
            def _(eng):
                for fn in P.cmds["tensor"]:
                    fn(eng)

            @block.vector
            def _(eng):
                for fn in P.cmds["vector"]:
                    fn(eng)

            @block.scalar
            def _(eng):
                for fn in P.cmds["scalar"]:
                    fn(eng)

            @block.gpsimd
            def _(eng):
                for fn in P.cmds["gpsimd"]:
                    fn(eng)

            @block.sync
            def _(eng):
                for fn in P.cmds["sync"]:
                    fn(eng)
        nc._keepalive_ctx = ctx
        return nc

    if PHASES <= 1:
        return _finish()

    P.fence()
    for half in range(2):
        O("tensor", "matmul", PA[2][:, half * 512:(half + 1) * 512],
          onesrow[:], rtok_sb[0:1, half * 512:(half + 1) * 512],
          start=True, stop=True, skip_group_check=True,
          waits=([(s["cst"], N_CST), (s["trAF"], NT * 8)] if half == 0 else []),
          inc=(s["rtrT"] if half == 1 else None))
    O("vector", "tensor_copy", rtok_bc[:], PA[2][:, 0:1024],
      waits=[(s["rtrT"], None)], inc=s["rtr"])
    pt2_free[0] = (s["rtr"], s["rtr"].n)
    pt2_free[1] = (s["rtr"], s["rtr"].n)
    for b in range(2):
        for j in range(NJ):
            src = xn_fm[:, j * NTOK + b * 512: j * NTOK + (b + 1) * 512]
            wts = [(s["rtr"], None)] if (b == 0 and j == 0) else []
            O("vector", "tensor_mul", kk[:], src,
              rtok_bc[:, b * 512:(b + 1) * 512], waits=wts)
            O("vector", "reduce_sum", rsum_sb[:, b * 32 + j: b * 32 + j + 1],
              kk[:], axis=AX.X,
              inc=(s["rtr"] if (b == 1 and j == NJ - 1) else None))
    O("gpsimd", "dma_start", out=r_in, in_=rsum_sb[:],
      waits=[(s["rtr"], None)], inc=s["rtrG"], inc_amt=16)
    P.coll("AllReduce", ALU.add, RG, [r_in.opt()], [r_out.opt()],
           waits=[(s["rtrG"], None)], inc=s["collR"])

    if PHASES <= 2:
        return _finish()

    P.fence()
    trk = 0
    for h in range(NHKV):
        for g4 in range(4):
            slot = g4 % 2
            ksrc = Wk[:, h * 128:(h + 1) * 128].rearrange(
                "(j p) i -> p j i", p=128)[:, g4 * 8:(g4 + 1) * 8, :]
            vsrc = Wv[:, h * 128:(h + 1) * 128].rearrange(
                "(j p) i -> p j i", p=128)[:, g4 * 8:(g4 + 1) * 8, :]
            kdst = WBUF[:, slot * 1024:(slot + 1) * 1024].rearrange(
                "p (j i) -> p j i", i=128)
            vdst = WBUF[:, 2048 + slot * 1024: 2048 + (slot + 1) * 1024].rearrange(
                "p (j i) -> p j i", i=128)
            gidx = h * 4 + g4
            wts = [(s["wkvF"], gidx - 1)] if gidx >= 2 else []
            O("gpsimd", "dma_start", out=kdst, in_=ksrc, waits=wts,
              inc=s[f"wkv{slot}"], inc_amt=16, future=True)
            O("gpsimd", "dma_start", out=vdst, in_=vsrc,
              inc=s[f"wkv{slot}"], inc_amt=16)
        for g4 in range(4):
            slot = g4 % 2
            for jj in range(8):
                j = g4 * 8 + jj
                wts = []
                if jj == 0:
                    wts.append((s[f"wkv{slot}"], ((h * 4 + g4) // 2 + 1) * 32))
                    if j == 0:
                        wts.append((s["trAF"], NT * 8))
                        if h > 0:
                            wts.append((s["kve"], None))
                for half in range(2):
                    O("tensor", "matmul", PA[0][:, half * 512:(half + 1) * 512],
                      WBUF[:, slot * 1024 + jj * 128: slot * 1024 + (jj + 1) * 128],
                      xn_fm[:, j * NTOK + half * 512: j * NTOK + (half + 1) * 512],
                      start=(j == 0), stop=(j == NJ - 1), skip_group_check=True,
                      waits=(wts if half == 0 else []))
                    wts = []
                    O("tensor", "matmul", PA[1][:, half * 512:(half + 1) * 512],
                      WBUF[:, 2048 + slot * 1024 + jj * 128:
                           2048 + slot * 1024 + (jj + 1) * 128],
                      xn_fm[:, j * NTOK + half * 512: j * NTOK + (half + 1) * 512],
                      start=(j == 0), stop=(j == NJ - 1), skip_group_check=True,
                      inc=(s["wkvF"] if (jj == 7 and half == 1) else None))
        wts = [(s["wkvF"], 4 * (h + 1))]
        if h >= 1:
            wts.append((s["kvs"], None))
        O("vector", "tensor_mul", rtmp1[0:64, :], PA[0][64:128, :],
          tab[0:64, 1024:2048], waits=wts)
        O("vector", "tensor_mul", rtmp1[64:128, :], PA[0][0:64, :],
          tab[64:128, 1024:2048])
        O("vector", "tensor_mul", rtmp2[:], PA[0][:, :], tab[:, 0:1024])
        O("vector", "tensor_add", K_sb[:], rtmp2[:], rtmp1[:], inc=s["kve"])
        O("vector", "tensor_copy", V_sb[:], PA[1][:, :], inc=s["kve"])
        for src_sb, dst_sb in ((K_sb, ktm), (V_sb, vtm)):
            for b4 in range(2):
                ps, off = (PB[2], 0) if trk % 2 == 0 else (PB[2], 1024)
                for q in range(4):
                    tb = b4 * 4 + q
                    wts = []
                    if q == 0:
                        wts = [(s["kve"], None)]
                        if trk >= 2:
                            wts.append((s["trKF"], trk - 1))
                        else:
                            wts += pt2_wait(trk % 2)
                    O("tensor", "transpose",
                      ps[:, off + q * 128: off + (q + 1) * 128],
                      src_sb[:, tb * 128:(tb + 1) * 128], identb[:], waits=wts,
                      inc=(s["trK"] if q == 3 else None))
                dst = dst_sb[:].rearrange("p (t i) -> p t i", i=128)[
                    :, b4 * 4:(b4 + 1) * 4, :]
                src = ps[:, off:off + 512].rearrange("p (t i) -> p t i", i=128)
                O("vector", "tensor_copy", dst, src, waits=[(s["trK"], None)],
                  inc=s["trKF"])
                trk += 1
        pt2_free[0] = (s["trKF"], s["trKF"].n)
        pt2_free[1] = (s["trKF"], s["trKF"].n)
        kdst = kv_in[h * NTOK:(h + 1) * NTOK, 0:128].rearrange(
            "(t p) c -> p t c", p=128)
        vdst = kv_in[h * NTOK:(h + 1) * NTOK, 128:256].rearrange(
            "(t p) c -> p t c", p=128)
        O("gpsimd", "dma_start", out=kdst,
          in_=ktm[:].rearrange("p (t i) -> p t i", i=128),
          waits=[(s["trKF"], None)], inc=s["kvs"], inc_amt=16)
        O("gpsimd", "dma_start", out=vdst,
          in_=vtm[:].rearrange("p (t i) -> p t i", i=128),
          inc=s["kvs"], inc_amt=16)
    P.coll("AllToAll", ALU.bypass, RG, [kv_in.opt()], [kv_out.opt()],
           waits=[(s["kvs"], None)], inc=s["collKV"])

    O("gpsimd", "dma_start", out=tab[:, 0:1024], in_=cosq_i,
      waits=[(s["kve"], None)], inc=s["cst"], inc_amt=16)
    O("gpsimd", "dma_start", out=tab[:, 1024:2048], in_=sinq_i,
      inc=s["cst"], inc_amt=16)
    TAB_Q = s["cst"].n

    if PHASES <= 3:
        return _finish()

    P.fence()
    for g in range(16):
        QPS = (PA[0], PA[1]) if g % 2 == 0 else (PA[2], PA[3])
        for g8 in range(8):
            slot = g8 % 4
            src = Wq[:, g * 256:(g + 1) * 256].rearrange(
                "(j p) i -> p j i", p=128)[:, g8 * 4:(g8 + 1) * 4, :]
            dst = WBUF[:, slot * 1024:(slot + 1) * 1024].rearrange(
                "p (j i) -> p j i", i=256)
            gidx = g * 8 + g8
            wts = []
            if gidx == 0:
                wts = [(s["wkvF"], 32)]
            elif gidx >= 4:
                wts = [(s["wqF"], gidx - 3)]
            O("gpsimd", "dma_start", out=dst, in_=src, waits=wts,
              inc=s[f"wq{slot}"], inc_amt=16, future=True)
        for g8 in range(8):
            slot = g8 % 4
            for jj in range(4):
                j = g8 * 4 + jj
                wts = []
                if jj == 0:
                    wts.append((s[f"wq{slot}"], ((g * 8 + g8) // 4 + 1) * 16))
                    if j == 0:
                        if g >= 2:
                            wts.append((s["qe"], 2 * (g - 1)))
                        if g == 0:
                            wts.append((s["kve"], None))
                            wts.append((s["cst"], TAB_Q))
                        if g == 1:
                            wts.append((s["trKF"], None))
                for hp in range(2):
                    for half in range(2):
                        O("tensor", "matmul",
                          QPS[hp][:, half * 512:(half + 1) * 512],
                          WBUF[:, slot * 1024 + jj * 256 + hp * 128:
                               slot * 1024 + jj * 256 + (hp + 1) * 128],
                          xn_fm[:, j * NTOK + half * 512:
                                j * NTOK + (half + 1) * 512],
                          start=(j == 0), stop=(j == NJ - 1),
                          skip_group_check=True, waits=wts,
                          inc=(s["wqF"] if (jj == 3 and hp == 1 and half == 1)
                               else None))
                        wts = []
        for hp in range(2):
            wts = [(s["wqF"], 8 * (g + 1))]
            if g > 0:
                wts.append((s[f"qs{hp}"], 16 * g))
            if g == 0 and hp == 0:
                wts.append((s["kvs"], None))
            O("vector", "tensor_mul", rtmp1[0:64, :], QPS[hp][64:128, :],
              tab[0:64, 1024:2048], waits=wts)
            O("vector", "tensor_mul", rtmp1[64:128, :], QPS[hp][0:64, :],
              tab[64:128, 1024:2048])
            O("vector", "tensor_mul", rtmp2[:], QPS[hp][:, :], tab[:, 0:1024])
            O("vector", "tensor_add", qstage[hp][:], rtmp2[:], rtmp1[:],
              inc=s["qe"])
            h = 2 * g + hp
            O("gpsimd", "dma_start", out=q_in[h * 128:(h + 1) * 128, :],
              in_=qstage[hp][:], waits=[(s["qe"], None)],
              inc=s[f"qs{hp}"], inc_amt=16)
    P.coll("AllToAll", ALU.bypass, RG, [q_in.opt()], [q_out.opt()],
           waits=[(s["qs0"], None), (s["qs1"], None)], inc=s["collQ"])

    if PHASES <= 4:
        return _finish()

    P.fence()
    O("gpsimd", "dma_start",
      out=At[:].rearrange("p (j i) -> p j i", i=128),
      in_=A2T.rearrange("(j p) i -> p j i", p=128),
      waits=[(s["qs0"], None), (s["qs1"], None)], inc=s["misc"], inc_amt=16)
    for j in range(NJ):
        for half in range(2):
            wts = []
            if j == 0 and half == 0:
                wts = [(s["misc"], None), (s["qe"], None)]
            O("tensor", "matmul", PA[0][:, half * 512:(half + 1) * 512],
              At[:, j * 128:(j + 1) * 128],
              xn_fm[:, j * NTOK + half * 512: j * NTOK + (half + 1) * 512],
              start=(j == 0), stop=(j == NJ - 1), skip_group_check=True,
              waits=wts, inc=(s["tfmT"] if (j == NJ - 1 and half == 1) else None))
    O("vector", "tensor_copy", tfm[:, 0:512], PA[0][:, 0:512],
      waits=[(s["tfmT"], None)])
    O("vector", "tensor_copy", tfm[:, 512:1024], PA[0][:, 512:1024], inc=s["tfm"])

    O("gpsimd", "dma_start", out=rsum_sb[:], in_=r_out,
      waits=[(s["collR"], 1)], inc=s["rtrG"], inc_amt=16)
    O("vector", "tensor_scalar_mul", rsum_sb[:], rsum_sb[:], 1.0 / NQTOK,
      waits=[(s["rtrG"], None)], inc=s["rtr"])
    rq = rsum_sb[:].rearrange("p (b j) -> p j b", b=2)
    for j in range(NJ):
        wts = ([(s["rtr"], None), (s["qe"], None)] + pt2_wait(0)) if j == 0 else []
        O("tensor", "matmul", PA[2][0:2, 0:8], rq[:, j, :],
          rwt_sb[:, j * 8:(j + 1) * 8], start=(j == 0), stop=(j == NJ - 1),
          skip_group_check=True, waits=wts,
          inc=(s["rtrT"] if j == NJ - 1 else None))
    O("vector", "tensor_add", logit_sb[:], PA[2][0:2, 0:8], rb2_sb[:],
      waits=[(s["rtrT"], None)], inc=s["rtr"])
    O("vector", "reduce_max", rmax[:], logit_sb[:], axis=AX.X, inc=s["rtr"])
    O("vector", "tensor_scalar_mul", rmax2[:], rmax[:], -1.0, inc=s["rtr"])
    O("scalar", "activation", rexp[:], logit_sb[:], AF.Exp, bias=rmax2[:, 0:1],
      waits=[(s["rtr"], None)], inc=s["rtrA2"])
    O("vector", "reduce_sum", rsumt[:], rexp[:], axis=AX.X,
      waits=[(s["rtrA2"], None)], inc=s["rtr"])
    O("vector", "reciprocal", rrec[:], rsumt[:], inc=s["rtr"])
    O("vector", "tensor_scalar", rsm[:], rexp[:], rrec[:, 0:1], None, ALU.mult,
      inc=s["rtr"])
    O("tensor", "transpose", PA[3][0:8, 0:2], rsm[:], identf[0:2, 0:2],
      waits=[(s["rtr"], None), (s["qe"], None)], inc=s["rtrT"])
    O("vector", "tensor_copy", routT[:], PA[3][0:8, 0:2],
      waits=[(s["rtrT"], None)], inc=s["rtr"])
    for b in range(2):
        O("tensor", "matmul", PA[2][:, 16 + b:17 + b], rep_f[:],
          routT[0:8, b:b + 1], start=True, stop=True, skip_group_check=True,
          waits=[(s["rtr"], None)], inc=s["rtrT"])
    O("vector", "tensor_copy", svec[:, 0:1], PA[2][:, 16:17],
      waits=[(s["rtrT"], None)], inc=s["rtr"])
    O("vector", "tensor_copy", svec[:, 1:2], PA[2][:, 17:18], inc=s["rtr"])
    pt2_free[0] = (s["rtr"], s["rtr"].n)
    pt2_free[1] = (s["qe"], s["qe"].n)
    pt2_free[2] = (s["tfm"], s["tfm"].n)
    pt2_free[3] = (s["tfm"], s["tfm"].n)
    for b in range(2):
        O("vector", "tensor_scalar", tsc[:, b * 512:(b + 1) * 512],
          tfm[:, b * 512:(b + 1) * 512], svec[:, b:b + 1], None, ALU.mult,
          inc=s["tfm"])
    if dbg:
        O("vector", "tensor_copy", rdbg[:, 0:8], logit_sb[:],
          waits=[(s["rtr"], None)])
        O("vector", "tensor_copy", rdbg[:, 8:9], rmax[:])
        O("vector", "tensor_copy", rdbg[:, 9:17], rexp[:])
        O("vector", "tensor_copy", rdbg[:, 17:18], rsumt[:])
        O("vector", "tensor_copy", rdbg[:, 18:19], rrec[:])
        O("vector", "tensor_copy", rdbg[:, 19:27], rsm[:], inc=s["rtr"])
        O("gpsimd", "dma_start", out=dbg_o["d_rout"], in_=rdbg[:, 19:27],
          waits=[(s["rtr"], None)], inc=s["dbg"], inc_amt=16)
        O("gpsimd", "dma_start", out=dbg_o["d_rdbg"], in_=rdbg[:],
          waits=[(s["rtr"], None)], inc=s["dbg"], inc_amt=16)

    if PHASES <= 5:
        return _finish()

    P.fence()
    SCSLOT = [(PA[2], 0), (PA[2], 512), (PA[0], 0), (PA[0], 512)]
    deferq = []
    sc_i = 0
    pt_i = 0
    pm_i = 0
    vt_n = 0
    n_pv = 0
    n_msk = 0
    pt_slot_free = {}
    pm_slot_free = {}
    vt_slot_free = {}
    first_vt = True
    for b in range(2):
        for p in range(8):
            wts = [(s["collKV"], 1)]
            if b == 1 and p == 0:
                wts.append((s["pvd"], None))
            O("sync", "dma_start", out=K_res[:, p * 512:(p + 1) * 512],
              in_=kv_out[1024 * p + 512 * b: 1024 * p + 512 * b + 512, 0:128],
              transpose=True, waits=wts, inc=s["kres"], inc_amt=16)
        for p in range(8):
            wts = [(s["kres"], (b * 8 + 8) * 16)]
            if b == 1 and p == 0:
                wts.append((s["aev"], None))
            if b == 0 and p == 0:
                wts.append((s["tfm"], None))
            O("vector", "tensor_mul", qq[:], K_res[:, p * 512:(p + 1) * 512],
              K_res[:, p * 512:(p + 1) * 512], waits=wts, inc=s["qn"])
            O("tensor", "matmul", PA[2][0:1, 0:512], onescol[:], qq[:],
              start=True, stop=True, skip_group_check=True,
              waits=[(s["qn"], None)] + pt2_wait(0), inc=s["nmT"])
            O("vector", "reduce_max", kn_sb[0:1, p:p + 1], PA[2][0:1, 0:512],
              axis=AX.X, waits=[(s["nmT"], None)], inc=s["scF"])
            pt2_free[0] = (s["scF"], s["scF"].n)
        while deferq:
            deferq.pop(0)()
        if b == 1:
            P.coll("AllToAll", ALU.bypass, RG, [ao_in[0].opt()],
                   [ao_out[0].opt()],
                   waits=[(s["aos0"], None), (s["aos1"], None)],
                   inc=s["collAO"])
            for j in range(NCORE):
                for hp in range(4):
                    h = 4 * j + hp
                    O("gpsimd", "dma_start",
                      out=attn_fm[:, h * NTOK: h * NTOK + 512],
                      in_=ao_out[0][512 * j + 128 * hp:
                                    512 * j + 128 * (hp + 1), :],
                      waits=([(s["collAO"], 1)] if (j == 0 and hp == 0) else []),
                      inc=s["afm"], inc_amt=16)
        for qp in range(8):
            O("vector", "reduce_max", kmaxr[0:1, qp:qp + 1],
              kn_sb[0:1, 0:qp + 1], axis=AX.X, inc=s["qn"])
        qn_slot = 0
        qn_mul_at = {}
        for qp in range(8):
            for hh in range(4):
                sl = qn_slot % 4
                wts = [(s["collQ"], 1)]
                if qn_slot >= 4:
                    wts.append((s["qn"], qn_mul_at[qn_slot - 4]))
                O("scalar", "dma_start", out=qsb[:, sl * 512:(sl + 1) * 512],
                  in_=q_out[512 * qp + 128 * hh: 512 * qp + 128 * (hh + 1),
                            512 * b: 512 * (b + 1)],
                  waits=wts, inc=s[f"qnl{sl}"], inc_amt=16)
                O("vector", "tensor_mul", qq[:],
                  qsb[:, sl * 512:(sl + 1) * 512],
                  qsb[:, sl * 512:(sl + 1) * 512],
                  waits=[(s[f"qnl{sl}"], s[f"qnl{sl}"].n)], inc=s["qn"])
                qn_mul_at[qn_slot] = s["qn"].n
                scs0 = qn_slot % 2
                O("tensor", "matmul", PA[2][0:1, scs0 * 512:scs0 * 512 + 512],
                  onescol[:], qq[:], start=True, stop=True,
                  skip_group_check=True,
                  waits=[(s["qn"], None)] + pt2_wait(scs0), inc=s["nmT"])
                O("vector", "reduce_max", qn2a[0:1, qp * 4 + hh: qp * 4 + hh + 1],
                  PA[2][0:1, scs0 * 512:scs0 * 512 + 512], axis=AX.X,
                  waits=[(s["nmT"], None)], inc=s["scF"])
                pt2_free[scs0] = (s["scF"], s["scF"].n)
                qn_slot += 1
        for qp in range(8):
            O("vector", "tensor_scalar", qn2a[0:1, qp * 4: qp * 4 + 4],
              qn2a[0:1, qp * 4: qp * 4 + 4], kmaxr[0:1, qp:qp + 1], None,
              ALU.mult, inc=s["qn"])
        O("scalar", "activation", mtmp2_row[:], qn2a[:], AF.Ln,
          waits=[(s["qn"], None)], inc=s["nmA"])
        O("vector", "tensor_scalar", mtmp2_row[:], mtmp2_row[:], 0.5, LN105,
          ALU.mult, ALU.add, waits=[(s["nmA"], None)], inc=s["qn"])
        O("scalar", "activation", mtmp2_row[:], mtmp2_row[:], AF.Exp,
          waits=[(s["qn"], None)], inc=s["nmA"])
        O("vector", "tensor_scalar_mul", negm_bf[:], mtmp2_row[:], -1.0,
          waits=[(s["nmA"], None)], inc=s["qn"])
        wts = [(s["qn"], None), (s["aev"], None)] + pt2_wait(0)
        if b == 0:
            wts.append((s["tfm"], None))
        O("tensor", "matmul", PA[2][:, 0:32], onesrow[:], negm_bf[:],
          start=True, stop=True, skip_group_check=True, waits=wts,
          inc=s["nmT"])
        O("vector", "tensor_copy", negm[:], PA[2][:, 0:32],
          waits=[(s["nmT"], None)], inc=s["nm"])
        pt2_free[0] = (s["nm"], s["nm"].n)
        for qp in range(8):
            for hh in range(4):
                wts = []
                if hh == 0:
                    wts = [(s["collQ"], 1)]
                    if qp > 0:
                        wts.append((s["pvd"], None))
                    else:
                        wts.append((s["qn"], None))
                O("scalar", "dma_start", out=qsb[:, hh * 512:(hh + 1) * 512],
                  in_=q_out[512 * qp + 128 * hh: 512 * qp + 128 * (hh + 1),
                            512 * b: 512 * (b + 1)],
                  waits=wts, inc=s["qld"], inc_amt=16)
            for hpair in range(2):
                pend = []

                def flush_one():
                    nonlocal pend, n_pv
                    if not pend:
                        return
                    pd = pend.pop(0)
                    O("tensor", "matmul",
                      PA[1][0:1, pd["h2"] * 512:(pd["h2"] + 1) * 512],
                      onescol[:], pd["puse"], start=pd["first"],
                      stop=pd["last"], skip_group_check=True,
                      waits=pd["lw"])
                    O("tensor", "matmul",
                      PA[3][:, pd["h2"] * 512:(pd["h2"] + 1) * 512],
                      pd["vtile"], pd["puse"], start=pd["first"],
                      stop=pd["last"], skip_group_check=True,
                      waits=[pd["vw"]], inc=s["pvd"])
                    n_pv += 1
                    if pd["tile_of_tk_done"] is not None:
                        vt_slot_free[pd["tile_of_tk_done"]] = (s["pvd"], n_pv)
                    if pd["ptidx"] is not None:
                        pt_slot_free[pd["ptidx"]] = (s["pvd"], n_pv)
                    if pd["pmidx"] is not None:
                        pm_slot_free[pd["pmidx"]] = (s["pvd"], n_pv)

                for tk in range(qp + 1):
                    vslot = vt_n % 2
                    wts = [(s["collKV"], 1)]
                    if vt_n >= 2:
                        wts.append(vt_slot_free[vt_n - 2])
                    if first_vt:
                        wts.append((s["tfm"], None))
                        first_vt = False
                    O("scalar", "dma_start",
                      out=vt[vslot][:].rearrange("p (t c) -> p t c", c=128),
                      in_=kv_out[1024 * tk + 512 * b: 1024 * tk + 512 * b + 512,
                                 128:256].rearrange("(t p) c -> p t c", p=128),
                      waits=wts, inc=s[f"vt{vslot}"], inc_amt=16)
                    for tb in range(4):
                        for h2 in range(2):
                            if len(pend) >= 2:
                                flush_one()
                            if deferq:
                                deferq.pop(0)()
                            hh = hpair * 2 + h2
                            first = (tk == 0 and tb == 0)
                            last = (tk == qp and tb == 3)
                            scs = sc_i % 4
                            sap, soff = SCSLOT[scs]
                            wts = list(pt2_wait(scs))
                            if first and tb == 0 and h2 == 0:
                                wts.append((s["qld"], s["qld"].n))
                                if hpair == 0:
                                    wts.append((s["nm"], None))
                                    wts.append((s["kres"], (b * 8 + 8) * 16))
                            O("tensor", "matmul",
                              sap[:, soff:soff + 512],
                              K_res[:, tk * 512 + tb * 128:
                                    tk * 512 + (tb + 1) * 128],
                              qsb[:, hh * 512:(hh + 1) * 512],
                              start=True, stop=True, skip_group_check=True,
                              waits=wts, inc=s["sc"])
                            ptidx = pt_i
                            pts = ptidx % 5
                            ew = [(s["sc"], None)]
                            if ptidx >= 5:
                                ew.append(pt_slot_free[ptidx - 5])
                            O("scalar", "activation", pt[pts][:],
                              sap[:, soff:soff + 512], AF.Exp,
                              bias=negm[:, qp * 4 + hh: qp * 4 + hh + 1],
                              waits=ew, inc=s["exp"])
                            pt2_free[scs] = (s["exp"], s["exp"].n)
                            sc_i += 1
                            pt_i += 1
                            pmidx = None
                            if tk == qp:
                                pms = pm_i % 2
                                mw = [(s["exp"], None)]
                                if pm_i >= 2:
                                    mw.append(pm_slot_free[pm_i - 2])
                                O("vector", "tensor_mul", pm[pms][:], pt[pts][:],
                                  masks[:, tb * 512:(tb + 1) * 512],
                                  waits=mw, inc=s["msk"])
                                n_msk += 1
                                pt_slot_free[ptidx] = (s["msk"], n_msk)
                                pmidx = pm_i
                                pm_i += 1
                                puse = pm[pms][:]
                                pwait = [(s["msk"], s["msk"].n)]
                                ptrec = None
                            else:
                                puse = pt[pts][:]
                                pwait = [(s["exp"], s["exp"].n)]
                                ptrec = ptidx
                            lw = list(pwait)
                            if first:
                                lw.append((s["aev"], s["aev"].n))
                                if b == 0 and qp == 0 and hpair == 0:
                                    lw.append((s["rtr"], s["rtr"].n))
                            cur = dict(h2=h2, puse=puse, first=first, last=last,
                                       lw=lw,
                                       vtile=vt[vslot][:, tb * 128:(tb + 1) * 128],
                                       vw=(s[f"vt{vslot}"], (vt_n // 2 + 1) * 16),
                                       ptidx=ptrec, pmidx=pmidx,
                                       tile_of_tk_done=(
                                           vt_n if (tb == 3 and h2 == 1) else None))
                            pend.append(cur)
                    vt_n += 1
                while pend:
                    flush_one()
                for h2 in range(2):
                    hh = hpair * 2 + h2
                    O("vector", "reciprocal",
                      qn_row[0:1, h2 * 512:(h2 + 1) * 512],
                      PA[1][0:1, h2 * 512:(h2 + 1) * 512],
                      waits=[(s["pvd"], None)], inc=s["aev"])
                    O("vector", "tensor_copy", linvb[:, h2 * 512:(h2 + 1) * 512],
                      PA[3][:, h2 * 512:(h2 + 1) * 512], inc=s["aev"])
                    aouse = (b * 8 + qp) * 2 + hpair

                    def d_bcast_mul(h2=h2, aev_n=s["aev"].n, aouse=aouse):
                        O("tensor", "matmul",
                          PA[0][:, h2 * 512:(h2 + 1) * 512],
                          onesrow_f[:], qn_row[0:1, h2 * 512:(h2 + 1) * 512],
                          start=True, stop=True, skip_group_check=True,
                          waits=[(s["aev"], aev_n)] + pt2_wait(2 + h2),
                          inc=s["nmT"])
                        O("vector", "tensor_mul", aosb[h2][:],
                          PA[0][:, h2 * 512:(h2 + 1) * 512],
                          linvb[:, h2 * 512:(h2 + 1) * 512],
                          waits=[(s["nmT"], None),
                                 (s[f"aos{h2}"], 16 * aouse)],
                          inc=s["aev"])
                        pt2_free[2 + h2] = (s["aev"], s["aev"].n)

                    def d_store(h2=h2, hh=hh, qp=qp, b=b):
                        O("gpsimd", "dma_start",
                          out=ao_in[b][512 * qp + 128 * hh:
                                       512 * qp + 128 * (hh + 1), :],
                          in_=aosb[h2][:], waits=[(s["aev"], None)],
                          inc=s[f"aos{h2}"], inc_amt=16)
                    deferq.extend([d_bcast_mul, d_store])
    while deferq:
        deferq.pop(0)()
    P.coll("AllToAll", ALU.bypass, RG, [ao_in[1].opt()], [ao_out[1].opt()],
           waits=[(s["aos0"], None), (s["aos1"], None)], inc=s["collAO1"])

    if PHASES <= 6:
        return _finish()

    for j in range(NCORE):
        for hp in range(4):
            h = 4 * j + hp
            O("sync", "dma_start",
              out=attn_fm[:, h * NTOK + 512: h * NTOK + 1024],
              in_=ao_out[1][512 * j + 128 * hp: 512 * j + 128 * (hp + 1), :],
              waits=([(s["collAO1"], 1)] if (j == 0 and hp == 0) else []),
              inc=s["afm1"], inc_amt=16)
    N_AFM = s["afm"].n
    N_AFM1 = s["afm1"].n
    P.fence()
    O("gpsimd", "dma_start", out=fbb[:], in_=fb,
      waits=[(s["pvd"], None), (s["exp"], None), (s["msk"], None),
             (s["aev"], None), (s["qld"], None), (s["aos0"], None),
             (s["aos1"], None)],
      inc=s["misc"], inc_amt=16)
    O("gpsimd", "dma_start", out=bcat[:], in_=B2T,
      waits=[(s["pvd"], None), (s["aev"], None), (s["msk"], None),
             (s["qn"], None), (s["afm"], N_AFM), (s["afm1"], N_AFM1)],
      inc=s["misc"], inc_amt=16)
    N_MISC = s["misc"].n

    if AFM_ONLY:
        return _finish()
    OUTB = [(PA[i // 2], (i % 2) * 512) for i in range(8)]
    for o in range(OLIM):
        for j in range(NJ):
            gidx = o * NJ + j
            O("gpsimd", "dma_start", out=wo_t[j % 4][:],
              in_=Wo[j * 128:(j + 1) * 128, o * 512:(o + 1) * 512],
              waits=([(s["woF"], gidx - 3)] if gidx >= 4 else []),
              inc=s[f"wo{j % 4}"], inc_amt=16, future=True)
        if FINMODE >= 2:
            for j in range(NJ):
                gidx = o * NJ + j
                O("gpsimd", "dma_start", out=wf_t[j % 4][:],
                  in_=WfT[j * 128:(j + 1) * 128, o * 512:(o + 1) * 512],
                  waits=([(s["wfF"], gidx - 3)] if gidx >= 4 else []),
                  inc=s[f"wf{j % 4}"], inc_amt=16, future=True)
        if FINMODE >= 3:
            for tt in range(NT):
                nxr = o * NT + tt
                O("scalar", "dma_start", out=xres[tt % 2][:],
                  in_=x_my[tt * 128:(tt + 1) * 128, o * 512:(o + 1) * 512],
                  waits=([(s["ouV"], nxr - 1)] if nxr >= 2 else []),
                  inc=s[f"xr{tt % 2}"], inc_amt=16, future=True)
        for j in range(NJ):
            for tt in range(NT):
                pa, po = OUTB[tt]
                wts = []
                if tt == 0:
                    wts.append((s[f"wo{j % 4}"], ((o * NJ + j) // 4 + 1) * 16))
                    if j == 0:
                        wts.append((s["afm"], N_AFM))
                        wts.append((s["afm1"], N_AFM1))
                        if o == 0:
                            wts.append((s["nmT"], None))
                            wts.append((s["aev"], None))
                            wts.append((s["exp"], None))
                        elif FINMODE >= 3:
                            wts.append((s["ouV"], o * 8))
                O("tensor", "matmul", pa[:, po:po + 512],
                  attn_fm[:, j * NTOK + tt * 128: j * NTOK + (tt + 1) * 128],
                  wo_t[j % 4][:], start=(j == 0), stop=False,
                  skip_group_check=True, waits=wts,
                  inc=(s["woF"] if tt == NT - 1 else None))
        if FINMODE >= 2:
            for j in range(NJ):
                for tt in range(NT):
                    pa, po = OUTB[tt]
                    wts = []
                    if tt == 0:
                        wts.append((s[f"wf{j % 4}"], ((o * NJ + j) // 4 + 1) * 16))
                    O("tensor", "matmul", pa[:, po:po + 512],
                      xn_fm[:, j * NTOK + tt * 128: j * NTOK + (tt + 1) * 128],
                      wf_t[j % 4][:], start=False, stop=False,
                      skip_group_check=True, waits=wts,
                      inc=(s["wfF"] if tt == NT - 1 else None))
        if FINMODE >= 3:
            for tt in range(NT):
                pa, po = OUTB[tt]
                wts = [(s["misc"], N_MISC), (s["tfm"], None)] if tt == 0 else []
                O("tensor", "matmul", pa[:, po:po + 512],
                  tsc[:, tt * 128:(tt + 1) * 128], bcat[:, o * 512:(o + 1) * 512],
                  start=False, stop=False, skip_group_check=True, waits=wts)
                O("tensor", "matmul", pa[:, po:po + 512],
                  onesrow[:], fbb[0:1, o * 512:(o + 1) * 512],
                  start=False, stop=True, skip_group_check=True, inc=s["ou"])
            for tt in range(NT):
                pa, po = OUTB[tt]
                nxr2 = o * NT + tt
                O("vector", "tensor_add", outsb[tt % 2][:], pa[:, po:po + 512],
                  xres[tt % 2][:],
                  waits=[(s["ou"], o * NT + tt + 1),
                         (s[f"xr{tt % 2}"], (nxr2 // 2 + 1) * 16),
                         (s[f"ouF{tt % 2}"], (nxr2 // 2) * 16)],
                  inc=s["ouV"])
                O("sync", "dma_start",
                  out=out_my[tt * 128:(tt + 1) * 128, o * 512:(o + 1) * 512],
                  in_=outsb[tt % 2][:], waits=[(s["ouV"], None)],
                  inc=s[f"ouF{tt % 2}"], inc_amt=16)
    P.cmds["sync"].append(lambda e, h=s["ouF0"].h, v=s["ouF0"].n: e.wait_ge(h, v))
    P.cmds["sync"].append(lambda e, h=s["ouF1"].h, v=s["ouF1"].n: e.wait_ge(h, v))

    if dbg:
        for j in range(NJ):
            O("vector", "tensor_copy", linvb[:, 0:1024].bitcast(BF16)[:, 0:1024]
              if False else linvb[:, 0:512],
              xn_fm[:, j * NTOK: j * NTOK + 512], waits=[(s["ouF0"], None), (s["ouF1"], None)])
            O("vector", "tensor_copy", linvb[:, 512:1024],
              xn_fm[:, j * NTOK + 512: (j + 1) * NTOK], inc=s["dbgV"])
            O("gpsimd", "dma_start",
              out=dbg_o["d_xnfm"][:, j * NTOK:(j + 1) * NTOK],
              in_=linvb[:], waits=[(s["dbgV"], None)], inc=s["dbg"], inc_amt=16)
            O("vector", "tensor_copy", linvb[:, 0:512],
              attn_fm[:, j * NTOK: j * NTOK + 512], waits=[(s["dbg"], None)])
            O("vector", "tensor_copy", linvb[:, 512:1024],
              attn_fm[:, j * NTOK + 512: (j + 1) * NTOK], inc=s["dbgV"])
            O("gpsimd", "dma_start",
              out=dbg_o["d_attn"][:, j * NTOK:(j + 1) * NTOK],
              in_=linvb[:], waits=[(s["dbgV"], None)], inc=s["dbg"], inc_amt=16)
        O("gpsimd", "dma_start", out=dbg_o["d_q"], in_=q_out,
          waits=[(s["collQ"], 1)], inc=s["dbg"], inc_amt=16)
        O("gpsimd", "dma_start", out=dbg_o["d_kv"], in_=kv_out,
          waits=[(s["collKV"], 1)], inc=s["dbg"], inc_amt=16)
        for bb in range(2):
            O("gpsimd", "dma_start",
              out=dbg_o["d_ao"][bb * 4096:(bb + 1) * 4096, :],
              in_=ao_out[bb],
              waits=[(s["collAO" if bb == 0 else "collAO1"], 1)],
              inc=s["dbg"], inc_amt=16)
        P.cmds["gpsimd"].append(
            lambda e, h=s["dbg"].h, v=s["dbg"].n: e.wait_ge(h, v))

    return _finish()



def _consts():
    c = {}
    c["identb"] = np.eye(128, dtype=ml_dtypes.bfloat16)
    c["identf"] = np.eye(128, dtype=np.float32)
    c["onescol"] = np.ones((128, 1), dtype=ml_dtypes.bfloat16)
    c["onesrow"] = np.ones((1, 128), dtype=ml_dtypes.bfloat16)
    c["onesrow_f"] = np.ones((1, 128), dtype=np.float32)
    rp = np.zeros((8, 128), dtype=np.float32)
    for e in range(8):
        rp[e, e * 16:(e + 1) * 16] = 2.0
    c["rep_f"] = rp
    m = np.zeros((512, 512), dtype=np.float32)
    i = np.arange(128)[:, None]
    jj = np.arange(512)[None, :]
    for blk in range(4):
        m[blk * 128:(blk + 1) * 128] = (blk * 128 + i <= jj)
    c["masks"] = m.astype(ml_dtypes.bfloat16)
    return c


def _rope_tables(core):
    pos = np.arange(512 * core, 512 * core + 512, dtype=np.float64)
    inv = 1.0 / (THETA ** (np.arange(0, HD, 2, dtype=np.float64) / HD))
    ang = pos[None, :] * inv[:, None]
    cos = np.cos(ang); sin = np.sin(ang)
    cosf = np.concatenate([cos, cos], 0)
    sinf = np.concatenate([-sin, sin], 0)
    cosf = np.concatenate([cosf, cosf], 1)
    sinf = np.concatenate([sinf, sinf], 1)
    sc = 1.0 / np.sqrt(HD)
    return (np.asarray(cosf * sc, np.float32), np.asarray(sinf * sc, np.float32),
            np.asarray(cosf, np.float32), np.asarray(sinf, np.float32))


def kernel(L0_attn_x, L0_attn_Wq, L0_attn_Wk, L0_attn_Wv, L0_attn_Wo,
           L0_ffn_W, L0_ffn_b, L0_ffn_A, L0_ffn_B, L0_ffn_router_W,
           L0_ffn_router_b):
    x = np.ascontiguousarray(np.asarray(L0_attn_x, np.float32))
    Wq_ = np.ascontiguousarray(np.asarray(L0_attn_Wq, np.float32))
    Wk_ = np.ascontiguousarray(np.asarray(L0_attn_Wk, np.float32))
    Wv_ = np.ascontiguousarray(np.asarray(L0_attn_Wv, np.float32))
    Wo_ = np.ascontiguousarray(np.asarray(L0_attn_Wo, np.float32))
    WfT_ = np.ascontiguousarray(np.asarray(L0_ffn_W, np.float32).T)
    fb_ = np.ascontiguousarray(np.asarray(L0_ffn_b, np.float32)).reshape(1, D)
    A2T_ = np.ascontiguousarray(np.asarray(L0_ffn_A, np.float32).reshape(128, D).T)
    B2T_ = np.ascontiguousarray(
        np.asarray(L0_ffn_B, np.float32).transpose(0, 2, 1).reshape(128, 4096))
    rW_ = np.ascontiguousarray(np.asarray(L0_ffn_router_W, np.float32))
    rb_ = np.ascontiguousarray(np.asarray(L0_ffn_router_b, np.float32))

    rwt_ = np.ascontiguousarray(
        rW_.T.reshape(32, 128, 8).transpose(1, 0, 2).reshape(128, 256))
    rb2_ = np.ascontiguousarray(np.tile(rb_.reshape(1, 8), (2, 1)))

    consts = _consts()
    shared = dict(Wq=Wq_, Wk=Wk_, Wv=Wv_, Wo=Wo_, WfT=WfT_, fb=fb_, A2T=A2T_,
                  B2T=B2T_, rwt=rwt_, rb2=rb2_, **consts)

    in_maps = []
    for c in range(NCORE):
        cq, sq, ck, sk = _rope_tables(c)
        x_myc = np.ascontiguousarray(
            np.concatenate([x[0, 512 * c:512 * (c + 1), :],
                            x[1, 512 * c:512 * (c + 1), :]], axis=0))
        rt = np.zeros((1, NTOK), np.float32)
        lo = max(Q_LO, 512 * c) - 512 * c
        hi = min(Q_HI, 512 * c + 512) - 512 * c
        if hi > lo:
            rt[0, lo:hi] = 1.0
            rt[0, 512 + lo:512 + hi] = 1.0
        m = dict(shared)
        m.update(x_my=x_myc, cosq=cq, sinq=sq, cosk=ck, sink=sk,
                 rtok=rt.astype(ml_dtypes.bfloat16))
        in_maps.append(m)

    nc = build_program(dbg=DBG)
    res = run_bass_kernel_spmd(nc, in_maps, core_ids=list(range(NCORE)))

    out = np.empty((2, SEQ, D), dtype=np.float32)
    for c in range(NCORE):
        o = res.results[c]["out_my"]
        out[0, 512 * c:512 * (c + 1), :] = o[0:512, :]
        out[1, 512 * c:512 * (c + 1), :] = o[512:1024, :]
    if DBG:
        kernel._dbg = res
    return out


# version 10 (final)
# speedup vs baseline: 1.8903x; 1.0818x over previous
import numpy as np
import ml_dtypes
from contextlib import ExitStack

import concourse.bass as bass
import concourse.mybir as mybir
from concourse.bass_utils import run_bass_kernel_spmd
from concourse.aot_env import allow_ici_all_to_all

F32 = mybir.dt.float32
BF16 = mybir.dt.bfloat16
AF = mybir.ActivationFunctionType
ALU = mybir.AluOpType
AX = mybir.AxisListType

NCORE = 8
D = 4096
SEQ = 4096
HD = 128
CH = 512
NTOK = 1024
NT = 8
NJ = 32
NHKV = 8
Q_LO, Q_HI = 611, 4095
NQTOK = Q_HI - Q_LO
EPS = 1e-5
LN105 = float(np.log(1.05))
THETA = 10000.0

DBG = False
PHASES = 7
OLIM = 8
AFM_ONLY = False
FINMODE = 3


class Sem:
    __slots__ = ("h", "n", "name", "vmap")

    def __init__(self, h, name):
        self.h, self.n, self.name = h, 0, name
        self.vmap = {}


class Prog:

    SELF_ENGS = ("vector", "scalar")

    def __init__(self, nc, ctx):
        self.nc, self.ctx = nc, ctx
        self.cmds = {e: [] for e in ("tensor", "vector", "scalar", "gpsimd", "sync")}
        self.selfsem = {e: Sem(ctx.enter_context(nc.semaphore(f"self_{e}")),
                               f"self_{e}") for e in self.SELF_ENGS}
        self.deps = {e: {"w": {}, "r": {}} for e in self.SELF_ENGS}
        self.fence_val = {e: 0 for e in self.SELF_ENGS}
        self.owner = {}

    def sem(self, name):
        return Sem(self.ctx.enter_context(self.nc.semaphore(name)), name)

    @staticmethod
    def _ap_name(a):
        t = getattr(a, "tensor", None)
        if t is not None and hasattr(t, "name"):
            return t.name
        if hasattr(a, "name") and hasattr(a, "ap"):
            return a.name
        return None

    def _rw_sets(self, args, kwargs):
        rset, wset = set(), set()
        for i, a in enumerate(args):
            nm = self._ap_name(a)
            if nm is None:
                continue
            (wset if i == 0 else rset).add(nm)
        for k, a in kwargs.items():
            nm = self._ap_name(a)
            if nm is None:
                continue
            if k in ("out", "accum_out"):
                wset.add(nm)
            else:
                rset.add(nm)
        return rset, wset

    def _resolve(self, sm, vv):
        eng = self.owner.get(sm.name)
        if eng is None:
            return (sm.h, vv)
        assert vv in sm.vmap, f"no vmap[{vv}] for {sm.name} (owner {eng})"
        return (self.selfsem[eng].h, sm.vmap[vv])

    def op(self, eng, meth, *args, waits=(), inc=None, inc_amt=1, future=False,
           **kwargs):
        wl = []
        for (sm, v) in waits:
            vv = sm.n if v is None else v
            if not future:
                assert vv <= sm.n, f"wait {sm.name}>={vv} but emitted {sm.n}"
            if vv > 0:
                wl.append((sm, vv))
        is_dma = meth in ("dma_start", "dma_start_transpose")
        ih = None
        if eng in self.SELF_ENGS:
            ss = self.selfsem[eng]
            if not is_dma:
                rset, wset = self._rw_sets(args, kwargs)
                dep = self.deps[eng]
                need = self.fence_val[eng]
                for t in rset | wset:
                    need = max(need, dep["w"].get(t, 0))
                for t in wset:
                    need = max(need, dep["r"].get(t, 0))
                if need > 0:
                    wl.insert(0, (ss, need))
                ss.n += 1
                for t in wset:
                    dep["w"][t] = ss.n
                for t in rset:
                    dep["r"][t] = max(dep["r"].get(t, 0), ss.n)
                ih = (ss.h, 1)
                if inc is not None:
                    assert inc_amt == 1
                    inc.n += 1
                    o = self.owner.setdefault(inc.name, eng)
                    assert o == eng, f"{inc.name} inc'd from {o} and {eng}"
                    inc.vmap[inc.n] = ss.n
            else:
                if inc is not None:
                    inc.n += (inc_amt if inc_amt is not None else 1)
                    ih = (inc.h, inc_amt)
        else:
            if inc is not None:
                assert inc.name not in self.owner, \
                    f"{inc.name} owned by {self.owner.get(inc.name)}"
                inc.n += (inc_amt if inc_amt is not None else 1)
                ih = (inc.h, inc_amt)

        def run(e, meth=meth, args=args, kwargs=kwargs, wl=wl, ih=ih):
            for (sm, v) in wl:
                h, v = self._resolve(sm, v)
                e.wait_ge(h, v)
            ins = getattr(e, meth)(*args, **kwargs)
            if ih is not None:
                if ih[1] is None or ih[1] == 1:
                    ins.then_inc(ih[0], 1)
                else:
                    ins.then_inc(ih[0], ih[1])
        self.cmds[eng].append(run)

    def coll(self, kind, op_, rg, ins_, outs_, waits=(), inc=None):
        wl = []
        for (sm, v) in waits:
            vv = sm.n if v is None else v
            if vv > 0:
                wl.append((sm, vv))
        inc.n += 1

        def run(e, wl=wl, ih=inc.h):
            for (sm, v) in wl:
                h, v = self._resolve(sm, v)
                e.wait_ge(h, v)
            with allow_ici_all_to_all():
                e.collective_compute(kind, op_, rg, ins_, outs_).then_inc(ih)
        self.cmds["gpsimd"].append(run)

    def fence(self):
        for e in self.SELF_ENGS:
            self.fence_val[e] = self.selfsem[e].n

    def bump(self, eng, sem, amt=1):
        raise RuntimeError("standalone bumps are forbidden (writeback race)")


def build_program(dbg=False):
    nc = bass.Bass("TRN2", target_bir_lowering=False)
    ctx = ExitStack()

    def din(name, shape, dt=F32):
        return nc.dram_tensor(name, shape, dt, kind="ExternalInput").ap()

    x_my = din("x_my", [NTOK, D])
    Wq = din("Wq", [D, D]); Wk = din("Wk", [D, 1024]); Wv = din("Wv", [D, 1024])
    Wo = din("Wo", [D, D]); WfT = din("WfT", [D, D])
    fb = din("fb", [1, D])
    A2T = din("A2T", [D, 128]); B2T = din("B2T", [128, D])
    rwt = din("rwt", [128, 256]); rb2 = din("rb2", [2, 8])
    cosq_i = din("cosq", [128, NTOK]); sinq_i = din("sinq", [128, NTOK])
    cosk_i = din("cosk", [128, NTOK]); sink_i = din("sink", [128, NTOK])
    masks_i = din("masks", [512, 512], BF16)
    identb_i = din("identb", [128, 128], BF16)
    identf_i = din("identf", [128, 128], F32)
    onescol_i = din("onescol", [128, 1], BF16)
    onesrow_i = din("onesrow", [1, 128], BF16)
    onesrow_f_i = din("onesrow_f", [1, 128], F32)
    rep_i = din("rep_f", [8, 128], F32)
    rtok_i = din("rtok", [1, NTOK], BF16)

    out_my = nc.dram_tensor("out_my", [NTOK, D], F32, kind="ExternalOutput").ap()

    kv_in = nc.dram_tensor("kv_in", [NCORE * NTOK, 256], BF16, kind="Internal").ap()
    kv_out = nc.dram_tensor("kv_out", [NCORE * NTOK, 256], BF16, kind="Internal").ap()
    q_in = nc.dram_tensor("q_in", [D, NTOK], BF16, kind="Internal").ap()
    q_out = nc.dram_tensor("q_out", [D, NTOK], BF16, kind="Internal").ap()
    ao_in = [nc.dram_tensor(f"ao_in{i}", [NCORE * 512, 512], BF16,
                            kind="Internal").ap() for i in range(2)]
    ao_out = [nc.dram_tensor(f"ao_out{i}", [NCORE * 512, 512], BF16,
                             kind="Internal").ap() for i in range(2)]
    r_in = nc.dram_tensor("r_in", [128, 64], F32, kind="Internal").ap()
    r_out = nc.dram_tensor("r_out", [128, 64], F32, kind="Internal").ap()

    dbg_o = {}
    if dbg:
        for nm_, shp, dt in (("d_xnfm", [128, NJ * NTOK], F32),
                             ("d_q", [D, NTOK], F32),
                             ("d_kv", [NCORE * NTOK, 256], F32),
                             ("d_ao", [NCORE * NTOK, 512], F32),
                             ("d_rout", [2, 8], F32),
                             ("d_rdbg", [2, 32], F32),
                             ("d_attn", [128, NJ * NTOK], F32)):
            dbg_o[nm_] = nc.dram_tensor(nm_, shp, dt, kind="ExternalOutput").ap()

    def sb(name, shape, dt):
        return nc.alloc_sbuf_tensor(name, shape, dt)

    xn_fm = sb("xn_fm", [128, NJ * NTOK], BF16)
    attn_fm = sb("attn_fm", [128, NJ * NTOK], BF16)
    tab = sb("tab", [128, 2048], F32)
    masks = sb("masks_s", [128, 2048], BF16)
    identb = sb("identb_s", [128, 128], BF16)
    identf = sb("identf_s", [128, 128], F32)
    onescol = sb("onescol_s", [128, 1], BF16)
    onesrow = sb("onesrow_s", [1, 128], BF16)
    onesrow_f = sb("onesrow_f_s", [1, 128], F32)
    rep_f = sb("rep_f_s", [8, 128], F32)
    rwt_sb = sb("rwt_sb", [128, 256], F32)
    rb2_sb = sb("rb2_sb", [2, 8], F32)
    tfm = sb("tfm", [128, NTOK], BF16)
    tsc = sb("tsc", [128, NTOK], BF16)
    kn_sb = sb("kn_sb", [1, 8], F32)
    qq = sb("qq", [128, 512], BF16)
    rsum_sb = sb("rsum_sb", [128, 64], F32)
    rtok_sb = sb("rtok_sb", [1, NTOK], BF16)
    qn2_sb = sb("qn2_sb", [1, 4], F32)
    mtmp = sb("mtmp", [1, 4], F32)
    mtmp_bf = sb("mtmp_bf", [1, 2], BF16)
    mtmp2_row = sb("mtmp2_row", [1, 32], F32)
    K_res = sb("K_res", [128, 4096], BF16)
    lsum = sb("lsum", [128, 1], F32); lsq = sb("lsq", [128, 1], F32)
    lmean = sb("lmean", [128, 1], F32); lvar = sb("lvar", [128, 1], F32)
    lrstd = sb("lrstd", [128, 1], F32); lnegmr = sb("lnegmr", [128, 1], F32)
    ltmp = sb("ltmp", [128, 1], F32)
    logit_sb = sb("logit_sb", [2, 8], F32); rmax = sb("rmax", [2, 1], F32)
    rexp = sb("rexp", [2, 8], F32); rsumt = sb("rsumt", [2, 1], F32)
    rrec = sb("rrec", [2, 1], F32); rsm = sb("rsm", [2, 8], F32)
    rmax2 = sb("rmax2", [2, 1], F32)
    rdbg = sb("rdbg", [2, 32], F32)
    routT = sb("routT", [8, 2], F32)
    svec = sb("svec", [128, 2], F32)
    wo_t = [sb(f"wo_t{i}", [128, 512], BF16) for i in range(4)]
    wf_t = [sb(f"wf_t{i}", [128, 512], BF16) for i in range(4)]
    xres = [sb(f"xres{i}", [128, 512], F32) for i in range(2)]
    outsb = [sb(f"outsb{i}", [128, 512], F32) for i in range(2)]
    WBUF = sb("WBUF", [128, 4096], BF16)

    ARENA = sb("ARENA", [128, 6144], F32)
    ab = nc.lookup_mloc(ARENA).addr

    def at_(name, shape, dt, off):
        nb = shape[1] * (4 if dt == F32 else 2)
        assert off + nb <= 24576, name
        return nc.alloc_sbuf_tensor_at(name, shape, dt, offset=ab + off)

    XBUF = at_("XBUF", [128, 4096], F32, 0)
    xn_tm = at_("xn_tm", [128, 4096], BF16, 16384)
    rtok_bc = at_("rtok_bc", [128, 1024], F32, 0)
    kk = at_("kk", [128, 512], F32, 4096)
    K_sb = at_("K_sb", [128, 1024], BF16, 0)
    V_sb = at_("V_sb", [128, 1024], BF16, 2048)
    ktm = at_("ktm", [128, 1024], BF16, 4096)
    vtm = at_("vtm", [128, 1024], BF16, 6144)
    rtmp1 = at_("rtmp1", [128, 1024], F32, 8192)
    rtmp2 = at_("rtmp2", [128, 1024], F32, 12288)
    qstage = [at_(f"qstage{i}", [128, 1024], BF16, 2048 * i) for i in range(2)]
    At = at_("At", [128, 4096], BF16, 16384)
    qsb = at_("qsb", [128, 2048], BF16, 0)
    vt = [at_(f"vt{i}", [128, 512], BF16, 4096 + 1024 * i) for i in range(2)]
    pt = [at_(f"pt{i}", [128, 512], BF16, off)
          for i, off in enumerate((6144, 7168, 8192, 15360, 22560))]
    pm = [at_(f"pm{i}", [128, 512], BF16, 9216 + 1024 * i) for i in range(2)]
    linvb = at_("linvb", [128, 1024], F32, 11264)
    qn_row = at_("qn_row", [1, 1024], F32, 16384)
    negm = at_("negm", [128, 32], F32, 23584)
    qn2a = sb("qn2a", [1, 32], F32)
    kmaxr = sb("kmaxr", [1, 8], F32)
    negm_bf = sb("negm_bf", [1, 32], BF16)
    aosb = [at_(f"aosb{i}", [128, 512], BF16, 20512 + 1024 * i) for i in range(2)]
    fbb = at_("fbb", [1, 4096], BF16, 0)
    bcat = at_("bcat", [128, 4096], BF16, 8192)

    PT = [ctx.enter_context(nc.psum_tensor(f"PT{i}", [128, 1024], F32)) for i in range(4)]
    PA = [p.ap() for p in PT]
    PB = [p.ap().bitcast(BF16) for p in PT]

    P = Prog(nc, ctx)
    O, B_ = P.op, P.bump
    RG = [list(range(NCORE))]

    s = {n: P.sem(n) for n in (
        "wkv0", "wkv1", "wq0", "wq1", "wq2", "wq3", "wo0", "wo1", "wf0", "wf1",
        "qnl0", "qnl1", "qnl2", "qnl3", "wo2", "wo3", "wf2", "wf3",
        "xr0", "xr1", "vt0", "vt1", "qs0", "qs1", "aos0", "aos1", "ouF0", "ouF1",
        "collR", "collKV", "collQ", "collAO", "collAO1", "afm1",
        "cst", "conv", "x", "xfree", "lnv", "lna", "xn", "xnfree",
        "trA", "trAF", "phA", "rtr", "rtrG", "rtrA2", "rtrT",
        "wkv", "wkvF", "kvt", "kve", "kvs",
        "wq", "wqF", "qp", "qe", "qs",
        "coll", "misc", "trK", "trKF", "tfm", "tfmT",
        "kres", "vt", "vtF", "qld", "qn", "nm", "nmA", "nmT",
        "sc", "scF", "exp", "ptF", "msk", "pmF", "lpv", "pvd", "aev", "aos",
        "afm", "wo", "woF", "wf", "wfF", "xr", "xrF", "ou", "ouV", "ouF",
        "dbg", "dbgV")}

    pt2_free = {0: None, 1: None, 2: None, 3: None}
    SCSLOT = None

    def pt2_wait(bank):
        f = pt2_free[bank]
        return [f] if f is not None else []

    for dst, src in ((identb[:], identb_i), (identf[:], identf_i),
                     (onescol[:], onescol_i), (onesrow[:], onesrow_i),
                     (onesrow_f[:], onesrow_f_i), (rep_f[:], rep_i),
                     (rwt_sb[:], rwt), (rb2_sb[:], rb2), (rtok_sb[:], rtok_i)):
        O("gpsimd", "dma_start", out=dst, in_=src, inc=s["cst"], inc_amt=16)
    O("gpsimd", "dma_start",
      out=masks[:].rearrange("p (m f) -> p m f", f=512),
      in_=masks_i.rearrange("(m p) f -> p m f", p=128), inc=s["cst"], inc_amt=16)
    O("gpsimd", "dma_start", out=tab[:, 0:1024], in_=cosk_i, inc=s["cst"], inc_amt=16)
    O("gpsimd", "dma_start", out=tab[:, 1024:2048], in_=sink_i, inc=s["cst"], inc_amt=16)
    N_CST = s["cst"].n

    TRSLOT = [(PB[0], 0), (PB[0], 1024), (PB[1], 0), (PB[1], 1024)]
    trb = 0
    for tt in range(NT):
        O("sync", "dma_start", out=XBUF[:], in_=x_my[tt * 128:(tt + 1) * 128, :],
          waits=[(s["xn"], None)], inc=s["x"], inc_amt=16)
        O("vector", "reduce_sum", lsum[:], XBUF[:], axis=AX.X,
          waits=[(s["x"], None)], inc=s["lnv"])
        O("scalar", "activation", xn_tm[:], XBUF[:], AF.Square, accum_out=lsq[:],
          waits=[(s["x"], None)] + ([(s["trAF"], 8 * tt)] if tt else []),
          inc=s["lna"])
        O("vector", "tensor_scalar_mul", lmean[:], lsum[:], 1.0 / D, inc=s["lnv"])
        O("vector", "tensor_scalar_mul", lvar[:], lsq[:], 1.0 / D,
          waits=[(s["lna"], None)], inc=s["lnv"])
        O("vector", "tensor_mul", ltmp[:], lmean[:], lmean[:], inc=s["lnv"])
        O("vector", "tensor_sub", lvar[:], lvar[:], ltmp[:], inc=s["lnv"])
        O("vector", "tensor_scalar_add", lvar[:], lvar[:], EPS, inc=s["lnv"])
        O("scalar", "activation", ltmp[:], lvar[:], AF.Sqrt,
          waits=[(s["lnv"], None)], inc=s["lna"])
        O("vector", "reciprocal", lrstd[:], ltmp[:], waits=[(s["lna"], None)],
          inc=s["lnv"])
        O("vector", "tensor_scalar", lnegmr[:], lmean[:], lrstd[:, 0:1], -1.0,
          ALU.mult, ALU.mult, inc=s["lnv"])
        O("scalar", "activation", xn_tm[:], XBUF[:], AF.Identity,
          scale=lrstd[:, 0:1], bias=lnegmr[:, 0:1],
          waits=[(s["lnv"], None)], inc=s["xn"])
        for b8 in range(8):
            ps, off = TRSLOT[trb % 4]
            for q in range(4):
                j = b8 * 4 + q
                wts = []
                if q == 0:
                    wts = [(s["xn"], None)]
                    if trb >= 4:
                        wts.append((s["trAF"], trb - 3))
                    if trb == 0:
                        wts.append((s["cst"], N_CST))
                O("tensor", "transpose", ps[:, off + q * 128: off + (q + 1) * 128],
                  xn_tm[:, j * 128:(j + 1) * 128], identb[:], waits=wts,
                  inc=(s["trA"] if q == 3 else None))
            dst = xn_fm[:].rearrange("p (j t) -> p j t", t=NTOK)[
                :, b8 * 4:(b8 + 1) * 4, tt * 128:(tt + 1) * 128]
            src = ps[:, off:off + 512].rearrange("p (j t) -> p j t", t=128)
            O("vector", "tensor_copy", dst, src, waits=[(s["trA"], None)],
              inc=s["trAF"])
            trb += 1

    def _finish():
        with nc.Block() as block:
            @block.tensor
            def _(eng):
                for fn in P.cmds["tensor"]:
                    fn(eng)

            @block.vector
            def _(eng):
                for fn in P.cmds["vector"]:
                    fn(eng)

            @block.scalar
            def _(eng):
                for fn in P.cmds["scalar"]:
                    fn(eng)

            @block.gpsimd
            def _(eng):
                for fn in P.cmds["gpsimd"]:
                    fn(eng)

            @block.sync
            def _(eng):
                for fn in P.cmds["sync"]:
                    fn(eng)
        nc._keepalive_ctx = ctx
        return nc

    if PHASES <= 1:
        return _finish()

    P.fence()
    for half in range(2):
        O("tensor", "matmul", PA[2][:, half * 512:(half + 1) * 512],
          onesrow[:], rtok_sb[0:1, half * 512:(half + 1) * 512],
          start=True, stop=True, skip_group_check=True,
          waits=([(s["cst"], N_CST), (s["trAF"], NT * 8)] if half == 0 else []),
          inc=(s["rtrT"] if half == 1 else None))
    O("vector", "tensor_copy", rtok_bc[:], PA[2][:, 0:1024],
      waits=[(s["rtrT"], None)], inc=s["rtr"])
    pt2_free[0] = (s["rtr"], s["rtr"].n)
    pt2_free[1] = (s["rtr"], s["rtr"].n)
    for b in range(2):
        for j in range(NJ):
            src = xn_fm[:, j * NTOK + b * 512: j * NTOK + (b + 1) * 512]
            wts = [(s["rtr"], None)] if (b == 0 and j == 0) else []
            O("vector", "tensor_mul", kk[:], src,
              rtok_bc[:, b * 512:(b + 1) * 512], waits=wts)
            O("vector", "reduce_sum", rsum_sb[:, b * 32 + j: b * 32 + j + 1],
              kk[:], axis=AX.X,
              inc=(s["rtr"] if (b == 1 and j == NJ - 1) else None))
    O("gpsimd", "dma_start", out=r_in, in_=rsum_sb[:],
      waits=[(s["rtr"], None)], inc=s["rtrG"], inc_amt=16)
    P.coll("AllReduce", ALU.add, RG, [r_in.opt()], [r_out.opt()],
           waits=[(s["rtrG"], None)], inc=s["collR"])

    if PHASES <= 2:
        return _finish()

    P.fence()
    trk = 0
    kv_store_q = []
    for h in range(NHKV):
        for g4 in range(4):
            slot = g4 % 2
            ksrc = Wk[:, h * 128:(h + 1) * 128].rearrange(
                "(j p) i -> p j i", p=128)[:, g4 * 8:(g4 + 1) * 8, :]
            vsrc = Wv[:, h * 128:(h + 1) * 128].rearrange(
                "(j p) i -> p j i", p=128)[:, g4 * 8:(g4 + 1) * 8, :]
            kdst = WBUF[:, slot * 1024:(slot + 1) * 1024].rearrange(
                "p (j i) -> p j i", i=128)
            vdst = WBUF[:, 2048 + slot * 1024: 2048 + (slot + 1) * 1024].rearrange(
                "p (j i) -> p j i", i=128)
            gidx = h * 4 + g4
            wts = [(s["wkvF"], gidx - 1)] if gidx >= 2 else []
            O("gpsimd", "dma_start", out=kdst, in_=ksrc, waits=wts,
              inc=s[f"wkv{slot}"], inc_amt=16, future=True)
            O("gpsimd", "dma_start", out=vdst, in_=vsrc,
              inc=s[f"wkv{slot}"], inc_amt=16)
        while kv_store_q:
            kv_store_q.pop(0)()
        for g4 in range(4):
            slot = g4 % 2
            for jj in range(8):
                j = g4 * 8 + jj
                wts = []
                if jj == 0:
                    wts.append((s[f"wkv{slot}"], ((h * 4 + g4) // 2 + 1) * 32))
                    if j == 0:
                        wts.append((s["trAF"], NT * 8))
                        if h > 0:
                            wts.append((s["kve"], None))
                for half in range(2):
                    O("tensor", "matmul", PA[0][:, half * 512:(half + 1) * 512],
                      WBUF[:, slot * 1024 + jj * 128: slot * 1024 + (jj + 1) * 128],
                      xn_fm[:, j * NTOK + half * 512: j * NTOK + (half + 1) * 512],
                      start=(j == 0), stop=(j == NJ - 1), skip_group_check=True,
                      waits=(wts if half == 0 else []))
                    wts = []
                    O("tensor", "matmul", PA[1][:, half * 512:(half + 1) * 512],
                      WBUF[:, 2048 + slot * 1024 + jj * 128:
                           2048 + slot * 1024 + (jj + 1) * 128],
                      xn_fm[:, j * NTOK + half * 512: j * NTOK + (half + 1) * 512],
                      start=(j == 0), stop=(j == NJ - 1), skip_group_check=True,
                      inc=(s["wkvF"] if (jj == 7 and half == 1) else None))
        wts = [(s["wkvF"], 4 * (h + 1))]
        if h >= 1:
            wts.append((s["kvs"], None))
        O("vector", "tensor_mul", rtmp1[0:64, :], PA[0][64:128, :],
          tab[0:64, 1024:2048], waits=wts)
        O("vector", "tensor_mul", rtmp1[64:128, :], PA[0][0:64, :],
          tab[64:128, 1024:2048])
        O("vector", "tensor_mul", rtmp2[:], PA[0][:, :], tab[:, 0:1024])
        O("vector", "tensor_add", K_sb[:], rtmp2[:], rtmp1[:], inc=s["kve"])
        O("vector", "tensor_copy", V_sb[:], PA[1][:, :], inc=s["kve"])
        for src_sb, dst_sb in ((K_sb, ktm), (V_sb, vtm)):
            for b4 in range(2):
                ps, off = (PB[2], 0) if trk % 2 == 0 else (PB[2], 1024)
                for q in range(4):
                    tb = b4 * 4 + q
                    wts = []
                    if q == 0:
                        wts = [(s["kve"], None)]
                        if trk >= 2:
                            wts.append((s["trKF"], trk - 1))
                        else:
                            wts += pt2_wait(trk % 2)
                    O("tensor", "transpose",
                      ps[:, off + q * 128: off + (q + 1) * 128],
                      src_sb[:, tb * 128:(tb + 1) * 128], identb[:], waits=wts,
                      inc=(s["trK"] if q == 3 else None))
                dst = dst_sb[:].rearrange("p (t i) -> p t i", i=128)[
                    :, b4 * 4:(b4 + 1) * 4, :]
                src = ps[:, off:off + 512].rearrange("p (t i) -> p t i", i=128)
                O("vector", "tensor_copy", dst, src, waits=[(s["trK"], None)],
                  inc=s["trKF"])
                trk += 1
        pt2_free[0] = (s["trKF"], s["trKF"].n)
        pt2_free[1] = (s["trKF"], s["trKF"].n)
        def kv_store(h=h, trkf_n=s["trKF"].n):
            kdst = kv_in[h * NTOK:(h + 1) * NTOK, 0:128].rearrange(
                "(t p) c -> p t c", p=128)
            vdst = kv_in[h * NTOK:(h + 1) * NTOK, 128:256].rearrange(
                "(t p) c -> p t c", p=128)
            O("gpsimd", "dma_start", out=kdst,
              in_=ktm[:].rearrange("p (t i) -> p t i", i=128),
              waits=[(s["trKF"], trkf_n)], inc=s["kvs"], inc_amt=16)
            O("gpsimd", "dma_start", out=vdst,
              in_=vtm[:].rearrange("p (t i) -> p t i", i=128),
              inc=s["kvs"], inc_amt=16)
        kv_store_q.append(kv_store)
    while kv_store_q:
        kv_store_q.pop(0)()
    P.coll("AllToAll", ALU.bypass, RG, [kv_in.opt()], [kv_out.opt()],
           waits=[(s["kvs"], None)], inc=s["collKV"])

    O("gpsimd", "dma_start", out=tab[:, 0:1024], in_=cosq_i,
      waits=[(s["kve"], None)], inc=s["cst"], inc_amt=16)
    O("gpsimd", "dma_start", out=tab[:, 1024:2048], in_=sinq_i,
      inc=s["cst"], inc_amt=16)
    TAB_Q = s["cst"].n

    if PHASES <= 3:
        return _finish()

    P.fence()
    q_store_q = []
    for g in range(16):
        QPS = (PA[0], PA[1]) if g % 2 == 0 else (PA[2], PA[3])
        for g8 in range(8):
            slot = g8 % 4
            src = Wq[:, g * 256:(g + 1) * 256].rearrange(
                "(j p) i -> p j i", p=128)[:, g8 * 4:(g8 + 1) * 4, :]
            dst = WBUF[:, slot * 1024:(slot + 1) * 1024].rearrange(
                "p (j i) -> p j i", i=256)
            gidx = g * 8 + g8
            wts = []
            if gidx == 0:
                wts = [(s["wkvF"], 32)]
            elif gidx >= 4:
                wts = [(s["wqF"], gidx - 3)]
            O("gpsimd", "dma_start", out=dst, in_=src, waits=wts,
              inc=s[f"wq{slot}"], inc_amt=16, future=True)
        while q_store_q:
            q_store_q.pop(0)()
        for g8 in range(8):
            slot = g8 % 4
            for jj in range(4):
                j = g8 * 4 + jj
                wts = []
                if jj == 0:
                    wts.append((s[f"wq{slot}"], ((g * 8 + g8) // 4 + 1) * 16))
                    if j == 0:
                        if g >= 2:
                            wts.append((s["qe"], 2 * (g - 1)))
                        if g == 0:
                            wts.append((s["kve"], None))
                            wts.append((s["cst"], TAB_Q))
                        if g == 1:
                            wts.append((s["trKF"], None))
                for hp in range(2):
                    for half in range(2):
                        O("tensor", "matmul",
                          QPS[hp][:, half * 512:(half + 1) * 512],
                          WBUF[:, slot * 1024 + jj * 256 + hp * 128:
                               slot * 1024 + jj * 256 + (hp + 1) * 128],
                          xn_fm[:, j * NTOK + half * 512:
                                j * NTOK + (half + 1) * 512],
                          start=(j == 0), stop=(j == NJ - 1),
                          skip_group_check=True, waits=wts,
                          inc=(s["wqF"] if (jj == 3 and hp == 1 and half == 1)
                               else None))
                        wts = []
        for hp in range(2):
            wts = [(s["wqF"], 8 * (g + 1))]
            if g > 0:
                wts.append((s[f"qs{hp}"], 16 * g))
            if g == 0 and hp == 0:
                wts.append((s["kvs"], None))
            O("vector", "tensor_mul", rtmp1[0:64, :], QPS[hp][64:128, :],
              tab[0:64, 1024:2048], waits=wts)
            O("vector", "tensor_mul", rtmp1[64:128, :], QPS[hp][0:64, :],
              tab[64:128, 1024:2048])
            O("vector", "tensor_mul", rtmp2[:], QPS[hp][:, :], tab[:, 0:1024])
            O("vector", "tensor_add", qstage[hp][:], rtmp2[:], rtmp1[:],
              inc=s["qe"])
            h = 2 * g + hp

            def q_store(h=h, hp=hp, qe_n=s["qe"].n):
                O("gpsimd", "dma_start", out=q_in[h * 128:(h + 1) * 128, :],
                  in_=qstage[hp][:], waits=[(s["qe"], qe_n)],
                  inc=s[f"qs{hp}"], inc_amt=16)
            q_store_q.append(q_store)
    while q_store_q:
        q_store_q.pop(0)()
    P.coll("AllToAll", ALU.bypass, RG, [q_in.opt()], [q_out.opt()],
           waits=[(s["qs0"], None), (s["qs1"], None)], inc=s["collQ"])

    if PHASES <= 4:
        return _finish()

    P.fence()
    O("gpsimd", "dma_start",
      out=At[:].rearrange("p (j i) -> p j i", i=128),
      in_=A2T.rearrange("(j p) i -> p j i", p=128),
      waits=[(s["qs0"], None), (s["qs1"], None)], inc=s["misc"], inc_amt=16)
    for j in range(NJ):
        for half in range(2):
            wts = []
            if j == 0 and half == 0:
                wts = [(s["misc"], None), (s["qe"], None)]
            O("tensor", "matmul", PA[0][:, half * 512:(half + 1) * 512],
              At[:, j * 128:(j + 1) * 128],
              xn_fm[:, j * NTOK + half * 512: j * NTOK + (half + 1) * 512],
              start=(j == 0), stop=(j == NJ - 1), skip_group_check=True,
              waits=wts, inc=(s["tfmT"] if (j == NJ - 1 and half == 1) else None))
    O("vector", "tensor_copy", tfm[:, 0:512], PA[0][:, 0:512],
      waits=[(s["tfmT"], None)])
    O("vector", "tensor_copy", tfm[:, 512:1024], PA[0][:, 512:1024], inc=s["tfm"])

    O("gpsimd", "dma_start", out=rsum_sb[:], in_=r_out,
      waits=[(s["collR"], 1)], inc=s["rtrG"], inc_amt=16)
    O("vector", "tensor_scalar_mul", rsum_sb[:], rsum_sb[:], 1.0 / NQTOK,
      waits=[(s["rtrG"], None)], inc=s["rtr"])
    rq = rsum_sb[:].rearrange("p (b j) -> p j b", b=2)
    for j in range(NJ):
        wts = ([(s["rtr"], None), (s["qe"], None)] + pt2_wait(0)) if j == 0 else []
        O("tensor", "matmul", PA[2][0:2, 0:8], rq[:, j, :],
          rwt_sb[:, j * 8:(j + 1) * 8], start=(j == 0), stop=(j == NJ - 1),
          skip_group_check=True, waits=wts,
          inc=(s["rtrT"] if j == NJ - 1 else None))
    O("vector", "tensor_add", logit_sb[:], PA[2][0:2, 0:8], rb2_sb[:],
      waits=[(s["rtrT"], None)], inc=s["rtr"])
    O("vector", "reduce_max", rmax[:], logit_sb[:], axis=AX.X, inc=s["rtr"])
    O("vector", "tensor_scalar_mul", rmax2[:], rmax[:], -1.0, inc=s["rtr"])
    O("scalar", "activation", rexp[:], logit_sb[:], AF.Exp, bias=rmax2[:, 0:1],
      waits=[(s["rtr"], None)], inc=s["rtrA2"])
    O("vector", "reduce_sum", rsumt[:], rexp[:], axis=AX.X,
      waits=[(s["rtrA2"], None)], inc=s["rtr"])
    O("vector", "reciprocal", rrec[:], rsumt[:], inc=s["rtr"])
    O("vector", "tensor_scalar", rsm[:], rexp[:], rrec[:, 0:1], None, ALU.mult,
      inc=s["rtr"])
    O("tensor", "transpose", PA[3][0:8, 0:2], rsm[:], identf[0:2, 0:2],
      waits=[(s["rtr"], None), (s["qe"], None)], inc=s["rtrT"])
    O("vector", "tensor_copy", routT[:], PA[3][0:8, 0:2],
      waits=[(s["rtrT"], None)], inc=s["rtr"])
    for b in range(2):
        O("tensor", "matmul", PA[2][:, 16 + b:17 + b], rep_f[:],
          routT[0:8, b:b + 1], start=True, stop=True, skip_group_check=True,
          waits=[(s["rtr"], None)], inc=s["rtrT"])
    O("vector", "tensor_copy", svec[:, 0:1], PA[2][:, 16:17],
      waits=[(s["rtrT"], None)], inc=s["rtr"])
    O("vector", "tensor_copy", svec[:, 1:2], PA[2][:, 17:18], inc=s["rtr"])
    pt2_free[0] = (s["rtr"], s["rtr"].n)
    pt2_free[1] = (s["qe"], s["qe"].n)
    pt2_free[2] = (s["tfm"], s["tfm"].n)
    pt2_free[3] = (s["tfm"], s["tfm"].n)
    for b in range(2):
        O("vector", "tensor_scalar", tsc[:, b * 512:(b + 1) * 512],
          tfm[:, b * 512:(b + 1) * 512], svec[:, b:b + 1], None, ALU.mult,
          inc=s["tfm"])
    if dbg:
        O("vector", "tensor_copy", rdbg[:, 0:8], logit_sb[:],
          waits=[(s["rtr"], None)])
        O("vector", "tensor_copy", rdbg[:, 8:9], rmax[:])
        O("vector", "tensor_copy", rdbg[:, 9:17], rexp[:])
        O("vector", "tensor_copy", rdbg[:, 17:18], rsumt[:])
        O("vector", "tensor_copy", rdbg[:, 18:19], rrec[:])
        O("vector", "tensor_copy", rdbg[:, 19:27], rsm[:], inc=s["rtr"])
        O("gpsimd", "dma_start", out=dbg_o["d_rout"], in_=rdbg[:, 19:27],
          waits=[(s["rtr"], None)], inc=s["dbg"], inc_amt=16)
        O("gpsimd", "dma_start", out=dbg_o["d_rdbg"], in_=rdbg[:],
          waits=[(s["rtr"], None)], inc=s["dbg"], inc_amt=16)

    if PHASES <= 5:
        return _finish()

    P.fence()
    SCSLOT = [(PA[2], 0), (PA[2], 512), (PA[0], 0), (PA[0], 512)]
    deferq = []
    sc_i = 0
    pt_i = 0
    pm_i = 0
    vt_n = 0
    n_pv = 0
    n_msk = 0
    pt_slot_free = {}
    pm_slot_free = {}
    vt_slot_free = {}
    first_vt = True
    for b in range(2):
        for p in range(8):
            wts = [(s["collKV"], 1)]
            if b == 1 and p == 0:
                wts.append((s["pvd"], None))
            O("sync", "dma_start", out=K_res[:, p * 512:(p + 1) * 512],
              in_=kv_out[1024 * p + 512 * b: 1024 * p + 512 * b + 512, 0:128],
              transpose=True, waits=wts, inc=s["kres"], inc_amt=16)
        for p in range(8):
            wts = [(s["kres"], (b * 8 + 8) * 16)]
            if b == 1 and p == 0:
                wts.append((s["aev"], None))
            if b == 0 and p == 0:
                wts.append((s["tfm"], None))
            O("vector", "tensor_mul", qq[:], K_res[:, p * 512:(p + 1) * 512],
              K_res[:, p * 512:(p + 1) * 512], waits=wts, inc=s["qn"])
            O("tensor", "matmul", PA[2][0:1, 0:512], onescol[:], qq[:],
              start=True, stop=True, skip_group_check=True,
              waits=[(s["qn"], None)] + pt2_wait(0), inc=s["nmT"])
            O("vector", "reduce_max", kn_sb[0:1, p:p + 1], PA[2][0:1, 0:512],
              axis=AX.X, waits=[(s["nmT"], None)], inc=s["scF"])
            pt2_free[0] = (s["scF"], s["scF"].n)
        while deferq:
            deferq.pop(0)()
        if b == 1:
            P.coll("AllToAll", ALU.bypass, RG, [ao_in[0].opt()],
                   [ao_out[0].opt()],
                   waits=[(s["aos0"], None), (s["aos1"], None)],
                   inc=s["collAO"])
            for j in range(NCORE):
                for hp in range(4):
                    h = 4 * j + hp
                    O("gpsimd", "dma_start",
                      out=attn_fm[:, h * NTOK: h * NTOK + 512],
                      in_=ao_out[0][512 * j + 128 * hp:
                                    512 * j + 128 * (hp + 1), :],
                      waits=([(s["collAO"], 1)] if (j == 0 and hp == 0) else []),
                      inc=s["afm"], inc_amt=16)
        for qp in range(8):
            O("vector", "reduce_max", kmaxr[0:1, qp:qp + 1],
              kn_sb[0:1, 0:qp + 1], axis=AX.X, inc=s["qn"])
        qn_slot = 0
        qn_mul_at = {}
        for qp in range(8):
            for hh in range(4):
                sl = qn_slot % 4
                wts = [(s["collQ"], 1)]
                if qn_slot >= 4:
                    wts.append((s["qn"], qn_mul_at[qn_slot - 4]))
                O("scalar", "dma_start", out=qsb[:, sl * 512:(sl + 1) * 512],
                  in_=q_out[512 * qp + 128 * hh: 512 * qp + 128 * (hh + 1),
                            512 * b: 512 * (b + 1)],
                  waits=wts, inc=s[f"qnl{sl}"], inc_amt=16)
                O("vector", "tensor_mul", qq[:],
                  qsb[:, sl * 512:(sl + 1) * 512],
                  qsb[:, sl * 512:(sl + 1) * 512],
                  waits=[(s[f"qnl{sl}"], s[f"qnl{sl}"].n)], inc=s["qn"])
                qn_mul_at[qn_slot] = s["qn"].n
                scs0 = qn_slot % 2
                O("tensor", "matmul", PA[2][0:1, scs0 * 512:scs0 * 512 + 512],
                  onescol[:], qq[:], start=True, stop=True,
                  skip_group_check=True,
                  waits=[(s["qn"], None)] + pt2_wait(scs0), inc=s["nmT"])
                O("vector", "reduce_max", qn2a[0:1, qp * 4 + hh: qp * 4 + hh + 1],
                  PA[2][0:1, scs0 * 512:scs0 * 512 + 512], axis=AX.X,
                  waits=[(s["nmT"], None)], inc=s["scF"])
                pt2_free[scs0] = (s["scF"], s["scF"].n)
                qn_slot += 1
        for qp in range(8):
            O("vector", "tensor_scalar", qn2a[0:1, qp * 4: qp * 4 + 4],
              qn2a[0:1, qp * 4: qp * 4 + 4], kmaxr[0:1, qp:qp + 1], None,
              ALU.mult, inc=s["qn"])
        O("scalar", "activation", mtmp2_row[:], qn2a[:], AF.Ln,
          waits=[(s["qn"], None)], inc=s["nmA"])
        O("vector", "tensor_scalar", mtmp2_row[:], mtmp2_row[:], 0.5, LN105,
          ALU.mult, ALU.add, waits=[(s["nmA"], None)], inc=s["qn"])
        O("scalar", "activation", mtmp2_row[:], mtmp2_row[:], AF.Exp,
          waits=[(s["qn"], None)], inc=s["nmA"])
        O("vector", "tensor_scalar_mul", negm_bf[:], mtmp2_row[:], -1.0,
          waits=[(s["nmA"], None)], inc=s["qn"])
        wts = [(s["qn"], None), (s["aev"], None)] + pt2_wait(0)
        if b == 0:
            wts.append((s["tfm"], None))
        O("tensor", "matmul", PA[2][:, 0:32], onesrow[:], negm_bf[:],
          start=True, stop=True, skip_group_check=True, waits=wts,
          inc=s["nmT"])
        O("vector", "tensor_copy", negm[:], PA[2][:, 0:32],
          waits=[(s["nmT"], None)], inc=s["nm"])
        pt2_free[0] = (s["nm"], s["nm"].n)
        for qp in range(8):
            for hh in range(4):
                wts = []
                if hh == 0:
                    wts = [(s["collQ"], 1)]
                    if qp > 0:
                        wts.append((s["pvd"], None))
                    else:
                        wts.append((s["qn"], None))
                O("scalar", "dma_start", out=qsb[:, hh * 512:(hh + 1) * 512],
                  in_=q_out[512 * qp + 128 * hh: 512 * qp + 128 * (hh + 1),
                            512 * b: 512 * (b + 1)],
                  waits=wts, inc=s["qld"], inc_amt=16)
            for hpair in range(2):
                pend = []

                def flush_one():
                    nonlocal pend, n_pv
                    if not pend:
                        return
                    pd = pend.pop(0)
                    O("tensor", "matmul",
                      PA[1][0:1, pd["h2"] * 512:(pd["h2"] + 1) * 512],
                      onescol[:], pd["puse"], start=pd["first"],
                      stop=pd["last"], skip_group_check=True,
                      waits=pd["lw"])
                    O("tensor", "matmul",
                      PA[3][:, pd["h2"] * 512:(pd["h2"] + 1) * 512],
                      pd["vtile"], pd["puse"], start=pd["first"],
                      stop=pd["last"], skip_group_check=True,
                      waits=[pd["vw"]], inc=s["pvd"])
                    n_pv += 1
                    if pd["tile_of_tk_done"] is not None:
                        vt_slot_free[pd["tile_of_tk_done"]] = (s["pvd"], n_pv)
                    if pd["ptidx"] is not None:
                        pt_slot_free[pd["ptidx"]] = (s["pvd"], n_pv)
                    if pd["pmidx"] is not None:
                        pm_slot_free[pd["pmidx"]] = (s["pvd"], n_pv)

                for tk in range(qp + 1):
                    vslot = vt_n % 2
                    wts = [(s["collKV"], 1)]
                    if vt_n >= 2:
                        wts.append(vt_slot_free[vt_n - 2])
                    if first_vt:
                        wts.append((s["tfm"], None))
                        first_vt = False
                    O("scalar", "dma_start",
                      out=vt[vslot][:].rearrange("p (t c) -> p t c", c=128),
                      in_=kv_out[1024 * tk + 512 * b: 1024 * tk + 512 * b + 512,
                                 128:256].rearrange("(t p) c -> p t c", p=128),
                      waits=wts, inc=s[f"vt{vslot}"], inc_amt=16)
                    for tb in range(4):
                        for h2 in range(2):
                            if len(pend) >= 2:
                                flush_one()
                            if deferq:
                                deferq.pop(0)()
                            hh = hpair * 2 + h2
                            first = (tk == 0 and tb == 0)
                            last = (tk == qp and tb == 3)
                            scs = sc_i % 4
                            sap, soff = SCSLOT[scs]
                            wts = list(pt2_wait(scs))
                            if first and tb == 0 and h2 == 0:
                                wts.append((s["qld"], s["qld"].n))
                                if hpair == 0:
                                    wts.append((s["nm"], None))
                                    wts.append((s["kres"], (b * 8 + 8) * 16))
                            O("tensor", "matmul",
                              sap[:, soff:soff + 512],
                              K_res[:, tk * 512 + tb * 128:
                                    tk * 512 + (tb + 1) * 128],
                              qsb[:, hh * 512:(hh + 1) * 512],
                              start=True, stop=True, skip_group_check=True,
                              waits=wts, inc=s["sc"])
                            ptidx = pt_i
                            pts = ptidx % 5
                            ew = [(s["sc"], None)]
                            if ptidx >= 5:
                                ew.append(pt_slot_free[ptidx - 5])
                            O("scalar", "activation", pt[pts][:],
                              sap[:, soff:soff + 512], AF.Exp,
                              bias=negm[:, qp * 4 + hh: qp * 4 + hh + 1],
                              waits=ew, inc=s["exp"])
                            pt2_free[scs] = (s["exp"], s["exp"].n)
                            sc_i += 1
                            pt_i += 1
                            pmidx = None
                            if tk == qp:
                                pms = pm_i % 2
                                mw = [(s["exp"], None)]
                                if pm_i >= 2:
                                    mw.append(pm_slot_free[pm_i - 2])
                                O("vector", "tensor_mul", pm[pms][:], pt[pts][:],
                                  masks[:, tb * 512:(tb + 1) * 512],
                                  waits=mw, inc=s["msk"])
                                n_msk += 1
                                pt_slot_free[ptidx] = (s["msk"], n_msk)
                                pmidx = pm_i
                                pm_i += 1
                                puse = pm[pms][:]
                                pwait = [(s["msk"], s["msk"].n)]
                                ptrec = None
                            else:
                                puse = pt[pts][:]
                                pwait = [(s["exp"], s["exp"].n)]
                                ptrec = ptidx
                            lw = list(pwait)
                            if first:
                                lw.append((s["aev"], s["aev"].n))
                                if b == 0 and qp == 0 and hpair == 0:
                                    lw.append((s["rtr"], s["rtr"].n))
                            cur = dict(h2=h2, puse=puse, first=first, last=last,
                                       lw=lw,
                                       vtile=vt[vslot][:, tb * 128:(tb + 1) * 128],
                                       vw=(s[f"vt{vslot}"], (vt_n // 2 + 1) * 16),
                                       ptidx=ptrec, pmidx=pmidx,
                                       tile_of_tk_done=(
                                           vt_n if (tb == 3 and h2 == 1) else None))
                            pend.append(cur)
                    vt_n += 1
                while pend:
                    flush_one()
                for h2 in range(2):
                    hh = hpair * 2 + h2
                    O("vector", "reciprocal",
                      qn_row[0:1, h2 * 512:(h2 + 1) * 512],
                      PA[1][0:1, h2 * 512:(h2 + 1) * 512],
                      waits=[(s["pvd"], None)], inc=s["aev"])
                    O("vector", "tensor_copy", linvb[:, h2 * 512:(h2 + 1) * 512],
                      PA[3][:, h2 * 512:(h2 + 1) * 512], inc=s["aev"])
                    aouse = (b * 8 + qp) * 2 + hpair

                    def d_bcast_mul(h2=h2, aev_n=s["aev"].n, aouse=aouse):
                        O("tensor", "matmul",
                          PA[0][:, h2 * 512:(h2 + 1) * 512],
                          onesrow_f[:], qn_row[0:1, h2 * 512:(h2 + 1) * 512],
                          start=True, stop=True, skip_group_check=True,
                          waits=[(s["aev"], aev_n)] + pt2_wait(2 + h2),
                          inc=s["nmT"])
                        O("vector", "tensor_mul", aosb[h2][:],
                          PA[0][:, h2 * 512:(h2 + 1) * 512],
                          linvb[:, h2 * 512:(h2 + 1) * 512],
                          waits=[(s["nmT"], None),
                                 (s[f"aos{h2}"], 16 * aouse)],
                          inc=s["aev"])
                        pt2_free[2 + h2] = (s["aev"], s["aev"].n)

                    def d_store(h2=h2, hh=hh, qp=qp, b=b):
                        O("gpsimd", "dma_start",
                          out=ao_in[b][512 * qp + 128 * hh:
                                       512 * qp + 128 * (hh + 1), :],
                          in_=aosb[h2][:], waits=[(s["aev"], None)],
                          inc=s[f"aos{h2}"], inc_amt=16)
                    deferq.extend([d_bcast_mul, d_store])
    while deferq:
        deferq.pop(0)()
    P.coll("AllToAll", ALU.bypass, RG, [ao_in[1].opt()], [ao_out[1].opt()],
           waits=[(s["aos0"], None), (s["aos1"], None)], inc=s["collAO1"])

    if PHASES <= 6:
        return _finish()

    for j in range(NCORE):
        for hp in range(4):
            h = 4 * j + hp
            O("sync", "dma_start",
              out=attn_fm[:, h * NTOK + 512: h * NTOK + 1024],
              in_=ao_out[1][512 * j + 128 * hp: 512 * j + 128 * (hp + 1), :],
              waits=([(s["collAO1"], 1)] if (j == 0 and hp == 0) else []),
              inc=s["afm1"], inc_amt=16)
    N_AFM = s["afm"].n
    N_AFM1 = s["afm1"].n
    P.fence()
    O("gpsimd", "dma_start", out=fbb[:], in_=fb,
      waits=[(s["pvd"], None), (s["exp"], None), (s["msk"], None),
             (s["aev"], None), (s["qld"], None), (s["aos0"], None),
             (s["aos1"], None)],
      inc=s["misc"], inc_amt=16)
    O("gpsimd", "dma_start", out=bcat[:], in_=B2T,
      waits=[(s["pvd"], None), (s["aev"], None), (s["msk"], None),
             (s["qn"], None), (s["afm"], N_AFM), (s["afm1"], N_AFM1)],
      inc=s["misc"], inc_amt=16)
    N_MISC = s["misc"].n

    if AFM_ONLY:
        return _finish()
    OUTB = [(PA[i // 2], (i % 2) * 512) for i in range(8)]
    for o in range(OLIM):
        for j in range(NJ):
            gidx = o * NJ + j
            O("gpsimd", "dma_start", out=wo_t[j % 4][:],
              in_=Wo[j * 128:(j + 1) * 128, o * 512:(o + 1) * 512],
              waits=([(s["woF"], gidx - 3)] if gidx >= 4 else []),
              inc=s[f"wo{j % 4}"], inc_amt=16, future=True)
        if FINMODE >= 2:
            for j in range(NJ):
                gidx = o * NJ + j
                O("gpsimd", "dma_start", out=wf_t[j % 4][:],
                  in_=WfT[j * 128:(j + 1) * 128, o * 512:(o + 1) * 512],
                  waits=([(s["wfF"], gidx - 3)] if gidx >= 4 else []),
                  inc=s[f"wf{j % 4}"], inc_amt=16, future=True)
        if FINMODE >= 3:
            for tt in range(NT):
                nxr = o * NT + tt
                O("scalar", "dma_start", out=xres[tt % 2][:],
                  in_=x_my[tt * 128:(tt + 1) * 128, o * 512:(o + 1) * 512],
                  waits=([(s["ouV"], nxr - 1)] if nxr >= 2 else []),
                  inc=s[f"xr{tt % 2}"], inc_amt=16, future=True)
        for j in range(NJ):
            for tt in range(NT):
                pa, po = OUTB[tt]
                wts = []
                if tt == 0:
                    wts.append((s[f"wo{j % 4}"], ((o * NJ + j) // 4 + 1) * 16))
                    if j == 0:
                        wts.append((s["afm"], N_AFM))
                        wts.append((s["afm1"], N_AFM1))
                        if o == 0:
                            wts.append((s["nmT"], None))
                            wts.append((s["aev"], None))
                            wts.append((s["exp"], None))
                        elif FINMODE >= 3:
                            wts.append((s["ouV"], o * 8))
                O("tensor", "matmul", pa[:, po:po + 512],
                  attn_fm[:, j * NTOK + tt * 128: j * NTOK + (tt + 1) * 128],
                  wo_t[j % 4][:], start=(j == 0), stop=False,
                  skip_group_check=True, waits=wts,
                  inc=(s["woF"] if tt == NT - 1 else None))
        if FINMODE >= 2:
            for j in range(NJ):
                for tt in range(NT):
                    pa, po = OUTB[tt]
                    wts = []
                    if tt == 0:
                        wts.append((s[f"wf{j % 4}"], ((o * NJ + j) // 4 + 1) * 16))
                    O("tensor", "matmul", pa[:, po:po + 512],
                      xn_fm[:, j * NTOK + tt * 128: j * NTOK + (tt + 1) * 128],
                      wf_t[j % 4][:], start=False, stop=False,
                      skip_group_check=True, waits=wts,
                      inc=(s["wfF"] if tt == NT - 1 else None))
        if FINMODE >= 3:
            for tt in range(NT):
                pa, po = OUTB[tt]
                wts = [(s["misc"], N_MISC), (s["tfm"], None)] if tt == 0 else []
                O("tensor", "matmul", pa[:, po:po + 512],
                  tsc[:, tt * 128:(tt + 1) * 128], bcat[:, o * 512:(o + 1) * 512],
                  start=False, stop=False, skip_group_check=True, waits=wts)
                O("tensor", "matmul", pa[:, po:po + 512],
                  onesrow[:], fbb[0:1, o * 512:(o + 1) * 512],
                  start=False, stop=True, skip_group_check=True, inc=s["ou"])
            for tt in range(NT):
                pa, po = OUTB[tt]
                nxr2 = o * NT + tt
                O("vector", "tensor_add", outsb[tt % 2][:], pa[:, po:po + 512],
                  xres[tt % 2][:],
                  waits=[(s["ou"], o * NT + tt + 1),
                         (s[f"xr{tt % 2}"], (nxr2 // 2 + 1) * 16),
                         (s[f"ouF{tt % 2}"], (nxr2 // 2) * 16)],
                  inc=s["ouV"])
                O("sync", "dma_start",
                  out=out_my[tt * 128:(tt + 1) * 128, o * 512:(o + 1) * 512],
                  in_=outsb[tt % 2][:], waits=[(s["ouV"], None)],
                  inc=s[f"ouF{tt % 2}"], inc_amt=16)
    P.cmds["sync"].append(lambda e, h=s["ouF0"].h, v=s["ouF0"].n: e.wait_ge(h, v))
    P.cmds["sync"].append(lambda e, h=s["ouF1"].h, v=s["ouF1"].n: e.wait_ge(h, v))

    if dbg:
        for j in range(NJ):
            O("vector", "tensor_copy", linvb[:, 0:1024].bitcast(BF16)[:, 0:1024]
              if False else linvb[:, 0:512],
              xn_fm[:, j * NTOK: j * NTOK + 512], waits=[(s["ouF0"], None), (s["ouF1"], None)])
            O("vector", "tensor_copy", linvb[:, 512:1024],
              xn_fm[:, j * NTOK + 512: (j + 1) * NTOK], inc=s["dbgV"])
            O("gpsimd", "dma_start",
              out=dbg_o["d_xnfm"][:, j * NTOK:(j + 1) * NTOK],
              in_=linvb[:], waits=[(s["dbgV"], None)], inc=s["dbg"], inc_amt=16)
            O("vector", "tensor_copy", linvb[:, 0:512],
              attn_fm[:, j * NTOK: j * NTOK + 512], waits=[(s["dbg"], None)])
            O("vector", "tensor_copy", linvb[:, 512:1024],
              attn_fm[:, j * NTOK + 512: (j + 1) * NTOK], inc=s["dbgV"])
            O("gpsimd", "dma_start",
              out=dbg_o["d_attn"][:, j * NTOK:(j + 1) * NTOK],
              in_=linvb[:], waits=[(s["dbgV"], None)], inc=s["dbg"], inc_amt=16)
        O("gpsimd", "dma_start", out=dbg_o["d_q"], in_=q_out,
          waits=[(s["collQ"], 1)], inc=s["dbg"], inc_amt=16)
        O("gpsimd", "dma_start", out=dbg_o["d_kv"], in_=kv_out,
          waits=[(s["collKV"], 1)], inc=s["dbg"], inc_amt=16)
        for bb in range(2):
            O("gpsimd", "dma_start",
              out=dbg_o["d_ao"][bb * 4096:(bb + 1) * 4096, :],
              in_=ao_out[bb],
              waits=[(s["collAO" if bb == 0 else "collAO1"], 1)],
              inc=s["dbg"], inc_amt=16)
        P.cmds["gpsimd"].append(
            lambda e, h=s["dbg"].h, v=s["dbg"].n: e.wait_ge(h, v))

    return _finish()



def _consts():
    c = {}
    c["identb"] = np.eye(128, dtype=ml_dtypes.bfloat16)
    c["identf"] = np.eye(128, dtype=np.float32)
    c["onescol"] = np.ones((128, 1), dtype=ml_dtypes.bfloat16)
    c["onesrow"] = np.ones((1, 128), dtype=ml_dtypes.bfloat16)
    c["onesrow_f"] = np.ones((1, 128), dtype=np.float32)
    rp = np.zeros((8, 128), dtype=np.float32)
    for e in range(8):
        rp[e, e * 16:(e + 1) * 16] = 2.0
    c["rep_f"] = rp
    m = np.zeros((512, 512), dtype=np.float32)
    i = np.arange(128)[:, None]
    jj = np.arange(512)[None, :]
    for blk in range(4):
        m[blk * 128:(blk + 1) * 128] = (blk * 128 + i <= jj)
    c["masks"] = m.astype(ml_dtypes.bfloat16)
    return c


def _rope_tables(core):
    pos = np.arange(512 * core, 512 * core + 512, dtype=np.float64)
    inv = 1.0 / (THETA ** (np.arange(0, HD, 2, dtype=np.float64) / HD))
    ang = pos[None, :] * inv[:, None]
    cos = np.cos(ang); sin = np.sin(ang)
    cosf = np.concatenate([cos, cos], 0)
    sinf = np.concatenate([-sin, sin], 0)
    cosf = np.concatenate([cosf, cosf], 1)
    sinf = np.concatenate([sinf, sinf], 1)
    sc = 1.0 / np.sqrt(HD)
    return (np.asarray(cosf * sc, np.float32), np.asarray(sinf * sc, np.float32),
            np.asarray(cosf, np.float32), np.asarray(sinf, np.float32))


def kernel(L0_attn_x, L0_attn_Wq, L0_attn_Wk, L0_attn_Wv, L0_attn_Wo,
           L0_ffn_W, L0_ffn_b, L0_ffn_A, L0_ffn_B, L0_ffn_router_W,
           L0_ffn_router_b):
    x = np.ascontiguousarray(np.asarray(L0_attn_x, np.float32))
    Wq_ = np.ascontiguousarray(np.asarray(L0_attn_Wq, np.float32))
    Wk_ = np.ascontiguousarray(np.asarray(L0_attn_Wk, np.float32))
    Wv_ = np.ascontiguousarray(np.asarray(L0_attn_Wv, np.float32))
    Wo_ = np.ascontiguousarray(np.asarray(L0_attn_Wo, np.float32))
    WfT_ = np.ascontiguousarray(np.asarray(L0_ffn_W, np.float32).T)
    fb_ = np.ascontiguousarray(np.asarray(L0_ffn_b, np.float32)).reshape(1, D)
    A2T_ = np.ascontiguousarray(np.asarray(L0_ffn_A, np.float32).reshape(128, D).T)
    B2T_ = np.ascontiguousarray(
        np.asarray(L0_ffn_B, np.float32).transpose(0, 2, 1).reshape(128, 4096))
    rW_ = np.ascontiguousarray(np.asarray(L0_ffn_router_W, np.float32))
    rb_ = np.ascontiguousarray(np.asarray(L0_ffn_router_b, np.float32))

    rwt_ = np.ascontiguousarray(
        rW_.T.reshape(32, 128, 8).transpose(1, 0, 2).reshape(128, 256))
    rb2_ = np.ascontiguousarray(np.tile(rb_.reshape(1, 8), (2, 1)))

    consts = _consts()
    shared = dict(Wq=Wq_, Wk=Wk_, Wv=Wv_, Wo=Wo_, WfT=WfT_, fb=fb_, A2T=A2T_,
                  B2T=B2T_, rwt=rwt_, rb2=rb2_, **consts)

    in_maps = []
    for c in range(NCORE):
        cq, sq, ck, sk = _rope_tables(c)
        x_myc = np.ascontiguousarray(
            np.concatenate([x[0, 512 * c:512 * (c + 1), :],
                            x[1, 512 * c:512 * (c + 1), :]], axis=0))
        rt = np.zeros((1, NTOK), np.float32)
        lo = max(Q_LO, 512 * c) - 512 * c
        hi = min(Q_HI, 512 * c + 512) - 512 * c
        if hi > lo:
            rt[0, lo:hi] = 1.0
            rt[0, 512 + lo:512 + hi] = 1.0
        m = dict(shared)
        m.update(x_my=x_myc, cosq=cq, sinq=sq, cosk=ck, sink=sk,
                 rtok=rt.astype(ml_dtypes.bfloat16))
        in_maps.append(m)

    nc = build_program(dbg=DBG)
    res = run_bass_kernel_spmd(nc, in_maps, core_ids=list(range(NCORE)))

    out = np.empty((2, SEQ, D), dtype=np.float32)
    for c in range(NCORE):
        o = res.results[c]["out_my"]
        out[0, 512 * c:512 * (c + 1), :] = o[0:512, :]
        out[1, 512 * c:512 * (c + 1), :] = o[512:1024, :]
    if DBG:
        kernel._dbg = res
    return out
